# version 04
# baseline (speedup 1.0000x reference)
import numpy as np
import ml_dtypes
import concourse.bass as bass
import concourse.mybir as mybir
from concourse.bass_utils import run_bass_kernel_spmd

F32 = mybir.dt.float32
BF16 = mybir.dt.bfloat16
AF = mybir.ActivationFunctionType
ALU = mybir.AluOpType
AX = mybir.AxisListType
BF = ml_dtypes.bfloat16

NEG = -1.0e30


class _Op:
    __slots__ = ("eng", "fn", "reads", "writes", "dma_key", "idx", "deps",
                 "inc", "cnt", "tag")
    def __init__(self):
        self.deps = set()
        self.inc = False
        self.cnt = None
        self.dma_key = None
        self.tag = ""

class Prog:
    COMPUTE = ("tensor", "vector", "scalar", "gpsimd", "sync")

    def __init__(self, nc):
        self.nc = nc
        self.ops = []
        self.wmap = {}
        self.rmap = {}

    def op(self, eng, fn, reads=(), writes=(), dma_key=None, tag=""):
        idx = len(self.ops)
        o = _Op()
        o.eng = eng; o.fn = fn; o.reads = tuple(reads); o.writes = tuple(writes)
        o.dma_key = dma_key; o.idx = idx; o.tag = tag
        self.ops.append(o)
        dep = self._dep
        for r in o.reads:
            j = self.wmap.get(r)
            if j is not None:
                dep(o, j)
            self.rmap.setdefault(r, []).append(idx)
        for r in o.writes:
            j = self.wmap.get(r)
            if j is not None:
                dep(o, j)
            for k in self.rmap.get(r, ()):
                if k != idx:
                    dep(o, k)
            self.wmap[r] = idx
            self.rmap[r] = []
        return idx

    def _dep(self, o, j):
        if j == o.idx:
            return
        oj = self.ops[j]
        if o.eng == "tensor" and oj.eng == "tensor":
            return
        o.deps.add(j)

    def dma(self, eng, out, in_, key, reads=(), writes=(), **kw):
        return self.op(eng, lambda e: e.dma_start(out=out, in_=in_, **kw),
                       reads=reads, writes=writes, dma_key=key)

    def fence(self, reads, writes, eng="sync"):
        return self.op(eng, lambda e: e.nop(), reads=list(reads), writes=list(writes))

    def barrier_all_dma(self):
        deps = set()
        for o in self.ops:
            if o.dma_key is not None:
                deps.add(o.idx)
        idx = self.op("sync", None)
        self.ops[idx].deps = deps
        return idx

    def emit(self):
        nc = self.nc
        for o in self.ops:
            for j in o.deps:
                self.ops[j].inc = True
        eng_sem = {}
        eng_cnt = {}
        dma_sem = {}
        dma_cnt = {}
        for o in self.ops:
            if o.dma_key is not None:
                if o.dma_key not in dma_sem:
                    dma_sem[o.dma_key] = nc.alloc_semaphore(f"d_{o.dma_key}"[:60])
                    dma_cnt[o.dma_key] = 0
                dma_cnt[o.dma_key] += 16
                o.cnt = dma_cnt[o.dma_key]
                o.inc = True
            elif o.inc:
                if o.eng not in eng_sem:
                    eng_sem[o.eng] = nc.alloc_semaphore(f"e_{o.eng}")
                    eng_cnt[o.eng] = 0
                eng_cnt[o.eng] += 1
                o.cnt = eng_cnt[o.eng]
        def sem_of(o):
            return dma_sem[o.dma_key] if o.dma_key is not None else eng_sem[o.eng]
        streams = {}
        for o in self.ops:
            streams.setdefault(o.eng, []).append(o)
        ops_by_idx = self.ops
        with nc.Block() as blk:
            for eng_name, olist in streams.items():
                def section(e, olist=olist):
                    water = {}
                    for o in olist:
                        waits = {}
                        for j in o.deps:
                            oj = ops_by_idx[j]
                            s = sem_of(oj)
                            v = oj.cnt
                            if waits.get(s, 0) < v:
                                waits[s] = v
                        for s, v in waits.items():
                            if water.get(s, 0) >= v:
                                continue
                            e.wait_ge(s, v)
                            water[s] = v
                        if o.fn is None:
                            continue
                        ins = o.fn(e)
                        if o.inc:
                            ins.then_inc(sem_of(o), 16 if o.dma_key is not None else 1)
                getattr(blk, eng_name)(section)
        return dma_cnt


def ceil_div(a, b):
    return (a + b - 1) // b


def emit_ln(p, ctx, xdram, stat, out_emit, tag="", src_res=None):
    nc, T = ctx["nc"], ctx["T"]
    NS = T // 512
    scr = ctx["scrA"]
    S_ps = ctx["S_ps"]
    P2 = ctx["P2_ps"]
    ones_f = ctx["ones_f32"]
    xst = ctx["xstage"]
    for c in range(8):
        for s in range(NS):
            k = (c * NS + s) % 4
            xa = xst[:, 512 * k:512 * k + 512]
            xr = f"xs{k}"
            sl = slice(512 * s, 512 * s + 512)
            p.dma("sync", xa, xdram[c * 128:(c + 1) * 128, sl], xr, writes=[xr],
                  reads=(src_res(c, s) if src_res else []))
            p.op("scalar", lambda e, xa=xa, sl=sl: e.activation(scr[:, sl], xa, AF.Square),
                 reads=[xr], writes=[f"scrA{s}"], tag=f"ln{tag}.sq{c}.{s}")
            p.op("tensor", lambda e, xa=xa, sl=sl, c=c: e.matmul(
                S_ps[0:1, sl], ones_f[:, 0:1], xa, start=(c == 0), stop=(c == 7)),
                reads=[xr, "cones"], writes=[f"S{s}"], tag=f"ln{tag}.sum{c}.{s}")
            p.op("tensor", lambda e, sl=sl, c=c: e.matmul(
                S_ps[32:33, sl], ones_f[:, 0:1], scr[:, sl], start=(c == 0), stop=(c == 7)),
                reads=[f"scrA{s}", "cones"], writes=[f"S{s}"], tag=f"ln{tag}.sqs{c}.{s}")
    mu = stat[0:1, 0:T]
    m2 = scr[0:1, 0:T]
    rs = stat[0:1, T:2 * T]
    Sres = [f"S{s}" for s in range(NS)]
    scrres = [f"scrA{s}" for s in range(NS)]
    p.op("vector", lambda e: e.tensor_scalar_mul(mu, S_ps[0:1, 0:T], 1.0 / 1024.0),
         reads=Sres, writes=["stat.mu"], tag=f"ln{tag}.mu")
    p.op("vector", lambda e: e.tensor_mul(m2, mu, mu),
         reads=["stat.mu"] + scrres, writes=scrres, tag=f"ln{tag}.m2")
    p.op("vector", lambda e: e.scalar_tensor_tensor(
        m2, S_ps[32:33, 0:T], 1.0 / 1024.0, m2, op0=ALU.mult, op1=ALU.subtract),
        reads=Sres + scrres, writes=scrres, tag=f"ln{tag}.var")
    p.op("scalar", lambda e: e.activation(rs, m2, AF.Sqrt, bias=ctx["cv"](1e-5, 1)),
         reads=scrres + ["cv"], writes=["stat.rs"], tag=f"ln{tag}.sqrt")
    p.op("vector", lambda e: e.reciprocal(rs, rs),
         reads=["stat.rs"], writes=["stat.rs"], tag=f"ln{tag}.recip")
    for c in range(8):
        for s in range(NS):
            k = (c * NS + s) % 4
            xa = xst[:, 512 * k:512 * k + 512]
            xr = f"xs{k}"
            sl = slice(512 * s, 512 * s + 512)
            p.dma("sync", xa, xdram[c * 128:(c + 1) * 128, sl], xr, writes=[xr],
                  reads=(src_res(c, s) if src_res else []))
            mu_b = P2[:, 0:512]
            p.op("tensor", lambda e, sl=sl, mu_b=mu_b: e.matmul(
                mu_b, ones_f[0:1, 0:128], mu[:, sl], start=True, stop=True),
                reads=["stat.mu", "cones"], writes=["P0"], tag=f"ln{tag}.bmu{c}.{s}")
            rs_b = P2[:, 512:1024]
            p.op("tensor", lambda e, sl=sl, rs_b=rs_b: e.matmul(
                rs_b, ones_f[0:1, 0:128], rs[:, sl], start=True, stop=True),
                reads=["stat.rs", "cones"], writes=["P1"], tag=f"ln{tag}.brs{c}.{s}")
            out_emit(p, c, s, xa, xr, (mu_b, "P0"), (rs_b, "P1"))


def emit_ln_to_bf16(p, ctx, xdram, stat, xn, xn_res_prefix, tag="0", fp32dst=None,
                    src_res=None):
    T = ctx["T"]
    scrB = ctx["scrB"]

    def apply(p, c, s, xa, xr, mu_bi, rs_bi):
        mu_b, mu_r = mu_bi
        rs_b, rs_r = rs_bi
        sl = slice(512 * s, 512 * s + 512)
        t1 = scrB[:, sl]
        p.op("vector", lambda e, xa=xa, t1=t1, mu_b=mu_b: e.scalar_tensor_tensor(
            t1, xa, 1.0, mu_b, op0=ALU.mult, op1=ALU.subtract),
            reads=[xr, mu_r], writes=["scrB"], tag=f"lnap{tag}.sub{c}.{s}")
        if fp32dst is None:
            dst = xn[:, c * T + 512 * s: c * T + 512 * s + 512]
            rr = f"{xn_res_prefix}{c}"
        else:
            ap, stride, guard = fp32dst
            dst = ap[:, c * stride + guard + 512 * s: c * stride + guard + 512 * s + 512]
            rr = f"{xn_res_prefix}{c}"
        p.op("vector", lambda e, dst=dst, t1=t1, rs_b=rs_b: e.tensor_mul(dst, t1, rs_b),
             reads=["scrB", rs_r], writes=[rr], tag=f"lnap{tag}.mul{c}.{s}")

    emit_ln(p, ctx, xdram, stat, apply, tag=tag, src_res=src_res)



MTOP = 229344
RB_SZ = 88 * 1024
RB0 = MTOP - RB_SZ
RZ_SZ = 8 * (2048 + 1) * 4
RZ0 = RB0 - RZ_SZ

def alloc_at(nc, name, shape, dtype, off):
    return nc.alloc_sbuf_tensor_at(name, shape, dtype, offset=off).ap()

def make_ctx(nc, T, xstage_at=None):
    ctx = {"nc": nc, "T": T}
    ctx["S_ps"] = nc.alloc_psum_tensor("S_ps", [128, 2048], F32).ap()
    ctx["P2_ps"] = nc.alloc_psum_tensor("P2_ps", [128, 1024], F32).ap()
    ctx["TR_ps"] = nc.alloc_psum_tensor("TR_ps", [128, 2048], BF16).ap()
    ctx["scrA"] = nc.alloc_sbuf_tensor("scrA", [128, max(T, 2048)], F32).ap()
    if xstage_at is None:
        ctx["xstage"] = nc.alloc_sbuf_tensor("xstage", [128, 2048], F32).ap()
    else:
        ctx["xstage"] = alloc_at(nc, "xstage", [128, 2048], F32, xstage_at)
    ctx["scrB"] = nc.alloc_sbuf_tensor("scrB", [128, max(T, 2048)], F32).ap()
    ctx["stats2"] = nc.alloc_sbuf_tensor("stats2", [128, 64], F32).ap()
    ctx["Pbuf"] = nc.alloc_sbuf_tensor("Pbuf", [128, max(T, 2048)], BF16).ap()
    ctx["Pbuf2"] = nc.alloc_sbuf_tensor("Pbuf2", [128, max(T, 2048)], BF16).ap()
    ctx["ptsb"] = nc.alloc_sbuf_tensor("ptsb", [128, 2048], BF16).ap()
    CVALS = [1e-5, 1.1920929e-07, 0.0]
    ctx["cvals_list"] = CVALS
    cv = np.tile(np.asarray(CVALS, np.float32)[None, :], (128, 1))
    ctx["cv_dram"] = nc.inline_tensor(cv, "cvals").ap()
    ctx["cv_sb"] = nc.alloc_sbuf_tensor("cvals_s", [128, len(CVALS)], F32).ap()
    def cvap(val, parts=128):
        k = CVALS.index(val)
        return ctx["cv_sb"][0:parts, k:k + 1]
    ctx["cv"] = cvap
    ones = np.ones((128, 128), np.float32)
    ctx["ones_f32_dram"] = nc.inline_tensor(ones, "ones_f32").ap()
    ctx["ones_bf_dram"] = nc.inline_tensor(ones.astype(BF), "ones_bf").ap()
    ctx["ident_dram"] = nc.inline_tensor(np.eye(128, dtype=BF), "ident_bf").ap()
    ctx["ones_f32"] = nc.alloc_sbuf_tensor("ones_f32s", [128, 128], F32).ap()
    ctx["ones_bf"] = nc.alloc_sbuf_tensor("ones_bfs", [128, 128], BF16).ap()
    ctx["ident"] = nc.alloc_sbuf_tensor("idents", [128, 128], BF16).ap()
    idf = np.eye(128, dtype=np.float32)
    ctx["identf_dram"] = nc.inline_tensor(idf, "identf").ap()
    ctx["identf"] = nc.alloc_sbuf_tensor("identfs", [128, 128], F32).ap()
    return ctx

def load_consts(p, ctx):
    p.dma("sync", ctx["cv_sb"][:, :], ctx["cv_dram"][:, :], "cv",
          writes=["cv"])
    p.dma("sync", ctx["ones_f32"][:, :], ctx["ones_f32_dram"][:, :], "cones",
          writes=["cones"])
    p.dma("sync", ctx["ones_bf"][:, :], ctx["ones_bf_dram"][:, :], "cones_bf",
          writes=["cones_bf"])
    p.dma("sync", ctx["ident"][:, :], ctx["ident_dram"][:, :], "cident",
          writes=["cident"])
    p.dma("sync", ctx["identf"][:, :], ctx["identf_dram"][:, :], "identf", writes=["identf"])


def prep_l0(inp, T):
    D = 1024
    Wq = inp["L0_attn_Wq"]; Wk = inp["L0_attn_Wk"]; Wv = inp["L0_attn_Wv"]
    Wo = inp["L0_attn_Wo"]
    r = np.arange(512)
    h = 4 * ((r % 128) // 32) + r // 128
    j = r % 32
    pidx = 32 * h + j
    permE = 2 * pidx
    qperm = np.concatenate([permE, permE + 1])
    Wq_p = np.ascontiguousarray(Wq[:, qperm])
    rk = np.arange(128)
    gk = rk // 32; jk = rk % 32
    kpermE = 64 * gk + 2 * jk
    kperm = np.concatenate([kpermE, kpermE + 1])
    Wk_p = np.ascontiguousarray(Wk[:, kperm])
    def fmaj(W):
        nf = W.shape[1] // 128
        return np.ascontiguousarray(W.reshape(W.shape[0], nf, 128).transpose(1, 0, 2))
    out = {
        "l0_wq": fmaj(Wq_p), "l0_wk": fmaj(Wk_p), "l0_wv": np.ascontiguousarray(Wv),
        "l0_wo": fmaj(Wo),
        "l0_w1": fmaj(inp["L0_ffn_w1"]), "l0_w2": fmaj(inp["L0_ffn_w2"]),
        "l0_b1": np.ascontiguousarray(inp["L0_ffn_b1"].reshape(32, 128).T),
        "l0_b2": np.ascontiguousarray(inp["L0_ffn_b2"].reshape(8, 128).T),
    }
    t = np.arange(T, dtype=np.float64)
    inv = 1.0 / (10000.0 ** (np.arange(0, D, 2, dtype=np.float64) / D))
    ang_q = t[None, :] * inv[pidx][:, None]
    out["l0_ropeq"] = np.stack([np.cos(ang_q) / 32.0, np.sin(ang_q) / 32.0]).astype(np.float32)
    ang_k = t[None, :] * inv[jk][:, None]
    out["l0_ropek"] = np.stack([np.cos(ang_k), np.sin(ang_k)]).astype(np.float32)
    return out


def declare_l0(nc, T):
    d = {}
    def dram(name, shape):
        d[name] = nc.dram_tensor(name, list(shape), F32, kind="ExternalInput").ap()
    dram("l0_wq", (8, 1024, 128)); dram("l0_wk", (2, 1024, 128))
    dram("l0_wv", (1024, 256)); dram("l0_wo", (8, 1024, 128))
    dram("l0_w1", (32, 1024, 128)); dram("l0_w2", (8, 4096, 128))
    dram("l0_b1", (128, 32)); dram("l0_b2", (128, 8))
    dram("l0_ropeq", (2, 512, T)); dram("l0_ropek", (2, 128, T))
    return d


def emit_l0_attn(p, ctx, d, bufs):
    T = ctx["T"]; NS = T // 512; NT = T // 128
    S_ps = ctx["S_ps"]; P2 = ctx["P2_ps"]; TR = ctx["TR_ps"]
    scrA, scrB, scrC = ctx["scrA"], ctx["scrB"], bufs["scrC"]
    st2 = ctx["stats2"]
    xnb = bufs["xnb"]; qrope = bufs["qrope"]; krope = bufs["krope"]
    vtok = bufs["vtok"]; Pb = ctx["Pbuf"]; ptsb = ctx["ptsb"]; ctxT = bufs["ctxT"]
    ws = bufs["wslots"]
    vw = bufs["vw"]
    rc = bufs["ropec"]; rs_ = bufs["ropes"]
    ident = ctx["ident"]
    Sres = ["S0", "S1", "S2", "S3"]

    SCRA = [f"scrA{i}" for i in range(NS)]

    def wload(dram3, idx, slot, key):
        ap = dram3[idx].rearrange("(c p) f -> p c f", p=128)
        out = ws[slot][:, :].rearrange("p (c f) -> p c f", f=128)
        p.dma("gpsimd", out, ap, key, writes=[f"ws{slot}"])

    def proj_chunk(fm, slot, psum_banks=Sres, dst=None):
        for s in range(NS):
            sl = slice(512 * s, 512 * s + 512)
            for dk in range(8):
                p.op("tensor", lambda e, sl=sl, dk=dk, slot=slot: e.matmul(
                    S_ps[:, sl], ws[slot][:, 128 * dk:128 * dk + 128],
                    xnb[:, dk * T + sl.start: dk * T + sl.stop],
                    start=(dk == 0), stop=(dk == 7)),
                    reads=[f"ws{slot}", f"a{dk}"], writes=[f"S{s}"],
                    tag=f"l0.proj{fm}.{s}")

    def rope_load(tab, c, key):
        p.dma("sync", rc[:, 0:T], tab[c * 128:(c + 1) * 128, 0:T], "ropec",
              writes=["ropec"])
        p.dma("sync", rs_[:, 0:T], tab[512 + 128 * c if False else 0, 0:T], key,
              writes=["ropes"])

    def qk_phase(nf_half, wdram, tabdram, dst, dst_prefix, nrows):
        for c in range(nf_half):
            p.dma("sync", rc[:, 0:T], tabdram[0, 128 * c:128 * c + nrows, 0:T], "ropec",
                  writes=["ropec"])
            p.dma("sync", rs_[:, 0:T], tabdram[1, 128 * c:128 * c + nrows, 0:T], "ropes",
                  writes=["ropes"])
            wload(wdram, c, 0, "ws0")
            proj_chunk(c, 0)
            p.op("vector", lambda e: e.tensor_copy(scrA[0:nrows, 0:T], S_ps[0:nrows, 0:T]),
                 reads=Sres, writes=SCRA, tag="l0.evacE")
            wload(wdram, nf_half + c, 1, "ws1")
            proj_chunk(nf_half + c, 1)
            eT = slice(c * T, c * T + T)
            oT = slice((nf_half + c) * T, (nf_half + c) * T + T)
            p.op("gpsimd", lambda e: e.tensor_mul(scrB[0:nrows, 0:T], scrA[0:nrows, 0:T], rc[0:nrows, 0:T]),
                 reads=SCRA + ["ropec"], writes=["scrB"], tag="l0.r1")
            p.op("vector", lambda e: e.tensor_mul(scrC[0:nrows, 0:T], S_ps[0:nrows, 0:T], rs_[0:nrows, 0:T]),
                 reads=Sres + ["ropes"], writes=["scrC"], tag="l0.r2")
            p.op("vector", lambda e, eT=eT: e.tensor_sub(dst[0:nrows, eT], scrB[0:nrows, 0:T], scrC[0:nrows, 0:T]),
                 reads=["scrB", "scrC"], writes=[f"{dst_prefix}{c}"], tag="l0.r3")
            p.op("gpsimd", lambda e: e.tensor_mul(scrB[0:nrows, 0:T], scrA[0:nrows, 0:T], rs_[0:nrows, 0:T]),
                 reads=SCRA + ["ropes"], writes=["scrB"], tag="l0.r4")
            p.op("vector", lambda e: e.tensor_mul(scrC[0:nrows, 0:T], S_ps[0:nrows, 0:T], rc[0:nrows, 0:T]),
                 reads=Sres + ["ropec"], writes=["scrC"], tag="l0.r5")
            p.op("vector", lambda e, oT=oT: e.tensor_add(dst[0:nrows, oT], scrB[0:nrows, 0:T], scrC[0:nrows, 0:T]),
                 reads=["scrB", "scrC"], writes=[f"{dst_prefix}{nf_half + c}"], tag="l0.r6")

    qk_phase(4, d["l0_wq"], d["l0_ropeq"], qrope, "qr", 128)
    qk_phase(1, d["l0_wk"], d["l0_ropek"], krope, "kr", 128)

    p.fence([], ["scrC", "vtok"])
    p.dma("gpsimd", vw[:, :].rearrange("p (c f) -> p c f", f=256),
          d["l0_wv"].rearrange("(c p) f -> p c f", p=128), "vw", writes=["vw"])
    for ti in range(NT):
        for dk in range(8):
            p.op("tensor", lambda e, ti=ti, dk=dk: e.matmul(
                P2[:, 0:256], xnb[:, dk * T + 128 * ti: dk * T + 128 * ti + 128],
                vw[:, 256 * dk:256 * dk + 256], start=(dk == 0), stop=(dk == 7)),
                reads=[f"a{dk}", "vw"], writes=["P0"], tag=f"l0.v{ti}")
        p.op("vector", lambda e, ti=ti: e.tensor_copy(vtok[:, 256 * ti:256 * ti + 256], P2[:, 0:256]),
             reads=["P0"], writes=["vtok"], tag=f"l0.vev{ti}")

    for h in range(16):
        base = 32 * (h // 4)
        ce, co = h % 4, 4 + h % 4
        for qt in range(NT):
            qsl = slice(128 * qt, 128 * qt + 128)
            for js in range(NS):
                jsl = slice(512 * js, 512 * js + 512)
                p.op("tensor", lambda e, qsl=qsl, jsl=jsl, base=base, ce=ce: e.matmul(
                    S_ps[:, jsl],
                    qrope[base:base + 32, ce * T + qsl.start: ce * T + qsl.stop],
                    krope[base:base + 32, jsl], start=True, stop=False,
                    tile_position=(base, 0)),
                    reads=[f"qr{ce}", "kr0"], writes=[f"S{js}"], tag=f"l0.se{h}.{qt}")
                p.op("tensor", lambda e, qsl=qsl, jsl=jsl, base=base, co=co: e.matmul(
                    S_ps[:, jsl],
                    qrope[base:base + 32, co * T + qsl.start: co * T + qsl.stop],
                    krope[base:base + 32, T + jsl.start: T + jsl.stop],
                    start=False, stop=True, tile_position=(base, 0)),
                    reads=[f"qr{co}", "kr1"], writes=[f"S{js}"], tag=f"l0.so{h}.{qt}")
            pe = qt % 2
            Pba = Pb if pe == 0 else bufs["pbalt"]
            negm = st2[:, 4 * pe:4 * pe + 1]
            sums = st2[:, 4 * pe + 1:4 * pe + 2]
            rsum = st2[:, 4 * pe + 2:4 * pe + 3]
            p.op("vector", lambda e, negm=negm: e.reduce_max(negm, S_ps[:, 0:T], axis=AX.X, negate=True),
                 reads=Sres, writes=[f"negm{pe}"], tag=f"l0.max{h}.{qt}")
            p.op("scalar", lambda e, negm=negm, sums=sums, Pba=Pba: e.activation(
                Pba[:, 0:T], S_ps[:, 0:T], AF.Exp, bias=negm, accum_out=sums),
                 reads=Sres + [f"negm{pe}"], writes=[f"P.{pe}", f"sums{pe}"],
                 tag=f"l0.exp{h}.{qt}")
            p.op("vector", lambda e, sums=sums, rsum=rsum: e.reciprocal(rsum, sums),
                 reads=[f"sums{pe}"], writes=[f"rsum{pe}"], tag=f"l0.rec{h}.{qt}")
            Pb2 = ctx["Pbuf2"]
            p.op("vector", lambda e, Pb2=Pb2, Pba=Pba, rsum=rsum: e.tensor_scalar_mul(
                Pb2[:, 0:T], Pba[:, 0:T], rsum),
                 reads=[f"P.{pe}", f"rsum{pe}"], writes=["P2b"], tag=f"l0.ps{h}.{qt}")
            for jt in range(NT):
                sl2 = jt % 2
                trof = 1024 * sl2
                p.op("tensor", lambda e, jt=jt, trof=trof, Pb2=Pb2: e.transpose(
                    TR[:, trof:trof + 128], Pb2[:, 128 * jt:128 * jt + 128], ident),
                    reads=["P2b", "cident"], writes=[f"TR{sl2}"], tag=f"l0.tr{h}.{qt}.{jt}")
                p.op("vector" if jt % 2 == 0 else "scalar", lambda e, jt=jt, trof=trof:
                     (e.tensor_copy if hasattr(e, "tensor_copy") else e.copy)(
                    ptsb[:, 128 * jt:128 * jt + 128], TR[:, trof:trof + 128]),
                    reads=[f"TR{sl2}"], writes=[f"pt{jt}"], tag=f"l0.trev{h}.{qt}.{jt}")
            pvb = (P2[64 * (h % 2):64 * (h % 2) + 64, 256:384] if pe == 0
                   else P2[64 * (h % 2):64 * (h % 2) + 64, 768:896])
            pvres = "P0" if pe == 0 else "P1"
            for jt in range(NT):
                p.op("tensor", lambda e, jt=jt, h=h, pvb=pvb: e.matmul(
                    pvb,
                    vtok[:, 256 * jt + 64 * (h // 4): 256 * jt + 64 * (h // 4) + 64],
                    ptsb[:, 128 * jt:128 * jt + 128],
                    start=(jt == 0), stop=(jt == NT - 1), tile_position=(0, 64 * (h % 2))),
                    reads=["vtok", f"pt{jt}"], writes=[pvres], tag=f"l0.pv{h}.{qt}.{jt}")
            half = 64 * (h % 2)
            p.op("vector", lambda e, h=h, qsl=qsl, half=half, pvb=pvb: e.tensor_copy(
                ctxT[half:half + 64, (h // 2) * T + qsl.start: (h // 2) * T + qsl.stop],
                pvb),
                reads=[pvres], writes=[f"ct{h // 2}"], tag=f"l0.cev{h}.{qt}")

    xst = ctx["xstage"]
    xsrc = bufs["xsrc"]; xdst = bufs["xdst"]
    for e_ in range(8):
        wload(d["l0_wo"], e_, e_ % 2, f"ws{e_ % 2}")
        for s in range(NS):
            sl = slice(512 * s, 512 * s + 512)
            slot = (e_ * NS + s) % 4
            xa = xst[:, 512 * slot:512 * slot + 512]
            xr = f"xs{slot}"
            p.dma("sync", xa, xsrc[e_ * 128:(e_ + 1) * 128, sl], xr, writes=[xr])
            for ic in range(8):
                p.op("tensor", lambda e, sl=sl, ic=ic, slot2=e_ % 2: e.matmul(
                    S_ps[:, sl], ws[slot2][:, 128 * ic:128 * ic + 128],
                    ctxT[:, ic * T + sl.start: ic * T + sl.stop],
                    start=(ic == 0), stop=(ic == 7)),
                    reads=[f"ws{e_ % 2}", f"ct{ic}"], writes=[f"S{s}"],
                    tag=f"l0.wo{e_}.{s}")
            p.op("vector", lambda e, xa=xa, s=s, sl=sl: e.tensor_add(
                xa, xa, S_ps[:, sl]),
                reads=[f"S{s}", xr], writes=[xr], tag=f"l0.xadd{e_}.{s}")
            p.dma("sync", xdst[e_ * 128:(e_ + 1) * 128, sl], xa, xr, reads=[xr],
                  writes=[f"xA{e_}.{s}"])


def emit_l0_ffn(p, ctx, d, bufs):
    T = ctx["T"]; NS = T // 512
    S_ps = ctx["S_ps"]; P2 = ctx["P2_ps"]
    xnb = bufs["xnb"]; hbuf = bufs["hbuf"]
    ws = bufs["wslots"]; w2s = bufs["w2slot"]
    b1 = bufs["b1sb"]; b2 = bufs["b2sb"]
    p.dma("sync", b1[:, :], d["l0_b1"][:, :], "b1sb", writes=["b1"])
    p.dma("sync", b2[:, :], d["l0_b2"][:, :], "b2sb", writes=["b2"])
    TH = T // 1024 if T >= 1024 else 1
    HW = T // TH
    NS2 = HW // 512
    for th in range(TH):
        t0 = th * HW
        for fm in range(32):
            slot = fm % 2
            p.dma("gpsimd", ws[slot][:, :].rearrange("p (c f) -> p c f", f=128),
                  d["l0_w1"][fm].rearrange("(c p) f -> p c f", p=128),
                  f"ws{slot}", writes=[f"ws{slot}"])
            for s in range(NS2):
                sl = slice(t0 + 512 * s, t0 + 512 * s + 512)
                bank = "P0" if fm % 2 == 0 else "P1"
                pa = P2[:, 0:512] if fm % 2 == 0 else P2[:, 512:1024]
                for dk in range(8):
                    p.op("tensor", lambda e, sl=sl, dk=dk, slot=slot, pa=pa: e.matmul(
                        pa, ws[slot][:, 128 * dk:128 * dk + 128],
                        xnb[:, dk * T + sl.start: dk * T + sl.stop],
                        start=(dk == 0), stop=(dk == 7)),
                        reads=[f"ws{slot}", f"a{dk}"], writes=[bank],
                        tag=f"l0f.w1.{fm}.{s}")
                p.op("scalar", lambda e, fm=fm, s=s, pa=pa: e.activation(
                    hbuf[:, fm * HW + 512 * s: fm * HW + 512 * s + 512], pa,
                    AF.Gelu, bias=b1[:, fm:fm + 1]),
                    reads=[bank, "b1"], writes=[f"h{fm}"], tag=f"l0f.gelu.{fm}.{s}")
        for e_ in range(8):
            p.dma("gpsimd", w2s[:, :].rearrange("p (c f) -> p c f", f=128),
                  d["l0_w2"][e_].rearrange("(c p) f -> p c f", p=128),
                  "w2s", writes=["w2s"])
            for s in range(NS2):
                sl = slice(t0 + 512 * s, t0 + 512 * s + 512)
                sps = s % 4
                for k in range(32):
                    p.op("tensor", lambda e, s=s, k=k, sps=sps: e.matmul(
                        S_ps[:, 512 * sps:512 * sps + 512], w2s[:, 128 * k:128 * k + 128],
                        hbuf[:, k * HW + 512 * s: k * HW + 512 * s + 512],
                        start=(k == 0), stop=(k == 31)),
                        reads=["w2s", f"h{k}"], writes=[f"S{sps}"],
                        tag=f"l0f.w2.{e_}.{s}")
                slot = (e_ * NS2 + s) % 4
                xa = ctx["xstage"][:, 512 * slot:512 * slot + 512]
                xr = f"xs{slot}"
                p.dma("sync", xa, bufs["xsrc2"][e_ * 128:(e_ + 1) * 128, sl], xr,
                      writes=[xr], reads=[f"xA{e_}.{s}"])
                p.op("vector", lambda e, xa=xa, sps=sps, e_=e_: e.scalar_tensor_tensor(
                    xa, S_ps[:, 512 * sps:512 * sps + 512], b2[:, e_:e_ + 1],
                    xa, op0=ALU.add, op1=ALU.add),
                    reads=[f"S{sps}", "b2", xr], writes=[xr],
                    tag=f"l0f.add.{e_}.{s}")
                p.dma("sync", bufs["xdst2"][e_ * 128:(e_ + 1) * 128, sl], xa, xr,
                      reads=[xr], writes=[f"xB{e_}.{s}"])


def declare_mla(nc, T):
    d = {}
    def dram(name, shape):
        d[name] = nc.dram_tensor(name, list(shape), F32, kind="ExternalInput").ap()
    dram("mla_wdq", (512, 64)); dram("mla_wuq", (64, 512))
    dram("mla_wdkv", (1024, 64)); dram("mla_wukv", (128, 1024))
    dram("mla_wqr", (64, 64)); dram("mla_wkr", (64, 64)); dram("mla_wo", (64, 64))
    dram("mla_qw", (64, 1)); dram("mla_kw", (64, 1))
    return d


def prep_mla(inp):
    return {
        "mla_wdq": inp["L1_attn_W_DQ"], "mla_wuq": inp["L1_attn_W_UQ"],
        "mla_wdkv": inp["L1_attn_W_DKV"], "mla_wukv": inp["L1_attn_W_UKV"],
        "mla_wqr": inp["L1_attn_W_QR"], "mla_wkr": inp["L1_attn_W_KR"],
        "mla_wo": inp["L1_attn_W_O"],
        "mla_qw": np.ascontiguousarray(inp["L1_attn_q_norm_w"].reshape(64, 1)),
        "mla_kw": np.ascontiguousarray(inp["L1_attn_k_norm_w"].reshape(64, 1)),
    }


def alloc_mla_bufs(nc, ctx, T, base):
    NT = T // 128
    b = {}
    off = [base]
    def A(name, shape, dtype):
        ap = alloc_at(nc, name, shape, dtype, off[0])
        off[0] += shape[-1] * (4 if dtype == F32 else 2)
        off[0] = (off[0] + 31) & ~31
        return ap
    b["qcomb"] = A("m_qcomb", [128, T], BF16)
    b["kcomb"] = A("m_kcomb", [128, T], BF16)
    b["vf"] = A("m_vf", [128, T], BF16)
    b["vtokh"] = A("m_vtokh", [128, NT * 64], BF16)
    b["ctxh"] = A("m_ctxh", [64, T], BF16)
    b["wdq"] = A("m_wdq", [128, 256], F32)
    b["wuq"] = A("m_wuq", [64, 512], F32)
    b["wuqT"] = A("m_wuqT", [128, 256], F32)
    b["wdkv"] = A("m_wdkv", [128, 512], F32)
    b["wukv"] = A("m_wukv", [128, 1024], F32)
    b["wukvT"] = A("m_wukvT", [128, 1024], F32)
    b["mq"] = A("m_mq", [128, 64], F32)
    b["mkv"] = A("m_mkv", [128, 128], F32)
    b["wqrT"] = A("m_wqrT", [64, 64], BF16)
    b["wkrT"] = A("m_wkrT", [128, 64], F32)
    b["woT"] = A("m_woT", [64, 64], BF16)
    b["wn"] = A("m_wn", [64, 2], F32)
    b["ss"] = A("m_ss", [16, T], F32)
    b["rstd"] = A("m_rstd", [16, T], F32)
    b["pbalt"] = A("m_pbalt", [128, T], BF16)
    ctx["oh64"] = A("oh64s", [64, 256], F32)
    ctx["e16"] = A("e16s", [16, 1024], F32)
    ctx["diagm"] = A("diagms", [128, 128], F32)
    b["_end"] = off[0]
    return b


def check_sbuf(nc, limit=None):
    fill = 16481 + (212863 - nc.sbuf_bytes_remaining)
    lim = RZ0 if limit is None else limit
    assert fill <= lim, f"auto sbuf fill {fill} exceeds manual region base {lim}"


def mla_consts(nc, ctx):
    oh = np.zeros((64, 16 * 16), np.float32)
    for h in range(16):
        oh[:, 16 * h + h] = 1.0
    ctx["oh64_dram"] = nc.inline_tensor(oh, "oh64").ap()
    e16 = np.zeros((16, 16 * 64), np.float32)
    for h in range(16):
        e16[h, 64 * h:64 * h + 64] = 1.0
    ctx["e16_dram"] = nc.inline_tensor(e16, "e16").ap()
    dm = np.zeros((128, 128), np.float32)
    for i in range(128):
        dm[i, i + 1:] = NEG
    ctx["diag_dram"] = nc.inline_tensor(dm, "diagm").ap()


def load_mla_consts(p, ctx):
    p.dma("sync", ctx["oh64"][:, :], ctx["oh64_dram"][:, :], "oh64", writes=["oh64"])
    p.dma("sync", ctx["e16"][:, :], ctx["e16_dram"][:, :], "e16", writes=["e16"])
    p.dma("sync", ctx["diagm"][:, :], ctx["diag_dram"][:, :], "diagm", writes=["diagm"])



def emit_mla_setup(p, ctx, d, b):
    S_ps = ctx["S_ps"]; P2 = ctx["P2_ps"]
    idf = ctx["identf"]
    def ld(dst, src, key, rearr=None):
        p.dma("sync", dst, src, key, writes=[key])
    ld(b["wdq"][:, :].rearrange("p (c f) -> p c f", f=64),
       d["mla_wdq"].rearrange("(c p) f -> p c f", p=128), "m_wdq")
    ld(b["wuq"][:, :], d["mla_wuq"][:, :], "m_wuq")
    ld(b["wdkv"][:, :].rearrange("p (c f) -> p c f", f=64),
       d["mla_wdkv"].rearrange("(c p) f -> p c f", p=128), "m_wdkv")
    ld(b["wukv"][:, :], d["mla_wukv"][:, :], "m_wukv")
    ld(b["wn"][:, 0:1], d["mla_qw"][:, :], "m_wn")
    p.dma("sync", b["wn"][:, 1:2], d["mla_kw"][:, :], "m_wn", writes=["m_wn"])
    p.op("vector", lambda e: e.tensor_scalar_mul(b["wn"][:, 0:1], b["wn"][:, 0:1], 0.125),
         reads=["m_wn"], writes=["m_wn"], tag="mla.wq8")
    for c in range(4):
        p.op("tensor", lambda e, c=c: e.transpose(
            S_ps[:, 0:64], b["wuq"][0:64, 128 * c:128 * c + 128], idf[0:64, 0:64]),
            reads=["m_wuq", "identf"], writes=["S0"], tag=f"mla.tuq{c}")
        p.op("vector", lambda e, c=c: e.tensor_copy(b["wuqT"][:, 64 * c:64 * c + 64], S_ps[:, 0:64]),
             reads=["S0"], writes=["m_wuqT"], tag=f"mla.tuqe{c}")
    for c in range(8):
        p.op("tensor", lambda e, c=c: e.transpose(
            S_ps[:, 0:128], b["wukv"][:, 128 * c:128 * c + 128], idf),
            reads=["m_wukv", "identf"], writes=["S0"], tag=f"mla.tukv{c}")
        p.op("vector", lambda e, c=c: e.tensor_copy(b["wukvT"][:, 128 * c:128 * c + 128], S_ps[:, 0:128]),
             reads=["S0"], writes=["m_wukvT"], tag=f"mla.tukve{c}")
    for half in (0, 64):
        for c in range(4):
            p.op("tensor", lambda e, c=c, half=half: e.matmul(
                P2[half:half + 64, 0:64], b["wdq"][:, 64 * c:64 * c + 64],
                b["wuqT"][:, 64 * c:64 * c + 64], start=(c == 0), stop=(c == 3),
                tile_position=(0, half)),
                reads=["m_wdq", "m_wuqT"], writes=["P0"], tag=f"mla.mq{half}.{c}")
        p.op("vector", lambda e, half=half: e.tensor_copy(
            b["mq"][half:half + 64, 0:64], P2[half:half + 64, 0:64]),
            reads=["P0"], writes=["m_mq"], tag=f"mla.mqe{half}")
        for c in range(8):
            p.op("tensor", lambda e, c=c, half=half: e.matmul(
                P2[half:half + 64, 512:640], b["wdkv"][:, 64 * c:64 * c + 64],
                b["wukvT"][:, 128 * c:128 * c + 128], start=(c == 0), stop=(c == 7),
                tile_position=(0, half)),
                reads=["m_wdkv", "m_wukvT"], writes=["P1"], tag=f"mla.mkv{half}.{c}")
        p.op("vector", lambda e, half=half: e.tensor_copy(
            b["mkv"][half:half + 64, 0:128], P2[half:half + 64, 512:640]),
            reads=["P1"], writes=["m_mkv"], tag=f"mla.mkve{half}")
    ld(b["wuq"][0:64, 0:64], d["mla_wqr"][:, :], "m_wuq")
    p.op("tensor", lambda e: e.transpose(S_ps[0:64, 0:64], b["wuq"][0:64, 0:64], idf[0:64, 0:64]),
         reads=["m_wuq", "identf"], writes=["S0"], tag="mla.tqr")
    p.op("vector", lambda e: e.tensor_copy(b["wqrT"][:, :], S_ps[0:64, 0:64]),
         reads=["S0"], writes=["m_wqrT"], tag="mla.tqre")
    ld(b["wuq"][0:64, 64:128], d["mla_wo"][:, :], "m_wuq")
    p.op("tensor", lambda e: e.transpose(S_ps[0:64, 64:128], b["wuq"][0:64, 64:128], idf[0:64, 0:64]),
         reads=["m_wuq", "identf"], writes=["S0"], tag="mla.two")
    p.op("vector", lambda e: e.tensor_copy(b["woT"][:, :], S_ps[0:64, 64:128]),
         reads=["S0"], writes=["m_woT"], tag="mla.twoe")
    ld(b["wuq"][0:64, 128:192], d["mla_wkr"][:, :], "m_wuq")
    p.op("tensor", lambda e: e.transpose(
        S_ps[0:64, 128:192], b["wuq"][0:64, 128:192], idf[0:64, 0:64]),
        reads=["m_wuq", "identf"], writes=["S0"], tag="mla.tkr")
    p.op("vector", lambda e: e.tensor_copy(
        b["wkrT"][0:64, 0:64], S_ps[0:64, 128:192]),
        reads=["S0"], writes=["m_wkrT"], tag="mla.tkre")
    p.dma("sync", b["wkrT"][64:128, 0:64], b["wkrT"][0:64, 0:64], "m_wkrTd",
          reads=["m_wkrT"], writes=["m_wkrT"])


def emit_mla_head(p, ctx, b, h, zap, zres, xsrc, xdst, first_for_chunk):
    T = ctx["T"]; NS = T // 512; NT = T // 128
    S_ps = ctx["S_ps"]; P2 = ctx["P2_ps"]; TR = ctx["TR_ps"]
    scrA, scrB = ctx["scrA"], ctx["scrB"]
    st2 = ctx["stats2"]
    Pb = ctx["Pbuf"]; ptsb = ctx["ptsb"]
    half = 64 * (h % 2)
    c2 = h // 2
    Sres = [f"S{i}" for i in range(NS)]
    SCRA = [f"scrA{i}" for i in range(NS)]
    eps = ctx["cv"](1.1920929e-07, 16)

    def norm_side(src_dst, w_col, tagp):
        p.op("scalar", lambda e: e.activation(scrB[0:64, 0:T], scrA[0:64, 0:T], AF.Square),
             reads=SCRA, writes=["scrB"], tag=f"{tagp}.sq")
        for s in range(NS):
            sl = slice(512 * s, 512 * s + 512)
            p.op("tensor", lambda e, sl=sl: e.matmul(
                P2[0:16, 0:512], ctx["oh64"][:, 16 * h:16 * h + 16], scrB[0:64, sl],
                start=True, stop=True),
                reads=["scrB", "oh64"], writes=["P0"], tag=f"{tagp}.ss{s}")
            p.op("vector", lambda e, sl=sl: e.tensor_copy(b["ss"][:, sl], P2[0:16, 0:512]),
                 reads=["P0"], writes=["m_ss"], tag=f"{tagp}.sse{s}")
        p.op("vector", lambda e: e.tensor_scalar(
            b["rstd"][:, 0:T], b["ss"][:, 0:T], 1.0 / 64.0, eps, op0=ALU.mult, op1=ALU.add),
            reads=["m_ss", "cv"], writes=["m_rstd"], tag=f"{tagp}.ms")
        p.op("scalar", lambda e: e.activation(b["rstd"][:, 0:T], b["rstd"][:, 0:T], AF.Sqrt),
             reads=["m_rstd"], writes=["m_rstd"], tag=f"{tagp}.sqrt")
        p.op("vector", lambda e: e.reciprocal(b["rstd"][:, 0:T], b["rstd"][:, 0:T]),
             reads=["m_rstd"], writes=["m_rstd"], tag=f"{tagp}.rec")
        for s in range(NS):
            sl = slice(512 * s, 512 * s + 512)
            p.op("tensor", lambda e, sl=sl: e.matmul(
                P2[0:64, 0:512], ctx["e16"][:, 64 * h:64 * h + 64], b["rstd"][:, sl],
                start=True, stop=True),
                reads=["m_rstd", "e16"], writes=["P0"], tag=f"{tagp}.bc{s}")
            p.op("vector", lambda e, sl=sl, w_col=w_col: e.scalar_tensor_tensor(
                src_dst[0:64, sl], scrA[0:64, sl], b["wn"][:, w_col:w_col + 1],
                P2[0:64, 0:512], op0=ALU.mult, op1=ALU.mult),
                reads=[f"scrA{sl.start // 512}", "m_wn", "P0"], writes=[src_dst is b["qcomb"] and "m_qc" or "m_kc"],
                tag=f"{tagp}.ap{s}")

    for s in range(NS):
        sl = slice(512 * s, 512 * s + 512)
        p.op("tensor", lambda e, sl=sl: e.matmul(
            S_ps[0:64, sl], b["mq"][half:half + 64, 0:64], zap[:, sl],
            start=True, stop=True),
            reads=["m_mq", zres], writes=[f"S{s}"], tag=f"mla{h}.qc{s}")
    p.op("vector", lambda e: e.tensor_copy(scrA[0:64, 0:T], S_ps[0:64, 0:T]),
         reads=Sres, writes=SCRA, tag=f"mla{h}.qce")
    norm_side(b["qcomb"], 0, f"mla{h}.qn")
    for s in range(NS):
        sl = slice(512 * s, 512 * s + 512)
        p.op("tensor", lambda e, sl=sl: e.matmul(
            S_ps[64:128, sl], b["wqrT"][:, :], b["qcomb"][0:64, sl],
            start=True, stop=True, tile_position=(0, 64)),
            reads=["m_wqrT", "m_qc"], writes=[f"S{s}"], tag=f"mla{h}.qr{s}")
        p.op("vector", lambda e, sl=sl: e.tensor_copy(
            b["qcomb"][64:128, sl], S_ps[64:128, sl]),
            reads=[f"S{s}"], writes=["m_qc"], tag=f"mla{h}.qre{s}")
    for s in range(NS):
        sl = slice(512 * s, 512 * s + 512)
        p.op("tensor", lambda e, sl=sl: e.matmul(
            S_ps[:, sl], b["mkv"][half:half + 64, 0:128], zap[:, sl],
            start=True, stop=True),
            reads=["m_mkv", zres], writes=[f"S{s}"], tag=f"mla{h}.kv{s}")
        p.op("vector", lambda e, sl=sl: e.tensor_copy(scrA[0:64, sl], S_ps[0:64, sl]),
             reads=[f"S{s}"], writes=[f"scrA{s}"], tag=f"mla{h}.kce{s}")
        p.op("vector", lambda e, sl=sl: e.tensor_copy(b["vf"][64:128, sl], S_ps[64:128, sl]),
             reads=[f"S{s}"], writes=["m_vf"], tag=f"mla{h}.vce{s}")
    norm_side(b["kcomb"], 1, f"mla{h}.kn")
    for s in range(NS):
        sl = slice(512 * s, 512 * s + 512)
        p.op("tensor", lambda e, sl=sl: e.matmul(
            S_ps[64:128, sl], b["wkrT"][half:half + 64, 0:64], zap[:, sl],
            start=True, stop=True, tile_position=(half, 64)),
            reads=["m_wkrT", zres], writes=[f"S{s}"], tag=f"mla{h}.kr{s}")
        p.op("vector", lambda e, sl=sl: e.tensor_copy(
            b["kcomb"][64:128, sl], S_ps[64:128, sl]),
            reads=[f"S{s}"], writes=["m_kc"], tag=f"mla{h}.kre{s}")
    for jt in range(NT):
        sl2 = jt % 2
        trof = 1024 * sl2
        p.op("tensor", lambda e, jt=jt, trof=trof: e.transpose(
            TR[:, trof:trof + 64], b["vf"][64:128, 128 * jt:128 * jt + 128],
            ctx["ident"][64:128, 64:128]),
            reads=["m_vf", "cident"], writes=[f"TR{sl2}"], tag=f"mla{h}.vt{jt}")
        p.op("vector", lambda e, jt=jt, trof=trof: e.tensor_copy(
            b["vtokh"][:, 64 * jt:64 * jt + 64], TR[:, trof:trof + 64]),
            reads=[f"TR{sl2}"], writes=["m_vtokh"], tag=f"mla{h}.vte{jt}")
    for qt in range(NT):
        nj = 128 * (qt + 1)
        njs = ceil_div(nj, 512)
        pe = qt % 2
        Pba = Pb if pe == 0 else b["pbalt"]
        negm = st2[:, 4 * pe:4 * pe + 1]
        sums = st2[:, 4 * pe + 1:4 * pe + 2]
        rsum = st2[:, 4 * pe + 2:4 * pe + 3]
        for js in range(njs):
            jl = min(512, nj - 512 * js)
            sl = slice(512 * js, 512 * js + jl)
            p.op("tensor", lambda e, sl=sl, qt=qt: e.matmul(
                S_ps[:, sl], b["qcomb"][:, 128 * qt:128 * qt + 128], b["kcomb"][:, sl],
                start=True, stop=True),
                reads=["m_qc", "m_kc"], writes=[f"S{js}"], tag=f"mla{h}.s{qt}.{js}")
        dt_ = slice(128 * qt, 128 * qt + 128)
        p.op("vector", lambda e, dt_=dt_: e.tensor_add(S_ps[:, dt_], S_ps[:, dt_], ctx["diagm"][:, :]),
             reads=[f"S{qt // 4}", "diagm"], writes=[f"S{qt // 4}"], tag=f"mla{h}.dm{qt}")
        p.op("vector", lambda e, nj=nj, negm=negm: e.reduce_max(negm, S_ps[:, 0:nj], axis=AX.X, negate=True),
             reads=[f"S{js2}" for js2 in range(njs)], writes=[f"negm{pe}"], tag=f"mla{h}.mx{qt}")
        p.op("scalar", lambda e, nj=nj, negm=negm, sums=sums, Pba=Pba: e.activation(
            Pba[:, 0:nj], S_ps[:, 0:nj], AF.Exp, bias=negm, accum_out=sums),
             reads=[f"S{js2}" for js2 in range(njs)] + [f"negm{pe}"],
             writes=[f"P.{pe}", f"sums{pe}"], tag=f"mla{h}.ex{qt}")
        p.op("vector", lambda e, sums=sums, rsum=rsum: e.reciprocal(rsum, sums),
             reads=[f"sums{pe}"], writes=[f"rsum{pe}"], tag=f"mla{h}.rc{qt}")
        Pb2 = ctx["Pbuf2"]
        p.op("vector", lambda e, nj=nj, Pb2=Pb2, Pba=Pba, rsum=rsum: e.tensor_scalar_mul(
            Pb2[:, 0:nj], Pba[:, 0:nj], rsum),
            reads=[f"P.{pe}", f"rsum{pe}"], writes=["P2b"], tag=f"mla{h}.pm{qt}")
        for jt in range(qt + 1):
            sl2 = jt % 2
            trof = 1024 * sl2
            p.op("tensor", lambda e, jt=jt, trof=trof, Pb2=Pb2: e.transpose(
                TR[:, trof:trof + 128], Pb2[:, 128 * jt:128 * jt + 128], ctx["ident"]),
                reads=["P2b", "cident"], writes=[f"TR{sl2}"], tag=f"mla{h}.tr{qt}.{jt}")
            p.op("vector" if jt % 2 == 0 else "scalar", lambda e, jt=jt, trof=trof:
                 (e.tensor_copy if hasattr(e, "tensor_copy") else e.copy)(
                ptsb[:, 128 * jt:128 * jt + 128], TR[:, trof:trof + 128]),
                reads=[f"TR{sl2}"], writes=[f"pt{jt}"], tag=f"mla{h}.tre{qt}.{jt}")
        pvb = P2[0:64, 256:384] if pe == 0 else P2[0:64, 768:896]
        pvres = "P0" if pe == 0 else "P1"
        for jt in range(qt + 1):
            p.op("tensor", lambda e, jt=jt, qt=qt, pvb=pvb: e.matmul(
                pvb, b["vtokh"][:, 64 * jt:64 * jt + 64],
                ptsb[:, 128 * jt:128 * jt + 128],
                start=(jt == 0), stop=(jt == qt)),
                reads=["m_vtokh", f"pt{jt}"], writes=[pvres], tag=f"mla{h}.pv{qt}.{jt}")
        p.op("vector", lambda e, qt=qt, pvb=pvb: e.tensor_copy(
            b["ctxh"][0:64, 128 * qt:128 * qt + 128], pvb),
            reads=[pvres], writes=["m_ctxh"], tag=f"mla{h}.ce{qt}")
    xst = ctx["xstage"]
    for s in range(NS):
        sl = slice(512 * s, 512 * s + 512)
        p.op("tensor", lambda e, sl=sl: e.matmul(
            P2[half:half + 64, 0:512], b["woT"][:, :], b["ctxh"][0:64, sl],
            start=True, stop=True, tile_position=(0, half)),
            reads=["m_woT", "m_ctxh"], writes=["P0"], tag=f"mla{h}.wo{s}")
        slot = s % 4
        xa = xst[:, 512 * slot:512 * slot + 512]
        xr = f"xs{slot}"
        p.dma("sync", xa[half:half + 64, :], xsrc[c2 * 128 + half: c2 * 128 + half + 64, sl],
              xr, writes=[xr], reads=[f"xB{c2}.{s}"])
        p.op("vector", lambda e, xa=xa: e.tensor_add(
            xa[half:half + 64, :], xa[half:half + 64, :], P2[half:half + 64, 0:512]),
            reads=["P0", xr], writes=[xr], tag=f"mla{h}.xa{s}")
        p.dma("sync", xdst[c2 * 128 + half: c2 * 128 + half + 64, sl], xa[half:half + 64, :],
              xr, reads=[xr], writes=[f"xC{c2}.{s}.{half}"])


L1F_DEPTH = 6
DIM_INNER = 2730
DIM_PAD = 2816
NCH_H = DIM_PAD // 128

def prep_l1f(inp, T):
    out = {}
    Wqkv = inp["L1_ffn_Wqkv"]
    r = np.arange(512)
    c = r // 128; b = (r % 128) // 32; j = r % 32
    h = 4 * c + b
    qA = 64 * h + j; qB = 64 * h + 32 + j
    perm_qk = np.concatenate([qA, qB, 1024 + qA, 1024 + qB])
    qk = Wqkv[:, :, perm_qk]
    out["l1_wqk"] = np.ascontiguousarray(
        qk.reshape(6, 8, 128, 16, 128).transpose(0, 2, 3, 1, 4).reshape(6, 128, 16384))
    out["l1_wv"] = np.ascontiguousarray(Wqkv[:, :, 2048:3072])
    out["l1_wo"] = np.ascontiguousarray(
        inp["L1_ffn_Wo"].reshape(6, 8, 128, 8, 128).transpose(0, 2, 3, 1, 4)
        .reshape(6, 128, 8192))
    W1 = inp["L1_ffn_W1"]; b1 = inp["L1_ffn_b1"]
    W1a = np.zeros((6, 1024, DIM_PAD), np.float32); W1a[:, :, :DIM_INNER] = W1[:, :, :DIM_INNER]
    W1g = np.zeros((6, 1024, DIM_PAD), np.float32); W1g[:, :, :DIM_INNER] = W1[:, :, DIM_INNER:]
    b1a = np.zeros((6, DIM_PAD), np.float32); b1a[:, :DIM_INNER] = b1[:, :DIM_INNER]
    b1g = np.zeros((6, DIM_PAD), np.float32); b1g[:, :DIM_INNER] = b1[:, DIM_INNER:]
    w1i = np.zeros((6, 2 * NCH_H, 1024, 128), np.float32)
    b1i = np.zeros((6, 128, 2 * NCH_H), np.float32)
    for jc in range(NCH_H):
        w1i[:, 2 * jc] = W1a[:, :, 128 * jc:128 * jc + 128]
        w1i[:, 2 * jc + 1] = W1g[:, :, 128 * jc:128 * jc + 128]
        b1i[:, :, 2 * jc] = b1a[:, 128 * jc:128 * jc + 128]
        b1i[:, :, 2 * jc + 1] = b1g[:, 128 * jc:128 * jc + 128]
    out["l1_w1"] = np.ascontiguousarray(w1i)
    out["l1_b1"] = np.ascontiguousarray(b1i)
    W2 = inp["L1_ffn_W2"]
    W2p = np.zeros((6, DIM_PAD, 1024), np.float32); W2p[:, :DIM_INNER] = W2
    out["l1_w2"] = np.ascontiguousarray(
        W2p.reshape(6, DIM_PAD, 8, 128).transpose(0, 2, 1, 3))
    out["l1_b2"] = np.ascontiguousarray(
        inp["L1_ffn_b2"].reshape(6, 8, 128).transpose(0, 2, 1))
    out["l1_lng"] = np.ascontiguousarray(
        inp["L1_ffn_ln_g"].reshape(6, 8, 128).transpose(0, 2, 1))
    out["l1_lnb"] = np.ascontiguousarray(
        inp["L1_ffn_ln_b"].reshape(6, 8, 128).transpose(0, 2, 1))
    qs = inp["L1_ffn_q_scale"]; ks = inp["L1_ffn_k_scale"]
    def tile2(v):
        return np.stack([np.tile(v[:, 0:32], (1, 4)), np.tile(v[:, 32:64], (1, 4))],
                        axis=2)
    out["l1_qs"] = np.ascontiguousarray(tile2(qs))
    out["l1_ks"] = np.ascontiguousarray(tile2(ks))
    return out


def l1f_consts(nc, ctx, T):
    DIM_HEAD = 64; WINDOW = 256
    half = np.arange(0, DIM_HEAD, 2, dtype=np.float64)
    inv = 1.0 / 10000.0 ** (half / DIM_HEAD)
    t = np.arange(T, dtype=np.float64)
    freqs = t[:, None] * inv[None, :]
    base_scale = (half + 0.4 * DIM_HEAD) / (1.4 * DIM_HEAD)
    power = (t - T // 2) / (WINDOW // 2)
    scale = base_scale[None, :] ** power[:, None]
    cs = np.cos(freqs); sn = np.sin(freqs)
    jj = np.arange(128) % 32
    csq = (cs * scale * 8.0).T[jj]
    ssq = (sn * scale * 8.0).T[jj]
    cik = (cs / scale).T[jj]
    sik = (sn / scale).T[jj]
    NW = T // 256
    tp_ = np.stack([a.reshape(128, NW, 256) for a in (csq, ssq, cik, sik)], axis=2)
    ctx["tabpack_dram"] = nc.inline_tensor(
        np.ascontiguousarray(tp_.reshape(128, NW * 1024)).astype(BF), "tabpack").ap()
    m = np.zeros((2, 128, 512), np.float32)
    for pp in range(2):
        qi = np.arange(128)[:, None]
        jx = np.arange(512)[None, :]
        rel = 256 + 128 * pp + qi - jx
        m[pp] = np.where((rel >= 0) & (rel <= 256), 0.0, NEG)
    ctx["lmask_dram"] = nc.inline_tensor(
        np.ascontiguousarray(m.transpose(1, 0, 2).reshape(128, 1024)).astype(BF), "lmask").ap()
    ctx["lmask"] = nc.alloc_sbuf_tensor("lmasks", [128, 1024], BF16).ap()
    I = np.zeros((128, 4, 16), np.float32)
    for cc in range(4):
        for rr_ in range(128):
            I[rr_, cc, 4 * cc + rr_ // 32] = 1.0
    ctx["i16_dram"] = nc.inline_tensor(np.ascontiguousarray(I.reshape(128, 64)), "i16").ap()
    ctx["i16"] = nc.alloc_sbuf_tensor("i16s", [128, 64], F32).ap()
    E = np.zeros((16, 4, 128), np.float32)
    for cc in range(4):
        for p_ in range(128):
            E[4 * cc + p_ // 32, cc, p_] = 1.0
    ctx["e16c_dram"] = nc.inline_tensor(np.ascontiguousarray(E.reshape(16, 512)), "e16c").ap()
    ctx["e16c"] = nc.alloc_sbuf_tensor("e16cs", [16, 512], F32).ap()


def load_l1f_consts(p, ctx):
    p.dma("sync", ctx["lmask"][:, :], ctx["lmask_dram"][:, :], "lmask", writes=["lmask"])
    p.dma("sync", ctx["i16"][:, :], ctx["i16_dram"][:, :], "i16", writes=["i16"])
    p.dma("sync", ctx["e16c"][:, :], ctx["e16c_dram"][:, :], "e16c", writes=["e16c"])


def declare_l1f(nc, T):
    d = {}
    def dram(name, shape):
        d[name] = nc.dram_tensor(name, list(shape), F32, kind="ExternalInput").ap()
    dram("l1_wqk", (6, 128, 16384)); dram("l1_wv", (6, 1024, 1024))
    dram("l1_wo", (6, 128, 8192))
    dram("l1_w1", (6, 2 * NCH_H, 1024, 128)); dram("l1_b1", (6, 128, 2 * NCH_H))
    dram("l1_w2", (6, 8, DIM_PAD, 128)); dram("l1_b2", (6, 128, 8))
    dram("l1_lng", (6, 128, 8)); dram("l1_lnb", (6, 128, 8))
    dram("l1_qs", (6, 128, 2)); dram("l1_ks", (6, 128, 2))
    return d

def alloc_l1f_bufs(nc, ctx, T, xs_at=None):
    b = {}
    b["qkw"] = alloc_at(nc, "f_qkw", [128, 16 * 1024], BF16, RB0)
    b["vw1"] = alloc_at(nc, "f_vw1", [128, 8 * 1024], BF16, RB0 + 32 * 1024)
    b["wo1"] = alloc_at(nc, "f_wo1", [128, 8 * 1024], BF16, RB0 + 48 * 1024)
    b["kring"] = alloc_at(nc, "f_kring", [128, 2 * 2048], BF16, RB0 + 64 * 1024)
    b["vring"] = alloc_at(nc, "f_vring", [128, 2 * 2048], BF16, RB0 + 72 * 1024)
    b["ctxw"] = alloc_at(nc, "f_ctxw", [128, 2048], BF16, RB0 + 80 * 1024)
    b["wtab"] = alloc_at(nc, "f_wtab", [128, 2048], BF16, RB0 + 84 * 1024)
    b["zn2"] = alloc_at(nc, "f_zn2", [128, 8 * 2048], BF16, RB0)
    b["stat"] = alloc_at(nc, "f_stat", [1, 2 * 2048], F32, RB0 + 32 * 1024)
    b["hR"] = alloc_at(nc, "f_hR", [128, 6 * 2048], BF16, RB0 + 48 * 1024)
    b["w1s"] = [alloc_at(nc, f"f_w1s{i}", [128, 1024], BF16, RB0 + 72 * 1024 + i * 2048)
                for i in range(2)]
    b["w2s"] = alloc_at(nc, "f_w2s", [128, NCH_H * 128], BF16, RB0 + 76 * 1024)
    b["hZ"] = alloc_at(nc, "f_hZ", [128, 16 * 2048], BF16, RZ0)
    if xs_at is None:
        b["zsw"] = nc.alloc_sbuf_tensor("f_zsw", [128, 2048], BF16).ap()
        b["qw"] = nc.alloc_sbuf_tensor("f_qw", [128, 2048], BF16).ap()
    else:
        b["zsw"] = alloc_at(nc, "f_zsw", [128, 2048], BF16, xs_at)
        b["qw"] = alloc_at(nc, "f_qw", [128, 2048], BF16, xs_at + 4096)
    b["ssw"] = nc.alloc_sbuf_tensor("f_ssw", [16, 256], F32).ap()
    b["rstdw"] = nc.alloc_sbuf_tensor("f_rstdw", [16, 256], F32).ap()
    b["rqt"] = nc.alloc_sbuf_tensor("f_rqt", [128, 32], F32).ap()
    b["tabst"] = nc.alloc_sbuf_tensor("f_tabst", [128, 1024], BF16).ap()
    b["qs"] = nc.alloc_sbuf_tensor("f_qs", [128, 4], F32).ap()
    b["b1"] = nc.alloc_sbuf_tensor("f_b1", [128, 2 * NCH_H], F32).ap()
    b["b2g"] = nc.alloc_sbuf_tensor("f_b2g", [128, 24], F32).ap()
    return b


def emit_l1f_layer(p, ctx, d, b, l, zbuf, zdram, T):
    NS = T // 512
    NW = T // 256
    S_ps = ctx["S_ps"]; P2 = ctx["P2_ps"]; TR = ctx["TR_ps"]
    scrA, scrB = ctx["scrA"], ctx["scrB"]
    st2 = ctx["stats2"]
    Pb = ctx["Pbuf"]; Pb2 = ctx["Pbuf2"]; ptsb = ctx["ptsb"]
    SCRA = [f"scrA{i}" for i in range(NS)]
    G = T + 1

    def zap(c, t0, n, shift=0):
        o = c * G + 1 + t0 - shift
        return zbuf[:, o:o + n]

    p.dma("gpsimd", b["qkw"][:, :], d["l1_wqk"][l], "f_qkw", writes=["f_qkw"])
    p.dma("gpsimd", b["vw1"][:, :].rearrange("p (c f) -> p c f", f=1024),
          d["l1_wv"][l].rearrange("(c p) f -> p c f", p=128), "f_vw1", writes=["f_vw1"])
    p.dma("gpsimd", b["wo1"][:, :], d["l1_wo"][l], "f_wo1", writes=["f_wo1"])
    p.dma("sync", b["qs"][:, 0:2], d["l1_qs"][l], "f_qs", writes=["f_qs"])
    p.dma("sync", b["qs"][:, 2:4], d["l1_ks"][l], "f_qs", writes=["f_qs"])

    for wi in range(NW):
        t0 = 256 * wi
        ks_ = wi % 2
        kp_ = (wi + 1) % 2
        j0 = 256 if wi == 0 else 0
        for dk in range(8):
            sh = 1 if dk >= 4 else 0
            p.op("vector", lambda e, dk=dk, sh=sh, t0=t0: e.tensor_copy(
                b["zsw"][:, 256 * dk:256 * dk + 256], zap(dk, t0, 256, sh)),
                reads=[f"z{dk}"], writes=["f_zsw"], tag=f"f{l}.zs{wi}.{dk}")
        p.dma("sync", b["tabst"][:, :], ctx["tabpack_dram"][:, 1024 * wi:1024 * wi + 1024],
              "f_tabst", writes=["f_tabst"])
        for ki in range(4):
            qcol = 0 if ki < 2 else 2
            for half_ in (0, 1):
                dst = b["wtab"][:, 256 * (2 * ki + half_):256 * (2 * ki + half_) + 256]
                p.op("gpsimd", lambda e, ki=ki, dst=dst, qcol=qcol, half_=half_:
                     e.tensor_scalar_mul(dst, b["tabst"][:, 256 * ki:256 * ki + 256],
                                         b["qs"][:, qcol + half_:qcol + half_ + 1]),
                     reads=["f_tabst", "f_qs"], writes=["f_wtab"], tag=f"f{l}.tab{wi}.{ki}.{half_}")
        def wtap(idx):
            return b["wtab"][:, 256 * idx:256 * idx + 256]
        CSQA, CSQB = wtap(0), wtap(1)
        SSQA, SSQB = wtap(2), wtap(3)
        CIKA, CIKB = wtap(4), wtap(5)
        SIKA, SIKB = wtap(6), wtap(7)
        def qk_chunks(base_fc, dst_scr, dres):
            for fc in range(8):
                bank = fc % 2
                pa = S_ps[:, 512 * bank:512 * bank + 256]
                for dk in range(8):
                    p.op("tensor", lambda e, fc=fc, dk=dk, pa=pa, base_fc=base_fc: e.matmul(
                        pa, b["qkw"][:, 1024 * (base_fc + fc) + 128 * dk:
                                     1024 * (base_fc + fc) + 128 * dk + 128],
                        b["zsw"][:, 256 * dk:256 * dk + 256],
                        start=(dk == 0), stop=(dk == 7)),
                        reads=["f_qkw", "f_zsw"], writes=[f"S{bank}"],
                        tag=f"f{l}.qk{wi}.{base_fc + fc}")
                p.op("vector", lambda e, fc=fc, pa=pa: e.tensor_copy(
                    dst_scr[:, 256 * fc:256 * fc + 256], pa),
                    reads=[f"S{bank}"], writes=[dres], tag=f"f{l}.qke{wi}.{base_fc + fc}")
        qk_chunks(0, scrA, "scrA0")
        p.op("scalar", lambda e: e.activation(scrB[:, 0:2048], scrA[:, 0:2048], AF.Square),
             reads=["scrA0"], writes=["scrB"], tag=f"f{l}.qsq{wi}")
        for fc in range(8):
            p.op("tensor", lambda e, fc=fc: e.matmul(
                P2[0:16, 0:256], ctx["i16"][:, 16 * (fc % 4):16 * (fc % 4) + 16],
                scrB[:, 256 * fc:256 * fc + 256], start=(fc == 0), stop=(fc == 7)),
                reads=["scrB", "i16"], writes=["P0"], tag=f"f{l}.qss{wi}.{fc}")
        p.op("vector", lambda e: e.tensor_copy(b["ssw"][:, 0:256], P2[0:16, 0:256]),
             reads=["P0"], writes=["f_ssw"], tag=f"f{l}.qsse{wi}")
        p.op("scalar", lambda e: e.activation(b["rstdw"][:, :], b["ssw"][:, 0:256], AF.Sqrt),
             reads=["f_ssw"], writes=["f_rstdw"], tag=f"f{l}.qsrt{wi}")
        p.op("vector", lambda e: e.tensor_scalar_max(b["rstdw"][:, :], b["rstdw"][:, :], 1e-12),
             reads=["f_rstdw"], writes=["f_rstdw"], tag=f"f{l}.qmx{wi}")
        p.op("vector", lambda e: e.reciprocal(b["rstdw"][:, :], b["rstdw"][:, :]),
             reads=["f_rstdw"], writes=["f_rstdw"], tag=f"f{l}.qrc{wi}")
        for pp in range(2):
            p.op("tensor", lambda e, pp=pp: e.transpose(
                TR[:, 1024 * pp:1024 * pp + 16].bitcast(F32) if False else
                S_ps[0:128, 768 + 16 * pp: 784 + 16 * pp],
                b["rstdw"][:, 128 * pp:128 * pp + 128], ctx["identf"][0:16, 0:16]),
                reads=["f_rstdw", "identf"], writes=["S1"], tag=f"f{l}.qrt{wi}.{pp}")
            p.op("vector", lambda e, pp=pp: e.tensor_copy(
                b["rqt"][:, 16 * pp:16 * pp + 16], S_ps[0:128, 768 + 16 * pp:784 + 16 * pp]),
                reads=["S1"], writes=["f_rqt"], tag=f"f{l}.qrte{wi}.{pp}")
        for c in range(4):
            qA = scrA[:, 256 * c:256 * c + 256]
            qB = scrA[:, 256 * (4 + c):256 * (4 + c) + 256]
            m1 = scrB[:, 0:256]; m2 = scrB[:, 256:512]
            m3 = scrB[:, 512:768]; m4 = scrB[:, 768:1024]
            p.op("gpsimd", lambda e, qA=qA, m1=m1, CSQA=CSQA: e.tensor_mul(m1, qA, CSQA),
                 reads=["scrA0", "f_wtab"], writes=["scrB"], tag=f"f{l}.qr1{wi}.{c}")
            p.op("vector", lambda e, qB=qB, m2=m2, SSQB=SSQB: e.tensor_mul(m2, qB, SSQB),
                 reads=["scrA0", "f_wtab"], writes=["scrB"], tag=f"f{l}.qr2{wi}.{c}")
            p.op("vector", lambda e, c=c, m1=m1, m2=m2: e.tensor_sub(
                b["qw"][:, 256 * c:256 * c + 256], m1, m2),
                reads=["scrB"], writes=["f_qw"], tag=f"f{l}.qr3{wi}.{c}")
            p.op("gpsimd", lambda e, qB=qB, m3=m3, CSQB=CSQB: e.tensor_mul(m3, qB, CSQB),
                 reads=["scrA0", "f_wtab"], writes=["scrB"], tag=f"f{l}.qr4{wi}.{c}")
            p.op("vector", lambda e, qA=qA, m4=m4, SSQA=SSQA: e.tensor_mul(m4, qA, SSQA),
                 reads=["scrA0", "f_wtab"], writes=["scrB"], tag=f"f{l}.qr5{wi}.{c}")
            p.op("vector", lambda e, c=c, m3=m3, m4=m4: e.tensor_add(
                b["qw"][:, 256 * (4 + c):256 * (4 + c) + 256], m3, m4),
                reads=["scrB"], writes=["f_qw"], tag=f"f{l}.qr6{wi}.{c}")
        qk_chunks(8, scrA, "scrA0")
        p.op("scalar", lambda e: e.activation(scrB[:, 0:2048], scrA[:, 0:2048], AF.Square),
             reads=["scrA0"], writes=["scrB"], tag=f"f{l}.ksq{wi}")
        for fc in range(8):
            p.op("tensor", lambda e, fc=fc: e.matmul(
                P2[0:16, 0:256], ctx["i16"][:, 16 * (fc % 4):16 * (fc % 4) + 16],
                scrB[:, 256 * fc:256 * fc + 256], start=(fc == 0), stop=(fc == 7)),
                reads=["scrB", "i16"], writes=["P0"], tag=f"f{l}.kss{wi}.{fc}")
        p.op("vector", lambda e: e.tensor_copy(b["ssw"][:, 0:256], P2[0:16, 0:256]),
             reads=["P0"], writes=["f_ssw"], tag=f"f{l}.ksse{wi}")
        p.op("scalar", lambda e: e.activation(b["rstdw"][:, :], b["ssw"][:, 0:256], AF.Sqrt),
             reads=["f_ssw"], writes=["f_rstdw"], tag=f"f{l}.ksrt{wi}")
        p.op("vector", lambda e: e.tensor_scalar_max(b["rstdw"][:, :], b["rstdw"][:, :], 1e-12),
             reads=["f_rstdw"], writes=["f_rstdw"], tag=f"f{l}.kmx{wi}")
        p.op("vector", lambda e: e.reciprocal(b["rstdw"][:, :], b["rstdw"][:, :]),
             reads=["f_rstdw"], writes=["f_rstdw"], tag=f"f{l}.krc{wi}")
        for c in range(4):
            p.op("tensor", lambda e, c=c: e.matmul(
                P2[:, 0:256], ctx["e16c"][:, 128 * c:128 * c + 128], b["rstdw"][:, :],
                start=True, stop=True),
                reads=["f_rstdw", "e16c"], writes=["P0"], tag=f"f{l}.kbc{wi}.{c}")
            kA = scrA[:, 256 * c:256 * c + 256]
            kB = scrA[:, 256 * (4 + c):256 * (4 + c) + 256]
            m1 = scrB[:, 0:256]; m2 = scrB[:, 256:512]
            m3 = scrB[:, 512:768]; m4 = scrB[:, 768:1024]
            d1 = scrB[:, 1024:1280]; d2 = scrB[:, 1280:1536]
            ksl = b["kring"][:, 2048 * ks_:2048 * ks_ + 2048]
            p.op("gpsimd", lambda e, kA=kA, m1=m1, CIKA=CIKA: e.tensor_mul(m1, kA, CIKA),
                 reads=["scrA0", "f_wtab"], writes=["scrB"], tag=f"f{l}.kr1{wi}.{c}")
            p.op("vector", lambda e, kB=kB, m2=m2, SIKB=SIKB: e.tensor_mul(m2, kB, SIKB),
                 reads=["scrA0", "f_wtab"], writes=["scrB"], tag=f"f{l}.kr2{wi}.{c}")
            p.op("vector", lambda e, d1=d1, m1=m1, m2=m2: e.tensor_sub(d1, m1, m2),
                 reads=["scrB"], writes=["scrB"], tag=f"f{l}.kr3{wi}.{c}")
            p.op("vector", lambda e, c=c, d1=d1, ksl=ksl: e.tensor_mul(
                ksl[:, 256 * c:256 * c + 256], d1, P2[:, 0:256]),
                reads=["scrB", "P0"], writes=[f"f_kr{ks_}"], tag=f"f{l}.kr3b{wi}.{c}")
            p.op("gpsimd", lambda e, kB=kB, m3=m3, CIKB=CIKB: e.tensor_mul(m3, kB, CIKB),
                 reads=["scrA0", "f_wtab"], writes=["scrB"], tag=f"f{l}.kr4{wi}.{c}")
            p.op("vector", lambda e, kA=kA, m4=m4, SIKA=SIKA: e.tensor_mul(m4, kA, SIKA),
                 reads=["scrA0", "f_wtab"], writes=["scrB"], tag=f"f{l}.kr5{wi}.{c}")
            p.op("vector", lambda e, d2=d2, m3=m3, m4=m4: e.tensor_add(d2, m3, m4),
                 reads=["scrB"], writes=["scrB"], tag=f"f{l}.kr6{wi}.{c}")
            p.op("vector", lambda e, c=c, d2=d2, ksl=ksl: e.tensor_mul(
                ksl[:, 256 * (4 + c):256 * (4 + c) + 256], d2, P2[:, 0:256]),
                reads=["scrB", "P0"], writes=[f"f_kr{ks_}"], tag=f"f{l}.kr6b{wi}.{c}")
        for ti in range(2):
            for half2 in range(2):
                pa = P2[:, 512 * half2:512 * half2 + 512]
                for dk in range(8):
                    p.op("tensor", lambda e, ti=ti, dk=dk, half2=half2, pa=pa: e.matmul(
                        pa, b["zsw"][:, 256 * dk + 128 * ti:256 * dk + 128 * ti + 128],
                        b["vw1"][:, 1024 * dk + 512 * half2:1024 * dk + 512 * half2 + 512],
                        start=(dk == 0), stop=(dk == 7)),
                        reads=["f_zsw", "f_vw1"], writes=[f"P{half2}"],
                        tag=f"f{l}.v{wi}.{ti}.{half2}")
                p.op("vector", lambda e, ti=ti, half2=half2, pa=pa, ks_=ks_: e.tensor_copy(
                    b["vring"][:, 2048 * ks_ + 1024 * ti + 512 * half2:
                               2048 * ks_ + 1024 * ti + 512 * half2 + 512], pa),
                    reads=[f"P{half2}"], writes=[f"f_vr{ks_}"],
                    tag=f"f{l}.ve{wi}.{ti}.{half2}")
        for h in range(16):
            bb = 32 * (h % 4)
            cc = h // 4
            tp = (bb, 0)
            for pp in range(2):
                sb = (2 * h + pp) % 4
                SS = S_ps[:, 512 * sb:512 * sb + 512]
                srcs = []
                if wi > 0:
                    srcs.append((b["kring"], kp_, 0))
                srcs.append((b["kring"], ks_, 256))
                for (kr, slot, joff) in srcs:
                    kAp = kr[bb:bb + 32, 2048 * slot + 256 * cc: 2048 * slot + 256 * cc + 256]
                    kBp = kr[bb:bb + 32, 2048 * slot + 256 * (4 + cc): 2048 * slot + 256 * (4 + cc) + 256]
                    qAp = b["qw"][bb:bb + 32, 256 * cc + 128 * pp: 256 * cc + 128 * pp + 128]
                    qBp = b["qw"][bb:bb + 32, 256 * (4 + cc) + 128 * pp: 256 * (4 + cc) + 128 * pp + 128]
                    p.op("tensor", lambda e, SS=SS, qAp=qAp, kAp=kAp, joff=joff, tp=tp: e.matmul(
                        SS[:, joff:joff + 256], qAp, kAp, start=True, stop=False,
                        tile_position=tp),
                        reads=["f_qw", f"f_kr{ks_}" if joff else f"f_kr{kp_}"],
                        writes=[f"S{sb}"], tag=f"f{l}.sA{wi}.{h}.{pp}")
                    p.op("tensor", lambda e, SS=SS, qBp=qBp, kBp=kBp, joff=joff, tp=tp: e.matmul(
                        SS[:, joff:joff + 256], qBp, kBp, start=False, stop=True,
                        tile_position=tp),
                        reads=["f_qw", f"f_kr{ks_}" if joff else f"f_kr{kp_}"],
                        writes=[f"S{sb}"], tag=f"f{l}.sB{wi}.{h}.{pp}")
                g = (2 * h + pp) % 4
                g2 = (2 * h + pp) % 2
                negm = st2[:, 4 * g:4 * g + 1]
                sums = st2[:, 4 * g + 1:4 * g + 2]
                rsum = st2[:, 4 * g + 2:4 * g + 3]
                Pba = Pb[:, 1024 * g2:1024 * g2 + 512]
                Pba2 = Pb2[:, 1024 * g2:1024 * g2 + 512]
                pvb = P2[64 * (h % 2):64 * (h % 2) + 64,
                         (256 if g2 == 0 else 768):(256 if g2 == 0 else 768) + 128]
                pvres = "P0" if g2 == 0 else "P1"
                p.op("vector", lambda e, SS=SS, h=h, pp=pp, j0=j0: e.scalar_tensor_tensor(
                    SS[:, j0:512], SS[:, j0:512], b["rqt"][:, 16 * pp + h:16 * pp + h + 1],
                    ctx["lmask"][:, 512 * pp + j0:512 * pp + 512],
                    op0=ALU.mult, op1=ALU.add),
                    reads=[f"S{sb}", "f_rqt", "lmask"], writes=[f"S{sb}"],
                    tag=f"f{l}.msk{wi}.{h}.{pp}")
                p.op("vector", lambda e, SS=SS, j0=j0, negm=negm: e.reduce_max(
                    negm, SS[:, j0:512], axis=AX.X, negate=True),
                    reads=[f"S{sb}"], writes=[f"negm{g}"], tag=f"f{l}.mx{wi}.{h}.{pp}")
                p.op("scalar", lambda e, SS=SS, j0=j0, negm=negm, sums=sums, Pba=Pba:
                     e.activation(Pba[:, j0:512], SS[:, j0:512], AF.Exp, bias=negm,
                                  accum_out=sums),
                     reads=[f"S{sb}", f"negm{g}"], writes=[f"P.{g2}", f"sums{g}"],
                     tag=f"f{l}.ex{wi}.{h}.{pp}")
                p.op("vector", lambda e, sums=sums, rsum=rsum: e.reciprocal(rsum, sums),
                     reads=[f"sums{g}"], writes=[f"rsum{g}"], tag=f"f{l}.rc{wi}.{h}.{pp}")
                p.op("vector", lambda e, j0=j0, Pba=Pba, Pba2=Pba2, rsum=rsum:
                     e.tensor_scalar_mul(Pba2[:, j0:512], Pba[:, j0:512], rsum),
                     reads=[f"P.{g2}", f"rsum{g}"], writes=[f"P2b.{g2}"],
                     tag=f"f{l}.pm{wi}.{h}.{pp}")
                njt0 = j0 // 128
                for jt in range(njt0, 4):
                    sl2 = jt % 2
                    trof = 1024 * sl2
                    p.op("tensor", lambda e, jt=jt, trof=trof, Pba2=Pba2: e.transpose(
                        TR[:, trof:trof + 128], Pba2[:, 128 * jt:128 * jt + 128],
                        ctx["ident"]),
                        reads=[f"P2b.{g2}", "cident"], writes=[f"TR{sl2}"],
                        tag=f"f{l}.tr{wi}.{h}.{pp}.{jt}")
                    p.op("vector" if jt % 2 == 0 else "scalar",
                         lambda e, jt=jt, trof=trof:
                         (e.tensor_copy if hasattr(e, "tensor_copy") else e.copy)(
                        ptsb[:, 128 * jt:128 * jt + 128], TR[:, trof:trof + 128]),
                        reads=[f"TR{sl2}"], writes=[f"pt{jt}"],
                        tag=f"f{l}.tre{wi}.{h}.{pp}.{jt}")
                for jt in range(njt0, 4):
                    slot = kp_ if jt < 2 else ks_
                    ti = jt % 2
                    p.op("tensor", lambda e, jt=jt, slot=slot, ti=ti, h=h, njt0=njt0,
                         pvb=pvb: e.matmul(
                        pvb,
                        b["vring"][:, 2048 * slot + 1024 * ti + 64 * h:
                                   2048 * slot + 1024 * ti + 64 * h + 64],
                        ptsb[:, 128 * jt:128 * jt + 128],
                        start=(jt == njt0), stop=(jt == 3),
                        tile_position=(0, 64 * (h % 2))),
                        reads=[f"f_vr{slot}", f"pt{jt}"], writes=[pvres],
                        tag=f"f{l}.pv{wi}.{h}.{pp}.{jt}")
                p.op("vector", lambda e, h=h, pp=pp, pvb=pvb: e.tensor_copy(
                    b["ctxw"][64 * (h % 2):64 * (h % 2) + 64,
                              256 * (h // 2) + 128 * pp:256 * (h // 2) + 128 * pp + 128],
                    pvb),
                    reads=[pvres], writes=[f"f_cw{h // 2}"],
                    tag=f"f{l}.ce{wi}.{h}.{pp}")
        for e_ in range(8):
            sb = e_ % 2
            pa = S_ps[:, 512 * sb:512 * sb + 256]
            for ic in range(8):
                p.op("tensor", lambda e, e_=e_, ic=ic, pa=pa: e.matmul(
                    pa, b["wo1"][:, 1024 * e_ + 128 * ic:1024 * e_ + 128 * ic + 128],
                    b["ctxw"][:, 256 * ic:256 * ic + 256],
                    start=(ic == 0), stop=(ic == 7)),
                    reads=["f_wo1", f"f_cw{ic}"], writes=[f"S{sb}"],
                    tag=f"f{l}.wo{wi}.{e_}")
            p.op("vector", lambda e, e_=e_, t0=t0, pa=pa: e.tensor_add(
                zap(e_, t0, 256), zap(e_, t0, 256), pa),
                reads=[f"S{sb}", f"z{e_}"], writes=[f"z{e_}"],
                tag=f"f{l}.za{wi}.{e_}")

    p.fence([], ["f_qkw", "f_vw1", "f_wo1", "stat.mu", "stat.rs",
                 "f_kr0", "f_kr1", "f_vr0", "f_vr1", "f_wtab",
                 "f_w1s0", "f_w1s1", "f_w2s", "f_zsw", "f_qw",
                 "xs0", "xs1", "xs2", "xs3"]
            + [f"f_cw{i}" for i in range(8)]
            + [f"f_zn{c}" for c in range(8)]
            + [f"h{k}" for k in range(NCH_H)])
    stat = b["stat"]
    mu = stat[0:1, 0:T]
    rs = stat[0:1, T:2 * T]
    ones_f = ctx["ones_f32"]
    for c in range(8):
        sh = 1 if c >= 4 else 0
        p.op("scalar", lambda e, c=c, sh=sh: e.activation(
            scrA[:, 0:T], zap(c, 0, T, sh), AF.Square),
            reads=[f"z{c}"], writes=SCRA, tag=f"f{l}.lsq{c}")
        for s in range(NS):
            sl = slice(512 * s, 512 * s + 512)
            p.op("tensor", lambda e, c=c, s=s, sh=sh, sl=sl: e.matmul(
                S_ps[0:1, sl], ones_f[:, 0:1], zap(c, 512 * s, 512, sh),
                start=(c == 0), stop=(c == 7)),
                reads=[f"z{c}", "cones"], writes=[f"S{s}"], tag=f"f{l}.lsum{c}.{s}")
            p.op("tensor", lambda e, sl=sl, c=c: e.matmul(
                S_ps[32:33, sl], ones_f[:, 0:1], scrA[:, sl], start=(c == 0), stop=(c == 7)),
                reads=[f"scrA{s}"] + (["cones"] if True else []), writes=[f"S{s}"],
                tag=f"f{l}.lsqs{c}.{s}")
    Sres = [f"S{s}" for s in range(NS)]
    p.op("vector", lambda e: e.tensor_scalar_mul(mu, S_ps[0:1, 0:T], 1.0 / 1024.0),
         reads=Sres, writes=["stat.mu"], tag=f"f{l}.lmu")
    p.op("vector", lambda e: e.tensor_mul(scrA[0:1, 0:T], mu, mu),
         reads=["stat.mu"] + SCRA, writes=SCRA, tag=f"f{l}.lm2")
    p.op("vector", lambda e: e.scalar_tensor_tensor(
        scrA[0:1, 0:T], S_ps[32:33, 0:T], 1.0 / 1024.0, scrA[0:1, 0:T],
        op0=ALU.mult, op1=ALU.subtract),
        reads=Sres + SCRA, writes=SCRA, tag=f"f{l}.lvar")
    p.op("scalar", lambda e: e.activation(rs, scrA[0:1, 0:T], AF.Sqrt, bias=ctx["cv"](1e-5, 1)),
         reads=SCRA + ["cv"], writes=["stat.rs"], tag=f"f{l}.lsqrt")
    p.op("vector", lambda e: e.reciprocal(rs, rs),
         reads=["stat.rs"], writes=["stat.rs"], tag=f"f{l}.lrecip")
    p.dma("sync", b["b2g"][:, 0:8], d["l1_b2"][l], "f_b2g", writes=["f_b2g"])
    p.dma("sync", b["b2g"][:, 8:16], d["l1_lng"][l], "f_b2g", writes=["f_b2g"])
    p.dma("sync", b["b2g"][:, 16:24], d["l1_lnb"][l], "f_b2g", writes=["f_b2g"])
    p.dma("sync", b["b1"][:, :], d["l1_b1"][l], "f_b1", writes=["f_b1"])
    for c in range(8):
        sh = 1 if c >= 4 else 0
        for s in range(NS):
            sl = slice(512 * s, 512 * s + 512)
            p.op("tensor", lambda e, sl=sl: e.matmul(
                P2[:, 0:512], ones_f[0:1, 0:128], mu[:, sl], start=True, stop=True),
                reads=["stat.mu", "cones"], writes=["P0"], tag=f"f{l}.lbmu{c}.{s}")
            p.op("tensor", lambda e, sl=sl: e.matmul(
                P2[:, 512:1024], ones_f[0:1, 0:128], rs[:, sl], start=True, stop=True),
                reads=["stat.rs", "cones"], writes=["P1"], tag=f"f{l}.lbrs{c}.{s}")
            p.op("vector", lambda e, c=c, s=s, sh=sh: e.scalar_tensor_tensor(
                scrA[:, 512 * s:512 * s + 512], zap(c, 512 * s, 512, sh), 1.0,
                P2[:, 0:512], op0=ALU.mult, op1=ALU.subtract),
                reads=[f"z{c}", "P0"], writes=[f"scrA{s}"], tag=f"f{l}.lap1{c}.{s}")
            p.op("vector", lambda e, c=c, s=s: e.tensor_mul(
                scrB[:, 512 * s:512 * s + 512], scrA[:, 512 * s:512 * s + 512],
                P2[:, 512:1024]),
                reads=[f"scrA{s}", "P1"], writes=["scrB"], tag=f"f{l}.lap2{c}.{s}")
            p.op("vector", lambda e, c=c, s=s: e.tensor_scalar(
                b["zn2"][:, T * c + 512 * s:T * c + 512 * s + 512],
                scrB[:, 512 * s:512 * s + 512],
                b["b2g"][:, 8 + c:9 + c], b["b2g"][:, 16 + c:17 + c],
                op0=ALU.mult, op1=ALU.add),
                reads=["scrB", "f_b2g"], writes=[f"f_zn{c}"], tag=f"f{l}.lap3{c}.{s}")
    for c in range(8):
        p.dma("sync", zdram[128 * c:128 * c + 128, :], zap(c, 0, T), f"zsp{c}",
              reads=[f"z{c}"], writes=[f"zd{c}"])
    p.fence([], [f"z{c}" for c in range(8)] + [f"h{k}" for k in range(NCH_H)])
    def hap(jc, sl):
        if jc < 16:
            return b["hZ"][:, T * jc + sl.start:T * jc + sl.stop]
        return b["hR"][:, T * (jc - 16) + sl.start:T * (jc - 16) + sl.stop]
    for jc in range(NCH_H):
        for ag in range(2):
            p.dma("gpsimd", b["w1s"][ag][:, :].rearrange("p (c f) -> p c f", f=128),
                  d["l1_w1"][l, 2 * jc + ag].rearrange("(c p) f -> p c f", p=128),
                  f"f_w1s{ag}", writes=[f"f_w1s{ag}"])
        for s in range(NS):
            sl = slice(512 * s, 512 * s + 512)
            for ag in range(2):
                pa = P2[:, 0:512] if ag == 0 else P2[:, 512:1024]
                bank = "P0" if ag == 0 else "P1"
                for dk in range(8):
                    p.op("tensor", lambda e, dk=dk, sl=sl, pa=pa, ag=ag: e.matmul(
                        pa, b["w1s"][ag][:, 128 * dk:128 * dk + 128],
                        b["zn2"][:, T * dk + sl.start:T * dk + sl.stop],
                        start=(dk == 0), stop=(dk == 7)),
                        reads=[f"f_w1s{ag}", f"f_zn{dk}"], writes=[bank],
                        tag=f"f{l}.w1.{jc}.{ag}.{s}")
            p.op("scalar", lambda e, jc=jc, s=s: e.activation(
                scrB[:, 512 * s:512 * s + 512], P2[:, 512:1024], AF.Gelu,
                bias=b["b1"][:, 2 * jc + 1:2 * jc + 2]),
                reads=["P1", "f_b1"], writes=["scrB"], tag=f"f{l}.gg{jc}.{s}")
            p.op("vector", lambda e, jc=jc, s=s, sl=sl: e.scalar_tensor_tensor(
                hap(jc, sl), P2[:, 0:512], b["b1"][:, 2 * jc:2 * jc + 1],
                scrB[:, 512 * s:512 * s + 512], op0=ALU.add, op1=ALU.mult),
                reads=["P0", "f_b1", "scrB"], writes=[f"h{jc}"],
                tag=f"f{l}.hm{jc}.{s}")
    for e_ in range(8):
        p.dma("gpsimd", b["w2s"][:, :].rearrange("p (c f) -> p c f", f=128),
              d["l1_w2"][l, e_].rearrange("(c p) f -> p c f", p=128),
          "f_w2s", writes=["f_w2s"])
        for s in range(NS):
            sl = slice(512 * s, 512 * s + 512)
            sb = s % 4
            for k in range(NCH_H):
                p.op("tensor", lambda e, k=k, s=s, sl=sl, sb=sb: e.matmul(
                    S_ps[:, 512 * sb:512 * sb + 512], b["w2s"][:, 128 * k:128 * k + 128],
                    hap(k, sl), start=(k == 0), stop=(k == NCH_H - 1)),
                    reads=["f_w2s", f"h{k}"], writes=[f"S{sb}"],
                    tag=f"f{l}.w2.{e_}.{s}")
            slot = (e_ * NS + s) % 4
            xa = ctx["xstage"][:, 512 * slot:512 * slot + 512]
            xr = f"xs{slot}"
            p.dma("sync", xa, zdram[128 * e_:128 * e_ + 128, sl], xr,
                  reads=[f"zd{e_}"], writes=[xr])
            p.op("vector", lambda e, xa=xa, sb=sb, e_=e_: e.scalar_tensor_tensor(
                xa, S_ps[:, 512 * sb:512 * sb + 512], b["b2g"][:, e_:e_ + 1], xa,
                op0=ALU.add, op1=ALU.add),
                reads=[f"S{sb}", "f_b2g", xr], writes=[xr], tag=f"f{l}.w2a{e_}.{s}")
            p.dma("sync", zdram[128 * e_:128 * e_ + 128, sl], xa, xr,
                  reads=[xr], writes=[f"zd{e_}"])

def emit_l1f(p, ctx, d, b, zbuf, zdram, T):
    G = T + 1
    for c in range(8):
        p.op("vector", lambda e, c=c: e.memset(zbuf[:, c * G:c * G + 1], 0.0),
             reads=[], writes=[f"z{c}"], tag=f"f.guard{c}")
    SCRA = [f"scrA{i}" for i in range(T // 512)]
    for l in range(L1F_DEPTH):
        emit_l1f_layer(p, ctx, d, b, l, zbuf, zdram, T)
        if l < L1F_DEPTH - 1:
            p.fence([], [f"h{k}" for k in range(NCH_H)] + [f"z{c}" for c in range(8)]
                    + SCRA + ["scrB", "f_qkw", "f_vw1", "f_wo1", "f_wtab",
                              "f_kr0", "f_kr1", "f_vr0", "f_vr1",
                              "f_w1s0", "f_w1s1", "f_w2s", "stat.mu", "stat.rs",
                              "f_zsw", "f_qw", "xs0", "xs1", "xs2", "xs3"]
                    + [f"f_cw{i}" for i in range(8)]
                    + [f"f_zn{c}" for c in range(8)])
            for c in range(8):
                p.op("vector", lambda e, c=c: e.memset(zbuf[:, c * G:c * G + 1], 0.0),
                     reads=[], writes=[f"z{c}"], tag=f"f.g{l}.{c}")
                p.dma("sync", zbuf[:, c * G + 1:c * G + 1 + T],
                      zdram[128 * c:128 * c + 128, :], f"zrl{c}",
                      reads=[f"zd{c}"], writes=[f"z{c}"])


def emit_final_combine(p, ctx, zdram, xsrc, xdst, T):
    NS = T // 512
    xst = ctx["xstage"]
    scrB = ctx["scrB"]
    p.fence([], ["scrB", "xs0", "xs1", "fc_z0", "fc_z1"])
    for c in range(8):
        for s in range(NS):
            sl = slice(512 * s, 512 * s + 512)
            slot = (c * NS + s) % 2
            xa = xst[:, 512 * slot:512 * slot + 512]
            xr = f"xs{slot}"
            za = scrB[:, 512 * (2 + slot):512 * (2 + slot) + 512]
            zr = f"fc_z{slot}"
            p.dma("sync", xa, xsrc[128 * c:128 * c + 128, sl], xr, writes=[xr],
                  reads=[f"xC{c}.{s}.0", f"xC{c}.{s}.64"])
            p.dma("sync", za, zdram[128 * c:128 * c + 128, sl], zr,
                  reads=[f"zd{c}"], writes=[zr])
            p.op("vector", lambda e, xa=xa, za=za: e.tensor_add(xa, xa, za),
                 reads=[xr, zr], writes=[xr], tag=f"fc.add{c}.{s}")
            p.dma("sync", xdst[128 * c:128 * c + 128, sl], xa, xr, reads=[xr])


T_FULL = 2048

MLA_RES = ["m_qc", "m_kc", "m_vf", "m_vtokh", "m_ctxh", "m_wdq", "m_wuq",
           "m_wuqT", "m_wdkv", "m_wukv", "m_wukvT", "m_mq", "m_mkv", "m_wqrT",
           "m_wkrT", "m_woT", "m_wn", "m_ss", "m_rstd", "oh64", "e16", "diagm"]
L1F_RES = (["f_qkw", "f_vw1", "f_wo1", "f_wtab", "f_kr0", "f_kr1", "f_vr0",
            "f_vr1", "f_w1s0", "f_w1s1", "f_w2s", "f_zsw", "f_qw", "f_ssw",
            "f_rstdw", "f_rqt", "f_tabst", "f_qs", "f_b1", "f_b2g"]
           + [f"f_cw{i}" for i in range(8)] + [f"f_zn{c}" for c in range(8)])

def _build_fused(T):
    nc = bass.Bass("TRN2", target_bir_lowering=False, debug=False)
    xin = nc.dram_tensor("xin", [1024, T], F32, kind="ExternalInput").ap()
    xout = nc.dram_tensor("xout", [1024, T], F32, kind="ExternalOutput").ap()
    xmA = nc.dram_tensor("xmA", [1024, T], F32, kind="Internal").ap()
    x1 = nc.dram_tensor("x1", [1024, T], F32, kind="Internal").ap()
    x2 = nc.dram_tensor("x2", [1024, T], F32, kind="Internal").ap()
    zdram = nc.dram_tensor("zdram", [1024, T], F32, kind="Internal").ap()
    d = declare_l0(nc, T)
    d.update(declare_mla(nc, T))
    d.update(declare_l1f(nc, T))
    RS0 = RZ0 - 8192
    ctx = make_ctx(nc, T, xstage_at=RS0)
    mla_consts(nc, ctx)
    l1f_consts(nc, ctx, T)
    stat0 = alloc_at(nc, "stat0", [1, 2 * T], F32, RB0)
    bufs = {}
    bufs["xsrc"] = xin; bufs["xdst"] = xmA
    bufs["xsrc2"] = xmA; bufs["xdst2"] = x1
    bufs["xnb"] = alloc_at(nc, "xnb", [128, 8 * T], BF16, RZ0)
    bufs["vw"] = alloc_at(nc, "vwb", [128, 8 * 256], BF16, RZ0 + 16 * T)
    bufs["ropec"] = alloc_at(nc, "ropec", [128, T], F32, RZ0 + 16 * T + 4096)
    bufs["ropes"] = alloc_at(nc, "ropes", [128, T], F32, RZ0 + 16 * T + 4096 + 4 * T)
    bufs["qrope"] = alloc_at(nc, "qrope", [128, 8 * T], BF16, RB0)
    bufs["krope"] = alloc_at(nc, "krope", [128, 2 * T], BF16, RB0 + 16 * T)
    bufs["vtok"] = alloc_at(nc, "vtok", [128, 2 * T], BF16, RB0 + 20 * T)
    bufs["scrC"] = alloc_at(nc, "scrC", [128, T], F32, RB0 + 20 * T)
    bufs["ctxT"] = alloc_at(nc, "ctxT", [128, 8 * T], BF16, RB0 + 24 * T)
    bufs["hbuf"] = alloc_at(nc, "hbuf", [128, 32 * 1024], BF16, RB0)
    bufs["pbalt"] = alloc_at(nc, "l0_pbalt", [128, T], BF16, RB0 + 80 * 1024)
    bufs["wslots"] = [alloc_at(nc, f"wsl{i}", [128, 1024], BF16, RB0 + 84 * 1024 + 2048 * i)
                      for i in range(2)]
    bufs["w2slot"] = alloc_at(nc, "w2slot", [128, 32 * 128], BF16, RB0 + 64 * 1024)
    bufs["b1sb"] = nc.alloc_sbuf_tensor("b1sb", [128, 32], F32).ap()
    bufs["b2sb"] = nc.alloc_sbuf_tensor("b2sb", [128, 8], F32).ap()
    zbuf = alloc_at(nc, "zbuf", [128, 8 * (T + 1)], F32, RZ0)
    mb = alloc_mla_bufs(nc, ctx, T, RB0)
    fb = alloc_l1f_bufs(nc, ctx, T, xs_at=RS0)

    p = Prog(nc)
    load_consts(p, ctx)
    load_l1f_consts(p, ctx)
    emit_ln_to_bf16(p, ctx, xin, stat0, bufs["xnb"], "a", "0")
    p.fence([], ["stat.mu", "stat.rs"] + [f"qr{i}" for i in range(8)])
    emit_l0_attn(p, ctx, d, bufs)
    p.fence([f"qr{i}" for i in range(8)] + ["kr0", "kr1", "vtok"] + [f"ct{i}" for i in range(8)],
            [f"h{k}" for k in range(32)] + ["w2s"])
    emit_l0_ffn(p, ctx, d, bufs)
    p.fence([], [f"a{c}" for c in range(8)] + ["vw", "ropec", "ropes",
                 "stat.mu", "stat.rs"] + [f"z{c}" for c in range(8)]
            + [f"h{k}" for k in range(32)])
    emit_ln_to_bf16(p, ctx, x1, stat0, None, "z", "z", fp32dst=(zbuf, T + 1, 1),
                    src_res=lambda c, s: [f"xB{c}.{s}"])
    p.fence([], MLA_RES + ["stat.mu", "stat.rs", "scrC", "P.0", "P.1", "P2b", "w2s"]
            + [f"qr{i}" for i in range(8)] + ["kr0", "kr1", "vtok"]
            + [f"ct{i}" for i in range(8)] + [f"h{k}" for k in range(32)])
    load_mla_consts(p, ctx)
    emit_mla_setup(p, ctx, d, mb)
    for h in range(16):
        c2 = h // 2
        half = 64 * (h % 2)
        zap = zbuf[half:half + 64, c2 * (T + 1) + 1: c2 * (T + 1) + 1 + T]
        emit_mla_head(p, ctx, mb, h, zap, f"z{c2}", x1, x2, first_for_chunk=True)
    p.fence([], L1F_RES + MLA_RES + ["stat.mu", "stat.rs", "ws0", "ws1", "w2s",
                                      "xs0", "xs1", "xs2", "xs3", "P.0", "P.1", "P2b"])
    emit_l1f(p, ctx, d, fb, zbuf, zdram, T)
    emit_final_combine(p, ctx, zdram, x2, xout, T)
    p.barrier_all_dma()
    p.emit()
    check_sbuf(nc, RS0)
    return nc


def kernel(**inputs):
    T = T_FULL
    x = np.asarray(inputs["L0_attn_x"])
    B = x.shape[0]
    dall = prep_l0(inputs, T)
    dall.update(prep_mla(inputs))
    dall.update(prep_l1f(inputs, T))
    nc = _build_fused(T)
    maps = [{**dall, "xin": np.ascontiguousarray(x[b].T)} for b in range(B)]
    res = run_bass_kernel_spmd(nc, maps, core_ids=list(range(B)))
    out = np.stack([np.asarray(res.results[b]["xout"]).T for b in range(B)])
    return out.astype(np.float32)


# version 05
# speedup vs baseline: 1.1297x; 1.1297x over previous
import numpy as np
import ml_dtypes
import concourse.bass as bass
import concourse.mybir as mybir
from concourse.bass_utils import run_bass_kernel_spmd

F32 = mybir.dt.float32
BF16 = mybir.dt.bfloat16
AF = mybir.ActivationFunctionType
ALU = mybir.AluOpType
AX = mybir.AxisListType
BF = ml_dtypes.bfloat16

NEG = -1.0e30


class _Op:
    __slots__ = ("eng", "fn", "reads", "writes", "dma_key", "idx", "deps",
                 "inc", "cnt", "tag")
    def __init__(self):
        self.deps = set()
        self.inc = False
        self.cnt = None
        self.dma_key = None
        self.tag = ""

class Prog:
    COMPUTE = ("tensor", "vector", "scalar", "gpsimd", "sync")

    def __init__(self, nc):
        self.nc = nc
        self.ops = []
        self.wmap = {}
        self.rmap = {}

    def op(self, eng, fn, reads=(), writes=(), dma_key=None, tag=""):
        idx = len(self.ops)
        o = _Op()
        o.eng = eng; o.fn = fn; o.reads = tuple(reads); o.writes = tuple(writes)
        o.dma_key = dma_key; o.idx = idx; o.tag = tag
        self.ops.append(o)
        dep = self._dep
        for r in o.reads:
            j = self.wmap.get(r)
            if j is not None:
                dep(o, j)
            self.rmap.setdefault(r, []).append(idx)
        for r in o.writes:
            j = self.wmap.get(r)
            if j is not None:
                dep(o, j)
            for k in self.rmap.get(r, ()):
                if k != idx:
                    dep(o, k)
            self.wmap[r] = idx
            self.rmap[r] = []
        return idx

    def _dep(self, o, j):
        if j == o.idx:
            return
        oj = self.ops[j]
        if o.eng == "tensor" and oj.eng == "tensor":
            return
        o.deps.add(j)

    def dma(self, eng, out, in_, key, reads=(), writes=(), **kw):
        return self.op(eng, lambda e: e.dma_start(out=out, in_=in_, **kw),
                       reads=reads, writes=writes, dma_key=key)

    def fence(self, reads, writes, eng="sync"):
        return self.op(eng, lambda e: e.nop(), reads=list(reads), writes=list(writes))

    def barrier_all_dma(self):
        deps = set()
        for o in self.ops:
            if o.dma_key is not None:
                deps.add(o.idx)
        idx = self.op("sync", None)
        self.ops[idx].deps = deps
        return idx

    def emit(self):
        nc = self.nc
        for o in self.ops:
            for j in o.deps:
                self.ops[j].inc = True
        eng_sem = {}
        eng_cnt = {}
        dma_sem = {}
        dma_cnt = {}
        for o in self.ops:
            if o.dma_key is not None:
                if o.dma_key not in dma_sem:
                    dma_sem[o.dma_key] = nc.alloc_semaphore(f"d_{o.dma_key}"[:60])
                    dma_cnt[o.dma_key] = 0
                dma_cnt[o.dma_key] += 16
                o.cnt = dma_cnt[o.dma_key]
                o.inc = True
            elif o.inc:
                if o.eng not in eng_sem:
                    eng_sem[o.eng] = nc.alloc_semaphore(f"e_{o.eng}")
                    eng_cnt[o.eng] = 0
                eng_cnt[o.eng] += 1
                o.cnt = eng_cnt[o.eng]
        def sem_of(o):
            return dma_sem[o.dma_key] if o.dma_key is not None else eng_sem[o.eng]
        streams = {}
        for o in self.ops:
            streams.setdefault(o.eng, []).append(o)
        ops_by_idx = self.ops
        with nc.Block() as blk:
            for eng_name, olist in streams.items():
                def section(e, olist=olist):
                    water = {}
                    for o in olist:
                        waits = {}
                        for j in o.deps:
                            oj = ops_by_idx[j]
                            s = sem_of(oj)
                            v = oj.cnt
                            if waits.get(s, 0) < v:
                                waits[s] = v
                        for s, v in waits.items():
                            if water.get(s, 0) >= v:
                                continue
                            e.wait_ge(s, v)
                            water[s] = v
                        if o.fn is None:
                            continue
                        ins = o.fn(e)
                        if o.inc:
                            ins.then_inc(sem_of(o), 16 if o.dma_key is not None else 1)
                getattr(blk, eng_name)(section)
        return dma_cnt


def ceil_div(a, b):
    return (a + b - 1) // b


def emit_ln(p, ctx, xdram, stat, out_emit, tag="", src_res=None):
    nc, T = ctx["nc"], ctx["T"]
    NS = T // 512
    scr = ctx["scrA"]
    S_ps = ctx["S_ps"]
    P2 = ctx["P2_ps"]
    ones_f = ctx["ones_f32"]
    xst = ctx["xstage"]
    for c in range(8):
        for s in range(NS):
            k = (c * NS + s) % 4
            xa = xst[:, 512 * k:512 * k + 512]
            xr = f"xs{k}"
            sl = slice(512 * s, 512 * s + 512)
            p.dma("sync", xa, xdram[c * 128:(c + 1) * 128, sl], xr, writes=[xr],
                  reads=(src_res(c, s) if src_res else []))
            p.op("scalar", lambda e, xa=xa, sl=sl: e.activation(scr[:, sl], xa, AF.Square),
                 reads=[xr], writes=[f"scrA{s}"], tag=f"ln{tag}.sq{c}.{s}")
            p.op("tensor", lambda e, xa=xa, sl=sl, c=c: e.matmul(
                S_ps[0:1, sl], ones_f[:, 0:1], xa, start=(c == 0), stop=(c == 7)),
                reads=[xr, "cones"], writes=[f"S{s}"], tag=f"ln{tag}.sum{c}.{s}")
            p.op("tensor", lambda e, sl=sl, c=c: e.matmul(
                S_ps[32:33, sl], ones_f[:, 0:1], scr[:, sl], start=(c == 0), stop=(c == 7)),
                reads=[f"scrA{s}", "cones"], writes=[f"S{s}"], tag=f"ln{tag}.sqs{c}.{s}")
    mu = stat[0:1, 0:T]
    m2 = scr[0:1, 0:T]
    rs = stat[0:1, T:2 * T]
    Sres = [f"S{s}" for s in range(NS)]
    scrres = [f"scrA{s}" for s in range(NS)]
    p.op("vector", lambda e: e.tensor_scalar_mul(mu, S_ps[0:1, 0:T], 1.0 / 1024.0),
         reads=Sres, writes=["stat.mu"], tag=f"ln{tag}.mu")
    p.op("vector", lambda e: e.tensor_mul(m2, mu, mu),
         reads=["stat.mu"] + scrres, writes=scrres, tag=f"ln{tag}.m2")
    p.op("vector", lambda e: e.scalar_tensor_tensor(
        m2, S_ps[32:33, 0:T], 1.0 / 1024.0, m2, op0=ALU.mult, op1=ALU.subtract),
        reads=Sres + scrres, writes=scrres, tag=f"ln{tag}.var")
    p.op("scalar", lambda e: e.activation(rs, m2, AF.Sqrt, bias=ctx["cv"](1e-5, 1)),
         reads=scrres + ["cv"], writes=["stat.rs"], tag=f"ln{tag}.sqrt")
    p.op("vector", lambda e: e.reciprocal(rs, rs),
         reads=["stat.rs"], writes=["stat.rs"], tag=f"ln{tag}.recip")
    for c in range(8):
        for s in range(NS):
            k = (c * NS + s) % 4
            xa = xst[:, 512 * k:512 * k + 512]
            xr = f"xs{k}"
            sl = slice(512 * s, 512 * s + 512)
            p.dma("sync", xa, xdram[c * 128:(c + 1) * 128, sl], xr, writes=[xr],
                  reads=(src_res(c, s) if src_res else []))
            mu_b = P2[:, 0:512]
            p.op("tensor", lambda e, sl=sl, mu_b=mu_b: e.matmul(
                mu_b, ones_f[0:1, 0:128], mu[:, sl], start=True, stop=True),
                reads=["stat.mu", "cones"], writes=["P0"], tag=f"ln{tag}.bmu{c}.{s}")
            rs_b = P2[:, 512:1024]
            p.op("tensor", lambda e, sl=sl, rs_b=rs_b: e.matmul(
                rs_b, ones_f[0:1, 0:128], rs[:, sl], start=True, stop=True),
                reads=["stat.rs", "cones"], writes=["P1"], tag=f"ln{tag}.brs{c}.{s}")
            out_emit(p, c, s, xa, xr, (mu_b, "P0"), (rs_b, "P1"))


def emit_ln_to_bf16(p, ctx, xdram, stat, xn, xn_res_prefix, tag="0", fp32dst=None,
                    src_res=None):
    T = ctx["T"]
    scrB = ctx["scrB"]

    def apply(p, c, s, xa, xr, mu_bi, rs_bi):
        mu_b, mu_r = mu_bi
        rs_b, rs_r = rs_bi
        sl = slice(512 * s, 512 * s + 512)
        t1 = scrB[:, sl]
        p.op("vector", lambda e, xa=xa, t1=t1, mu_b=mu_b: e.scalar_tensor_tensor(
            t1, xa, 1.0, mu_b, op0=ALU.mult, op1=ALU.subtract),
            reads=[xr, mu_r], writes=["scrB"], tag=f"lnap{tag}.sub{c}.{s}")
        if fp32dst is None:
            dst = xn[:, c * T + 512 * s: c * T + 512 * s + 512]
            rr = f"{xn_res_prefix}{c}"
        else:
            ap, stride, guard = fp32dst
            dst = ap[:, c * stride + guard + 512 * s: c * stride + guard + 512 * s + 512]
            rr = f"{xn_res_prefix}{c}"
        p.op("vector", lambda e, dst=dst, t1=t1, rs_b=rs_b: e.tensor_mul(dst, t1, rs_b),
             reads=["scrB", rs_r], writes=[rr], tag=f"lnap{tag}.mul{c}.{s}")

    emit_ln(p, ctx, xdram, stat, apply, tag=tag, src_res=src_res)



MTOP = 229344
RB_SZ = 88 * 1024
RB0 = MTOP - RB_SZ
RZ_SZ = 8 * (2048 + 1) * 4
RZ0 = RB0 - RZ_SZ

def alloc_at(nc, name, shape, dtype, off):
    return nc.alloc_sbuf_tensor_at(name, shape, dtype, offset=off).ap()

def make_ctx(nc, T, xstage_at=None):
    ctx = {"nc": nc, "T": T}
    ctx["S_ps"] = nc.alloc_psum_tensor("S_ps", [128, 2048], F32).ap()
    ctx["P2_ps"] = nc.alloc_psum_tensor("P2_ps", [128, 1024], F32).ap()
    ctx["TR_ps"] = nc.alloc_psum_tensor("TR_ps", [128, 2048], BF16).ap()
    ctx["scrA"] = nc.alloc_sbuf_tensor("scrA", [128, max(T, 2048)], F32).ap()
    if xstage_at is None:
        ctx["xstage"] = nc.alloc_sbuf_tensor("xstage", [128, 2048], F32).ap()
    else:
        ctx["xstage"] = alloc_at(nc, "xstage", [128, 2048], F32, xstage_at)
    ctx["scrB"] = nc.alloc_sbuf_tensor("scrB", [128, max(T, 2048)], F32).ap()
    ctx["stats2"] = nc.alloc_sbuf_tensor("stats2", [128, 64], F32).ap()
    ctx["Pbuf"] = nc.alloc_sbuf_tensor("Pbuf", [128, max(T, 2048)], BF16).ap()
    ctx["Pbuf2"] = nc.alloc_sbuf_tensor("Pbuf2", [128, max(T, 2048)], BF16).ap()
    ctx["ptsb"] = nc.alloc_sbuf_tensor("ptsb", [128, 2048], BF16).ap()
    CVALS = [1e-5, 1.1920929e-07, 0.0]
    ctx["cvals_list"] = CVALS
    cv = np.tile(np.asarray(CVALS, np.float32)[None, :], (128, 1))
    ctx["cv_dram"] = nc.inline_tensor(cv, "cvals").ap()
    ctx["cv_sb"] = nc.alloc_sbuf_tensor("cvals_s", [128, len(CVALS)], F32).ap()
    def cvap(val, parts=128):
        k = CVALS.index(val)
        return ctx["cv_sb"][0:parts, k:k + 1]
    ctx["cv"] = cvap
    ones = np.ones((128, 128), np.float32)
    ctx["ones_f32_dram"] = nc.inline_tensor(ones, "ones_f32").ap()
    ctx["ones_bf_dram"] = nc.inline_tensor(ones.astype(BF), "ones_bf").ap()
    ctx["ident_dram"] = nc.inline_tensor(np.eye(128, dtype=BF), "ident_bf").ap()
    ctx["ones_f32"] = nc.alloc_sbuf_tensor("ones_f32s", [128, 128], F32).ap()
    ctx["ones_bf"] = nc.alloc_sbuf_tensor("ones_bfs", [128, 128], BF16).ap()
    ctx["ident"] = nc.alloc_sbuf_tensor("idents", [128, 128], BF16).ap()
    idf = np.eye(128, dtype=np.float32)
    ctx["identf_dram"] = nc.inline_tensor(idf, "identf").ap()
    ctx["identf"] = nc.alloc_sbuf_tensor("identfs", [128, 128], F32).ap()
    return ctx

def load_consts(p, ctx):
    p.dma("sync", ctx["cv_sb"][:, :], ctx["cv_dram"][:, :], "cv",
          writes=["cv"])
    p.dma("sync", ctx["ones_f32"][:, :], ctx["ones_f32_dram"][:, :], "cones",
          writes=["cones"])
    p.dma("sync", ctx["ones_bf"][:, :], ctx["ones_bf_dram"][:, :], "cones_bf",
          writes=["cones_bf"])
    p.dma("sync", ctx["ident"][:, :], ctx["ident_dram"][:, :], "cident",
          writes=["cident"])
    p.dma("sync", ctx["identf"][:, :], ctx["identf_dram"][:, :], "identf", writes=["identf"])


def prep_l0(inp, T):
    D = 1024
    Wq = inp["L0_attn_Wq"]; Wk = inp["L0_attn_Wk"]; Wv = inp["L0_attn_Wv"]
    Wo = inp["L0_attn_Wo"]
    r = np.arange(512)
    h = 4 * ((r % 128) // 32) + r // 128
    j = r % 32
    pidx = 32 * h + j
    permE = 2 * pidx
    qperm = np.concatenate([permE, permE + 1])
    Wq_p = np.ascontiguousarray(Wq[:, qperm])
    rk = np.arange(128)
    gk = rk // 32; jk = rk % 32
    kpermE = 64 * gk + 2 * jk
    kperm = np.concatenate([kpermE, kpermE + 1])
    Wk_p = np.ascontiguousarray(Wk[:, kperm])
    def fmaj(W):
        nf = W.shape[1] // 128
        return np.ascontiguousarray(W.reshape(W.shape[0], nf, 128).transpose(1, 0, 2))
    out = {
        "l0_wq": fmaj(Wq_p), "l0_wk": fmaj(Wk_p), "l0_wv": np.ascontiguousarray(Wv),
        "l0_wo": fmaj(Wo),
        "l0_w1": fmaj(inp["L0_ffn_w1"]), "l0_w2": fmaj(inp["L0_ffn_w2"]),
        "l0_b1": np.ascontiguousarray(inp["L0_ffn_b1"].reshape(32, 128).T),
        "l0_b2": np.ascontiguousarray(inp["L0_ffn_b2"].reshape(8, 128).T),
    }
    t = np.arange(T, dtype=np.float64)
    inv = 1.0 / (10000.0 ** (np.arange(0, D, 2, dtype=np.float64) / D))
    ang_q = t[None, :] * inv[pidx][:, None]
    out["l0_ropeq"] = np.stack([np.cos(ang_q) / 32.0, np.sin(ang_q) / 32.0]).astype(np.float32)
    ang_k = t[None, :] * inv[jk][:, None]
    out["l0_ropek"] = np.stack([np.cos(ang_k), np.sin(ang_k)]).astype(np.float32)
    return out


def declare_l0(nc, T):
    d = {}
    def dram(name, shape):
        d[name] = nc.dram_tensor(name, list(shape), F32, kind="ExternalInput").ap()
    dram("l0_wq", (8, 1024, 128)); dram("l0_wk", (2, 1024, 128))
    dram("l0_wv", (1024, 256)); dram("l0_wo", (8, 1024, 128))
    dram("l0_w1", (32, 1024, 128)); dram("l0_w2", (8, 4096, 128))
    dram("l0_b1", (128, 32)); dram("l0_b2", (128, 8))
    dram("l0_ropeq", (2, 512, T)); dram("l0_ropek", (2, 128, T))
    return d


def emit_l0_attn(p, ctx, d, bufs):
    T = ctx["T"]; NS = T // 512; NT = T // 128
    S_ps = ctx["S_ps"]; P2 = ctx["P2_ps"]; TR = ctx["TR_ps"]
    scrA, scrB, scrC = ctx["scrA"], ctx["scrB"], bufs["scrC"]
    st2 = ctx["stats2"]
    xnb = bufs["xnb"]; qrope = bufs["qrope"]; krope = bufs["krope"]
    vtok = bufs["vtok"]; Pb = ctx["Pbuf"]; ptsb = ctx["ptsb"]; ctxT = bufs["ctxT"]
    ws = bufs["wslots"]
    vw = bufs["vw"]
    rc = bufs["ropec"]; rs_ = bufs["ropes"]
    ident = ctx["ident"]
    Sres = ["S0", "S1", "S2", "S3"]

    SCRA = [f"scrA{i}" for i in range(NS)]

    def wload(dram3, idx, slot, key):
        ap = dram3[idx].rearrange("(c p) f -> p c f", p=128)
        out = ws[slot][:, :].rearrange("p (c f) -> p c f", f=128)
        p.dma("gpsimd", out, ap, key, writes=[f"ws{slot}"])

    def proj_chunk(fm, slot, psum_banks=Sres, dst=None):
        for s in range(NS):
            sl = slice(512 * s, 512 * s + 512)
            for dk in range(8):
                p.op("tensor", lambda e, sl=sl, dk=dk, slot=slot: e.matmul(
                    S_ps[:, sl], ws[slot][:, 128 * dk:128 * dk + 128],
                    xnb[:, dk * T + sl.start: dk * T + sl.stop],
                    start=(dk == 0), stop=(dk == 7)),
                    reads=[f"ws{slot}", f"a{dk}"], writes=[f"S{s}"],
                    tag=f"l0.proj{fm}.{s}")

    def rope_load(tab, c, key):
        p.dma("sync", rc[:, 0:T], tab[c * 128:(c + 1) * 128, 0:T], "ropec",
              writes=["ropec"])
        p.dma("sync", rs_[:, 0:T], tab[512 + 128 * c if False else 0, 0:T], key,
              writes=["ropes"])

    def qk_phase(nf_half, wdram, tabdram, dst, dst_prefix, nrows):
        for c in range(nf_half):
            p.dma("sync", rc[:, 0:T], tabdram[0, 128 * c:128 * c + nrows, 0:T], "ropec",
                  writes=["ropec"])
            p.dma("sync", rs_[:, 0:T], tabdram[1, 128 * c:128 * c + nrows, 0:T], "ropes",
                  writes=["ropes"])
            wload(wdram, c, 0, "ws0")
            proj_chunk(c, 0)
            p.op("vector", lambda e: e.tensor_copy(scrA[0:nrows, 0:T], S_ps[0:nrows, 0:T]),
                 reads=Sres, writes=SCRA, tag="l0.evacE")
            wload(wdram, nf_half + c, 1, "ws1")
            proj_chunk(nf_half + c, 1)
            eT = slice(c * T, c * T + T)
            oT = slice((nf_half + c) * T, (nf_half + c) * T + T)
            p.op("vector", lambda e: e.tensor_mul(scrB[0:nrows, 0:T], scrA[0:nrows, 0:T], rc[0:nrows, 0:T]),
                 reads=SCRA + ["ropec"], writes=["scrB"], tag="l0.r1")
            p.op("vector", lambda e: e.tensor_mul(scrC[0:nrows, 0:T], S_ps[0:nrows, 0:T], rs_[0:nrows, 0:T]),
                 reads=Sres + ["ropes"], writes=["scrC"], tag="l0.r2")
            p.op("vector", lambda e, eT=eT: e.tensor_sub(dst[0:nrows, eT], scrB[0:nrows, 0:T], scrC[0:nrows, 0:T]),
                 reads=["scrB", "scrC"], writes=[f"{dst_prefix}{c}"], tag="l0.r3")
            p.op("vector", lambda e: e.tensor_mul(scrB[0:nrows, 0:T], scrA[0:nrows, 0:T], rs_[0:nrows, 0:T]),
                 reads=SCRA + ["ropes"], writes=["scrB"], tag="l0.r4")
            p.op("vector", lambda e: e.tensor_mul(scrC[0:nrows, 0:T], S_ps[0:nrows, 0:T], rc[0:nrows, 0:T]),
                 reads=Sres + ["ropec"], writes=["scrC"], tag="l0.r5")
            p.op("vector", lambda e, oT=oT: e.tensor_add(dst[0:nrows, oT], scrB[0:nrows, 0:T], scrC[0:nrows, 0:T]),
                 reads=["scrB", "scrC"], writes=[f"{dst_prefix}{nf_half + c}"], tag="l0.r6")

    qk_phase(4, d["l0_wq"], d["l0_ropeq"], qrope, "qr", 128)
    qk_phase(1, d["l0_wk"], d["l0_ropek"], krope, "kr", 128)

    p.fence([], ["scrC", "vtok"])
    p.dma("gpsimd", vw[:, :].rearrange("p (c f) -> p c f", f=256),
          d["l0_wv"].rearrange("(c p) f -> p c f", p=128), "vw", writes=["vw"])
    for ti in range(NT):
        for dk in range(8):
            p.op("tensor", lambda e, ti=ti, dk=dk: e.matmul(
                P2[:, 0:256], xnb[:, dk * T + 128 * ti: dk * T + 128 * ti + 128],
                vw[:, 256 * dk:256 * dk + 256], start=(dk == 0), stop=(dk == 7)),
                reads=[f"a{dk}", "vw"], writes=["P0"], tag=f"l0.v{ti}")
        p.op("vector", lambda e, ti=ti: e.tensor_copy(vtok[:, 256 * ti:256 * ti + 256], P2[:, 0:256]),
             reads=["P0"], writes=["vtok"], tag=f"l0.vev{ti}")

    def l0_s1(h, qt):
        base = 32 * (h // 4)
        ce_, co = h % 4, 4 + h % 4
        qsl = slice(128 * qt, 128 * qt + 128)
        pe = qt % 2
        negm = st2[:, 4 * pe:4 * pe + 1]
        for js in range(NS):
            jsl = slice(512 * js, 512 * js + 512)
            p.op("tensor", lambda e, qsl=qsl, jsl=jsl, base=base, ce_=ce_: e.matmul(
                S_ps[:, jsl],
                qrope[base:base + 32, ce_ * T + qsl.start: ce_ * T + qsl.stop],
                krope[base:base + 32, jsl], start=True, stop=False,
                tile_position=(base, 0)),
                reads=[f"qr{ce_}", "kr0"], writes=[f"S{js}"], tag=f"l0.se{h}.{qt}")
            p.op("tensor", lambda e, qsl=qsl, jsl=jsl, base=base, co=co: e.matmul(
                S_ps[:, jsl],
                qrope[base:base + 32, co * T + qsl.start: co * T + qsl.stop],
                krope[base:base + 32, T + jsl.start: T + jsl.stop],
                start=False, stop=True, tile_position=(base, 0)),
                reads=[f"qr{co}", "kr1"], writes=[f"S{js}"], tag=f"l0.so{h}.{qt}")
        p.op("vector", lambda e, negm=negm: e.reduce_max(negm, S_ps[:, 0:T], axis=AX.X, negate=True),
             reads=Sres, writes=[f"negm{pe}"], tag=f"l0.max{h}.{qt}")

    def l0_s2a(h, qt):
        pe = qt % 2
        Pba = Pb if pe == 0 else bufs["pbalt"]
        negm = st2[:, 4 * pe:4 * pe + 1]
        sums = st2[:, 4 * pe + 1:4 * pe + 2]
        rsum = st2[:, 4 * pe + 2:4 * pe + 3]
        p.op("scalar", lambda e, negm=negm, sums=sums, Pba=Pba: e.activation(
            Pba[:, 0:T], S_ps[:, 0:T], AF.Exp, bias=negm, accum_out=sums),
             reads=Sres + [f"negm{pe}"], writes=[f"P.{pe}", f"sums{pe}"],
             tag=f"l0.exp{h}.{qt}")
        p.op("vector", lambda e, sums=sums, rsum=rsum: e.reciprocal(rsum, sums),
             reads=[f"sums{pe}"], writes=[f"rsum{pe}"], tag=f"l0.rec{h}.{qt}")
        Pb2 = ctx["Pbuf2"]
        p.op("vector", lambda e, Pb2=Pb2, Pba=Pba, rsum=rsum: e.tensor_scalar_mul(
            Pb2[:, 0:T], Pba[:, 0:T], rsum),
             reads=[f"P.{pe}", f"rsum{pe}"], writes=["P2b"], tag=f"l0.ps{h}.{qt}")

    def l0_s2b(h, qt):
        pe = qt % 2
        qsl = slice(128 * qt, 128 * qt + 128)
        Pb2 = ctx["Pbuf2"]
        for jt in range(NT):
            sl2 = jt % 2
            trof = 1024 * sl2
            p.op("tensor", lambda e, jt=jt, trof=trof, Pb2=Pb2: e.transpose(
                TR[:, trof:trof + 128], Pb2[:, 128 * jt:128 * jt + 128], ident),
                reads=["P2b", "cident"], writes=[f"TR{sl2}"], tag=f"l0.tr{h}.{qt}.{jt}")
            p.op("vector" if jt % 2 == 0 else "scalar", lambda e, jt=jt, trof=trof:
                 (e.tensor_copy if hasattr(e, "tensor_copy") else e.copy)(
                ptsb[:, 128 * jt:128 * jt + 128], TR[:, trof:trof + 128]),
                reads=[f"TR{sl2}"], writes=[f"pt{jt}"], tag=f"l0.trev{h}.{qt}.{jt}")
        pvb = (P2[64 * (h % 2):64 * (h % 2) + 64, 256:384] if pe == 0
               else P2[64 * (h % 2):64 * (h % 2) + 64, 768:896])
        pvres = "P0" if pe == 0 else "P1"
        for jt in range(NT):
            p.op("tensor", lambda e, jt=jt, h=h, pvb=pvb: e.matmul(
                pvb,
                vtok[:, 256 * jt + 64 * (h // 4): 256 * jt + 64 * (h // 4) + 64],
                ptsb[:, 128 * jt:128 * jt + 128],
                start=(jt == 0), stop=(jt == NT - 1), tile_position=(0, 64 * (h % 2))),
                reads=["vtok", f"pt{jt}"], writes=[pvres], tag=f"l0.pv{h}.{qt}.{jt}")
        half = 64 * (h % 2)
        p.op("vector", lambda e, h=h, qsl=qsl, half=half, pvb=pvb: e.tensor_copy(
            ctxT[half:half + 64, (h // 2) * T + qsl.start: (h // 2) * T + qsl.stop],
            pvb),
            reads=[pvres], writes=[f"ct{h // 2}"], tag=f"l0.cev{h}.{qt}")

    tiles0 = [(h, qt) for h in range(16) for qt in range(NT)]
    l0_s1(*tiles0[0])
    l0_s2a(*tiles0[0])
    for ti_ in range(1, len(tiles0)):
        l0_s1(*tiles0[ti_])
        l0_s2b(*tiles0[ti_ - 1])
        l0_s2a(*tiles0[ti_])
    l0_s2b(*tiles0[-1])

    xst = ctx["xstage"]
    xsrc = bufs["xsrc"]; xdst = bufs["xdst"]
    for e_ in range(8):
        wload(d["l0_wo"], e_, e_ % 2, f"ws{e_ % 2}")
        for s in range(NS):
            sl = slice(512 * s, 512 * s + 512)
            slot = (e_ * NS + s) % 4
            xa = xst[:, 512 * slot:512 * slot + 512]
            xr = f"xs{slot}"
            p.dma("sync", xa, xsrc[e_ * 128:(e_ + 1) * 128, sl], xr, writes=[xr])
            for ic in range(8):
                p.op("tensor", lambda e, sl=sl, ic=ic, slot2=e_ % 2: e.matmul(
                    S_ps[:, sl], ws[slot2][:, 128 * ic:128 * ic + 128],
                    ctxT[:, ic * T + sl.start: ic * T + sl.stop],
                    start=(ic == 0), stop=(ic == 7)),
                    reads=[f"ws{e_ % 2}", f"ct{ic}"], writes=[f"S{s}"],
                    tag=f"l0.wo{e_}.{s}")
            p.op("vector", lambda e, xa=xa, s=s, sl=sl: e.tensor_add(
                xa, xa, S_ps[:, sl]),
                reads=[f"S{s}", xr], writes=[xr], tag=f"l0.xadd{e_}.{s}")
            p.dma("sync", xdst[e_ * 128:(e_ + 1) * 128, sl], xa, xr, reads=[xr],
                  writes=[f"xA{e_}.{s}"])


def emit_l0_ffn(p, ctx, d, bufs):
    T = ctx["T"]; NS = T // 512
    S_ps = ctx["S_ps"]; P2 = ctx["P2_ps"]
    xnb = bufs["xnb"]; hbuf = bufs["hbuf"]
    ws = bufs["wslots"]; w2s = bufs["w2slot"]
    b1 = bufs["b1sb"]; b2 = bufs["b2sb"]
    p.dma("sync", b1[:, :], d["l0_b1"][:, :], "b1sb", writes=["b1"])
    p.dma("sync", b2[:, :], d["l0_b2"][:, :], "b2sb", writes=["b2"])
    TH = T // 1024 if T >= 1024 else 1
    HW = T // TH
    NS2 = HW // 512
    for th in range(TH):
        t0 = th * HW
        for fm in range(32):
            slot = fm % 2
            p.dma("gpsimd", ws[slot][:, :].rearrange("p (c f) -> p c f", f=128),
                  d["l0_w1"][fm].rearrange("(c p) f -> p c f", p=128),
                  f"ws{slot}", writes=[f"ws{slot}"])
            for s in range(NS2):
                sl = slice(t0 + 512 * s, t0 + 512 * s + 512)
                bank = "P0" if fm % 2 == 0 else "P1"
                pa = P2[:, 0:512] if fm % 2 == 0 else P2[:, 512:1024]
                for dk in range(8):
                    p.op("tensor", lambda e, sl=sl, dk=dk, slot=slot, pa=pa: e.matmul(
                        pa, ws[slot][:, 128 * dk:128 * dk + 128],
                        xnb[:, dk * T + sl.start: dk * T + sl.stop],
                        start=(dk == 0), stop=(dk == 7)),
                        reads=[f"ws{slot}", f"a{dk}"], writes=[bank],
                        tag=f"l0f.w1.{fm}.{s}")
                p.op("scalar", lambda e, fm=fm, s=s, pa=pa: e.activation(
                    hbuf[:, fm * HW + 512 * s: fm * HW + 512 * s + 512], pa,
                    AF.Gelu, bias=b1[:, fm:fm + 1]),
                    reads=[bank, "b1"], writes=[f"h{fm}"], tag=f"l0f.gelu.{fm}.{s}")
        for e_ in range(8):
            p.dma("gpsimd", w2s[:, :].rearrange("p (c f) -> p c f", f=128),
                  d["l0_w2"][e_].rearrange("(c p) f -> p c f", p=128),
                  "w2s", writes=["w2s"])
            for s in range(NS2):
                sl = slice(t0 + 512 * s, t0 + 512 * s + 512)
                sps = s % 4
                for k in range(32):
                    p.op("tensor", lambda e, s=s, k=k, sps=sps: e.matmul(
                        S_ps[:, 512 * sps:512 * sps + 512], w2s[:, 128 * k:128 * k + 128],
                        hbuf[:, k * HW + 512 * s: k * HW + 512 * s + 512],
                        start=(k == 0), stop=(k == 31)),
                        reads=["w2s", f"h{k}"], writes=[f"S{sps}"],
                        tag=f"l0f.w2.{e_}.{s}")
                slot = (e_ * NS2 + s) % 4
                xa = ctx["xstage"][:, 512 * slot:512 * slot + 512]
                xr = f"xs{slot}"
                p.dma("sync", xa, bufs["xsrc2"][e_ * 128:(e_ + 1) * 128, sl], xr,
                      writes=[xr], reads=[f"xA{e_}.{s}"])
                p.op("vector", lambda e, xa=xa, sps=sps, e_=e_: e.scalar_tensor_tensor(
                    xa, S_ps[:, 512 * sps:512 * sps + 512], b2[:, e_:e_ + 1],
                    xa, op0=ALU.add, op1=ALU.add),
                    reads=[f"S{sps}", "b2", xr], writes=[xr],
                    tag=f"l0f.add.{e_}.{s}")
                p.dma("sync", bufs["xdst2"][e_ * 128:(e_ + 1) * 128, sl], xa, xr,
                      reads=[xr], writes=[f"xB{e_}.{s}"])


def declare_mla(nc, T):
    d = {}
    def dram(name, shape):
        d[name] = nc.dram_tensor(name, list(shape), F32, kind="ExternalInput").ap()
    dram("mla_wdq", (512, 64)); dram("mla_wuq", (64, 512))
    dram("mla_wdkv", (1024, 64)); dram("mla_wukv", (128, 1024))
    dram("mla_wqr", (64, 64)); dram("mla_wkr", (64, 64)); dram("mla_wo", (64, 64))
    dram("mla_qw", (64, 1)); dram("mla_kw", (64, 1))
    return d


def prep_mla(inp):
    return {
        "mla_wdq": inp["L1_attn_W_DQ"], "mla_wuq": inp["L1_attn_W_UQ"],
        "mla_wdkv": inp["L1_attn_W_DKV"], "mla_wukv": inp["L1_attn_W_UKV"],
        "mla_wqr": inp["L1_attn_W_QR"], "mla_wkr": inp["L1_attn_W_KR"],
        "mla_wo": inp["L1_attn_W_O"],
        "mla_qw": np.ascontiguousarray(inp["L1_attn_q_norm_w"].reshape(64, 1)),
        "mla_kw": np.ascontiguousarray(inp["L1_attn_k_norm_w"].reshape(64, 1)),
    }


def alloc_mla_bufs(nc, ctx, T, base):
    NT = T // 128
    b = {}
    off = [base]
    def A(name, shape, dtype):
        ap = alloc_at(nc, name, shape, dtype, off[0])
        off[0] += shape[-1] * (4 if dtype == F32 else 2)
        off[0] = (off[0] + 31) & ~31
        return ap
    b["qcomb"] = A("m_qcomb", [128, T], BF16)
    b["kcomb"] = A("m_kcomb", [128, T], BF16)
    b["vf"] = A("m_vf", [128, T], BF16)
    b["vtokh"] = A("m_vtokh", [128, NT * 64], BF16)
    b["ctxh"] = A("m_ctxh", [64, T], BF16)
    b["wdq"] = A("m_wdq", [128, 256], F32)
    b["wuq"] = A("m_wuq", [64, 512], F32)
    b["wuqT"] = A("m_wuqT", [128, 256], F32)
    b["wdkv"] = A("m_wdkv", [128, 512], F32)
    b["wukv"] = A("m_wukv", [128, 1024], F32)
    b["wukvT"] = A("m_wukvT", [128, 1024], F32)
    b["mq"] = A("m_mq", [128, 64], F32)
    b["mkv"] = A("m_mkv", [128, 128], F32)
    b["wqrT"] = A("m_wqrT", [64, 64], BF16)
    b["wkrT"] = A("m_wkrT", [128, 64], F32)
    b["woT"] = A("m_woT", [64, 64], BF16)
    b["wn"] = A("m_wn", [64, 2], F32)
    b["ss"] = A("m_ss", [16, T], F32)
    b["rstd"] = A("m_rstd", [16, T], F32)
    b["pbalt"] = A("m_pbalt", [128, T], BF16)
    ctx["oh64"] = A("oh64s", [64, 256], F32)
    ctx["e16"] = A("e16s", [16, 1024], F32)
    ctx["diagm"] = A("diagms", [128, 128], F32)
    b["_end"] = off[0]
    return b


def check_sbuf(nc, limit=None):
    fill = 16481 + (212863 - nc.sbuf_bytes_remaining)
    lim = RZ0 if limit is None else limit
    assert fill <= lim, f"auto sbuf fill {fill} exceeds manual region base {lim}"


def mla_consts(nc, ctx):
    oh = np.zeros((64, 16 * 16), np.float32)
    for h in range(16):
        oh[:, 16 * h + h] = 1.0
    ctx["oh64_dram"] = nc.inline_tensor(oh, "oh64").ap()
    e16 = np.zeros((16, 16 * 64), np.float32)
    for h in range(16):
        e16[h, 64 * h:64 * h + 64] = 1.0
    ctx["e16_dram"] = nc.inline_tensor(e16, "e16").ap()
    dm = np.zeros((128, 128), np.float32)
    for i in range(128):
        dm[i, i + 1:] = NEG
    ctx["diag_dram"] = nc.inline_tensor(dm, "diagm").ap()


def load_mla_consts(p, ctx):
    p.dma("sync", ctx["oh64"][:, :], ctx["oh64_dram"][:, :], "oh64", writes=["oh64"])
    p.dma("sync", ctx["e16"][:, :], ctx["e16_dram"][:, :], "e16", writes=["e16"])
    p.dma("sync", ctx["diagm"][:, :], ctx["diag_dram"][:, :], "diagm", writes=["diagm"])



def emit_mla_setup(p, ctx, d, b):
    S_ps = ctx["S_ps"]; P2 = ctx["P2_ps"]
    idf = ctx["identf"]
    def ld(dst, src, key, rearr=None):
        p.dma("sync", dst, src, key, writes=[key])
    ld(b["wdq"][:, :].rearrange("p (c f) -> p c f", f=64),
       d["mla_wdq"].rearrange("(c p) f -> p c f", p=128), "m_wdq")
    ld(b["wuq"][:, :], d["mla_wuq"][:, :], "m_wuq")
    ld(b["wdkv"][:, :].rearrange("p (c f) -> p c f", f=64),
       d["mla_wdkv"].rearrange("(c p) f -> p c f", p=128), "m_wdkv")
    ld(b["wukv"][:, :], d["mla_wukv"][:, :], "m_wukv")
    ld(b["wn"][:, 0:1], d["mla_qw"][:, :], "m_wn")
    p.dma("sync", b["wn"][:, 1:2], d["mla_kw"][:, :], "m_wn", writes=["m_wn"])
    p.op("vector", lambda e: e.tensor_scalar_mul(b["wn"][:, 0:1], b["wn"][:, 0:1], 0.125),
         reads=["m_wn"], writes=["m_wn"], tag="mla.wq8")
    for c in range(4):
        p.op("tensor", lambda e, c=c: e.transpose(
            S_ps[:, 0:64], b["wuq"][0:64, 128 * c:128 * c + 128], idf[0:64, 0:64]),
            reads=["m_wuq", "identf"], writes=["S0"], tag=f"mla.tuq{c}")
        p.op("vector", lambda e, c=c: e.tensor_copy(b["wuqT"][:, 64 * c:64 * c + 64], S_ps[:, 0:64]),
             reads=["S0"], writes=["m_wuqT"], tag=f"mla.tuqe{c}")
    for c in range(8):
        p.op("tensor", lambda e, c=c: e.transpose(
            S_ps[:, 0:128], b["wukv"][:, 128 * c:128 * c + 128], idf),
            reads=["m_wukv", "identf"], writes=["S0"], tag=f"mla.tukv{c}")
        p.op("vector", lambda e, c=c: e.tensor_copy(b["wukvT"][:, 128 * c:128 * c + 128], S_ps[:, 0:128]),
             reads=["S0"], writes=["m_wukvT"], tag=f"mla.tukve{c}")
    for half in (0, 64):
        for c in range(4):
            p.op("tensor", lambda e, c=c, half=half: e.matmul(
                P2[half:half + 64, 0:64], b["wdq"][:, 64 * c:64 * c + 64],
                b["wuqT"][:, 64 * c:64 * c + 64], start=(c == 0), stop=(c == 3),
                tile_position=(0, half)),
                reads=["m_wdq", "m_wuqT"], writes=["P0"], tag=f"mla.mq{half}.{c}")
        p.op("vector", lambda e, half=half: e.tensor_copy(
            b["mq"][half:half + 64, 0:64], P2[half:half + 64, 0:64]),
            reads=["P0"], writes=["m_mq"], tag=f"mla.mqe{half}")
        for c in range(8):
            p.op("tensor", lambda e, c=c, half=half: e.matmul(
                P2[half:half + 64, 512:640], b["wdkv"][:, 64 * c:64 * c + 64],
                b["wukvT"][:, 128 * c:128 * c + 128], start=(c == 0), stop=(c == 7),
                tile_position=(0, half)),
                reads=["m_wdkv", "m_wukvT"], writes=["P1"], tag=f"mla.mkv{half}.{c}")
        p.op("vector", lambda e, half=half: e.tensor_copy(
            b["mkv"][half:half + 64, 0:128], P2[half:half + 64, 512:640]),
            reads=["P1"], writes=["m_mkv"], tag=f"mla.mkve{half}")
    ld(b["wuq"][0:64, 0:64], d["mla_wqr"][:, :], "m_wuq")
    p.op("tensor", lambda e: e.transpose(S_ps[0:64, 0:64], b["wuq"][0:64, 0:64], idf[0:64, 0:64]),
         reads=["m_wuq", "identf"], writes=["S0"], tag="mla.tqr")
    p.op("vector", lambda e: e.tensor_copy(b["wqrT"][:, :], S_ps[0:64, 0:64]),
         reads=["S0"], writes=["m_wqrT"], tag="mla.tqre")
    ld(b["wuq"][0:64, 64:128], d["mla_wo"][:, :], "m_wuq")
    p.op("tensor", lambda e: e.transpose(S_ps[0:64, 64:128], b["wuq"][0:64, 64:128], idf[0:64, 0:64]),
         reads=["m_wuq", "identf"], writes=["S0"], tag="mla.two")
    p.op("vector", lambda e: e.tensor_copy(b["woT"][:, :], S_ps[0:64, 64:128]),
         reads=["S0"], writes=["m_woT"], tag="mla.twoe")
    ld(b["wuq"][0:64, 128:192], d["mla_wkr"][:, :], "m_wuq")
    p.op("tensor", lambda e: e.transpose(
        S_ps[0:64, 128:192], b["wuq"][0:64, 128:192], idf[0:64, 0:64]),
        reads=["m_wuq", "identf"], writes=["S0"], tag="mla.tkr")
    p.op("vector", lambda e: e.tensor_copy(
        b["wkrT"][0:64, 0:64], S_ps[0:64, 128:192]),
        reads=["S0"], writes=["m_wkrT"], tag="mla.tkre")
    p.dma("sync", b["wkrT"][64:128, 0:64], b["wkrT"][0:64, 0:64], "m_wkrTd",
          reads=["m_wkrT"], writes=["m_wkrT"])


def emit_mla_head(p, ctx, b, h, zap, zres, xsrc, xdst, first_for_chunk):
    T = ctx["T"]; NS = T // 512; NT = T // 128
    S_ps = ctx["S_ps"]; P2 = ctx["P2_ps"]; TR = ctx["TR_ps"]
    scrA, scrB = ctx["scrA"], ctx["scrB"]
    st2 = ctx["stats2"]
    Pb = ctx["Pbuf"]; ptsb = ctx["ptsb"]
    half = 64 * (h % 2)
    c2 = h // 2
    Sres = [f"S{i}" for i in range(NS)]
    SCRA = [f"scrA{i}" for i in range(NS)]
    eps = ctx["cv"](1.1920929e-07, 16)

    def norm_side(src_dst, w_col, tagp):
        p.op("scalar", lambda e: e.activation(scrB[0:64, 0:T], scrA[0:64, 0:T], AF.Square),
             reads=SCRA, writes=["scrB"], tag=f"{tagp}.sq")
        for s in range(NS):
            sl = slice(512 * s, 512 * s + 512)
            p.op("tensor", lambda e, sl=sl: e.matmul(
                P2[0:16, 0:512], ctx["oh64"][:, 16 * h:16 * h + 16], scrB[0:64, sl],
                start=True, stop=True),
                reads=["scrB", "oh64"], writes=["P0"], tag=f"{tagp}.ss{s}")
            p.op("vector", lambda e, sl=sl: e.tensor_copy(b["ss"][:, sl], P2[0:16, 0:512]),
                 reads=["P0"], writes=["m_ss"], tag=f"{tagp}.sse{s}")
        p.op("vector", lambda e: e.tensor_scalar(
            b["rstd"][:, 0:T], b["ss"][:, 0:T], 1.0 / 64.0, eps, op0=ALU.mult, op1=ALU.add),
            reads=["m_ss", "cv"], writes=["m_rstd"], tag=f"{tagp}.ms")
        p.op("scalar", lambda e: e.activation(b["rstd"][:, 0:T], b["rstd"][:, 0:T], AF.Sqrt),
             reads=["m_rstd"], writes=["m_rstd"], tag=f"{tagp}.sqrt")
        p.op("vector", lambda e: e.reciprocal(b["rstd"][:, 0:T], b["rstd"][:, 0:T]),
             reads=["m_rstd"], writes=["m_rstd"], tag=f"{tagp}.rec")
        for s in range(NS):
            sl = slice(512 * s, 512 * s + 512)
            p.op("tensor", lambda e, sl=sl: e.matmul(
                P2[0:64, 0:512], ctx["e16"][:, 64 * h:64 * h + 64], b["rstd"][:, sl],
                start=True, stop=True),
                reads=["m_rstd", "e16"], writes=["P0"], tag=f"{tagp}.bc{s}")
            p.op("vector", lambda e, sl=sl, w_col=w_col: e.scalar_tensor_tensor(
                src_dst[0:64, sl], scrA[0:64, sl], b["wn"][:, w_col:w_col + 1],
                P2[0:64, 0:512], op0=ALU.mult, op1=ALU.mult),
                reads=[f"scrA{sl.start // 512}", "m_wn", "P0"], writes=[src_dst is b["qcomb"] and "m_qc" or "m_kc"],
                tag=f"{tagp}.ap{s}")

    for s in range(NS):
        sl = slice(512 * s, 512 * s + 512)
        p.op("tensor", lambda e, sl=sl: e.matmul(
            S_ps[0:64, sl], b["mq"][half:half + 64, 0:64], zap[:, sl],
            start=True, stop=True),
            reads=["m_mq", zres], writes=[f"S{s}"], tag=f"mla{h}.qc{s}")
    p.op("vector", lambda e: e.tensor_copy(scrA[0:64, 0:T], S_ps[0:64, 0:T]),
         reads=Sres, writes=SCRA, tag=f"mla{h}.qce")
    norm_side(b["qcomb"], 0, f"mla{h}.qn")
    for s in range(NS):
        sl = slice(512 * s, 512 * s + 512)
        p.op("tensor", lambda e, sl=sl: e.matmul(
            S_ps[64:128, sl], b["wqrT"][:, :], b["qcomb"][0:64, sl],
            start=True, stop=True, tile_position=(0, 64)),
            reads=["m_wqrT", "m_qc"], writes=[f"S{s}"], tag=f"mla{h}.qr{s}")
        p.op("vector", lambda e, sl=sl: e.tensor_copy(
            b["qcomb"][64:128, sl], S_ps[64:128, sl]),
            reads=[f"S{s}"], writes=["m_qc"], tag=f"mla{h}.qre{s}")
    for s in range(NS):
        sl = slice(512 * s, 512 * s + 512)
        p.op("tensor", lambda e, sl=sl: e.matmul(
            S_ps[:, sl], b["mkv"][half:half + 64, 0:128], zap[:, sl],
            start=True, stop=True),
            reads=["m_mkv", zres], writes=[f"S{s}"], tag=f"mla{h}.kv{s}")
        p.op("vector", lambda e, sl=sl: e.tensor_copy(scrA[0:64, sl], S_ps[0:64, sl]),
             reads=[f"S{s}"], writes=[f"scrA{s}"], tag=f"mla{h}.kce{s}")
        p.op("vector", lambda e, sl=sl: e.tensor_copy(b["vf"][64:128, sl], S_ps[64:128, sl]),
             reads=[f"S{s}"], writes=["m_vf"], tag=f"mla{h}.vce{s}")
    norm_side(b["kcomb"], 1, f"mla{h}.kn")
    for s in range(NS):
        sl = slice(512 * s, 512 * s + 512)
        p.op("tensor", lambda e, sl=sl: e.matmul(
            S_ps[64:128, sl], b["wkrT"][half:half + 64, 0:64], zap[:, sl],
            start=True, stop=True, tile_position=(half, 64)),
            reads=["m_wkrT", zres], writes=[f"S{s}"], tag=f"mla{h}.kr{s}")
        p.op("vector", lambda e, sl=sl: e.tensor_copy(
            b["kcomb"][64:128, sl], S_ps[64:128, sl]),
            reads=[f"S{s}"], writes=["m_kc"], tag=f"mla{h}.kre{s}")
    for jt in range(NT):
        sl2 = jt % 2
        trof = 1024 * sl2
        p.op("tensor", lambda e, jt=jt, trof=trof: e.transpose(
            TR[:, trof:trof + 64], b["vf"][64:128, 128 * jt:128 * jt + 128],
            ctx["ident"][64:128, 64:128]),
            reads=["m_vf", "cident"], writes=[f"TR{sl2}"], tag=f"mla{h}.vt{jt}")
        p.op("vector", lambda e, jt=jt, trof=trof: e.tensor_copy(
            b["vtokh"][:, 64 * jt:64 * jt + 64], TR[:, trof:trof + 64]),
            reads=[f"TR{sl2}"], writes=["m_vtokh"], tag=f"mla{h}.vte{jt}")
    def m_stage1(qt):
        nj = 128 * (qt + 1)
        njs = ceil_div(nj, 512)
        pe = qt % 2
        negm = st2[:, 4 * pe:4 * pe + 1]
        for js in range(njs):
            jl = min(512, nj - 512 * js)
            sl = slice(512 * js, 512 * js + jl)
            p.op("tensor", lambda e, sl=sl, qt=qt: e.matmul(
                S_ps[:, sl], b["qcomb"][:, 128 * qt:128 * qt + 128], b["kcomb"][:, sl],
                start=True, stop=True),
                reads=["m_qc", "m_kc"], writes=[f"S{js}"], tag=f"mla{h}.s{qt}.{js}")
        dt_ = slice(128 * qt, 128 * qt + 128)
        p.op("vector", lambda e, dt_=dt_: e.tensor_add(S_ps[:, dt_], S_ps[:, dt_], ctx["diagm"][:, :]),
             reads=[f"S{qt // 4}", "diagm"], writes=[f"S{qt // 4}"], tag=f"mla{h}.dm{qt}")
        p.op("vector", lambda e, nj=nj, negm=negm: e.reduce_max(negm, S_ps[:, 0:nj], axis=AX.X, negate=True),
             reads=[f"S{js2}" for js2 in range(njs)], writes=[f"negm{pe}"], tag=f"mla{h}.mx{qt}")

    def m_stage2a(qt):
        nj = 128 * (qt + 1)
        njs = ceil_div(nj, 512)
        pe = qt % 2
        Pba = Pb if pe == 0 else b["pbalt"]
        negm = st2[:, 4 * pe:4 * pe + 1]
        sums = st2[:, 4 * pe + 1:4 * pe + 2]
        rsum = st2[:, 4 * pe + 2:4 * pe + 3]
        p.op("scalar", lambda e, nj=nj, negm=negm, sums=sums, Pba=Pba: e.activation(
            Pba[:, 0:nj], S_ps[:, 0:nj], AF.Exp, bias=negm, accum_out=sums),
             reads=[f"S{js2}" for js2 in range(njs)] + [f"negm{pe}"],
             writes=[f"P.{pe}", f"sums{pe}"], tag=f"mla{h}.ex{qt}")
        p.op("vector", lambda e, sums=sums, rsum=rsum: e.reciprocal(rsum, sums),
             reads=[f"sums{pe}"], writes=[f"rsum{pe}"], tag=f"mla{h}.rc{qt}")
        Pb2 = ctx["Pbuf2"]
        p.op("vector", lambda e, nj=nj, Pb2=Pb2, Pba=Pba, rsum=rsum: e.tensor_scalar_mul(
            Pb2[:, 0:nj], Pba[:, 0:nj], rsum),
            reads=[f"P.{pe}", f"rsum{pe}"], writes=["P2b"], tag=f"mla{h}.pm{qt}")

    def m_stage2b(qt):
        pe = qt % 2
        Pb2 = ctx["Pbuf2"]
        for jt in range(qt + 1):
            sl2 = jt % 2
            trof = 1024 * sl2
            p.op("tensor", lambda e, jt=jt, trof=trof, Pb2=Pb2: e.transpose(
                TR[:, trof:trof + 128], Pb2[:, 128 * jt:128 * jt + 128], ctx["ident"]),
                reads=["P2b", "cident"], writes=[f"TR{sl2}"], tag=f"mla{h}.tr{qt}.{jt}")
            p.op("vector" if jt % 2 == 0 else "scalar", lambda e, jt=jt, trof=trof:
                 (e.tensor_copy if hasattr(e, "tensor_copy") else e.copy)(
                ptsb[:, 128 * jt:128 * jt + 128], TR[:, trof:trof + 128]),
                reads=[f"TR{sl2}"], writes=[f"pt{jt}"], tag=f"mla{h}.tre{qt}.{jt}")
        pvb = P2[0:64, 256:384] if pe == 0 else P2[0:64, 768:896]
        pvres = "P0" if pe == 0 else "P1"
        for jt in range(qt + 1):
            p.op("tensor", lambda e, jt=jt, qt=qt, pvb=pvb: e.matmul(
                pvb, b["vtokh"][:, 64 * jt:64 * jt + 64],
                ptsb[:, 128 * jt:128 * jt + 128],
                start=(jt == 0), stop=(jt == qt)),
                reads=["m_vtokh", f"pt{jt}"], writes=[pvres], tag=f"mla{h}.pv{qt}.{jt}")
        p.op("vector", lambda e, qt=qt, pvb=pvb: e.tensor_copy(
            b["ctxh"][0:64, 128 * qt:128 * qt + 128], pvb),
            reads=[pvres], writes=["m_ctxh"], tag=f"mla{h}.ce{qt}")

    m_stage1(0)
    m_stage2a(0)
    for qt in range(1, NT):
        m_stage1(qt)
        m_stage2b(qt - 1)
        m_stage2a(qt)
    m_stage2b(NT - 1)
    xst = ctx["xstage"]
    for s in range(NS):
        sl = slice(512 * s, 512 * s + 512)
        p.op("tensor", lambda e, sl=sl: e.matmul(
            P2[half:half + 64, 0:512], b["woT"][:, :], b["ctxh"][0:64, sl],
            start=True, stop=True, tile_position=(0, half)),
            reads=["m_woT", "m_ctxh"], writes=["P0"], tag=f"mla{h}.wo{s}")
        slot = s % 4
        xa = xst[:, 512 * slot:512 * slot + 512]
        xr = f"xs{slot}"
        p.dma("sync", xa[half:half + 64, :], xsrc[c2 * 128 + half: c2 * 128 + half + 64, sl],
              xr, writes=[xr], reads=[f"xB{c2}.{s}"])
        p.op("vector", lambda e, xa=xa: e.tensor_add(
            xa[half:half + 64, :], xa[half:half + 64, :], P2[half:half + 64, 0:512]),
            reads=["P0", xr], writes=[xr], tag=f"mla{h}.xa{s}")
        p.dma("sync", xdst[c2 * 128 + half: c2 * 128 + half + 64, sl], xa[half:half + 64, :],
              xr, reads=[xr], writes=[f"xC{c2}.{s}.{half}"])


L1F_DEPTH = 6
DIM_INNER = 2730
DIM_PAD = 2816
NCH_H = DIM_PAD // 128

def prep_l1f(inp, T):
    out = {}
    Wqkv = inp["L1_ffn_Wqkv"]
    r = np.arange(512)
    c = r // 128; b = (r % 128) // 32; j = r % 32
    h = 4 * c + b
    qA = 64 * h + j; qB = 64 * h + 32 + j
    perm_qk = np.concatenate([qA, qB, 1024 + qA, 1024 + qB])
    qk = Wqkv[:, :, perm_qk]
    out["l1_wqk"] = np.ascontiguousarray(
        qk.reshape(6, 8, 128, 16, 128).transpose(0, 2, 3, 1, 4).reshape(6, 128, 16384))
    out["l1_wv"] = np.ascontiguousarray(Wqkv[:, :, 2048:3072])
    out["l1_wo"] = np.ascontiguousarray(
        inp["L1_ffn_Wo"].reshape(6, 8, 128, 8, 128).transpose(0, 2, 3, 1, 4)
        .reshape(6, 128, 8192))
    W1 = inp["L1_ffn_W1"]; b1 = inp["L1_ffn_b1"]
    W1a = np.zeros((6, 1024, DIM_PAD), np.float32); W1a[:, :, :DIM_INNER] = W1[:, :, :DIM_INNER]
    W1g = np.zeros((6, 1024, DIM_PAD), np.float32); W1g[:, :, :DIM_INNER] = W1[:, :, DIM_INNER:]
    b1a = np.zeros((6, DIM_PAD), np.float32); b1a[:, :DIM_INNER] = b1[:, :DIM_INNER]
    b1g = np.zeros((6, DIM_PAD), np.float32); b1g[:, :DIM_INNER] = b1[:, DIM_INNER:]
    w1i = np.zeros((6, 2 * NCH_H, 1024, 128), np.float32)
    b1i = np.zeros((6, 128, 2 * NCH_H), np.float32)
    for jc in range(NCH_H):
        w1i[:, 2 * jc] = W1a[:, :, 128 * jc:128 * jc + 128]
        w1i[:, 2 * jc + 1] = W1g[:, :, 128 * jc:128 * jc + 128]
        b1i[:, :, 2 * jc] = b1a[:, 128 * jc:128 * jc + 128]
        b1i[:, :, 2 * jc + 1] = b1g[:, 128 * jc:128 * jc + 128]
    out["l1_w1"] = np.ascontiguousarray(w1i)
    out["l1_b1"] = np.ascontiguousarray(b1i)
    W2 = inp["L1_ffn_W2"]
    W2p = np.zeros((6, DIM_PAD, 1024), np.float32); W2p[:, :DIM_INNER] = W2
    out["l1_w2"] = np.ascontiguousarray(
        W2p.reshape(6, DIM_PAD, 8, 128).transpose(0, 2, 1, 3))
    out["l1_b2"] = np.ascontiguousarray(
        inp["L1_ffn_b2"].reshape(6, 8, 128).transpose(0, 2, 1))
    out["l1_lng"] = np.ascontiguousarray(
        inp["L1_ffn_ln_g"].reshape(6, 8, 128).transpose(0, 2, 1))
    out["l1_lnb"] = np.ascontiguousarray(
        inp["L1_ffn_ln_b"].reshape(6, 8, 128).transpose(0, 2, 1))
    qs = inp["L1_ffn_q_scale"]; ks = inp["L1_ffn_k_scale"]
    def tile2(v):
        return np.stack([np.tile(v[:, 0:32], (1, 4)), np.tile(v[:, 32:64], (1, 4))],
                        axis=2)
    out["l1_qs"] = np.ascontiguousarray(tile2(qs))
    out["l1_ks"] = np.ascontiguousarray(tile2(ks))
    return out


def l1f_consts(nc, ctx, T):
    DIM_HEAD = 64; WINDOW = 256
    half = np.arange(0, DIM_HEAD, 2, dtype=np.float64)
    inv = 1.0 / 10000.0 ** (half / DIM_HEAD)
    t = np.arange(T, dtype=np.float64)
    freqs = t[:, None] * inv[None, :]
    base_scale = (half + 0.4 * DIM_HEAD) / (1.4 * DIM_HEAD)
    power = (t - T // 2) / (WINDOW // 2)
    scale = base_scale[None, :] ** power[:, None]
    cs = np.cos(freqs); sn = np.sin(freqs)
    jj = np.arange(128) % 32
    csq = (cs * scale * 8.0).T[jj]
    ssq = (sn * scale * 8.0).T[jj]
    cik = (cs / scale).T[jj]
    sik = (sn / scale).T[jj]
    NW = T // 256
    tp_ = np.stack([a.reshape(128, NW, 256) for a in (csq, ssq, cik, sik)], axis=2)
    ctx["tabpack_dram"] = nc.inline_tensor(
        np.ascontiguousarray(tp_.reshape(128, NW * 1024)).astype(BF), "tabpack").ap()
    m = np.zeros((2, 128, 512), np.float32)
    for pp in range(2):
        qi = np.arange(128)[:, None]
        jx = np.arange(512)[None, :]
        rel = 256 + 128 * pp + qi - jx
        m[pp] = np.where((rel >= 0) & (rel <= 256), 0.0, NEG)
    ctx["lmask_dram"] = nc.inline_tensor(
        np.ascontiguousarray(m.transpose(1, 0, 2).reshape(128, 1024)).astype(BF), "lmask").ap()
    ctx["lmask"] = nc.alloc_sbuf_tensor("lmasks", [128, 1024], BF16).ap()
    I = np.zeros((128, 4, 16), np.float32)
    for cc in range(4):
        for rr_ in range(128):
            I[rr_, cc, 4 * cc + rr_ // 32] = 1.0
    ctx["i16_dram"] = nc.inline_tensor(np.ascontiguousarray(I.reshape(128, 64)), "i16").ap()
    ctx["i16"] = nc.alloc_sbuf_tensor("i16s", [128, 64], F32).ap()
    E = np.zeros((16, 4, 128), np.float32)
    for cc in range(4):
        for p_ in range(128):
            E[4 * cc + p_ // 32, cc, p_] = 1.0
    ctx["e16c_dram"] = nc.inline_tensor(np.ascontiguousarray(E.reshape(16, 512)), "e16c").ap()
    ctx["e16c"] = nc.alloc_sbuf_tensor("e16cs", [16, 512], F32).ap()


def load_l1f_consts(p, ctx):
    p.dma("sync", ctx["lmask"][:, :], ctx["lmask_dram"][:, :], "lmask", writes=["lmask"])
    p.dma("sync", ctx["i16"][:, :], ctx["i16_dram"][:, :], "i16", writes=["i16"])
    p.dma("sync", ctx["e16c"][:, :], ctx["e16c_dram"][:, :], "e16c", writes=["e16c"])


def declare_l1f(nc, T):
    d = {}
    def dram(name, shape):
        d[name] = nc.dram_tensor(name, list(shape), F32, kind="ExternalInput").ap()
    dram("l1_wqk", (6, 128, 16384)); dram("l1_wv", (6, 1024, 1024))
    dram("l1_wo", (6, 128, 8192))
    dram("l1_w1", (6, 2 * NCH_H, 1024, 128)); dram("l1_b1", (6, 128, 2 * NCH_H))
    dram("l1_w2", (6, 8, DIM_PAD, 128)); dram("l1_b2", (6, 128, 8))
    dram("l1_lng", (6, 128, 8)); dram("l1_lnb", (6, 128, 8))
    dram("l1_qs", (6, 128, 2)); dram("l1_ks", (6, 128, 2))
    return d

def alloc_l1f_bufs(nc, ctx, T, xs_at=None):
    b = {}
    b["qkw"] = alloc_at(nc, "f_qkw", [128, 16 * 1024], BF16, RB0)
    b["vw1"] = alloc_at(nc, "f_vw1", [128, 8 * 1024], BF16, RB0 + 32 * 1024)
    b["wo1"] = alloc_at(nc, "f_wo1", [128, 8 * 1024], BF16, RB0 + 48 * 1024)
    b["kring"] = alloc_at(nc, "f_kring", [128, 2 * 2048], BF16, RB0 + 64 * 1024)
    b["vring"] = alloc_at(nc, "f_vring", [128, 2 * 2048], BF16, RB0 + 72 * 1024)
    b["ctxw"] = alloc_at(nc, "f_ctxw", [128, 2048], BF16, RB0 + 80 * 1024)
    b["wtab"] = alloc_at(nc, "f_wtab", [128, 2048], BF16, RB0 + 84 * 1024)
    b["zn2"] = alloc_at(nc, "f_zn2", [128, 8 * 2048], BF16, RB0)
    b["stat"] = alloc_at(nc, "f_stat", [1, 2 * 2048], F32, RB0 + 32 * 1024)
    b["hR"] = alloc_at(nc, "f_hR", [128, 6 * 2048], BF16, RB0 + 48 * 1024)
    b["w1s"] = [alloc_at(nc, f"f_w1s{i}", [128, 1024], BF16, RB0 + 72 * 1024 + i * 2048)
                for i in range(2)]
    b["w2s"] = alloc_at(nc, "f_w2s", [128, NCH_H * 128], BF16, RB0 + 76 * 1024)
    b["hZ"] = alloc_at(nc, "f_hZ", [128, 16 * 2048], BF16, RZ0)
    if xs_at is None:
        b["zsw"] = nc.alloc_sbuf_tensor("f_zsw", [128, 2048], BF16).ap()
        b["qw"] = nc.alloc_sbuf_tensor("f_qw", [128, 2048], BF16).ap()
    else:
        b["zsw"] = alloc_at(nc, "f_zsw", [128, 2048], BF16, xs_at)
        b["qw"] = alloc_at(nc, "f_qw", [128, 2048], BF16, xs_at + 4096)
    b["ssw"] = nc.alloc_sbuf_tensor("f_ssw", [16, 256], F32).ap()
    b["rstdw"] = nc.alloc_sbuf_tensor("f_rstdw", [16, 256], F32).ap()
    b["rqt"] = nc.alloc_sbuf_tensor("f_rqt", [128, 32], F32).ap()
    b["tabst"] = nc.alloc_sbuf_tensor("f_tabst", [128, 1024], BF16).ap()
    b["qs"] = nc.alloc_sbuf_tensor("f_qs", [128, 4], F32).ap()
    b["b1"] = nc.alloc_sbuf_tensor("f_b1", [128, 2 * NCH_H], F32).ap()
    b["b2g"] = nc.alloc_sbuf_tensor("f_b2g", [128, 24], F32).ap()
    return b


def emit_l1f_layer(p, ctx, d, b, l, zbuf, zdram, T):
    NS = T // 512
    NW = T // 256
    S_ps = ctx["S_ps"]; P2 = ctx["P2_ps"]; TR = ctx["TR_ps"]
    scrA, scrB = ctx["scrA"], ctx["scrB"]
    st2 = ctx["stats2"]
    Pb = ctx["Pbuf"]; Pb2 = ctx["Pbuf2"]; ptsb = ctx["ptsb"]
    SCRA = [f"scrA{i}" for i in range(NS)]
    G = T + 1

    def zap(c, t0, n, shift=0):
        o = c * G + 1 + t0 - shift
        return zbuf[:, o:o + n]

    p.dma("gpsimd", b["qkw"][:, :], d["l1_wqk"][l], "f_qkw", writes=["f_qkw"])
    p.dma("gpsimd", b["vw1"][:, :].rearrange("p (c f) -> p c f", f=1024),
          d["l1_wv"][l].rearrange("(c p) f -> p c f", p=128), "f_vw1", writes=["f_vw1"])
    p.dma("gpsimd", b["wo1"][:, :], d["l1_wo"][l], "f_wo1", writes=["f_wo1"])
    p.dma("sync", b["qs"][:, 0:2], d["l1_qs"][l], "f_qs", writes=["f_qs"])
    p.dma("sync", b["qs"][:, 2:4], d["l1_ks"][l], "f_qs", writes=["f_qs"])

    for wi in range(NW):
        t0 = 256 * wi
        ks_ = wi % 2
        kp_ = (wi + 1) % 2
        j0 = 256 if wi == 0 else 0
        for dk in range(8):
            sh = 1 if dk >= 4 else 0
            p.op("vector", lambda e, dk=dk, sh=sh, t0=t0: e.tensor_copy(
                b["zsw"][:, 256 * dk:256 * dk + 256], zap(dk, t0, 256, sh)),
                reads=[f"z{dk}"], writes=["f_zsw"], tag=f"f{l}.zs{wi}.{dk}")
        p.dma("sync", b["tabst"][:, :], ctx["tabpack_dram"][:, 1024 * wi:1024 * wi + 1024],
              "f_tabst", writes=["f_tabst"])
        for ki in range(4):
            qcol = 0 if ki < 2 else 2
            for half_ in (0, 1):
                dst = b["wtab"][:, 256 * (2 * ki + half_):256 * (2 * ki + half_) + 256]
                p.op("vector", lambda e, ki=ki, dst=dst, qcol=qcol, half_=half_:
                     e.tensor_scalar_mul(dst, b["tabst"][:, 256 * ki:256 * ki + 256],
                                         b["qs"][:, qcol + half_:qcol + half_ + 1]),
                     reads=["f_tabst", "f_qs"], writes=["f_wtab"], tag=f"f{l}.tab{wi}.{ki}.{half_}")
        def wtap(idx):
            return b["wtab"][:, 256 * idx:256 * idx + 256]
        CSQA, CSQB = wtap(0), wtap(1)
        SSQA, SSQB = wtap(2), wtap(3)
        CIKA, CIKB = wtap(4), wtap(5)
        SIKA, SIKB = wtap(6), wtap(7)
        def qk_chunks(base_fc, dst_scr, dres):
            for fc in range(8):
                bank = fc % 2
                pa = S_ps[:, 512 * bank:512 * bank + 256]
                for dk in range(8):
                    p.op("tensor", lambda e, fc=fc, dk=dk, pa=pa, base_fc=base_fc: e.matmul(
                        pa, b["qkw"][:, 1024 * (base_fc + fc) + 128 * dk:
                                     1024 * (base_fc + fc) + 128 * dk + 128],
                        b["zsw"][:, 256 * dk:256 * dk + 256],
                        start=(dk == 0), stop=(dk == 7)),
                        reads=["f_qkw", "f_zsw"], writes=[f"S{bank}"],
                        tag=f"f{l}.qk{wi}.{base_fc + fc}")
                p.op("vector", lambda e, fc=fc, pa=pa: e.tensor_copy(
                    dst_scr[:, 256 * fc:256 * fc + 256], pa),
                    reads=[f"S{bank}"], writes=[dres], tag=f"f{l}.qke{wi}.{base_fc + fc}")
        qk_chunks(0, scrA, "scrA0")
        p.op("scalar", lambda e: e.activation(scrB[:, 0:2048], scrA[:, 0:2048], AF.Square),
             reads=["scrA0"], writes=["scrB"], tag=f"f{l}.qsq{wi}")
        for fc in range(8):
            p.op("tensor", lambda e, fc=fc: e.matmul(
                P2[0:16, 0:256], ctx["i16"][:, 16 * (fc % 4):16 * (fc % 4) + 16],
                scrB[:, 256 * fc:256 * fc + 256], start=(fc == 0), stop=(fc == 7)),
                reads=["scrB", "i16"], writes=["P0"], tag=f"f{l}.qss{wi}.{fc}")
        p.op("vector", lambda e: e.tensor_copy(b["ssw"][:, 0:256], P2[0:16, 0:256]),
             reads=["P0"], writes=["f_ssw"], tag=f"f{l}.qsse{wi}")
        p.op("scalar", lambda e: e.activation(b["rstdw"][:, :], b["ssw"][:, 0:256], AF.Sqrt),
             reads=["f_ssw"], writes=["f_rstdw"], tag=f"f{l}.qsrt{wi}")
        p.op("vector", lambda e: e.tensor_scalar_max(b["rstdw"][:, :], b["rstdw"][:, :], 1e-12),
             reads=["f_rstdw"], writes=["f_rstdw"], tag=f"f{l}.qmx{wi}")
        p.op("vector", lambda e: e.reciprocal(b["rstdw"][:, :], b["rstdw"][:, :]),
             reads=["f_rstdw"], writes=["f_rstdw"], tag=f"f{l}.qrc{wi}")
        for pp in range(2):
            p.op("tensor", lambda e, pp=pp: e.transpose(
                TR[:, 1024 * pp:1024 * pp + 16].bitcast(F32) if False else
                S_ps[0:128, 768 + 16 * pp: 784 + 16 * pp],
                b["rstdw"][:, 128 * pp:128 * pp + 128], ctx["identf"][0:16, 0:16]),
                reads=["f_rstdw", "identf"], writes=["S1"], tag=f"f{l}.qrt{wi}.{pp}")
            p.op("vector", lambda e, pp=pp: e.tensor_copy(
                b["rqt"][:, 16 * pp:16 * pp + 16], S_ps[0:128, 768 + 16 * pp:784 + 16 * pp]),
                reads=["S1"], writes=["f_rqt"], tag=f"f{l}.qrte{wi}.{pp}")
        for c in range(4):
            qA = scrA[:, 256 * c:256 * c + 256]
            qB = scrA[:, 256 * (4 + c):256 * (4 + c) + 256]
            m1 = scrB[:, 0:256]; m2 = scrB[:, 256:512]
            m3 = scrB[:, 512:768]; m4 = scrB[:, 768:1024]
            p.op("vector", lambda e, qA=qA, m1=m1, CSQA=CSQA: e.tensor_mul(m1, qA, CSQA),
                 reads=["scrA0", "f_wtab"], writes=["scrB"], tag=f"f{l}.qr1{wi}.{c}")
            p.op("vector", lambda e, qB=qB, m2=m2, SSQB=SSQB: e.tensor_mul(m2, qB, SSQB),
                 reads=["scrA0", "f_wtab"], writes=["scrB"], tag=f"f{l}.qr2{wi}.{c}")
            p.op("vector", lambda e, c=c, m1=m1, m2=m2: e.tensor_sub(
                b["qw"][:, 256 * c:256 * c + 256], m1, m2),
                reads=["scrB"], writes=["f_qw"], tag=f"f{l}.qr3{wi}.{c}")
            p.op("vector", lambda e, qB=qB, m3=m3, CSQB=CSQB: e.tensor_mul(m3, qB, CSQB),
                 reads=["scrA0", "f_wtab"], writes=["scrB"], tag=f"f{l}.qr4{wi}.{c}")
            p.op("vector", lambda e, qA=qA, m4=m4, SSQA=SSQA: e.tensor_mul(m4, qA, SSQA),
                 reads=["scrA0", "f_wtab"], writes=["scrB"], tag=f"f{l}.qr5{wi}.{c}")
            p.op("vector", lambda e, c=c, m3=m3, m4=m4: e.tensor_add(
                b["qw"][:, 256 * (4 + c):256 * (4 + c) + 256], m3, m4),
                reads=["scrB"], writes=["f_qw"], tag=f"f{l}.qr6{wi}.{c}")
        qk_chunks(8, scrA, "scrA0")
        p.op("scalar", lambda e: e.activation(scrB[:, 0:2048], scrA[:, 0:2048], AF.Square),
             reads=["scrA0"], writes=["scrB"], tag=f"f{l}.ksq{wi}")
        for fc in range(8):
            p.op("tensor", lambda e, fc=fc: e.matmul(
                P2[0:16, 0:256], ctx["i16"][:, 16 * (fc % 4):16 * (fc % 4) + 16],
                scrB[:, 256 * fc:256 * fc + 256], start=(fc == 0), stop=(fc == 7)),
                reads=["scrB", "i16"], writes=["P0"], tag=f"f{l}.kss{wi}.{fc}")
        p.op("vector", lambda e: e.tensor_copy(b["ssw"][:, 0:256], P2[0:16, 0:256]),
             reads=["P0"], writes=["f_ssw"], tag=f"f{l}.ksse{wi}")
        p.op("scalar", lambda e: e.activation(b["rstdw"][:, :], b["ssw"][:, 0:256], AF.Sqrt),
             reads=["f_ssw"], writes=["f_rstdw"], tag=f"f{l}.ksrt{wi}")
        p.op("vector", lambda e: e.tensor_scalar_max(b["rstdw"][:, :], b["rstdw"][:, :], 1e-12),
             reads=["f_rstdw"], writes=["f_rstdw"], tag=f"f{l}.kmx{wi}")
        p.op("vector", lambda e: e.reciprocal(b["rstdw"][:, :], b["rstdw"][:, :]),
             reads=["f_rstdw"], writes=["f_rstdw"], tag=f"f{l}.krc{wi}")
        for c in range(4):
            p.op("tensor", lambda e, c=c: e.matmul(
                P2[:, 0:256], ctx["e16c"][:, 128 * c:128 * c + 128], b["rstdw"][:, :],
                start=True, stop=True),
                reads=["f_rstdw", "e16c"], writes=["P0"], tag=f"f{l}.kbc{wi}.{c}")
            kA = scrA[:, 256 * c:256 * c + 256]
            kB = scrA[:, 256 * (4 + c):256 * (4 + c) + 256]
            m1 = scrB[:, 0:256]; m2 = scrB[:, 256:512]
            m3 = scrB[:, 512:768]; m4 = scrB[:, 768:1024]
            d1 = scrB[:, 1024:1280]; d2 = scrB[:, 1280:1536]
            ksl = b["kring"][:, 2048 * ks_:2048 * ks_ + 2048]
            p.op("vector", lambda e, kA=kA, m1=m1, CIKA=CIKA: e.tensor_mul(m1, kA, CIKA),
                 reads=["scrA0", "f_wtab"], writes=["scrB"], tag=f"f{l}.kr1{wi}.{c}")
            p.op("vector", lambda e, kB=kB, m2=m2, SIKB=SIKB: e.tensor_mul(m2, kB, SIKB),
                 reads=["scrA0", "f_wtab"], writes=["scrB"], tag=f"f{l}.kr2{wi}.{c}")
            p.op("vector", lambda e, d1=d1, m1=m1, m2=m2: e.tensor_sub(d1, m1, m2),
                 reads=["scrB"], writes=["scrB"], tag=f"f{l}.kr3{wi}.{c}")
            p.op("vector", lambda e, c=c, d1=d1, ksl=ksl: e.tensor_mul(
                ksl[:, 256 * c:256 * c + 256], d1, P2[:, 0:256]),
                reads=["scrB", "P0"], writes=[f"f_kr{ks_}"], tag=f"f{l}.kr3b{wi}.{c}")
            p.op("vector", lambda e, kB=kB, m3=m3, CIKB=CIKB: e.tensor_mul(m3, kB, CIKB),
                 reads=["scrA0", "f_wtab"], writes=["scrB"], tag=f"f{l}.kr4{wi}.{c}")
            p.op("vector", lambda e, kA=kA, m4=m4, SIKA=SIKA: e.tensor_mul(m4, kA, SIKA),
                 reads=["scrA0", "f_wtab"], writes=["scrB"], tag=f"f{l}.kr5{wi}.{c}")
            p.op("vector", lambda e, d2=d2, m3=m3, m4=m4: e.tensor_add(d2, m3, m4),
                 reads=["scrB"], writes=["scrB"], tag=f"f{l}.kr6{wi}.{c}")
            p.op("vector", lambda e, c=c, d2=d2, ksl=ksl: e.tensor_mul(
                ksl[:, 256 * (4 + c):256 * (4 + c) + 256], d2, P2[:, 0:256]),
                reads=["scrB", "P0"], writes=[f"f_kr{ks_}"], tag=f"f{l}.kr6b{wi}.{c}")
        for ti in range(2):
            for half2 in range(2):
                pa = P2[:, 512 * half2:512 * half2 + 512]
                for dk in range(8):
                    p.op("tensor", lambda e, ti=ti, dk=dk, half2=half2, pa=pa: e.matmul(
                        pa, b["zsw"][:, 256 * dk + 128 * ti:256 * dk + 128 * ti + 128],
                        b["vw1"][:, 1024 * dk + 512 * half2:1024 * dk + 512 * half2 + 512],
                        start=(dk == 0), stop=(dk == 7)),
                        reads=["f_zsw", "f_vw1"], writes=[f"P{half2}"],
                        tag=f"f{l}.v{wi}.{ti}.{half2}")
                p.op("vector", lambda e, ti=ti, half2=half2, pa=pa, ks_=ks_: e.tensor_copy(
                    b["vring"][:, 2048 * ks_ + 1024 * ti + 512 * half2:
                               2048 * ks_ + 1024 * ti + 512 * half2 + 512], pa),
                    reads=[f"P{half2}"], writes=[f"f_vr{ks_}"],
                    tag=f"f{l}.ve{wi}.{ti}.{half2}")
        def f_stage1(h, pp):
            bb = 32 * (h % 4)
            cc = h // 4
            tp = (bb, 0)
            sb = (2 * h + pp) % 4
            g = (2 * h + pp) % 4
            SS = S_ps[:, 512 * sb:512 * sb + 512]
            negm = st2[:, 4 * g:4 * g + 1]
            srcs = []
            if wi > 0:
                srcs.append((b["kring"], kp_, 0))
            srcs.append((b["kring"], ks_, 256))
            for (kr, slot, joff) in srcs:
                kAp = kr[bb:bb + 32, 2048 * slot + 256 * cc: 2048 * slot + 256 * cc + 256]
                kBp = kr[bb:bb + 32, 2048 * slot + 256 * (4 + cc): 2048 * slot + 256 * (4 + cc) + 256]
                qAp = b["qw"][bb:bb + 32, 256 * cc + 128 * pp: 256 * cc + 128 * pp + 128]
                qBp = b["qw"][bb:bb + 32, 256 * (4 + cc) + 128 * pp: 256 * (4 + cc) + 128 * pp + 128]
                p.op("tensor", lambda e, SS=SS, qAp=qAp, kAp=kAp, joff=joff, tp=tp: e.matmul(
                    SS[:, joff:joff + 256], qAp, kAp, start=True, stop=False,
                    tile_position=tp),
                    reads=["f_qw", f"f_kr{ks_}" if joff else f"f_kr{kp_}"],
                    writes=[f"S{sb}"], tag=f"f{l}.sA{wi}.{h}.{pp}")
                p.op("tensor", lambda e, SS=SS, qBp=qBp, kBp=kBp, joff=joff, tp=tp: e.matmul(
                    SS[:, joff:joff + 256], qBp, kBp, start=False, stop=True,
                    tile_position=tp),
                    reads=["f_qw", f"f_kr{ks_}" if joff else f"f_kr{kp_}"],
                    writes=[f"S{sb}"], tag=f"f{l}.sB{wi}.{h}.{pp}")
            p.op("vector", lambda e, SS=SS, h=h, pp=pp, j0=j0: e.scalar_tensor_tensor(
                SS[:, j0:512], SS[:, j0:512], b["rqt"][:, 16 * pp + h:16 * pp + h + 1],
                ctx["lmask"][:, 512 * pp + j0:512 * pp + 512],
                op0=ALU.mult, op1=ALU.add),
                reads=[f"S{sb}", "f_rqt", "lmask"], writes=[f"S{sb}"],
                tag=f"f{l}.msk{wi}.{h}.{pp}")
            p.op("vector", lambda e, SS=SS, j0=j0, negm=negm: e.reduce_max(
                negm, SS[:, j0:512], axis=AX.X, negate=True),
                reads=[f"S{sb}"], writes=[f"negm{g}"], tag=f"f{l}.mx{wi}.{h}.{pp}")

        def f_stage2(h, pp):
            sb = (2 * h + pp) % 4
            g = (2 * h + pp) % 4
            g2 = (2 * h + pp) % 2
            SS = S_ps[:, 512 * sb:512 * sb + 512]
            negm = st2[:, 4 * g:4 * g + 1]
            sums = st2[:, 4 * g + 1:4 * g + 2]
            rsum = st2[:, 4 * g + 2:4 * g + 3]
            Pba = Pb[:, 1024 * g2:1024 * g2 + 512]
            Pba2 = Pb2[:, 1024 * g2:1024 * g2 + 512]
            pvb = P2[64 * (h % 2):64 * (h % 2) + 64,
                     (256 if g2 == 0 else 768):(256 if g2 == 0 else 768) + 128]
            pvres = "P0" if g2 == 0 else "P1"
            p.op("scalar", lambda e, SS=SS, j0=j0, negm=negm, sums=sums, Pba=Pba:
                 e.activation(Pba[:, j0:512], SS[:, j0:512], AF.Exp, bias=negm,
                              accum_out=sums),
                 reads=[f"S{sb}", f"negm{g}"], writes=[f"P.{g2}", f"sums{g}"],
                 tag=f"f{l}.ex{wi}.{h}.{pp}")
            p.op("vector", lambda e, sums=sums, rsum=rsum: e.reciprocal(rsum, sums),
                 reads=[f"sums{g}"], writes=[f"rsum{g}"], tag=f"f{l}.rc{wi}.{h}.{pp}")
            p.op("vector", lambda e, j0=j0, Pba=Pba, Pba2=Pba2, rsum=rsum:
                 e.tensor_scalar_mul(Pba2[:, j0:512], Pba[:, j0:512], rsum),
                 reads=[f"P.{g2}", f"rsum{g}"], writes=[f"P2b.{g2}"],
                 tag=f"f{l}.pm{wi}.{h}.{pp}")
            njt0 = j0 // 128
            for jt in range(njt0, 4):
                sl2 = jt % 2
                trof = 1024 * sl2
                p.op("tensor", lambda e, jt=jt, trof=trof, Pba2=Pba2: e.transpose(
                    TR[:, trof:trof + 128], Pba2[:, 128 * jt:128 * jt + 128],
                    ctx["ident"]),
                    reads=[f"P2b.{g2}", "cident"], writes=[f"TR{sl2}"],
                    tag=f"f{l}.tr{wi}.{h}.{pp}.{jt}")
                p.op("vector" if jt % 2 == 0 else "scalar",
                     lambda e, jt=jt, trof=trof:
                     (e.tensor_copy if hasattr(e, "tensor_copy") else e.copy)(
                    ptsb[:, 128 * jt:128 * jt + 128], TR[:, trof:trof + 128]),
                    reads=[f"TR{sl2}"], writes=[f"pt{jt}"],
                    tag=f"f{l}.tre{wi}.{h}.{pp}.{jt}")
            for jt in range(njt0, 4):
                slot = kp_ if jt < 2 else ks_
                ti = jt % 2
                p.op("tensor", lambda e, jt=jt, slot=slot, ti=ti, h=h, njt0=njt0,
                     pvb=pvb: e.matmul(
                    pvb,
                    b["vring"][:, 2048 * slot + 1024 * ti + 64 * h:
                               2048 * slot + 1024 * ti + 64 * h + 64],
                    ptsb[:, 128 * jt:128 * jt + 128],
                    start=(jt == njt0), stop=(jt == 3),
                    tile_position=(0, 64 * (h % 2))),
                    reads=[f"f_vr{slot}", f"pt{jt}"], writes=[pvres],
                    tag=f"f{l}.pv{wi}.{h}.{pp}.{jt}")
            p.op("vector", lambda e, h=h, pp=pp, pvb=pvb: e.tensor_copy(
                b["ctxw"][64 * (h % 2):64 * (h % 2) + 64,
                          256 * (h // 2) + 128 * pp:256 * (h // 2) + 128 * pp + 128],
                pvb),
                reads=[pvres], writes=[f"f_cw{h // 2}"],
                tag=f"f{l}.ce{wi}.{h}.{pp}")

        tiles = [(h, pp) for h in range(16) for pp in range(2)]
        prev = None
        for t_ in tiles:
            f_stage1(*t_)
            if prev is not None:
                f_stage2(*prev)
            prev = t_
        f_stage2(*prev)
        for e_ in range(8):
            sb = e_ % 2
            pa = S_ps[:, 512 * sb:512 * sb + 256]
            for ic in range(8):
                p.op("tensor", lambda e, e_=e_, ic=ic, pa=pa: e.matmul(
                    pa, b["wo1"][:, 1024 * e_ + 128 * ic:1024 * e_ + 128 * ic + 128],
                    b["ctxw"][:, 256 * ic:256 * ic + 256],
                    start=(ic == 0), stop=(ic == 7)),
                    reads=["f_wo1", f"f_cw{ic}"], writes=[f"S{sb}"],
                    tag=f"f{l}.wo{wi}.{e_}")
            p.op("vector", lambda e, e_=e_, t0=t0, pa=pa: e.tensor_add(
                zap(e_, t0, 256), zap(e_, t0, 256), pa),
                reads=[f"S{sb}", f"z{e_}"], writes=[f"z{e_}"],
                tag=f"f{l}.za{wi}.{e_}")

    p.fence([], ["f_qkw", "f_vw1", "f_wo1", "stat.mu", "stat.rs",
                 "f_kr0", "f_kr1", "f_vr0", "f_vr1", "f_wtab",
                 "f_w1s0", "f_w1s1", "f_w2s", "f_zsw", "f_qw",
                 "xs0", "xs1", "xs2", "xs3"]
            + [f"f_cw{i}" for i in range(8)]
            + [f"f_zn{c}" for c in range(8)]
            + [f"h{k}" for k in range(NCH_H)])
    stat = b["stat"]
    mu = stat[0:1, 0:T]
    rs = stat[0:1, T:2 * T]
    ones_f = ctx["ones_f32"]
    for c in range(8):
        sh = 1 if c >= 4 else 0
        p.op("scalar", lambda e, c=c, sh=sh: e.activation(
            scrA[:, 0:T], zap(c, 0, T, sh), AF.Square),
            reads=[f"z{c}"], writes=SCRA, tag=f"f{l}.lsq{c}")
        for s in range(NS):
            sl = slice(512 * s, 512 * s + 512)
            p.op("tensor", lambda e, c=c, s=s, sh=sh, sl=sl: e.matmul(
                S_ps[0:1, sl], ones_f[:, 0:1], zap(c, 512 * s, 512, sh),
                start=(c == 0), stop=(c == 7)),
                reads=[f"z{c}", "cones"], writes=[f"S{s}"], tag=f"f{l}.lsum{c}.{s}")
            p.op("tensor", lambda e, sl=sl, c=c: e.matmul(
                S_ps[32:33, sl], ones_f[:, 0:1], scrA[:, sl], start=(c == 0), stop=(c == 7)),
                reads=[f"scrA{s}"] + (["cones"] if True else []), writes=[f"S{s}"],
                tag=f"f{l}.lsqs{c}.{s}")
    Sres = [f"S{s}" for s in range(NS)]
    p.op("vector", lambda e: e.tensor_scalar_mul(mu, S_ps[0:1, 0:T], 1.0 / 1024.0),
         reads=Sres, writes=["stat.mu"], tag=f"f{l}.lmu")
    p.op("vector", lambda e: e.tensor_mul(scrA[0:1, 0:T], mu, mu),
         reads=["stat.mu"] + SCRA, writes=SCRA, tag=f"f{l}.lm2")
    p.op("vector", lambda e: e.scalar_tensor_tensor(
        scrA[0:1, 0:T], S_ps[32:33, 0:T], 1.0 / 1024.0, scrA[0:1, 0:T],
        op0=ALU.mult, op1=ALU.subtract),
        reads=Sres + SCRA, writes=SCRA, tag=f"f{l}.lvar")
    p.op("scalar", lambda e: e.activation(rs, scrA[0:1, 0:T], AF.Sqrt, bias=ctx["cv"](1e-5, 1)),
         reads=SCRA + ["cv"], writes=["stat.rs"], tag=f"f{l}.lsqrt")
    p.op("vector", lambda e: e.reciprocal(rs, rs),
         reads=["stat.rs"], writes=["stat.rs"], tag=f"f{l}.lrecip")
    p.dma("sync", b["b2g"][:, 0:8], d["l1_b2"][l], "f_b2g", writes=["f_b2g"])
    p.dma("sync", b["b2g"][:, 8:16], d["l1_lng"][l], "f_b2g", writes=["f_b2g"])
    p.dma("sync", b["b2g"][:, 16:24], d["l1_lnb"][l], "f_b2g", writes=["f_b2g"])
    p.dma("sync", b["b1"][:, :], d["l1_b1"][l], "f_b1", writes=["f_b1"])
    for c in range(8):
        sh = 1 if c >= 4 else 0
        for s in range(NS):
            sl = slice(512 * s, 512 * s + 512)
            p.op("tensor", lambda e, sl=sl: e.matmul(
                P2[:, 0:512], ones_f[0:1, 0:128], mu[:, sl], start=True, stop=True),
                reads=["stat.mu", "cones"], writes=["P0"], tag=f"f{l}.lbmu{c}.{s}")
            p.op("tensor", lambda e, sl=sl: e.matmul(
                P2[:, 512:1024], ones_f[0:1, 0:128], rs[:, sl], start=True, stop=True),
                reads=["stat.rs", "cones"], writes=["P1"], tag=f"f{l}.lbrs{c}.{s}")
            p.op("vector", lambda e, c=c, s=s, sh=sh: e.scalar_tensor_tensor(
                scrA[:, 512 * s:512 * s + 512], zap(c, 512 * s, 512, sh), 1.0,
                P2[:, 0:512], op0=ALU.mult, op1=ALU.subtract),
                reads=[f"z{c}", "P0"], writes=[f"scrA{s}"], tag=f"f{l}.lap1{c}.{s}")
            p.op("vector", lambda e, c=c, s=s: e.tensor_mul(
                scrB[:, 512 * s:512 * s + 512], scrA[:, 512 * s:512 * s + 512],
                P2[:, 512:1024]),
                reads=[f"scrA{s}", "P1"], writes=["scrB"], tag=f"f{l}.lap2{c}.{s}")
            p.op("vector", lambda e, c=c, s=s: e.tensor_scalar(
                b["zn2"][:, T * c + 512 * s:T * c + 512 * s + 512],
                scrB[:, 512 * s:512 * s + 512],
                b["b2g"][:, 8 + c:9 + c], b["b2g"][:, 16 + c:17 + c],
                op0=ALU.mult, op1=ALU.add),
                reads=["scrB", "f_b2g"], writes=[f"f_zn{c}"], tag=f"f{l}.lap3{c}.{s}")
    for c in range(8):
        p.dma("sync", zdram[128 * c:128 * c + 128, :], zap(c, 0, T), f"zsp{c}",
              reads=[f"z{c}"], writes=[f"zd{c}"])
    p.fence([], [f"z{c}" for c in range(8)] + [f"h{k}" for k in range(NCH_H)])
    def hap(jc, sl):
        if jc < 16:
            return b["hZ"][:, T * jc + sl.start:T * jc + sl.stop]
        return b["hR"][:, T * (jc - 16) + sl.start:T * (jc - 16) + sl.stop]
    for jc in range(NCH_H):
        for ag in range(2):
            p.dma("gpsimd", b["w1s"][ag][:, :].rearrange("p (c f) -> p c f", f=128),
                  d["l1_w1"][l, 2 * jc + ag].rearrange("(c p) f -> p c f", p=128),
                  f"f_w1s{ag}", writes=[f"f_w1s{ag}"])
        for s in range(NS):
            sl = slice(512 * s, 512 * s + 512)
            for ag in range(2):
                pa = P2[:, 0:512] if ag == 0 else P2[:, 512:1024]
                bank = "P0" if ag == 0 else "P1"
                for dk in range(8):
                    p.op("tensor", lambda e, dk=dk, sl=sl, pa=pa, ag=ag: e.matmul(
                        pa, b["w1s"][ag][:, 128 * dk:128 * dk + 128],
                        b["zn2"][:, T * dk + sl.start:T * dk + sl.stop],
                        start=(dk == 0), stop=(dk == 7)),
                        reads=[f"f_w1s{ag}", f"f_zn{dk}"], writes=[bank],
                        tag=f"f{l}.w1.{jc}.{ag}.{s}")
            p.op("scalar", lambda e, jc=jc, s=s: e.activation(
                scrB[:, 512 * s:512 * s + 512], P2[:, 512:1024], AF.Gelu,
                bias=b["b1"][:, 2 * jc + 1:2 * jc + 2]),
                reads=["P1", "f_b1"], writes=["scrB"], tag=f"f{l}.gg{jc}.{s}")
            p.op("vector", lambda e, jc=jc, s=s, sl=sl: e.scalar_tensor_tensor(
                hap(jc, sl), P2[:, 0:512], b["b1"][:, 2 * jc:2 * jc + 1],
                scrB[:, 512 * s:512 * s + 512], op0=ALU.add, op1=ALU.mult),
                reads=["P0", "f_b1", "scrB"], writes=[f"h{jc}"],
                tag=f"f{l}.hm{jc}.{s}")
    for e_ in range(8):
        p.dma("gpsimd", b["w2s"][:, :].rearrange("p (c f) -> p c f", f=128),
              d["l1_w2"][l, e_].rearrange("(c p) f -> p c f", p=128),
          "f_w2s", writes=["f_w2s"])
        for s in range(NS):
            sl = slice(512 * s, 512 * s + 512)
            sb = s % 4
            for k in range(NCH_H):
                p.op("tensor", lambda e, k=k, s=s, sl=sl, sb=sb: e.matmul(
                    S_ps[:, 512 * sb:512 * sb + 512], b["w2s"][:, 128 * k:128 * k + 128],
                    hap(k, sl), start=(k == 0), stop=(k == NCH_H - 1)),
                    reads=["f_w2s", f"h{k}"], writes=[f"S{sb}"],
                    tag=f"f{l}.w2.{e_}.{s}")
            slot = (e_ * NS + s) % 4
            xa = ctx["xstage"][:, 512 * slot:512 * slot + 512]
            xr = f"xs{slot}"
            p.dma("sync", xa, zdram[128 * e_:128 * e_ + 128, sl], xr,
                  reads=[f"zd{e_}"], writes=[xr])
            p.op("vector", lambda e, xa=xa, sb=sb, e_=e_: e.scalar_tensor_tensor(
                xa, S_ps[:, 512 * sb:512 * sb + 512], b["b2g"][:, e_:e_ + 1], xa,
                op0=ALU.add, op1=ALU.add),
                reads=[f"S{sb}", "f_b2g", xr], writes=[xr], tag=f"f{l}.w2a{e_}.{s}")
            p.dma("sync", zdram[128 * e_:128 * e_ + 128, sl], xa, xr,
                  reads=[xr], writes=[f"zd{e_}"])

def emit_l1f(p, ctx, d, b, zbuf, zdram, T):
    G = T + 1
    for c in range(8):
        p.op("vector", lambda e, c=c: e.memset(zbuf[:, c * G:c * G + 1], 0.0),
             reads=[], writes=[f"z{c}"], tag=f"f.guard{c}")
    SCRA = [f"scrA{i}" for i in range(T // 512)]
    for l in range(L1F_DEPTH):
        emit_l1f_layer(p, ctx, d, b, l, zbuf, zdram, T)
        if l < L1F_DEPTH - 1:
            p.fence([], [f"h{k}" for k in range(NCH_H)] + [f"z{c}" for c in range(8)]
                    + SCRA + ["scrB", "f_qkw", "f_vw1", "f_wo1", "f_wtab",
                              "f_kr0", "f_kr1", "f_vr0", "f_vr1",
                              "f_w1s0", "f_w1s1", "f_w2s", "stat.mu", "stat.rs",
                              "f_zsw", "f_qw", "xs0", "xs1", "xs2", "xs3"]
                    + [f"f_cw{i}" for i in range(8)]
                    + [f"f_zn{c}" for c in range(8)])
            for c in range(8):
                p.op("vector", lambda e, c=c: e.memset(zbuf[:, c * G:c * G + 1], 0.0),
                     reads=[], writes=[f"z{c}"], tag=f"f.g{l}.{c}")
                p.dma("sync", zbuf[:, c * G + 1:c * G + 1 + T],
                      zdram[128 * c:128 * c + 128, :], f"zrl{c}",
                      reads=[f"zd{c}"], writes=[f"z{c}"])


def emit_final_combine(p, ctx, zdram, xsrc, xdst, T):
    NS = T // 512
    xst = ctx["xstage"]
    scrB = ctx["scrB"]
    p.fence([], ["scrB", "xs0", "xs1", "fc_z0", "fc_z1"])
    for c in range(8):
        for s in range(NS):
            sl = slice(512 * s, 512 * s + 512)
            slot = (c * NS + s) % 2
            xa = xst[:, 512 * slot:512 * slot + 512]
            xr = f"xs{slot}"
            za = scrB[:, 512 * (2 + slot):512 * (2 + slot) + 512]
            zr = f"fc_z{slot}"
            p.dma("sync", xa, xsrc[128 * c:128 * c + 128, sl], xr, writes=[xr],
                  reads=[f"xC{c}.{s}.0", f"xC{c}.{s}.64"])
            p.dma("sync", za, zdram[128 * c:128 * c + 128, sl], zr,
                  reads=[f"zd{c}"], writes=[zr])
            p.op("vector", lambda e, xa=xa, za=za: e.tensor_add(xa, xa, za),
                 reads=[xr, zr], writes=[xr], tag=f"fc.add{c}.{s}")
            p.dma("sync", xdst[128 * c:128 * c + 128, sl], xa, xr, reads=[xr])


T_FULL = 2048

MLA_RES = ["m_qc", "m_kc", "m_vf", "m_vtokh", "m_ctxh", "m_wdq", "m_wuq",
           "m_wuqT", "m_wdkv", "m_wukv", "m_wukvT", "m_mq", "m_mkv", "m_wqrT",
           "m_wkrT", "m_woT", "m_wn", "m_ss", "m_rstd", "oh64", "e16", "diagm"]
L1F_RES = (["f_qkw", "f_vw1", "f_wo1", "f_wtab", "f_kr0", "f_kr1", "f_vr0",
            "f_vr1", "f_w1s0", "f_w1s1", "f_w2s", "f_zsw", "f_qw", "f_ssw",
            "f_rstdw", "f_rqt", "f_tabst", "f_qs", "f_b1", "f_b2g"]
           + [f"f_cw{i}" for i in range(8)] + [f"f_zn{c}" for c in range(8)])

def _build_fused(T):
    nc = bass.Bass("TRN2", target_bir_lowering=False, debug=False)
    xin = nc.dram_tensor("xin", [1024, T], F32, kind="ExternalInput").ap()
    xout = nc.dram_tensor("xout", [1024, T], F32, kind="ExternalOutput").ap()
    xmA = nc.dram_tensor("xmA", [1024, T], F32, kind="Internal").ap()
    x1 = nc.dram_tensor("x1", [1024, T], F32, kind="Internal").ap()
    x2 = nc.dram_tensor("x2", [1024, T], F32, kind="Internal").ap()
    zdram = nc.dram_tensor("zdram", [1024, T], F32, kind="Internal").ap()
    d = declare_l0(nc, T)
    d.update(declare_mla(nc, T))
    d.update(declare_l1f(nc, T))
    RS0 = RZ0 - 8192
    ctx = make_ctx(nc, T, xstage_at=RS0)
    mla_consts(nc, ctx)
    l1f_consts(nc, ctx, T)
    stat0 = alloc_at(nc, "stat0", [1, 2 * T], F32, RB0)
    bufs = {}
    bufs["xsrc"] = xin; bufs["xdst"] = xmA
    bufs["xsrc2"] = xmA; bufs["xdst2"] = x1
    bufs["xnb"] = alloc_at(nc, "xnb", [128, 8 * T], BF16, RZ0)
    bufs["vw"] = alloc_at(nc, "vwb", [128, 8 * 256], BF16, RZ0 + 16 * T)
    bufs["ropec"] = alloc_at(nc, "ropec", [128, T], F32, RZ0 + 16 * T + 4096)
    bufs["ropes"] = alloc_at(nc, "ropes", [128, T], F32, RZ0 + 16 * T + 4096 + 4 * T)
    bufs["qrope"] = alloc_at(nc, "qrope", [128, 8 * T], BF16, RB0)
    bufs["krope"] = alloc_at(nc, "krope", [128, 2 * T], BF16, RB0 + 16 * T)
    bufs["vtok"] = alloc_at(nc, "vtok", [128, 2 * T], BF16, RB0 + 20 * T)
    bufs["scrC"] = alloc_at(nc, "scrC", [128, T], F32, RB0 + 20 * T)
    bufs["ctxT"] = alloc_at(nc, "ctxT", [128, 8 * T], BF16, RB0 + 24 * T)
    bufs["hbuf"] = alloc_at(nc, "hbuf", [128, 32 * 1024], BF16, RB0)
    bufs["pbalt"] = alloc_at(nc, "l0_pbalt", [128, T], BF16, RB0 + 80 * 1024)
    bufs["wslots"] = [alloc_at(nc, f"wsl{i}", [128, 1024], BF16, RB0 + 84 * 1024 + 2048 * i)
                      for i in range(2)]
    bufs["w2slot"] = alloc_at(nc, "w2slot", [128, 32 * 128], BF16, RB0 + 64 * 1024)
    bufs["b1sb"] = nc.alloc_sbuf_tensor("b1sb", [128, 32], F32).ap()
    bufs["b2sb"] = nc.alloc_sbuf_tensor("b2sb", [128, 8], F32).ap()
    zbuf = alloc_at(nc, "zbuf", [128, 8 * (T + 1)], F32, RZ0)
    mb = alloc_mla_bufs(nc, ctx, T, RB0)
    fb = alloc_l1f_bufs(nc, ctx, T, xs_at=RS0)

    p = Prog(nc)
    load_consts(p, ctx)
    load_l1f_consts(p, ctx)
    emit_ln_to_bf16(p, ctx, xin, stat0, bufs["xnb"], "a", "0")
    p.fence([], ["stat.mu", "stat.rs"] + [f"qr{i}" for i in range(8)])
    emit_l0_attn(p, ctx, d, bufs)
    p.fence([f"qr{i}" for i in range(8)] + ["kr0", "kr1", "vtok"] + [f"ct{i}" for i in range(8)],
            [f"h{k}" for k in range(32)] + ["w2s"])
    emit_l0_ffn(p, ctx, d, bufs)
    p.fence([], [f"a{c}" for c in range(8)] + ["vw", "ropec", "ropes",
                 "stat.mu", "stat.rs"] + [f"z{c}" for c in range(8)]
            + [f"h{k}" for k in range(32)])
    emit_ln_to_bf16(p, ctx, x1, stat0, None, "z", "z", fp32dst=(zbuf, T + 1, 1),
                    src_res=lambda c, s: [f"xB{c}.{s}"])
    p.fence([], MLA_RES + ["stat.mu", "stat.rs", "scrC", "P.0", "P.1", "P2b", "w2s"]
            + [f"qr{i}" for i in range(8)] + ["kr0", "kr1", "vtok"]
            + [f"ct{i}" for i in range(8)] + [f"h{k}" for k in range(32)])
    load_mla_consts(p, ctx)
    emit_mla_setup(p, ctx, d, mb)
    for h in range(16):
        c2 = h // 2
        half = 64 * (h % 2)
        zap = zbuf[half:half + 64, c2 * (T + 1) + 1: c2 * (T + 1) + 1 + T]
        emit_mla_head(p, ctx, mb, h, zap, f"z{c2}", x1, x2, first_for_chunk=True)
    p.fence([], L1F_RES + MLA_RES + ["stat.mu", "stat.rs", "ws0", "ws1", "w2s",
                                      "xs0", "xs1", "xs2", "xs3", "P.0", "P.1", "P2b"])
    emit_l1f(p, ctx, d, fb, zbuf, zdram, T)
    emit_final_combine(p, ctx, zdram, x2, xout, T)
    p.barrier_all_dma()
    p.emit()
    check_sbuf(nc, RS0)
    return nc


def kernel(**inputs):
    T = T_FULL
    x = np.asarray(inputs["L0_attn_x"])
    B = x.shape[0]
    dall = prep_l0(inputs, T)
    dall.update(prep_mla(inputs))
    dall.update(prep_l1f(inputs, T))
    nc = _build_fused(T)
    maps = [{**dall, "xin": np.ascontiguousarray(x[b].T)} for b in range(B)]
    res = run_bass_kernel_spmd(nc, maps, core_ids=list(range(B)))
    out = np.stack([np.asarray(res.results[b]["xout"]).T for b in range(B)])
    return out.astype(np.float32)


# version 06
# speedup vs baseline: 1.1477x; 1.0160x over previous
import numpy as np
import ml_dtypes
import concourse.bass as bass
import concourse.mybir as mybir
from concourse.bass_utils import run_bass_kernel_spmd

F32 = mybir.dt.float32
BF16 = mybir.dt.bfloat16
AF = mybir.ActivationFunctionType
ALU = mybir.AluOpType
AX = mybir.AxisListType
BF = ml_dtypes.bfloat16

NEG = -1.0e30


class _Op:
    __slots__ = ("eng", "fn", "reads", "writes", "dma_key", "idx", "deps",
                 "inc", "cnt", "tag")
    def __init__(self):
        self.deps = set()
        self.inc = False
        self.cnt = None
        self.dma_key = None
        self.tag = ""

class Prog:
    COMPUTE = ("tensor", "vector", "scalar", "gpsimd", "sync")

    def __init__(self, nc):
        self.nc = nc
        self.ops = []
        self.wmap = {}
        self.rmap = {}

    def op(self, eng, fn, reads=(), writes=(), dma_key=None, tag=""):
        idx = len(self.ops)
        o = _Op()
        o.eng = eng; o.fn = fn; o.reads = tuple(reads); o.writes = tuple(writes)
        o.dma_key = dma_key; o.idx = idx; o.tag = tag
        self.ops.append(o)
        dep = self._dep
        for r in o.reads:
            j = self.wmap.get(r)
            if j is not None:
                dep(o, j)
            self.rmap.setdefault(r, []).append(idx)
        for r in o.writes:
            j = self.wmap.get(r)
            if j is not None:
                dep(o, j)
            for k in self.rmap.get(r, ()):
                if k != idx:
                    dep(o, k)
            self.wmap[r] = idx
            self.rmap[r] = []
        return idx

    def _dep(self, o, j):
        if j == o.idx:
            return
        oj = self.ops[j]
        if o.eng == "tensor" and oj.eng == "tensor":
            return
        o.deps.add(j)

    def dma(self, eng, out, in_, key, reads=(), writes=(), **kw):
        return self.op(eng, lambda e: e.dma_start(out=out, in_=in_, **kw),
                       reads=reads, writes=writes, dma_key=key)

    def fence(self, reads, writes, eng="sync"):
        return self.op(eng, lambda e: e.nop(), reads=list(reads), writes=list(writes))

    def barrier_all_dma(self):
        deps = set()
        for o in self.ops:
            if o.dma_key is not None:
                deps.add(o.idx)
        idx = self.op("sync", None)
        self.ops[idx].deps = deps
        return idx

    def emit(self):
        nc = self.nc
        for o in self.ops:
            for j in o.deps:
                self.ops[j].inc = True
        eng_sem = {}
        eng_cnt = {}
        dma_sem = {}
        dma_cnt = {}
        for o in self.ops:
            if o.dma_key is not None:
                if o.dma_key not in dma_sem:
                    dma_sem[o.dma_key] = nc.alloc_semaphore(f"d_{o.dma_key}"[:60])
                    dma_cnt[o.dma_key] = 0
                dma_cnt[o.dma_key] += 16
                o.cnt = dma_cnt[o.dma_key]
                o.inc = True
            elif o.inc:
                if o.eng not in eng_sem:
                    eng_sem[o.eng] = nc.alloc_semaphore(f"e_{o.eng}")
                    eng_cnt[o.eng] = 0
                eng_cnt[o.eng] += 1
                o.cnt = eng_cnt[o.eng]
        def sem_of(o):
            return dma_sem[o.dma_key] if o.dma_key is not None else eng_sem[o.eng]
        streams = {}
        for o in self.ops:
            streams.setdefault(o.eng, []).append(o)
        ops_by_idx = self.ops
        with nc.Block() as blk:
            for eng_name, olist in streams.items():
                def section(e, olist=olist):
                    water = {}
                    for o in olist:
                        waits = {}
                        for j in o.deps:
                            oj = ops_by_idx[j]
                            s = sem_of(oj)
                            v = oj.cnt
                            if waits.get(s, 0) < v:
                                waits[s] = v
                        for s, v in waits.items():
                            if water.get(s, 0) >= v:
                                continue
                            e.wait_ge(s, v)
                            water[s] = v
                        if o.fn is None:
                            continue
                        ins = o.fn(e)
                        if o.inc:
                            ins.then_inc(sem_of(o), 16 if o.dma_key is not None else 1)
                getattr(blk, eng_name)(section)
        return dma_cnt


def ceil_div(a, b):
    return (a + b - 1) // b


def emit_ln(p, ctx, xdram, stat, out_emit, tag="", src_res=None):
    nc, T = ctx["nc"], ctx["T"]
    NS = T // 512
    scr = ctx["scrA"]
    S_ps = ctx["S_ps"]
    P2 = ctx["P2_ps"]
    ones_f = ctx["ones_f32"]
    xst = ctx["xstage"]
    for c in range(8):
        for s in range(NS):
            k = (c * NS + s) % 4
            xa = xst[:, 512 * k:512 * k + 512]
            xr = f"xs{k}"
            sl = slice(512 * s, 512 * s + 512)
            p.dma("sync", xa, xdram[c * 128:(c + 1) * 128, sl], xr, writes=[xr],
                  reads=(src_res(c, s) if src_res else []))
            p.op("scalar", lambda e, xa=xa, sl=sl: e.activation(scr[:, sl], xa, AF.Square),
                 reads=[xr], writes=[f"scrA{s}"], tag=f"ln{tag}.sq{c}.{s}")
            p.op("tensor", lambda e, xa=xa, sl=sl, c=c: e.matmul(
                S_ps[0:1, sl], ones_f[:, 0:1], xa, start=(c == 0), stop=(c == 7)),
                reads=[xr, "cones"], writes=[f"S{s}"], tag=f"ln{tag}.sum{c}.{s}")
            p.op("tensor", lambda e, sl=sl, c=c: e.matmul(
                S_ps[32:33, sl], ones_f[:, 0:1], scr[:, sl], start=(c == 0), stop=(c == 7)),
                reads=[f"scrA{s}", "cones"], writes=[f"S{s}"], tag=f"ln{tag}.sqs{c}.{s}")
    mu = stat[0:1, 0:T]
    m2 = scr[0:1, 0:T]
    rs = stat[0:1, T:2 * T]
    Sres = [f"S{s}" for s in range(NS)]
    scrres = [f"scrA{s}" for s in range(NS)]
    p.op("vector", lambda e: e.tensor_scalar_mul(mu, S_ps[0:1, 0:T], 1.0 / 1024.0),
         reads=Sres, writes=["stat.mu"], tag=f"ln{tag}.mu")
    p.op("vector", lambda e: e.tensor_mul(m2, mu, mu),
         reads=["stat.mu"] + scrres, writes=scrres, tag=f"ln{tag}.m2")
    p.op("vector", lambda e: e.scalar_tensor_tensor(
        m2, S_ps[32:33, 0:T], 1.0 / 1024.0, m2, op0=ALU.mult, op1=ALU.subtract),
        reads=Sres + scrres, writes=scrres, tag=f"ln{tag}.var")
    p.op("scalar", lambda e: e.activation(rs, m2, AF.Sqrt, bias=ctx["cv"](1e-5, 1)),
         reads=scrres + ["cv"], writes=["stat.rs"], tag=f"ln{tag}.sqrt")
    p.op("vector", lambda e: e.reciprocal(rs, rs),
         reads=["stat.rs"], writes=["stat.rs"], tag=f"ln{tag}.recip")
    for c in range(8):
        for s in range(NS):
            k = (c * NS + s) % 4
            xa = xst[:, 512 * k:512 * k + 512]
            xr = f"xs{k}"
            sl = slice(512 * s, 512 * s + 512)
            p.dma("sync", xa, xdram[c * 128:(c + 1) * 128, sl], xr, writes=[xr],
                  reads=(src_res(c, s) if src_res else []))
            mu_b = P2[:, 0:512]
            p.op("tensor", lambda e, sl=sl, mu_b=mu_b: e.matmul(
                mu_b, ones_f[0:1, 0:128], mu[:, sl], start=True, stop=True),
                reads=["stat.mu", "cones"], writes=["P0"], tag=f"ln{tag}.bmu{c}.{s}")
            rs_b = P2[:, 512:1024]
            p.op("tensor", lambda e, sl=sl, rs_b=rs_b: e.matmul(
                rs_b, ones_f[0:1, 0:128], rs[:, sl], start=True, stop=True),
                reads=["stat.rs", "cones"], writes=["P1"], tag=f"ln{tag}.brs{c}.{s}")
            out_emit(p, c, s, xa, xr, (mu_b, "P0"), (rs_b, "P1"))


def emit_ln_to_bf16(p, ctx, xdram, stat, xn, xn_res_prefix, tag="0", fp32dst=None,
                    src_res=None):
    T = ctx["T"]
    scrB = ctx["scrB"]

    def apply(p, c, s, xa, xr, mu_bi, rs_bi):
        mu_b, mu_r = mu_bi
        rs_b, rs_r = rs_bi
        sl = slice(512 * s, 512 * s + 512)
        t1 = scrB[:, sl]
        p.op("vector", lambda e, xa=xa, t1=t1, mu_b=mu_b: e.scalar_tensor_tensor(
            t1, xa, 1.0, mu_b, op0=ALU.mult, op1=ALU.subtract),
            reads=[xr, mu_r], writes=["scrB"], tag=f"lnap{tag}.sub{c}.{s}")
        if fp32dst is None:
            dst = xn[:, c * T + 512 * s: c * T + 512 * s + 512]
            rr = f"{xn_res_prefix}{c}"
        else:
            ap, stride, guard = fp32dst
            dst = ap[:, c * stride + guard + 512 * s: c * stride + guard + 512 * s + 512]
            rr = f"{xn_res_prefix}{c}"
        p.op("vector", lambda e, dst=dst, t1=t1, rs_b=rs_b: e.tensor_mul(dst, t1, rs_b),
             reads=["scrB", rs_r], writes=[rr], tag=f"lnap{tag}.mul{c}.{s}")

    emit_ln(p, ctx, xdram, stat, apply, tag=tag, src_res=src_res)



MTOP = 229344
RB_SZ = 88 * 1024
RB0 = MTOP - RB_SZ
RZ_SZ = 8 * (2048 + 1) * 4
RZ0 = RB0 - RZ_SZ

def alloc_at(nc, name, shape, dtype, off):
    return nc.alloc_sbuf_tensor_at(name, shape, dtype, offset=off).ap()

def make_ctx(nc, T, xstage_at=None):
    ctx = {"nc": nc, "T": T}
    ctx["S_ps"] = nc.alloc_psum_tensor("S_ps", [128, 2048], F32).ap()
    ctx["P2_ps"] = nc.alloc_psum_tensor("P2_ps", [128, 1024], F32).ap()
    ctx["TR_ps"] = nc.alloc_psum_tensor("TR_ps", [128, 2048], BF16).ap()
    ctx["scrA"] = nc.alloc_sbuf_tensor("scrA", [128, max(T, 2048)], F32).ap()
    if xstage_at is None:
        ctx["xstage"] = nc.alloc_sbuf_tensor("xstage", [128, 2048], F32).ap()
    else:
        ctx["xstage"] = alloc_at(nc, "xstage", [128, 2048], F32, xstage_at)
    ctx["scrB"] = nc.alloc_sbuf_tensor("scrB", [128, max(T, 2048)], F32).ap()
    ctx["stats2"] = nc.alloc_sbuf_tensor("stats2", [128, 64], F32).ap()
    ctx["Pbuf"] = nc.alloc_sbuf_tensor("Pbuf", [128, max(T, 2048)], BF16).ap()
    ctx["Pbuf2"] = nc.alloc_sbuf_tensor("Pbuf2", [128, max(T, 2048)], BF16).ap()
    ctx["ptsb"] = nc.alloc_sbuf_tensor("ptsb", [128, 2048], BF16).ap()
    CVALS = [1e-5, 1.1920929e-07, 0.0]
    ctx["cvals_list"] = CVALS
    cv = np.tile(np.asarray(CVALS, np.float32)[None, :], (128, 1))
    ctx["cv_dram"] = nc.inline_tensor(cv, "cvals").ap()
    ctx["cv_sb"] = nc.alloc_sbuf_tensor("cvals_s", [128, len(CVALS)], F32).ap()
    def cvap(val, parts=128):
        k = CVALS.index(val)
        return ctx["cv_sb"][0:parts, k:k + 1]
    ctx["cv"] = cvap
    ones = np.ones((128, 128), np.float32)
    ctx["ones_f32_dram"] = nc.inline_tensor(ones, "ones_f32").ap()
    ctx["ones_bf_dram"] = nc.inline_tensor(ones.astype(BF), "ones_bf").ap()
    ctx["ident_dram"] = nc.inline_tensor(np.eye(128, dtype=BF), "ident_bf").ap()
    ctx["ones_f32"] = nc.alloc_sbuf_tensor("ones_f32s", [128, 128], F32).ap()
    ctx["ones_bf"] = nc.alloc_sbuf_tensor("ones_bfs", [128, 128], BF16).ap()
    ctx["ident"] = nc.alloc_sbuf_tensor("idents", [128, 128], BF16).ap()
    idf = np.eye(128, dtype=np.float32)
    ctx["identf_dram"] = nc.inline_tensor(idf, "identf").ap()
    ctx["identf"] = nc.alloc_sbuf_tensor("identfs", [128, 128], F32).ap()
    return ctx

def load_consts(p, ctx):
    p.dma("sync", ctx["cv_sb"][:, :], ctx["cv_dram"][:, :], "cv",
          writes=["cv"])
    p.dma("sync", ctx["ones_f32"][:, :], ctx["ones_f32_dram"][:, :], "cones",
          writes=["cones"])
    p.dma("sync", ctx["ones_bf"][:, :], ctx["ones_bf_dram"][:, :], "cones_bf",
          writes=["cones_bf"])
    p.dma("sync", ctx["ident"][:, :], ctx["ident_dram"][:, :], "cident",
          writes=["cident"])
    p.dma("sync", ctx["identf"][:, :], ctx["identf_dram"][:, :], "identf", writes=["identf"])


def prep_l0(inp, T):
    D = 1024
    Wq = inp["L0_attn_Wq"]; Wk = inp["L0_attn_Wk"]; Wv = inp["L0_attn_Wv"]
    Wo = inp["L0_attn_Wo"]
    r = np.arange(512)
    h = 4 * ((r % 128) // 32) + r // 128
    j = r % 32
    pidx = 32 * h + j
    permE = 2 * pidx
    qperm = np.concatenate([permE, permE + 1])
    Wq_p = np.ascontiguousarray(Wq[:, qperm])
    rk = np.arange(128)
    gk = rk // 32; jk = rk % 32
    kpermE = 64 * gk + 2 * jk
    kperm = np.concatenate([kpermE, kpermE + 1])
    Wk_p = np.ascontiguousarray(Wk[:, kperm])
    def fmaj(W):
        nf = W.shape[1] // 128
        return np.ascontiguousarray(W.reshape(W.shape[0], nf, 128).transpose(1, 0, 2))
    out = {
        "l0_wq": fmaj(Wq_p), "l0_wk": fmaj(Wk_p), "l0_wv": np.ascontiguousarray(Wv),
        "l0_wo": fmaj(Wo),
        "l0_w1": fmaj(inp["L0_ffn_w1"]), "l0_w2": fmaj(inp["L0_ffn_w2"]),
        "l0_b1": np.ascontiguousarray(inp["L0_ffn_b1"].reshape(32, 128).T),
        "l0_b2": np.ascontiguousarray(inp["L0_ffn_b2"].reshape(8, 128).T),
    }
    t = np.arange(T, dtype=np.float64)
    inv = 1.0 / (10000.0 ** (np.arange(0, D, 2, dtype=np.float64) / D))
    ang_q = t[None, :] * inv[pidx][:, None]
    out["l0_ropeq"] = np.stack([np.cos(ang_q) / 32.0, np.sin(ang_q) / 32.0]).astype(np.float32)
    ang_k = t[None, :] * inv[jk][:, None]
    out["l0_ropek"] = np.stack([np.cos(ang_k), np.sin(ang_k)]).astype(np.float32)
    return out


def declare_l0(nc, T):
    d = {}
    def dram(name, shape):
        d[name] = nc.dram_tensor(name, list(shape), F32, kind="ExternalInput").ap()
    dram("l0_wq", (8, 1024, 128)); dram("l0_wk", (2, 1024, 128))
    dram("l0_wv", (1024, 256)); dram("l0_wo", (8, 1024, 128))
    dram("l0_w1", (32, 1024, 128)); dram("l0_w2", (8, 4096, 128))
    dram("l0_b1", (128, 32)); dram("l0_b2", (128, 8))
    dram("l0_ropeq", (2, 512, T)); dram("l0_ropek", (2, 128, T))
    return d


def emit_l0_attn(p, ctx, d, bufs):
    T = ctx["T"]; NS = T // 512; NT = T // 128
    S_ps = ctx["S_ps"]; P2 = ctx["P2_ps"]; TR = ctx["TR_ps"]
    scrA, scrB, scrC = ctx["scrA"], ctx["scrB"], bufs["scrC"]
    st2 = ctx["stats2"]
    xnb = bufs["xnb"]; qrope = bufs["qrope"]; krope = bufs["krope"]
    vtok = bufs["vtok"]; Pb = ctx["Pbuf"]; ptsb = ctx["ptsb"]; ctxT = bufs["ctxT"]
    ws = bufs["wslots"]
    vw = bufs["vw"]
    rc = bufs["ropec"]; rs_ = bufs["ropes"]
    ident = ctx["ident"]
    Sres = ["S0", "S1", "S2", "S3"]

    SCRA = [f"scrA{i}" for i in range(NS)]

    def wload(dram3, idx, slot, key):
        ap = dram3[idx].rearrange("(c p) f -> p c f", p=128)
        out = ws[slot][:, :].rearrange("p (c f) -> p c f", f=128)
        p.dma("gpsimd", out, ap, key, writes=[f"ws{slot}"])

    def proj_chunk(fm, slot, psum_banks=Sres, dst=None):
        for s in range(NS):
            sl = slice(512 * s, 512 * s + 512)
            for dk in range(8):
                p.op("tensor", lambda e, sl=sl, dk=dk, slot=slot: e.matmul(
                    S_ps[:, sl], ws[slot][:, 128 * dk:128 * dk + 128],
                    xnb[:, dk * T + sl.start: dk * T + sl.stop],
                    start=(dk == 0), stop=(dk == 7)),
                    reads=[f"ws{slot}", f"a{dk}"], writes=[f"S{s}"],
                    tag=f"l0.proj{fm}.{s}")

    def rope_load(tab, c, key):
        p.dma("sync", rc[:, 0:T], tab[c * 128:(c + 1) * 128, 0:T], "ropec",
              writes=["ropec"])
        p.dma("sync", rs_[:, 0:T], tab[512 + 128 * c if False else 0, 0:T], key,
              writes=["ropes"])

    def qk_phase(nf_half, wdram, tabdram, dst, dst_prefix, nrows):
        for c in range(nf_half):
            p.dma("sync", rc[:, 0:T], tabdram[0, 128 * c:128 * c + nrows, 0:T], "ropec",
                  writes=["ropec"])
            p.dma("sync", rs_[:, 0:T], tabdram[1, 128 * c:128 * c + nrows, 0:T], "ropes",
                  writes=["ropes"])
            wload(wdram, c, 0, "ws0")
            proj_chunk(c, 0)
            p.op("vector", lambda e: e.tensor_copy(scrA[0:nrows, 0:T], S_ps[0:nrows, 0:T]),
                 reads=Sres, writes=SCRA, tag="l0.evacE")
            wload(wdram, nf_half + c, 1, "ws1")
            proj_chunk(nf_half + c, 1)
            eT = slice(c * T, c * T + T)
            oT = slice((nf_half + c) * T, (nf_half + c) * T + T)
            p.op("vector", lambda e: e.tensor_mul(scrB[0:nrows, 0:T], scrA[0:nrows, 0:T], rc[0:nrows, 0:T]),
                 reads=SCRA + ["ropec"], writes=["scrB"], tag="l0.r1")
            p.op("vector", lambda e: e.tensor_mul(scrC[0:nrows, 0:T], S_ps[0:nrows, 0:T], rs_[0:nrows, 0:T]),
                 reads=Sres + ["ropes"], writes=["scrC"], tag="l0.r2")
            p.op("vector", lambda e, eT=eT: e.tensor_sub(dst[0:nrows, eT], scrB[0:nrows, 0:T], scrC[0:nrows, 0:T]),
                 reads=["scrB", "scrC"], writes=[f"{dst_prefix}{c}"], tag="l0.r3")
            p.op("vector", lambda e: e.tensor_mul(scrB[0:nrows, 0:T], scrA[0:nrows, 0:T], rs_[0:nrows, 0:T]),
                 reads=SCRA + ["ropes"], writes=["scrB"], tag="l0.r4")
            p.op("vector", lambda e: e.tensor_mul(scrC[0:nrows, 0:T], S_ps[0:nrows, 0:T], rc[0:nrows, 0:T]),
                 reads=Sres + ["ropec"], writes=["scrC"], tag="l0.r5")
            p.op("vector", lambda e, oT=oT: e.tensor_add(dst[0:nrows, oT], scrB[0:nrows, 0:T], scrC[0:nrows, 0:T]),
                 reads=["scrB", "scrC"], writes=[f"{dst_prefix}{nf_half + c}"], tag="l0.r6")

    qk_phase(4, d["l0_wq"], d["l0_ropeq"], qrope, "qr", 128)
    qk_phase(1, d["l0_wk"], d["l0_ropek"], krope, "kr", 128)

    p.fence([], ["scrC", "vtok"])
    p.dma("gpsimd", vw[:, :].rearrange("p (c f) -> p c f", f=256),
          d["l0_wv"].rearrange("(c p) f -> p c f", p=128), "vw", writes=["vw"])
    for ti in range(NT):
        for dk in range(8):
            p.op("tensor", lambda e, ti=ti, dk=dk: e.matmul(
                P2[:, 0:256], xnb[:, dk * T + 128 * ti: dk * T + 128 * ti + 128],
                vw[:, 256 * dk:256 * dk + 256], start=(dk == 0), stop=(dk == 7)),
                reads=[f"a{dk}", "vw"], writes=["P0"], tag=f"l0.v{ti}")
        p.op("vector", lambda e, ti=ti: e.tensor_copy(vtok[:, 256 * ti:256 * ti + 256], P2[:, 0:256]),
             reads=["P0"], writes=["vtok"], tag=f"l0.vev{ti}")

    def l0_s1(h, qt):
        base = 32 * (h // 4)
        ce_, co = h % 4, 4 + h % 4
        qsl = slice(128 * qt, 128 * qt + 128)
        for js in range(NS):
            jsl = slice(512 * js, 512 * js + 512)
            p.op("tensor", lambda e, qsl=qsl, jsl=jsl, base=base, ce_=ce_: e.matmul(
                S_ps[:, jsl],
                qrope[base:base + 32, ce_ * T + qsl.start: ce_ * T + qsl.stop],
                krope[base:base + 32, jsl], start=True, stop=False,
                tile_position=(base, 0)),
                reads=[f"qr{ce_}", "kr0"], writes=[f"S{js}"], tag=f"l0.se{h}.{qt}")
            p.op("tensor", lambda e, qsl=qsl, jsl=jsl, base=base, co=co: e.matmul(
                S_ps[:, jsl],
                qrope[base:base + 32, co * T + qsl.start: co * T + qsl.stop],
                krope[base:base + 32, T + jsl.start: T + jsl.stop],
                start=False, stop=True, tile_position=(base, 0)),
                reads=[f"qr{co}", "kr1"], writes=[f"S{js}"], tag=f"l0.so{h}.{qt}")

    def l0_s2a(h, qt):
        pe = qt % 2
        Pba = Pb if pe == 0 else bufs["pbalt"]
        sums = st2[:, 4 * pe + 1:4 * pe + 2]
        rsum = st2[:, 4 * pe + 2:4 * pe + 3]
        p.op("scalar", lambda e, sums=sums, Pba=Pba: e.activation(
            Pba[:, 0:T], S_ps[:, 0:T], AF.Exp, bias=0.0, accum_out=sums),
             reads=Sres, writes=[f"P.{pe}", f"sums{pe}"],
             tag=f"l0.exp{h}.{qt}")
        p.op("vector", lambda e, sums=sums, rsum=rsum: e.reciprocal(rsum, sums),
             reads=[f"sums{pe}"], writes=[f"rsum{pe}"], tag=f"l0.rec{h}.{qt}")
        Pb2 = ctx["Pbuf2"]
        p.op("vector", lambda e, Pb2=Pb2, Pba=Pba, rsum=rsum: e.tensor_scalar_mul(
            Pb2[:, 0:T], Pba[:, 0:T], rsum),
             reads=[f"P.{pe}", f"rsum{pe}"], writes=["P2b"], tag=f"l0.ps{h}.{qt}")

    def l0_s2b(h, qt):
        pe = qt % 2
        qsl = slice(128 * qt, 128 * qt + 128)
        Pb2 = ctx["Pbuf2"]
        for jt in range(NT):
            sl2 = jt % 2
            trof = 1024 * sl2
            p.op("tensor", lambda e, jt=jt, trof=trof, Pb2=Pb2: e.transpose(
                TR[:, trof:trof + 128], Pb2[:, 128 * jt:128 * jt + 128], ident),
                reads=["P2b", "cident"], writes=[f"TR{sl2}"], tag=f"l0.tr{h}.{qt}.{jt}")
            p.op("vector" if jt % 2 == 0 else "scalar", lambda e, jt=jt, trof=trof:
                 (e.tensor_copy if hasattr(e, "tensor_copy") else e.copy)(
                ptsb[:, 128 * jt:128 * jt + 128], TR[:, trof:trof + 128]),
                reads=[f"TR{sl2}"], writes=[f"pt{jt}"], tag=f"l0.trev{h}.{qt}.{jt}")
        pvb = (P2[64 * (h % 2):64 * (h % 2) + 64, 256:384] if pe == 0
               else P2[64 * (h % 2):64 * (h % 2) + 64, 768:896])
        pvres = "P0" if pe == 0 else "P1"
        for jt in range(NT):
            p.op("tensor", lambda e, jt=jt, h=h, pvb=pvb: e.matmul(
                pvb,
                vtok[:, 256 * jt + 64 * (h // 4): 256 * jt + 64 * (h // 4) + 64],
                ptsb[:, 128 * jt:128 * jt + 128],
                start=(jt == 0), stop=(jt == NT - 1), tile_position=(0, 64 * (h % 2))),
                reads=["vtok", f"pt{jt}"], writes=[pvres], tag=f"l0.pv{h}.{qt}.{jt}")
        half = 64 * (h % 2)
        p.op("vector", lambda e, h=h, qsl=qsl, half=half, pvb=pvb: e.tensor_copy(
            ctxT[half:half + 64, (h // 2) * T + qsl.start: (h // 2) * T + qsl.stop],
            pvb),
            reads=[pvres], writes=[f"ct{h // 2}"], tag=f"l0.cev{h}.{qt}")

    tiles0 = [(h, qt) for h in range(16) for qt in range(NT)]
    l0_s1(*tiles0[0])
    l0_s2a(*tiles0[0])
    for ti_ in range(1, len(tiles0)):
        l0_s1(*tiles0[ti_])
        l0_s2b(*tiles0[ti_ - 1])
        l0_s2a(*tiles0[ti_])
    l0_s2b(*tiles0[-1])

    xst = ctx["xstage"]
    xsrc = bufs["xsrc"]; xdst = bufs["xdst"]
    for e_ in range(8):
        wload(d["l0_wo"], e_, e_ % 2, f"ws{e_ % 2}")
        for s in range(NS):
            sl = slice(512 * s, 512 * s + 512)
            slot = (e_ * NS + s) % 4
            xa = xst[:, 512 * slot:512 * slot + 512]
            xr = f"xs{slot}"
            p.dma("sync", xa, xsrc[e_ * 128:(e_ + 1) * 128, sl], xr, writes=[xr])
            for ic in range(8):
                p.op("tensor", lambda e, sl=sl, ic=ic, slot2=e_ % 2: e.matmul(
                    S_ps[:, sl], ws[slot2][:, 128 * ic:128 * ic + 128],
                    ctxT[:, ic * T + sl.start: ic * T + sl.stop],
                    start=(ic == 0), stop=(ic == 7)),
                    reads=[f"ws{e_ % 2}", f"ct{ic}"], writes=[f"S{s}"],
                    tag=f"l0.wo{e_}.{s}")
            p.op("vector", lambda e, xa=xa, s=s, sl=sl: e.tensor_add(
                xa, xa, S_ps[:, sl]),
                reads=[f"S{s}", xr], writes=[xr], tag=f"l0.xadd{e_}.{s}")
            p.dma("sync", xdst[e_ * 128:(e_ + 1) * 128, sl], xa, xr, reads=[xr],
                  writes=[f"xA{e_}.{s}"])


def emit_l0_ffn(p, ctx, d, bufs):
    T = ctx["T"]; NS = T // 512
    S_ps = ctx["S_ps"]; P2 = ctx["P2_ps"]
    xnb = bufs["xnb"]; hbuf = bufs["hbuf"]
    ws = bufs["wslots"]; w2s = bufs["w2slot"]
    b1 = bufs["b1sb"]; b2 = bufs["b2sb"]
    p.dma("sync", b1[:, :], d["l0_b1"][:, :], "b1sb", writes=["b1"])
    p.dma("sync", b2[:, :], d["l0_b2"][:, :], "b2sb", writes=["b2"])
    TH = T // 1024 if T >= 1024 else 1
    HW = T // TH
    NS2 = HW // 512
    for th in range(TH):
        t0 = th * HW
        for fm in range(32):
            slot = fm % 2
            p.dma("gpsimd", ws[slot][:, :].rearrange("p (c f) -> p c f", f=128),
                  d["l0_w1"][fm].rearrange("(c p) f -> p c f", p=128),
                  f"ws{slot}", writes=[f"ws{slot}"])
            for s in range(NS2):
                sl = slice(t0 + 512 * s, t0 + 512 * s + 512)
                bank = "P0" if fm % 2 == 0 else "P1"
                pa = P2[:, 0:512] if fm % 2 == 0 else P2[:, 512:1024]
                for dk in range(8):
                    p.op("tensor", lambda e, sl=sl, dk=dk, slot=slot, pa=pa: e.matmul(
                        pa, ws[slot][:, 128 * dk:128 * dk + 128],
                        xnb[:, dk * T + sl.start: dk * T + sl.stop],
                        start=(dk == 0), stop=(dk == 7)),
                        reads=[f"ws{slot}", f"a{dk}"], writes=[bank],
                        tag=f"l0f.w1.{fm}.{s}")
                p.op("scalar", lambda e, fm=fm, s=s, pa=pa: e.activation(
                    hbuf[:, fm * HW + 512 * s: fm * HW + 512 * s + 512], pa,
                    AF.Gelu, bias=b1[:, fm:fm + 1]),
                    reads=[bank, "b1"], writes=[f"h{fm}"], tag=f"l0f.gelu.{fm}.{s}")
        for e_ in range(8):
            p.dma("gpsimd", w2s[:, :].rearrange("p (c f) -> p c f", f=128),
                  d["l0_w2"][e_].rearrange("(c p) f -> p c f", p=128),
                  "w2s", writes=["w2s"])
            for s in range(NS2):
                sl = slice(t0 + 512 * s, t0 + 512 * s + 512)
                sps = s % 4
                for k in range(32):
                    p.op("tensor", lambda e, s=s, k=k, sps=sps: e.matmul(
                        S_ps[:, 512 * sps:512 * sps + 512], w2s[:, 128 * k:128 * k + 128],
                        hbuf[:, k * HW + 512 * s: k * HW + 512 * s + 512],
                        start=(k == 0), stop=(k == 31)),
                        reads=["w2s", f"h{k}"], writes=[f"S{sps}"],
                        tag=f"l0f.w2.{e_}.{s}")
                slot = (e_ * NS2 + s) % 4
                xa = ctx["xstage"][:, 512 * slot:512 * slot + 512]
                xr = f"xs{slot}"
                p.dma("sync", xa, bufs["xsrc2"][e_ * 128:(e_ + 1) * 128, sl], xr,
                      writes=[xr], reads=[f"xA{e_}.{s}"])
                p.op("vector", lambda e, xa=xa, sps=sps, e_=e_: e.scalar_tensor_tensor(
                    xa, S_ps[:, 512 * sps:512 * sps + 512], b2[:, e_:e_ + 1],
                    xa, op0=ALU.add, op1=ALU.add),
                    reads=[f"S{sps}", "b2", xr], writes=[xr],
                    tag=f"l0f.add.{e_}.{s}")
                p.dma("sync", bufs["xdst2"][e_ * 128:(e_ + 1) * 128, sl], xa, xr,
                      reads=[xr], writes=[f"xB{e_}.{s}"])


def declare_mla(nc, T):
    d = {}
    def dram(name, shape):
        d[name] = nc.dram_tensor(name, list(shape), F32, kind="ExternalInput").ap()
    dram("mla_wdq", (512, 64)); dram("mla_wuq", (64, 512))
    dram("mla_wdkv", (1024, 64)); dram("mla_wukv", (128, 1024))
    dram("mla_wqr", (64, 64)); dram("mla_wkr", (64, 64)); dram("mla_wo", (64, 64))
    dram("mla_qw", (64, 1)); dram("mla_kw", (64, 1))
    return d


def prep_mla(inp):
    return {
        "mla_wdq": inp["L1_attn_W_DQ"], "mla_wuq": inp["L1_attn_W_UQ"],
        "mla_wdkv": inp["L1_attn_W_DKV"], "mla_wukv": inp["L1_attn_W_UKV"],
        "mla_wqr": inp["L1_attn_W_QR"], "mla_wkr": inp["L1_attn_W_KR"],
        "mla_wo": inp["L1_attn_W_O"],
        "mla_qw": np.ascontiguousarray(inp["L1_attn_q_norm_w"].reshape(64, 1)),
        "mla_kw": np.ascontiguousarray(inp["L1_attn_k_norm_w"].reshape(64, 1)),
    }


def alloc_mla_bufs(nc, ctx, T, base):
    NT = T // 128
    b = {}
    off = [base]
    def A(name, shape, dtype):
        ap = alloc_at(nc, name, shape, dtype, off[0])
        off[0] += shape[-1] * (4 if dtype == F32 else 2)
        off[0] = (off[0] + 31) & ~31
        return ap
    b["qcomb"] = A("m_qcomb", [128, T], BF16)
    b["kcomb"] = A("m_kcomb", [128, T], BF16)
    b["vf"] = A("m_vf", [128, T], BF16)
    b["vtokh"] = A("m_vtokh", [128, NT * 64], BF16)
    b["ctxh"] = A("m_ctxh", [64, T], BF16)
    b["wdq"] = A("m_wdq", [128, 256], F32)
    b["wuq"] = A("m_wuq", [64, 512], F32)
    b["wuqT"] = A("m_wuqT", [128, 256], F32)
    b["wdkv"] = A("m_wdkv", [128, 512], F32)
    b["wukv"] = A("m_wukv", [128, 1024], F32)
    b["wukvT"] = A("m_wukvT", [128, 1024], F32)
    b["mq"] = A("m_mq", [128, 64], F32)
    b["mkv"] = A("m_mkv", [128, 128], F32)
    b["wqrT"] = A("m_wqrT", [64, 64], BF16)
    b["wkrT"] = A("m_wkrT", [128, 64], F32)
    b["woT"] = A("m_woT", [64, 64], BF16)
    b["wn"] = A("m_wn", [64, 2], F32)
    b["ss"] = A("m_ss", [16, T], F32)
    b["rstd"] = A("m_rstd", [16, T], F32)
    b["pbalt"] = A("m_pbalt", [128, T], BF16)
    ctx["oh64"] = A("oh64s", [64, 256], F32)
    ctx["e16"] = A("e16s", [16, 1024], F32)
    ctx["diagm"] = A("diagms", [128, 128], F32)
    b["_end"] = off[0]
    return b


def check_sbuf(nc, limit=None):
    fill = 16481 + (212863 - nc.sbuf_bytes_remaining)
    lim = RZ0 if limit is None else limit
    assert fill <= lim, f"auto sbuf fill {fill} exceeds manual region base {lim}"


def mla_consts(nc, ctx):
    oh = np.zeros((64, 16 * 16), np.float32)
    for h in range(16):
        oh[:, 16 * h + h] = 1.0
    ctx["oh64_dram"] = nc.inline_tensor(oh, "oh64").ap()
    e16 = np.zeros((16, 16 * 64), np.float32)
    for h in range(16):
        e16[h, 64 * h:64 * h + 64] = 1.0
    ctx["e16_dram"] = nc.inline_tensor(e16, "e16").ap()
    dm = np.zeros((128, 128), np.float32)
    for i in range(128):
        dm[i, i + 1:] = NEG
    ctx["diag_dram"] = nc.inline_tensor(dm, "diagm").ap()


def load_mla_consts(p, ctx):
    p.dma("sync", ctx["oh64"][:, :], ctx["oh64_dram"][:, :], "oh64", writes=["oh64"])
    p.dma("sync", ctx["e16"][:, :], ctx["e16_dram"][:, :], "e16", writes=["e16"])
    p.dma("sync", ctx["diagm"][:, :], ctx["diag_dram"][:, :], "diagm", writes=["diagm"])



def emit_mla_setup(p, ctx, d, b):
    S_ps = ctx["S_ps"]; P2 = ctx["P2_ps"]
    idf = ctx["identf"]
    def ld(dst, src, key, rearr=None):
        p.dma("sync", dst, src, key, writes=[key])
    ld(b["wdq"][:, :].rearrange("p (c f) -> p c f", f=64),
       d["mla_wdq"].rearrange("(c p) f -> p c f", p=128), "m_wdq")
    ld(b["wuq"][:, :], d["mla_wuq"][:, :], "m_wuq")
    ld(b["wdkv"][:, :].rearrange("p (c f) -> p c f", f=64),
       d["mla_wdkv"].rearrange("(c p) f -> p c f", p=128), "m_wdkv")
    ld(b["wukv"][:, :], d["mla_wukv"][:, :], "m_wukv")
    ld(b["wn"][:, 0:1], d["mla_qw"][:, :], "m_wn")
    p.dma("sync", b["wn"][:, 1:2], d["mla_kw"][:, :], "m_wn", writes=["m_wn"])
    p.op("vector", lambda e: e.tensor_scalar_mul(b["wn"][:, 0:1], b["wn"][:, 0:1], 0.125),
         reads=["m_wn"], writes=["m_wn"], tag="mla.wq8")
    for c in range(4):
        p.op("tensor", lambda e, c=c: e.transpose(
            S_ps[:, 0:64], b["wuq"][0:64, 128 * c:128 * c + 128], idf[0:64, 0:64]),
            reads=["m_wuq", "identf"], writes=["S0"], tag=f"mla.tuq{c}")
        p.op("vector", lambda e, c=c: e.tensor_copy(b["wuqT"][:, 64 * c:64 * c + 64], S_ps[:, 0:64]),
             reads=["S0"], writes=["m_wuqT"], tag=f"mla.tuqe{c}")
    for c in range(8):
        p.op("tensor", lambda e, c=c: e.transpose(
            S_ps[:, 0:128], b["wukv"][:, 128 * c:128 * c + 128], idf),
            reads=["m_wukv", "identf"], writes=["S0"], tag=f"mla.tukv{c}")
        p.op("vector", lambda e, c=c: e.tensor_copy(b["wukvT"][:, 128 * c:128 * c + 128], S_ps[:, 0:128]),
             reads=["S0"], writes=["m_wukvT"], tag=f"mla.tukve{c}")
    for half in (0, 64):
        for c in range(4):
            p.op("tensor", lambda e, c=c, half=half: e.matmul(
                P2[half:half + 64, 0:64], b["wdq"][:, 64 * c:64 * c + 64],
                b["wuqT"][:, 64 * c:64 * c + 64], start=(c == 0), stop=(c == 3),
                tile_position=(0, half)),
                reads=["m_wdq", "m_wuqT"], writes=["P0"], tag=f"mla.mq{half}.{c}")
        p.op("vector", lambda e, half=half: e.tensor_copy(
            b["mq"][half:half + 64, 0:64], P2[half:half + 64, 0:64]),
            reads=["P0"], writes=["m_mq"], tag=f"mla.mqe{half}")
        for c in range(8):
            p.op("tensor", lambda e, c=c, half=half: e.matmul(
                P2[half:half + 64, 512:640], b["wdkv"][:, 64 * c:64 * c + 64],
                b["wukvT"][:, 128 * c:128 * c + 128], start=(c == 0), stop=(c == 7),
                tile_position=(0, half)),
                reads=["m_wdkv", "m_wukvT"], writes=["P1"], tag=f"mla.mkv{half}.{c}")
        p.op("vector", lambda e, half=half: e.tensor_copy(
            b["mkv"][half:half + 64, 0:128], P2[half:half + 64, 512:640]),
            reads=["P1"], writes=["m_mkv"], tag=f"mla.mkve{half}")
    ld(b["wuq"][0:64, 0:64], d["mla_wqr"][:, :], "m_wuq")
    p.op("tensor", lambda e: e.transpose(S_ps[0:64, 0:64], b["wuq"][0:64, 0:64], idf[0:64, 0:64]),
         reads=["m_wuq", "identf"], writes=["S0"], tag="mla.tqr")
    p.op("vector", lambda e: e.tensor_copy(b["wqrT"][:, :], S_ps[0:64, 0:64]),
         reads=["S0"], writes=["m_wqrT"], tag="mla.tqre")
    ld(b["wuq"][0:64, 64:128], d["mla_wo"][:, :], "m_wuq")
    p.op("tensor", lambda e: e.transpose(S_ps[0:64, 64:128], b["wuq"][0:64, 64:128], idf[0:64, 0:64]),
         reads=["m_wuq", "identf"], writes=["S0"], tag="mla.two")
    p.op("vector", lambda e: e.tensor_copy(b["woT"][:, :], S_ps[0:64, 64:128]),
         reads=["S0"], writes=["m_woT"], tag="mla.twoe")
    ld(b["wuq"][0:64, 128:192], d["mla_wkr"][:, :], "m_wuq")
    p.op("tensor", lambda e: e.transpose(
        S_ps[0:64, 128:192], b["wuq"][0:64, 128:192], idf[0:64, 0:64]),
        reads=["m_wuq", "identf"], writes=["S0"], tag="mla.tkr")
    p.op("vector", lambda e: e.tensor_copy(
        b["wkrT"][0:64, 0:64], S_ps[0:64, 128:192]),
        reads=["S0"], writes=["m_wkrT"], tag="mla.tkre")
    p.dma("sync", b["wkrT"][64:128, 0:64], b["wkrT"][0:64, 0:64], "m_wkrTd",
          reads=["m_wkrT"], writes=["m_wkrT"])


def emit_mla_head(p, ctx, b, h, zap, zres, xsrc, xdst, first_for_chunk):
    T = ctx["T"]; NS = T // 512; NT = T // 128
    S_ps = ctx["S_ps"]; P2 = ctx["P2_ps"]; TR = ctx["TR_ps"]
    scrA, scrB = ctx["scrA"], ctx["scrB"]
    st2 = ctx["stats2"]
    Pb = ctx["Pbuf"]; ptsb = ctx["ptsb"]
    half = 64 * (h % 2)
    c2 = h // 2
    Sres = [f"S{i}" for i in range(NS)]
    SCRA = [f"scrA{i}" for i in range(NS)]
    eps = ctx["cv"](1.1920929e-07, 16)

    def norm_side(src_dst, w_col, tagp):
        p.op("scalar", lambda e: e.activation(scrB[0:64, 0:T], scrA[0:64, 0:T], AF.Square),
             reads=SCRA, writes=["scrB"], tag=f"{tagp}.sq")
        for s in range(NS):
            sl = slice(512 * s, 512 * s + 512)
            p.op("tensor", lambda e, sl=sl: e.matmul(
                P2[0:16, 0:512], ctx["oh64"][:, 16 * h:16 * h + 16], scrB[0:64, sl],
                start=True, stop=True),
                reads=["scrB", "oh64"], writes=["P0"], tag=f"{tagp}.ss{s}")
            p.op("vector", lambda e, sl=sl: e.tensor_copy(b["ss"][:, sl], P2[0:16, 0:512]),
                 reads=["P0"], writes=["m_ss"], tag=f"{tagp}.sse{s}")
        p.op("vector", lambda e: e.tensor_scalar(
            b["rstd"][:, 0:T], b["ss"][:, 0:T], 1.0 / 64.0, eps, op0=ALU.mult, op1=ALU.add),
            reads=["m_ss", "cv"], writes=["m_rstd"], tag=f"{tagp}.ms")
        p.op("scalar", lambda e: e.activation(b["rstd"][:, 0:T], b["rstd"][:, 0:T], AF.Sqrt),
             reads=["m_rstd"], writes=["m_rstd"], tag=f"{tagp}.sqrt")
        p.op("vector", lambda e: e.reciprocal(b["rstd"][:, 0:T], b["rstd"][:, 0:T]),
             reads=["m_rstd"], writes=["m_rstd"], tag=f"{tagp}.rec")
        for s in range(NS):
            sl = slice(512 * s, 512 * s + 512)
            p.op("tensor", lambda e, sl=sl: e.matmul(
                P2[0:64, 0:512], ctx["e16"][:, 64 * h:64 * h + 64], b["rstd"][:, sl],
                start=True, stop=True),
                reads=["m_rstd", "e16"], writes=["P0"], tag=f"{tagp}.bc{s}")
            p.op("vector", lambda e, sl=sl, w_col=w_col: e.scalar_tensor_tensor(
                src_dst[0:64, sl], scrA[0:64, sl], b["wn"][:, w_col:w_col + 1],
                P2[0:64, 0:512], op0=ALU.mult, op1=ALU.mult),
                reads=[f"scrA{sl.start // 512}", "m_wn", "P0"], writes=[src_dst is b["qcomb"] and "m_qc" or "m_kc"],
                tag=f"{tagp}.ap{s}")

    for s in range(NS):
        sl = slice(512 * s, 512 * s + 512)
        p.op("tensor", lambda e, sl=sl: e.matmul(
            S_ps[0:64, sl], b["mq"][half:half + 64, 0:64], zap[:, sl],
            start=True, stop=True),
            reads=["m_mq", zres], writes=[f"S{s}"], tag=f"mla{h}.qc{s}")
    p.op("vector", lambda e: e.tensor_copy(scrA[0:64, 0:T], S_ps[0:64, 0:T]),
         reads=Sres, writes=SCRA, tag=f"mla{h}.qce")
    norm_side(b["qcomb"], 0, f"mla{h}.qn")
    for s in range(NS):
        sl = slice(512 * s, 512 * s + 512)
        p.op("tensor", lambda e, sl=sl: e.matmul(
            S_ps[64:128, sl], b["wqrT"][:, :], b["qcomb"][0:64, sl],
            start=True, stop=True, tile_position=(0, 64)),
            reads=["m_wqrT", "m_qc"], writes=[f"S{s}"], tag=f"mla{h}.qr{s}")
        p.op("vector", lambda e, sl=sl: e.tensor_copy(
            b["qcomb"][64:128, sl], S_ps[64:128, sl]),
            reads=[f"S{s}"], writes=["m_qc"], tag=f"mla{h}.qre{s}")
    for s in range(NS):
        sl = slice(512 * s, 512 * s + 512)
        p.op("tensor", lambda e, sl=sl: e.matmul(
            S_ps[:, sl], b["mkv"][half:half + 64, 0:128], zap[:, sl],
            start=True, stop=True),
            reads=["m_mkv", zres], writes=[f"S{s}"], tag=f"mla{h}.kv{s}")
        p.op("vector", lambda e, sl=sl: e.tensor_copy(scrA[0:64, sl], S_ps[0:64, sl]),
             reads=[f"S{s}"], writes=[f"scrA{s}"], tag=f"mla{h}.kce{s}")
        p.op("vector", lambda e, sl=sl: e.tensor_copy(b["vf"][64:128, sl], S_ps[64:128, sl]),
             reads=[f"S{s}"], writes=["m_vf"], tag=f"mla{h}.vce{s}")
    norm_side(b["kcomb"], 1, f"mla{h}.kn")
    for s in range(NS):
        sl = slice(512 * s, 512 * s + 512)
        p.op("tensor", lambda e, sl=sl: e.matmul(
            S_ps[64:128, sl], b["wkrT"][half:half + 64, 0:64], zap[:, sl],
            start=True, stop=True, tile_position=(half, 64)),
            reads=["m_wkrT", zres], writes=[f"S{s}"], tag=f"mla{h}.kr{s}")
        p.op("vector", lambda e, sl=sl: e.tensor_copy(
            b["kcomb"][64:128, sl], S_ps[64:128, sl]),
            reads=[f"S{s}"], writes=["m_kc"], tag=f"mla{h}.kre{s}")
    for jt in range(NT):
        sl2 = jt % 2
        trof = 1024 * sl2
        p.op("tensor", lambda e, jt=jt, trof=trof: e.transpose(
            TR[:, trof:trof + 64], b["vf"][64:128, 128 * jt:128 * jt + 128],
            ctx["ident"][64:128, 64:128]),
            reads=["m_vf", "cident"], writes=[f"TR{sl2}"], tag=f"mla{h}.vt{jt}")
        p.op("vector", lambda e, jt=jt, trof=trof: e.tensor_copy(
            b["vtokh"][:, 64 * jt:64 * jt + 64], TR[:, trof:trof + 64]),
            reads=[f"TR{sl2}"], writes=["m_vtokh"], tag=f"mla{h}.vte{jt}")
    def m_stage1(qt):
        nj = 128 * (qt + 1)
        njs = ceil_div(nj, 512)
        pe = qt % 2
        negm = st2[:, 4 * pe:4 * pe + 1]
        for js in range(njs):
            jl = min(512, nj - 512 * js)
            sl = slice(512 * js, 512 * js + jl)
            p.op("tensor", lambda e, sl=sl, qt=qt: e.matmul(
                S_ps[:, sl], b["qcomb"][:, 128 * qt:128 * qt + 128], b["kcomb"][:, sl],
                start=True, stop=True),
                reads=["m_qc", "m_kc"], writes=[f"S{js}"], tag=f"mla{h}.s{qt}.{js}")
        dt_ = slice(128 * qt, 128 * qt + 128)
        p.op("vector", lambda e, dt_=dt_: e.tensor_add(S_ps[:, dt_], S_ps[:, dt_], ctx["diagm"][:, :]),
             reads=[f"S{qt // 4}", "diagm"], writes=[f"S{qt // 4}"], tag=f"mla{h}.dm{qt}")
        p.op("vector", lambda e, nj=nj, negm=negm: e.reduce_max(negm, S_ps[:, 0:nj], axis=AX.X, negate=True),
             reads=[f"S{js2}" for js2 in range(njs)], writes=[f"negm{pe}"], tag=f"mla{h}.mx{qt}")

    def m_stage2a(qt):
        nj = 128 * (qt + 1)
        njs = ceil_div(nj, 512)
        pe = qt % 2
        Pba = Pb if pe == 0 else b["pbalt"]
        negm = st2[:, 4 * pe:4 * pe + 1]
        sums = st2[:, 4 * pe + 1:4 * pe + 2]
        rsum = st2[:, 4 * pe + 2:4 * pe + 3]
        p.op("scalar", lambda e, nj=nj, negm=negm, sums=sums, Pba=Pba: e.activation(
            Pba[:, 0:nj], S_ps[:, 0:nj], AF.Exp, bias=negm, accum_out=sums),
             reads=[f"S{js2}" for js2 in range(njs)] + [f"negm{pe}"],
             writes=[f"P.{pe}", f"sums{pe}"], tag=f"mla{h}.ex{qt}")
        p.op("vector", lambda e, sums=sums, rsum=rsum: e.reciprocal(rsum, sums),
             reads=[f"sums{pe}"], writes=[f"rsum{pe}"], tag=f"mla{h}.rc{qt}")
        Pb2 = ctx["Pbuf2"]
        p.op("vector", lambda e, nj=nj, Pb2=Pb2, Pba=Pba, rsum=rsum: e.tensor_scalar_mul(
            Pb2[:, 0:nj], Pba[:, 0:nj], rsum),
            reads=[f"P.{pe}", f"rsum{pe}"], writes=["P2b"], tag=f"mla{h}.pm{qt}")

    def m_stage2b(qt):
        pe = qt % 2
        Pb2 = ctx["Pbuf2"]
        for jt in range(qt + 1):
            sl2 = jt % 2
            trof = 1024 * sl2
            p.op("tensor", lambda e, jt=jt, trof=trof, Pb2=Pb2: e.transpose(
                TR[:, trof:trof + 128], Pb2[:, 128 * jt:128 * jt + 128], ctx["ident"]),
                reads=["P2b", "cident"], writes=[f"TR{sl2}"], tag=f"mla{h}.tr{qt}.{jt}")
            p.op("vector" if jt % 2 == 0 else "scalar", lambda e, jt=jt, trof=trof:
                 (e.tensor_copy if hasattr(e, "tensor_copy") else e.copy)(
                ptsb[:, 128 * jt:128 * jt + 128], TR[:, trof:trof + 128]),
                reads=[f"TR{sl2}"], writes=[f"pt{jt}"], tag=f"mla{h}.tre{qt}.{jt}")
        pvb = P2[0:64, 256:384] if pe == 0 else P2[0:64, 768:896]
        pvres = "P0" if pe == 0 else "P1"
        for jt in range(qt + 1):
            p.op("tensor", lambda e, jt=jt, qt=qt, pvb=pvb: e.matmul(
                pvb, b["vtokh"][:, 64 * jt:64 * jt + 64],
                ptsb[:, 128 * jt:128 * jt + 128],
                start=(jt == 0), stop=(jt == qt)),
                reads=["m_vtokh", f"pt{jt}"], writes=[pvres], tag=f"mla{h}.pv{qt}.{jt}")
        p.op("vector", lambda e, qt=qt, pvb=pvb: e.tensor_copy(
            b["ctxh"][0:64, 128 * qt:128 * qt + 128], pvb),
            reads=[pvres], writes=["m_ctxh"], tag=f"mla{h}.ce{qt}")

    m_stage1(0)
    m_stage2a(0)
    for qt in range(1, NT):
        m_stage1(qt)
        m_stage2b(qt - 1)
        m_stage2a(qt)
    m_stage2b(NT - 1)
    xst = ctx["xstage"]
    for s in range(NS):
        sl = slice(512 * s, 512 * s + 512)
        p.op("tensor", lambda e, sl=sl: e.matmul(
            P2[half:half + 64, 0:512], b["woT"][:, :], b["ctxh"][0:64, sl],
            start=True, stop=True, tile_position=(0, half)),
            reads=["m_woT", "m_ctxh"], writes=["P0"], tag=f"mla{h}.wo{s}")
        slot = s % 4
        xa = xst[:, 512 * slot:512 * slot + 512]
        xr = f"xs{slot}"
        p.dma("sync", xa[half:half + 64, :], xsrc[c2 * 128 + half: c2 * 128 + half + 64, sl],
              xr, writes=[xr], reads=[f"xB{c2}.{s}"])
        p.op("vector", lambda e, xa=xa: e.tensor_add(
            xa[half:half + 64, :], xa[half:half + 64, :], P2[half:half + 64, 0:512]),
            reads=["P0", xr], writes=[xr], tag=f"mla{h}.xa{s}")
        p.dma("sync", xdst[c2 * 128 + half: c2 * 128 + half + 64, sl], xa[half:half + 64, :],
              xr, reads=[xr], writes=[f"xC{c2}.{s}.{half}"])


L1F_DEPTH = 6
DIM_INNER = 2730
DIM_PAD = 2816
NCH_H = DIM_PAD // 128

def prep_l1f(inp, T):
    out = {}
    Wqkv = inp["L1_ffn_Wqkv"]
    r = np.arange(512)
    c = r // 128; b = (r % 128) // 32; j = r % 32
    h = 4 * c + b
    qA = 64 * h + j; qB = 64 * h + 32 + j
    perm_qk = np.concatenate([qA, qB, 1024 + qA, 1024 + qB])
    qk = Wqkv[:, :, perm_qk]
    out["l1_wqk"] = np.ascontiguousarray(
        qk.reshape(6, 8, 128, 16, 128).transpose(0, 2, 3, 1, 4).reshape(6, 128, 16384))
    out["l1_wv"] = np.ascontiguousarray(Wqkv[:, :, 2048:3072])
    out["l1_wo"] = np.ascontiguousarray(
        inp["L1_ffn_Wo"].reshape(6, 8, 128, 8, 128).transpose(0, 2, 3, 1, 4)
        .reshape(6, 128, 8192))
    W1 = inp["L1_ffn_W1"]; b1 = inp["L1_ffn_b1"]
    W1a = np.zeros((6, 1024, DIM_PAD), np.float32); W1a[:, :, :DIM_INNER] = W1[:, :, :DIM_INNER]
    W1g = np.zeros((6, 1024, DIM_PAD), np.float32); W1g[:, :, :DIM_INNER] = W1[:, :, DIM_INNER:]
    b1a = np.zeros((6, DIM_PAD), np.float32); b1a[:, :DIM_INNER] = b1[:, :DIM_INNER]
    b1g = np.zeros((6, DIM_PAD), np.float32); b1g[:, :DIM_INNER] = b1[:, DIM_INNER:]
    w1i = np.zeros((6, 2 * NCH_H, 1024, 128), np.float32)
    b1i = np.zeros((6, 128, 2 * NCH_H), np.float32)
    for jc in range(NCH_H):
        w1i[:, 2 * jc] = W1a[:, :, 128 * jc:128 * jc + 128]
        w1i[:, 2 * jc + 1] = W1g[:, :, 128 * jc:128 * jc + 128]
        b1i[:, :, 2 * jc] = b1a[:, 128 * jc:128 * jc + 128]
        b1i[:, :, 2 * jc + 1] = b1g[:, 128 * jc:128 * jc + 128]
    out["l1_w1"] = np.ascontiguousarray(w1i)
    out["l1_b1"] = np.ascontiguousarray(b1i)
    W2 = inp["L1_ffn_W2"]
    W2p = np.zeros((6, DIM_PAD, 1024), np.float32); W2p[:, :DIM_INNER] = W2
    out["l1_w2"] = np.ascontiguousarray(
        W2p.reshape(6, DIM_PAD, 8, 128).transpose(0, 2, 1, 3))
    out["l1_b2"] = np.ascontiguousarray(
        inp["L1_ffn_b2"].reshape(6, 8, 128).transpose(0, 2, 1))
    out["l1_lng"] = np.ascontiguousarray(
        inp["L1_ffn_ln_g"].reshape(6, 8, 128).transpose(0, 2, 1))
    out["l1_lnb"] = np.ascontiguousarray(
        inp["L1_ffn_ln_b"].reshape(6, 8, 128).transpose(0, 2, 1))
    qs = inp["L1_ffn_q_scale"]; ks = inp["L1_ffn_k_scale"]
    def tile2(v):
        return np.stack([np.tile(v[:, 0:32], (1, 4)), np.tile(v[:, 32:64], (1, 4))],
                        axis=2)
    out["l1_qs"] = np.ascontiguousarray(tile2(qs))
    out["l1_ks"] = np.ascontiguousarray(tile2(ks))
    return out


def l1f_consts(nc, ctx, T):
    DIM_HEAD = 64; WINDOW = 256
    half = np.arange(0, DIM_HEAD, 2, dtype=np.float64)
    inv = 1.0 / 10000.0 ** (half / DIM_HEAD)
    t = np.arange(T, dtype=np.float64)
    freqs = t[:, None] * inv[None, :]
    base_scale = (half + 0.4 * DIM_HEAD) / (1.4 * DIM_HEAD)
    power = (t - T // 2) / (WINDOW // 2)
    scale = base_scale[None, :] ** power[:, None]
    cs = np.cos(freqs); sn = np.sin(freqs)
    jj = np.arange(128) % 32
    csq = (cs * scale * 8.0).T[jj]
    ssq = (sn * scale * 8.0).T[jj]
    cik = (cs / scale).T[jj]
    sik = (sn / scale).T[jj]
    NW = T // 256
    tp_ = np.stack([a.reshape(128, NW, 256) for a in (csq, ssq, cik, sik)], axis=2)
    ctx["tabpack_dram"] = nc.inline_tensor(
        np.ascontiguousarray(tp_.reshape(128, NW * 1024)).astype(BF), "tabpack").ap()
    m = np.zeros((2, 128, 512), np.float32)
    for pp in range(2):
        qi = np.arange(128)[:, None]
        jx = np.arange(512)[None, :]
        rel = 256 + 128 * pp + qi - jx
        m[pp] = np.where((rel >= 0) & (rel <= 256), 0.0, NEG)
    ctx["lmask_dram"] = nc.inline_tensor(
        np.ascontiguousarray(m.transpose(1, 0, 2).reshape(128, 1024)).astype(BF), "lmask").ap()
    ctx["lmask"] = nc.alloc_sbuf_tensor("lmasks", [128, 1024], BF16).ap()
    I = np.zeros((128, 4, 16), np.float32)
    for cc in range(4):
        for rr_ in range(128):
            I[rr_, cc, 4 * cc + rr_ // 32] = 1.0
    ctx["i16_dram"] = nc.inline_tensor(np.ascontiguousarray(I.reshape(128, 64)), "i16").ap()
    ctx["i16"] = nc.alloc_sbuf_tensor("i16s", [128, 64], F32).ap()
    E = np.zeros((16, 4, 128), np.float32)
    for cc in range(4):
        for p_ in range(128):
            E[4 * cc + p_ // 32, cc, p_] = 1.0
    ctx["e16c_dram"] = nc.inline_tensor(np.ascontiguousarray(E.reshape(16, 512)), "e16c").ap()
    ctx["e16c"] = nc.alloc_sbuf_tensor("e16cs", [16, 512], F32).ap()


def load_l1f_consts(p, ctx):
    p.dma("sync", ctx["lmask"][:, :], ctx["lmask_dram"][:, :], "lmask", writes=["lmask"])
    p.dma("sync", ctx["i16"][:, :], ctx["i16_dram"][:, :], "i16", writes=["i16"])
    p.dma("sync", ctx["e16c"][:, :], ctx["e16c_dram"][:, :], "e16c", writes=["e16c"])


def declare_l1f(nc, T):
    d = {}
    def dram(name, shape):
        d[name] = nc.dram_tensor(name, list(shape), F32, kind="ExternalInput").ap()
    dram("l1_wqk", (6, 128, 16384)); dram("l1_wv", (6, 1024, 1024))
    dram("l1_wo", (6, 128, 8192))
    dram("l1_w1", (6, 2 * NCH_H, 1024, 128)); dram("l1_b1", (6, 128, 2 * NCH_H))
    dram("l1_w2", (6, 8, DIM_PAD, 128)); dram("l1_b2", (6, 128, 8))
    dram("l1_lng", (6, 128, 8)); dram("l1_lnb", (6, 128, 8))
    dram("l1_qs", (6, 128, 2)); dram("l1_ks", (6, 128, 2))
    return d

def alloc_l1f_bufs(nc, ctx, T, xs_at=None):
    b = {}
    b["qkw"] = alloc_at(nc, "f_qkw", [128, 16 * 1024], BF16, RB0)
    b["vw1"] = alloc_at(nc, "f_vw1", [128, 8 * 1024], BF16, RB0 + 32 * 1024)
    b["wo1"] = alloc_at(nc, "f_wo1", [128, 8 * 1024], BF16, RB0 + 48 * 1024)
    b["kring"] = alloc_at(nc, "f_kring", [128, 2 * 2048], BF16, RB0 + 64 * 1024)
    b["vring"] = alloc_at(nc, "f_vring", [128, 2 * 2048], BF16, RB0 + 72 * 1024)
    b["ctxw"] = alloc_at(nc, "f_ctxw", [128, 2048], BF16, RB0 + 80 * 1024)
    b["wtab"] = alloc_at(nc, "f_wtab", [128, 2048], BF16, RB0 + 84 * 1024)
    b["zn2"] = alloc_at(nc, "f_zn2", [128, 8 * 2048], BF16, RB0)
    b["stat"] = alloc_at(nc, "f_stat", [1, 2 * 2048], F32, RB0 + 32 * 1024)
    b["hR"] = alloc_at(nc, "f_hR", [128, 6 * 2048], BF16, RB0 + 48 * 1024)
    b["w1s"] = [alloc_at(nc, f"f_w1s{i}", [128, 1024], BF16, RB0 + 72 * 1024 + i * 2048)
                for i in range(2)]
    b["w2s"] = alloc_at(nc, "f_w2s", [128, NCH_H * 128], BF16, RB0 + 76 * 1024)
    b["hZ"] = alloc_at(nc, "f_hZ", [128, 16 * 2048], BF16, RZ0)
    if xs_at is None:
        b["zsw"] = nc.alloc_sbuf_tensor("f_zsw", [128, 2048], BF16).ap()
        b["qw"] = nc.alloc_sbuf_tensor("f_qw", [128, 2048], BF16).ap()
    else:
        b["zsw"] = alloc_at(nc, "f_zsw", [128, 2048], BF16, xs_at)
        b["qw"] = alloc_at(nc, "f_qw", [128, 2048], BF16, xs_at + 4096)
    b["ssw"] = nc.alloc_sbuf_tensor("f_ssw", [16, 256], F32).ap()
    b["rstdw"] = nc.alloc_sbuf_tensor("f_rstdw", [16, 256], F32).ap()
    b["rqt"] = nc.alloc_sbuf_tensor("f_rqt", [128, 32], F32).ap()
    b["tabst"] = nc.alloc_sbuf_tensor("f_tabst", [128, 1024], BF16).ap()
    b["qs"] = nc.alloc_sbuf_tensor("f_qs", [128, 4], F32).ap()
    b["b1"] = nc.alloc_sbuf_tensor("f_b1", [128, 2 * NCH_H], F32).ap()
    b["b2g"] = nc.alloc_sbuf_tensor("f_b2g", [128, 24], F32).ap()
    return b


def emit_l1f_layer(p, ctx, d, b, l, zbuf, zdram, T):
    NS = T // 512
    NW = T // 256
    S_ps = ctx["S_ps"]; P2 = ctx["P2_ps"]; TR = ctx["TR_ps"]
    scrA, scrB = ctx["scrA"], ctx["scrB"]
    st2 = ctx["stats2"]
    Pb = ctx["Pbuf"]; Pb2 = ctx["Pbuf2"]; ptsb = ctx["ptsb"]
    SCRA = [f"scrA{i}" for i in range(NS)]
    G = T + 1

    def zap(c, t0, n, shift=0):
        o = c * G + 1 + t0 - shift
        return zbuf[:, o:o + n]

    for fc_ in range(16):
        p.dma("gpsimd", b["qkw"][:, 1024 * fc_:1024 * fc_ + 1024],
              d["l1_wqk"][l][:, 1024 * fc_:1024 * fc_ + 1024],
              f"f_qkw{fc_}", writes=[f"f_qkw.{fc_}"])
    for dk_ in range(8):
        p.dma("gpsimd", b["vw1"][:, 1024 * dk_:1024 * dk_ + 1024],
              d["l1_wv"][l][128 * dk_:128 * dk_ + 128, :],
              f"f_vw1{dk_}", writes=[f"f_vw1.{dk_}"])
    for e_ in range(8):
        p.dma("gpsimd", b["wo1"][:, 1024 * e_:1024 * e_ + 1024],
              d["l1_wo"][l][:, 1024 * e_:1024 * e_ + 1024],
              f"f_wo1{e_}", writes=[f"f_wo1.{e_}"])
    p.dma("sync", b["qs"][:, 0:2], d["l1_qs"][l], "f_qs", writes=["f_qs"])
    p.dma("sync", b["qs"][:, 2:4], d["l1_ks"][l], "f_qs", writes=["f_qs"])

    for wi in range(NW):
        t0 = 256 * wi
        ks_ = wi % 2
        kp_ = (wi + 1) % 2
        j0 = 256 if wi == 0 else 0
        for dk in range(8):
            sh = 1 if dk >= 4 else 0
            p.op("vector", lambda e, dk=dk, sh=sh, t0=t0: e.tensor_copy(
                b["zsw"][:, 256 * dk:256 * dk + 256], zap(dk, t0, 256, sh)),
                reads=[f"z{dk}"], writes=["f_zsw"], tag=f"f{l}.zs{wi}.{dk}")
        p.dma("sync", b["tabst"][:, :], ctx["tabpack_dram"][:, 1024 * wi:1024 * wi + 1024],
              "f_tabst", writes=["f_tabst"])
        for ki in range(4):
            qcol = 0 if ki < 2 else 2
            for half_ in (0, 1):
                dst = b["wtab"][:, 256 * (2 * ki + half_):256 * (2 * ki + half_) + 256]
                p.op("vector", lambda e, ki=ki, dst=dst, qcol=qcol, half_=half_:
                     e.tensor_scalar_mul(dst, b["tabst"][:, 256 * ki:256 * ki + 256],
                                         b["qs"][:, qcol + half_:qcol + half_ + 1]),
                     reads=["f_tabst", "f_qs"], writes=["f_wtab"], tag=f"f{l}.tab{wi}.{ki}.{half_}")
        def wtap(idx):
            return b["wtab"][:, 256 * idx:256 * idx + 256]
        CSQA, CSQB = wtap(0), wtap(1)
        SSQA, SSQB = wtap(2), wtap(3)
        CIKA, CIKB = wtap(4), wtap(5)
        SIKA, SIKB = wtap(6), wtap(7)
        def qk_chunks(base_fc, dst_scr, dres):
            for fc in range(8):
                bank = fc % 2
                pa = S_ps[:, 512 * bank:512 * bank + 256]
                for dk in range(8):
                    p.op("tensor", lambda e, fc=fc, dk=dk, pa=pa, base_fc=base_fc: e.matmul(
                        pa, b["qkw"][:, 1024 * (base_fc + fc) + 128 * dk:
                                     1024 * (base_fc + fc) + 128 * dk + 128],
                        b["zsw"][:, 256 * dk:256 * dk + 256],
                        start=(dk == 0), stop=(dk == 7)),
                        reads=[f"f_qkw.{base_fc + fc}", "f_zsw"], writes=[f"S{bank}"],
                        tag=f"f{l}.qk{wi}.{base_fc + fc}")
                p.op("vector", lambda e, fc=fc, pa=pa: e.tensor_copy(
                    dst_scr[:, 256 * fc:256 * fc + 256], pa),
                    reads=[f"S{bank}"], writes=[dres], tag=f"f{l}.qke{wi}.{base_fc + fc}")
        qk_chunks(0, scrA, "scrA0")
        p.op("scalar", lambda e: e.activation(scrB[:, 0:2048], scrA[:, 0:2048], AF.Square),
             reads=["scrA0"], writes=["scrB"], tag=f"f{l}.qsq{wi}")
        for fc in range(8):
            p.op("tensor", lambda e, fc=fc: e.matmul(
                P2[0:16, 0:256], ctx["i16"][:, 16 * (fc % 4):16 * (fc % 4) + 16],
                scrB[:, 256 * fc:256 * fc + 256], start=(fc == 0), stop=(fc == 7)),
                reads=["scrB", "i16"], writes=["P0"], tag=f"f{l}.qss{wi}.{fc}")
        p.op("vector", lambda e: e.tensor_copy(b["ssw"][:, 0:256], P2[0:16, 0:256]),
             reads=["P0"], writes=["f_ssw"], tag=f"f{l}.qsse{wi}")
        p.op("scalar", lambda e: e.activation(b["rstdw"][:, :], b["ssw"][:, 0:256], AF.Sqrt),
             reads=["f_ssw"], writes=["f_rstdw"], tag=f"f{l}.qsrt{wi}")
        p.op("vector", lambda e: e.tensor_scalar_max(b["rstdw"][:, :], b["rstdw"][:, :], 1e-12),
             reads=["f_rstdw"], writes=["f_rstdw"], tag=f"f{l}.qmx{wi}")
        p.op("vector", lambda e: e.reciprocal(b["rstdw"][:, :], b["rstdw"][:, :]),
             reads=["f_rstdw"], writes=["f_rstdw"], tag=f"f{l}.qrc{wi}")
        for pp in range(2):
            p.op("tensor", lambda e, pp=pp: e.transpose(
                TR[:, 1024 * pp:1024 * pp + 16].bitcast(F32) if False else
                S_ps[0:128, 768 + 16 * pp: 784 + 16 * pp],
                b["rstdw"][:, 128 * pp:128 * pp + 128], ctx["identf"][0:16, 0:16]),
                reads=["f_rstdw", "identf"], writes=["S1"], tag=f"f{l}.qrt{wi}.{pp}")
            p.op("vector", lambda e, pp=pp: e.tensor_copy(
                b["rqt"][:, 16 * pp:16 * pp + 16], S_ps[0:128, 768 + 16 * pp:784 + 16 * pp]),
                reads=["S1"], writes=["f_rqt"], tag=f"f{l}.qrte{wi}.{pp}")
        for c in range(4):
            qA = scrA[:, 256 * c:256 * c + 256]
            qB = scrA[:, 256 * (4 + c):256 * (4 + c) + 256]
            m1 = scrB[:, 0:256]; m2 = scrB[:, 256:512]
            m3 = scrB[:, 512:768]; m4 = scrB[:, 768:1024]
            p.op("vector", lambda e, qA=qA, m1=m1, CSQA=CSQA: e.tensor_mul(m1, qA, CSQA),
                 reads=["scrA0", "f_wtab"], writes=["scrB"], tag=f"f{l}.qr1{wi}.{c}")
            p.op("vector", lambda e, qB=qB, m2=m2, SSQB=SSQB: e.tensor_mul(m2, qB, SSQB),
                 reads=["scrA0", "f_wtab"], writes=["scrB"], tag=f"f{l}.qr2{wi}.{c}")
            p.op("vector", lambda e, c=c, m1=m1, m2=m2: e.tensor_sub(
                b["qw"][:, 256 * c:256 * c + 256], m1, m2),
                reads=["scrB"], writes=["f_qw"], tag=f"f{l}.qr3{wi}.{c}")
            p.op("vector", lambda e, qB=qB, m3=m3, CSQB=CSQB: e.tensor_mul(m3, qB, CSQB),
                 reads=["scrA0", "f_wtab"], writes=["scrB"], tag=f"f{l}.qr4{wi}.{c}")
            p.op("vector", lambda e, qA=qA, m4=m4, SSQA=SSQA: e.tensor_mul(m4, qA, SSQA),
                 reads=["scrA0", "f_wtab"], writes=["scrB"], tag=f"f{l}.qr5{wi}.{c}")
            p.op("vector", lambda e, c=c, m3=m3, m4=m4: e.tensor_add(
                b["qw"][:, 256 * (4 + c):256 * (4 + c) + 256], m3, m4),
                reads=["scrB"], writes=["f_qw"], tag=f"f{l}.qr6{wi}.{c}")
        qk_chunks(8, scrA, "scrA0")
        p.op("scalar", lambda e: e.activation(scrB[:, 0:2048], scrA[:, 0:2048], AF.Square),
             reads=["scrA0"], writes=["scrB"], tag=f"f{l}.ksq{wi}")
        for fc in range(8):
            p.op("tensor", lambda e, fc=fc: e.matmul(
                P2[0:16, 0:256], ctx["i16"][:, 16 * (fc % 4):16 * (fc % 4) + 16],
                scrB[:, 256 * fc:256 * fc + 256], start=(fc == 0), stop=(fc == 7)),
                reads=["scrB", "i16"], writes=["P0"], tag=f"f{l}.kss{wi}.{fc}")
        p.op("vector", lambda e: e.tensor_copy(b["ssw"][:, 0:256], P2[0:16, 0:256]),
             reads=["P0"], writes=["f_ssw"], tag=f"f{l}.ksse{wi}")
        p.op("scalar", lambda e: e.activation(b["rstdw"][:, :], b["ssw"][:, 0:256], AF.Sqrt),
             reads=["f_ssw"], writes=["f_rstdw"], tag=f"f{l}.ksrt{wi}")
        p.op("vector", lambda e: e.tensor_scalar_max(b["rstdw"][:, :], b["rstdw"][:, :], 1e-12),
             reads=["f_rstdw"], writes=["f_rstdw"], tag=f"f{l}.kmx{wi}")
        p.op("vector", lambda e: e.reciprocal(b["rstdw"][:, :], b["rstdw"][:, :]),
             reads=["f_rstdw"], writes=["f_rstdw"], tag=f"f{l}.krc{wi}")
        for c in range(4):
            p.op("tensor", lambda e, c=c: e.matmul(
                P2[:, 0:256], ctx["e16c"][:, 128 * c:128 * c + 128], b["rstdw"][:, :],
                start=True, stop=True),
                reads=["f_rstdw", "e16c"], writes=["P0"], tag=f"f{l}.kbc{wi}.{c}")
            kA = scrA[:, 256 * c:256 * c + 256]
            kB = scrA[:, 256 * (4 + c):256 * (4 + c) + 256]
            m1 = scrB[:, 0:256]; m2 = scrB[:, 256:512]
            m3 = scrB[:, 512:768]; m4 = scrB[:, 768:1024]
            d1 = scrB[:, 1024:1280]; d2 = scrB[:, 1280:1536]
            ksl = b["kring"][:, 2048 * ks_:2048 * ks_ + 2048]
            p.op("vector", lambda e, kA=kA, m1=m1, CIKA=CIKA: e.tensor_mul(m1, kA, CIKA),
                 reads=["scrA0", "f_wtab"], writes=["scrB"], tag=f"f{l}.kr1{wi}.{c}")
            p.op("vector", lambda e, kB=kB, m2=m2, SIKB=SIKB: e.tensor_mul(m2, kB, SIKB),
                 reads=["scrA0", "f_wtab"], writes=["scrB"], tag=f"f{l}.kr2{wi}.{c}")
            p.op("vector", lambda e, d1=d1, m1=m1, m2=m2: e.tensor_sub(d1, m1, m2),
                 reads=["scrB"], writes=["scrB"], tag=f"f{l}.kr3{wi}.{c}")
            p.op("vector", lambda e, c=c, d1=d1, ksl=ksl: e.tensor_mul(
                ksl[:, 256 * c:256 * c + 256], d1, P2[:, 0:256]),
                reads=["scrB", "P0"], writes=[f"f_kr{ks_}"], tag=f"f{l}.kr3b{wi}.{c}")
            p.op("vector", lambda e, kB=kB, m3=m3, CIKB=CIKB: e.tensor_mul(m3, kB, CIKB),
                 reads=["scrA0", "f_wtab"], writes=["scrB"], tag=f"f{l}.kr4{wi}.{c}")
            p.op("vector", lambda e, kA=kA, m4=m4, SIKA=SIKA: e.tensor_mul(m4, kA, SIKA),
                 reads=["scrA0", "f_wtab"], writes=["scrB"], tag=f"f{l}.kr5{wi}.{c}")
            p.op("vector", lambda e, d2=d2, m3=m3, m4=m4: e.tensor_add(d2, m3, m4),
                 reads=["scrB"], writes=["scrB"], tag=f"f{l}.kr6{wi}.{c}")
            p.op("vector", lambda e, c=c, d2=d2, ksl=ksl: e.tensor_mul(
                ksl[:, 256 * (4 + c):256 * (4 + c) + 256], d2, P2[:, 0:256]),
                reads=["scrB", "P0"], writes=[f"f_kr{ks_}"], tag=f"f{l}.kr6b{wi}.{c}")
        for ti in range(2):
            for half2 in range(2):
                pa = P2[:, 512 * half2:512 * half2 + 512]
                for dk in range(8):
                    p.op("tensor", lambda e, ti=ti, dk=dk, half2=half2, pa=pa: e.matmul(
                        pa, b["zsw"][:, 256 * dk + 128 * ti:256 * dk + 128 * ti + 128],
                        b["vw1"][:, 1024 * dk + 512 * half2:1024 * dk + 512 * half2 + 512],
                        start=(dk == 0), stop=(dk == 7)),
                        reads=["f_zsw", f"f_vw1.{dk}"], writes=[f"P{half2}"],
                        tag=f"f{l}.v{wi}.{ti}.{half2}")
                p.op("vector", lambda e, ti=ti, half2=half2, pa=pa, ks_=ks_: e.tensor_copy(
                    b["vring"][:, 2048 * ks_ + 1024 * ti + 512 * half2:
                               2048 * ks_ + 1024 * ti + 512 * half2 + 512], pa),
                    reads=[f"P{half2}"], writes=[f"f_vr{ks_}"],
                    tag=f"f{l}.ve{wi}.{ti}.{half2}")
        def f_stage1(h, pp):
            bb = 32 * (h % 4)
            cc = h // 4
            tp = (bb, 0)
            sb = (2 * h + pp) % 4
            g = (2 * h + pp) % 4
            SS = S_ps[:, 512 * sb:512 * sb + 512]
            negm = st2[:, 4 * g:4 * g + 1]
            srcs = []
            if wi > 0:
                srcs.append((b["kring"], kp_, 0))
            srcs.append((b["kring"], ks_, 256))
            for (kr, slot, joff) in srcs:
                kAp = kr[bb:bb + 32, 2048 * slot + 256 * cc: 2048 * slot + 256 * cc + 256]
                kBp = kr[bb:bb + 32, 2048 * slot + 256 * (4 + cc): 2048 * slot + 256 * (4 + cc) + 256]
                qAp = b["qw"][bb:bb + 32, 256 * cc + 128 * pp: 256 * cc + 128 * pp + 128]
                qBp = b["qw"][bb:bb + 32, 256 * (4 + cc) + 128 * pp: 256 * (4 + cc) + 128 * pp + 128]
                p.op("tensor", lambda e, SS=SS, qAp=qAp, kAp=kAp, joff=joff, tp=tp: e.matmul(
                    SS[:, joff:joff + 256], qAp, kAp, start=True, stop=False,
                    tile_position=tp),
                    reads=["f_qw", f"f_kr{ks_}" if joff else f"f_kr{kp_}"],
                    writes=[f"S{sb}"], tag=f"f{l}.sA{wi}.{h}.{pp}")
                p.op("tensor", lambda e, SS=SS, qBp=qBp, kBp=kBp, joff=joff, tp=tp: e.matmul(
                    SS[:, joff:joff + 256], qBp, kBp, start=False, stop=True,
                    tile_position=tp),
                    reads=["f_qw", f"f_kr{ks_}" if joff else f"f_kr{kp_}"],
                    writes=[f"S{sb}"], tag=f"f{l}.sB{wi}.{h}.{pp}")
            p.op("vector", lambda e, SS=SS, h=h, pp=pp, j0=j0: e.scalar_tensor_tensor(
                SS[:, j0:512], SS[:, j0:512], b["rqt"][:, 16 * pp + h:16 * pp + h + 1],
                ctx["lmask"][:, 512 * pp + j0:512 * pp + 512],
                op0=ALU.mult, op1=ALU.add),
                reads=[f"S{sb}", "f_rqt", "lmask"], writes=[f"S{sb}"],
                tag=f"f{l}.msk{wi}.{h}.{pp}")
            p.op("vector", lambda e, SS=SS, j0=j0, negm=negm: e.reduce_max(
                negm, SS[:, j0:512], axis=AX.X, negate=True),
                reads=[f"S{sb}"], writes=[f"negm{g}"], tag=f"f{l}.mx{wi}.{h}.{pp}")

        def f_stage2(h, pp):
            sb = (2 * h + pp) % 4
            g = (2 * h + pp) % 4
            g2 = (2 * h + pp) % 2
            SS = S_ps[:, 512 * sb:512 * sb + 512]
            negm = st2[:, 4 * g:4 * g + 1]
            sums = st2[:, 4 * g + 1:4 * g + 2]
            rsum = st2[:, 4 * g + 2:4 * g + 3]
            Pba = Pb[:, 1024 * g2:1024 * g2 + 512]
            Pba2 = Pb2[:, 1024 * g2:1024 * g2 + 512]
            pvb = P2[64 * (h % 2):64 * (h % 2) + 64,
                     (256 if g2 == 0 else 768):(256 if g2 == 0 else 768) + 128]
            pvres = "P0" if g2 == 0 else "P1"
            p.op("scalar", lambda e, SS=SS, j0=j0, negm=negm, sums=sums, Pba=Pba:
                 e.activation(Pba[:, j0:512], SS[:, j0:512], AF.Exp, bias=negm,
                              accum_out=sums),
                 reads=[f"S{sb}", f"negm{g}"], writes=[f"P.{g2}", f"sums{g}"],
                 tag=f"f{l}.ex{wi}.{h}.{pp}")
            p.op("vector", lambda e, sums=sums, rsum=rsum: e.reciprocal(rsum, sums),
                 reads=[f"sums{g}"], writes=[f"rsum{g}"], tag=f"f{l}.rc{wi}.{h}.{pp}")
            p.op("vector", lambda e, j0=j0, Pba=Pba, Pba2=Pba2, rsum=rsum:
                 e.tensor_scalar_mul(Pba2[:, j0:512], Pba[:, j0:512], rsum),
                 reads=[f"P.{g2}", f"rsum{g}"], writes=[f"P2b.{g2}"],
                 tag=f"f{l}.pm{wi}.{h}.{pp}")
            njt0 = j0 // 128
            for jt in range(njt0, 4):
                sl2 = jt % 2
                trof = 1024 * sl2
                p.op("tensor", lambda e, jt=jt, trof=trof, Pba2=Pba2: e.transpose(
                    TR[:, trof:trof + 128], Pba2[:, 128 * jt:128 * jt + 128],
                    ctx["ident"]),
                    reads=[f"P2b.{g2}", "cident"], writes=[f"TR{sl2}"],
                    tag=f"f{l}.tr{wi}.{h}.{pp}.{jt}")
                p.op("vector" if jt % 2 == 0 else "scalar",
                     lambda e, jt=jt, trof=trof:
                     (e.tensor_copy if hasattr(e, "tensor_copy") else e.copy)(
                    ptsb[:, 128 * jt:128 * jt + 128], TR[:, trof:trof + 128]),
                    reads=[f"TR{sl2}"], writes=[f"pt{jt}"],
                    tag=f"f{l}.tre{wi}.{h}.{pp}.{jt}")
            for jt in range(njt0, 4):
                slot = kp_ if jt < 2 else ks_
                ti = jt % 2
                p.op("tensor", lambda e, jt=jt, slot=slot, ti=ti, h=h, njt0=njt0,
                     pvb=pvb: e.matmul(
                    pvb,
                    b["vring"][:, 2048 * slot + 1024 * ti + 64 * h:
                               2048 * slot + 1024 * ti + 64 * h + 64],
                    ptsb[:, 128 * jt:128 * jt + 128],
                    start=(jt == njt0), stop=(jt == 3),
                    tile_position=(0, 64 * (h % 2))),
                    reads=[f"f_vr{slot}", f"pt{jt}"], writes=[pvres],
                    tag=f"f{l}.pv{wi}.{h}.{pp}.{jt}")
            p.op("vector", lambda e, h=h, pp=pp, pvb=pvb: e.tensor_copy(
                b["ctxw"][64 * (h % 2):64 * (h % 2) + 64,
                          256 * (h // 2) + 128 * pp:256 * (h // 2) + 128 * pp + 128],
                pvb),
                reads=[pvres], writes=[f"f_cw{h // 2}"],
                tag=f"f{l}.ce{wi}.{h}.{pp}")

        tiles = [(h, pp) for h in range(16) for pp in range(2)]
        prev = None
        for t_ in tiles:
            f_stage1(*t_)
            if prev is not None:
                f_stage2(*prev)
            prev = t_
        f_stage2(*prev)
        for e_ in range(8):
            sb = e_ % 2
            pa = S_ps[:, 512 * sb:512 * sb + 256]
            for ic in range(8):
                p.op("tensor", lambda e, e_=e_, ic=ic, pa=pa: e.matmul(
                    pa, b["wo1"][:, 1024 * e_ + 128 * ic:1024 * e_ + 128 * ic + 128],
                    b["ctxw"][:, 256 * ic:256 * ic + 256],
                    start=(ic == 0), stop=(ic == 7)),
                    reads=[f"f_wo1.{e_}", f"f_cw{ic}"], writes=[f"S{sb}"],
                    tag=f"f{l}.wo{wi}.{e_}")
            p.op("vector", lambda e, e_=e_, t0=t0, pa=pa: e.tensor_add(
                zap(e_, t0, 256), zap(e_, t0, 256), pa),
                reads=[f"S{sb}", f"z{e_}"], writes=[f"z{e_}"],
                tag=f"f{l}.za{wi}.{e_}")

    p.fence([], ['f_qkw.0', 'f_qkw.1', 'f_qkw.2', 'f_qkw.3', 'f_qkw.4', 'f_qkw.5', 'f_qkw.6', 'f_qkw.7', 'f_qkw.8', 'f_qkw.9', 'f_qkw.10', 'f_qkw.11', 'f_qkw.12', 'f_qkw.13', 'f_qkw.14', 'f_qkw.15', 'f_vw1.0', 'f_vw1.1', 'f_vw1.2', 'f_vw1.3', 'f_vw1.4', 'f_vw1.5', 'f_vw1.6', 'f_vw1.7', 'f_wo1.0', 'f_wo1.1', 'f_wo1.2', 'f_wo1.3', 'f_wo1.4', 'f_wo1.5', 'f_wo1.6', 'f_wo1.7'] + ["stat.mu", "stat.rs",
                 "f_kr0", "f_kr1", "f_vr0", "f_vr1", "f_wtab",
                 "f_w1s0", "f_w1s1", "f_w2s", "f_zsw", "f_qw",
                 "xs0", "xs1", "xs2", "xs3"]
            + [f"f_cw{i}" for i in range(8)]
            + [f"f_zn{c}" for c in range(8)]
            + [f"h{k}" for k in range(NCH_H)])
    stat = b["stat"]
    mu = stat[0:1, 0:T]
    rs = stat[0:1, T:2 * T]
    ones_f = ctx["ones_f32"]
    for c in range(8):
        sh = 1 if c >= 4 else 0
        p.op("scalar", lambda e, c=c, sh=sh: e.activation(
            scrA[:, 0:T], zap(c, 0, T, sh), AF.Square),
            reads=[f"z{c}"], writes=SCRA, tag=f"f{l}.lsq{c}")
        for s in range(NS):
            sl = slice(512 * s, 512 * s + 512)
            p.op("tensor", lambda e, c=c, s=s, sh=sh, sl=sl: e.matmul(
                S_ps[0:1, sl], ones_f[:, 0:1], zap(c, 512 * s, 512, sh),
                start=(c == 0), stop=(c == 7)),
                reads=[f"z{c}", "cones"], writes=[f"S{s}"], tag=f"f{l}.lsum{c}.{s}")
            p.op("tensor", lambda e, sl=sl, c=c: e.matmul(
                S_ps[32:33, sl], ones_f[:, 0:1], scrA[:, sl], start=(c == 0), stop=(c == 7)),
                reads=[f"scrA{s}"] + (["cones"] if True else []), writes=[f"S{s}"],
                tag=f"f{l}.lsqs{c}.{s}")
    Sres = [f"S{s}" for s in range(NS)]
    p.op("vector", lambda e: e.tensor_scalar_mul(mu, S_ps[0:1, 0:T], 1.0 / 1024.0),
         reads=Sres, writes=["stat.mu"], tag=f"f{l}.lmu")
    p.op("vector", lambda e: e.tensor_mul(scrA[0:1, 0:T], mu, mu),
         reads=["stat.mu"] + SCRA, writes=SCRA, tag=f"f{l}.lm2")
    p.op("vector", lambda e: e.scalar_tensor_tensor(
        scrA[0:1, 0:T], S_ps[32:33, 0:T], 1.0 / 1024.0, scrA[0:1, 0:T],
        op0=ALU.mult, op1=ALU.subtract),
        reads=Sres + SCRA, writes=SCRA, tag=f"f{l}.lvar")
    p.op("scalar", lambda e: e.activation(rs, scrA[0:1, 0:T], AF.Sqrt, bias=ctx["cv"](1e-5, 1)),
         reads=SCRA + ["cv"], writes=["stat.rs"], tag=f"f{l}.lsqrt")
    p.op("vector", lambda e: e.reciprocal(rs, rs),
         reads=["stat.rs"], writes=["stat.rs"], tag=f"f{l}.lrecip")
    p.dma("sync", b["b2g"][:, 0:8], d["l1_b2"][l], "f_b2g", writes=["f_b2g"])
    p.dma("sync", b["b2g"][:, 8:16], d["l1_lng"][l], "f_b2g", writes=["f_b2g"])
    p.dma("sync", b["b2g"][:, 16:24], d["l1_lnb"][l], "f_b2g", writes=["f_b2g"])
    p.dma("sync", b["b1"][:, :], d["l1_b1"][l], "f_b1", writes=["f_b1"])
    for c in range(8):
        sh = 1 if c >= 4 else 0
        for s in range(NS):
            sl = slice(512 * s, 512 * s + 512)
            p.op("tensor", lambda e, sl=sl: e.matmul(
                P2[:, 0:512], ones_f[0:1, 0:128], mu[:, sl], start=True, stop=True),
                reads=["stat.mu", "cones"], writes=["P0"], tag=f"f{l}.lbmu{c}.{s}")
            p.op("tensor", lambda e, sl=sl: e.matmul(
                P2[:, 512:1024], ones_f[0:1, 0:128], rs[:, sl], start=True, stop=True),
                reads=["stat.rs", "cones"], writes=["P1"], tag=f"f{l}.lbrs{c}.{s}")
            p.op("vector", lambda e, c=c, s=s, sh=sh: e.scalar_tensor_tensor(
                scrA[:, 512 * s:512 * s + 512], zap(c, 512 * s, 512, sh), 1.0,
                P2[:, 0:512], op0=ALU.mult, op1=ALU.subtract),
                reads=[f"z{c}", "P0"], writes=[f"scrA{s}"], tag=f"f{l}.lap1{c}.{s}")
            p.op("vector", lambda e, c=c, s=s: e.tensor_mul(
                scrB[:, 512 * s:512 * s + 512], scrA[:, 512 * s:512 * s + 512],
                P2[:, 512:1024]),
                reads=[f"scrA{s}", "P1"], writes=["scrB"], tag=f"f{l}.lap2{c}.{s}")
            p.op("vector", lambda e, c=c, s=s: e.tensor_scalar(
                b["zn2"][:, T * c + 512 * s:T * c + 512 * s + 512],
                scrB[:, 512 * s:512 * s + 512],
                b["b2g"][:, 8 + c:9 + c], b["b2g"][:, 16 + c:17 + c],
                op0=ALU.mult, op1=ALU.add),
                reads=["scrB", "f_b2g"], writes=[f"f_zn{c}"], tag=f"f{l}.lap3{c}.{s}")
    for c in range(8):
        p.dma("sync", zdram[128 * c:128 * c + 128, :], zap(c, 0, T), f"zsp{c}",
              reads=[f"z{c}"], writes=[f"zd{c}"])
    p.fence([], [f"z{c}" for c in range(8)] + [f"h{k}" for k in range(NCH_H)])
    def hap(jc, sl):
        if jc < 16:
            return b["hZ"][:, T * jc + sl.start:T * jc + sl.stop]
        return b["hR"][:, T * (jc - 16) + sl.start:T * (jc - 16) + sl.stop]
    for jc in range(NCH_H):
        for ag in range(2):
            p.dma("gpsimd", b["w1s"][ag][:, :].rearrange("p (c f) -> p c f", f=128),
                  d["l1_w1"][l, 2 * jc + ag].rearrange("(c p) f -> p c f", p=128),
                  f"f_w1s{ag}", writes=[f"f_w1s{ag}"])
        for s in range(NS):
            sl = slice(512 * s, 512 * s + 512)
            for ag in range(2):
                pa = P2[:, 0:512] if ag == 0 else P2[:, 512:1024]
                bank = "P0" if ag == 0 else "P1"
                for dk in range(8):
                    p.op("tensor", lambda e, dk=dk, sl=sl, pa=pa, ag=ag: e.matmul(
                        pa, b["w1s"][ag][:, 128 * dk:128 * dk + 128],
                        b["zn2"][:, T * dk + sl.start:T * dk + sl.stop],
                        start=(dk == 0), stop=(dk == 7)),
                        reads=[f"f_w1s{ag}", f"f_zn{dk}"], writes=[bank],
                        tag=f"f{l}.w1.{jc}.{ag}.{s}")
            p.op("scalar", lambda e, jc=jc, s=s: e.activation(
                scrB[:, 512 * s:512 * s + 512], P2[:, 512:1024], AF.Gelu,
                bias=b["b1"][:, 2 * jc + 1:2 * jc + 2]),
                reads=["P1", "f_b1"], writes=["scrB"], tag=f"f{l}.gg{jc}.{s}")
            p.op("vector", lambda e, jc=jc, s=s, sl=sl: e.scalar_tensor_tensor(
                hap(jc, sl), P2[:, 0:512], b["b1"][:, 2 * jc:2 * jc + 1],
                scrB[:, 512 * s:512 * s + 512], op0=ALU.add, op1=ALU.mult),
                reads=["P0", "f_b1", "scrB"], writes=[f"h{jc}"],
                tag=f"f{l}.hm{jc}.{s}")
    for e_ in range(8):
        p.dma("gpsimd", b["w2s"][:, :].rearrange("p (c f) -> p c f", f=128),
              d["l1_w2"][l, e_].rearrange("(c p) f -> p c f", p=128),
          "f_w2s", writes=["f_w2s"])
        for s in range(NS):
            sl = slice(512 * s, 512 * s + 512)
            sb = s % 4
            for k in range(NCH_H):
                p.op("tensor", lambda e, k=k, s=s, sl=sl, sb=sb: e.matmul(
                    S_ps[:, 512 * sb:512 * sb + 512], b["w2s"][:, 128 * k:128 * k + 128],
                    hap(k, sl), start=(k == 0), stop=(k == NCH_H - 1)),
                    reads=["f_w2s", f"h{k}"], writes=[f"S{sb}"],
                    tag=f"f{l}.w2.{e_}.{s}")
            slot = (e_ * NS + s) % 4
            xa = ctx["xstage"][:, 512 * slot:512 * slot + 512]
            xr = f"xs{slot}"
            p.dma("sync", xa, zdram[128 * e_:128 * e_ + 128, sl], xr,
                  reads=[f"zd{e_}"], writes=[xr])
            p.op("vector", lambda e, xa=xa, sb=sb, e_=e_: e.scalar_tensor_tensor(
                xa, S_ps[:, 512 * sb:512 * sb + 512], b["b2g"][:, e_:e_ + 1], xa,
                op0=ALU.add, op1=ALU.add),
                reads=[f"S{sb}", "f_b2g", xr], writes=[xr], tag=f"f{l}.w2a{e_}.{s}")
            p.dma("sync", zdram[128 * e_:128 * e_ + 128, sl], xa, xr,
                  reads=[xr], writes=[f"zd{e_}"])

def emit_l1f(p, ctx, d, b, zbuf, zdram, T):
    G = T + 1
    for c in range(8):
        p.op("vector", lambda e, c=c: e.memset(zbuf[:, c * G:c * G + 1], 0.0),
             reads=[], writes=[f"z{c}"], tag=f"f.guard{c}")
    SCRA = [f"scrA{i}" for i in range(T // 512)]
    for l in range(L1F_DEPTH):
        emit_l1f_layer(p, ctx, d, b, l, zbuf, zdram, T)
        if l < L1F_DEPTH - 1:
            p.fence([], [f"h{k}" for k in range(NCH_H)] + [f"z{c}" for c in range(8)]
                    + SCRA + ["scrB", "f_wtab"] + ['f_qkw.0', 'f_qkw.1', 'f_qkw.2', 'f_qkw.3', 'f_qkw.4', 'f_qkw.5', 'f_qkw.6', 'f_qkw.7', 'f_qkw.8', 'f_qkw.9', 'f_qkw.10', 'f_qkw.11', 'f_qkw.12', 'f_qkw.13', 'f_qkw.14', 'f_qkw.15', 'f_vw1.0', 'f_vw1.1', 'f_vw1.2', 'f_vw1.3', 'f_vw1.4', 'f_vw1.5', 'f_vw1.6', 'f_vw1.7', 'f_wo1.0', 'f_wo1.1', 'f_wo1.2', 'f_wo1.3', 'f_wo1.4', 'f_wo1.5', 'f_wo1.6', 'f_wo1.7'] + [
                              "f_kr0", "f_kr1", "f_vr0", "f_vr1",
                              "f_w1s0", "f_w1s1", "f_w2s", "stat.mu", "stat.rs",
                              "f_zsw", "f_qw", "xs0", "xs1", "xs2", "xs3"]
                    + [f"f_cw{i}" for i in range(8)]
                    + [f"f_zn{c}" for c in range(8)])
            for c in range(8):
                p.op("vector", lambda e, c=c: e.memset(zbuf[:, c * G:c * G + 1], 0.0),
                     reads=[], writes=[f"z{c}"], tag=f"f.g{l}.{c}")
                p.dma("sync", zbuf[:, c * G + 1:c * G + 1 + T],
                      zdram[128 * c:128 * c + 128, :], f"zrl{c}",
                      reads=[f"zd{c}"], writes=[f"z{c}"])


def emit_final_combine(p, ctx, zdram, xsrc, xdst, T):
    NS = T // 512
    xst = ctx["xstage"]
    scrB = ctx["scrB"]
    p.fence([], ["scrB", "xs0", "xs1", "fc_z0", "fc_z1"])
    for c in range(8):
        for s in range(NS):
            sl = slice(512 * s, 512 * s + 512)
            slot = (c * NS + s) % 2
            xa = xst[:, 512 * slot:512 * slot + 512]
            xr = f"xs{slot}"
            za = scrB[:, 512 * (2 + slot):512 * (2 + slot) + 512]
            zr = f"fc_z{slot}"
            p.dma("sync", xa, xsrc[128 * c:128 * c + 128, sl], xr, writes=[xr],
                  reads=[f"xC{c}.{s}.0", f"xC{c}.{s}.64"])
            p.dma("sync", za, zdram[128 * c:128 * c + 128, sl], zr,
                  reads=[f"zd{c}"], writes=[zr])
            p.op("vector", lambda e, xa=xa, za=za: e.tensor_add(xa, xa, za),
                 reads=[xr, zr], writes=[xr], tag=f"fc.add{c}.{s}")
            p.dma("sync", xdst[128 * c:128 * c + 128, sl], xa, xr, reads=[xr])


T_FULL = 2048

MLA_RES = ["m_qc", "m_kc", "m_vf", "m_vtokh", "m_ctxh", "m_wdq", "m_wuq",
           "m_wuqT", "m_wdkv", "m_wukv", "m_wukvT", "m_mq", "m_mkv", "m_wqrT",
           "m_wkrT", "m_woT", "m_wn", "m_ss", "m_rstd", "oh64", "e16", "diagm"]
L1F_RES = ([f"f_qkw.{i}" for i in range(16)] + [f"f_vw1.{i}" for i in range(8)]
           + [f"f_wo1.{i}" for i in range(8)]
           + ["f_wtab", "f_kr0", "f_kr1", "f_vr0",
            "f_vr1", "f_w1s0", "f_w1s1", "f_w2s", "f_zsw", "f_qw", "f_ssw",
            "f_rstdw", "f_rqt", "f_tabst", "f_qs", "f_b1", "f_b2g"]
           + [f"f_cw{i}" for i in range(8)] + [f"f_zn{c}" for c in range(8)])

def _build_fused(T):
    nc = bass.Bass("TRN2", target_bir_lowering=False, debug=False)
    xin = nc.dram_tensor("xin", [1024, T], F32, kind="ExternalInput").ap()
    xout = nc.dram_tensor("xout", [1024, T], F32, kind="ExternalOutput").ap()
    xmA = nc.dram_tensor("xmA", [1024, T], F32, kind="Internal").ap()
    x1 = nc.dram_tensor("x1", [1024, T], F32, kind="Internal").ap()
    x2 = nc.dram_tensor("x2", [1024, T], F32, kind="Internal").ap()
    zdram = nc.dram_tensor("zdram", [1024, T], F32, kind="Internal").ap()
    d = declare_l0(nc, T)
    d.update(declare_mla(nc, T))
    d.update(declare_l1f(nc, T))
    RS0 = RZ0 - 8192
    ctx = make_ctx(nc, T, xstage_at=RS0)
    mla_consts(nc, ctx)
    l1f_consts(nc, ctx, T)
    stat0 = alloc_at(nc, "stat0", [1, 2 * T], F32, RB0)
    bufs = {}
    bufs["xsrc"] = xin; bufs["xdst"] = xmA
    bufs["xsrc2"] = xmA; bufs["xdst2"] = x1
    bufs["xnb"] = alloc_at(nc, "xnb", [128, 8 * T], BF16, RZ0)
    bufs["vw"] = alloc_at(nc, "vwb", [128, 8 * 256], BF16, RZ0 + 16 * T)
    bufs["ropec"] = alloc_at(nc, "ropec", [128, T], F32, RZ0 + 16 * T + 4096)
    bufs["ropes"] = alloc_at(nc, "ropes", [128, T], F32, RZ0 + 16 * T + 4096 + 4 * T)
    bufs["qrope"] = alloc_at(nc, "qrope", [128, 8 * T], BF16, RB0)
    bufs["krope"] = alloc_at(nc, "krope", [128, 2 * T], BF16, RB0 + 16 * T)
    bufs["vtok"] = alloc_at(nc, "vtok", [128, 2 * T], BF16, RB0 + 20 * T)
    bufs["scrC"] = alloc_at(nc, "scrC", [128, T], F32, RB0 + 20 * T)
    bufs["ctxT"] = alloc_at(nc, "ctxT", [128, 8 * T], BF16, RB0 + 24 * T)
    bufs["hbuf"] = alloc_at(nc, "hbuf", [128, 32 * 1024], BF16, RB0)
    bufs["pbalt"] = alloc_at(nc, "l0_pbalt", [128, T], BF16, RB0 + 80 * 1024)
    bufs["wslots"] = [alloc_at(nc, f"wsl{i}", [128, 1024], BF16, RB0 + 84 * 1024 + 2048 * i)
                      for i in range(2)]
    bufs["w2slot"] = alloc_at(nc, "w2slot", [128, 32 * 128], BF16, RB0 + 64 * 1024)
    bufs["b1sb"] = nc.alloc_sbuf_tensor("b1sb", [128, 32], F32).ap()
    bufs["b2sb"] = nc.alloc_sbuf_tensor("b2sb", [128, 8], F32).ap()
    zbuf = alloc_at(nc, "zbuf", [128, 8 * (T + 1)], F32, RZ0)
    mb = alloc_mla_bufs(nc, ctx, T, RB0)
    fb = alloc_l1f_bufs(nc, ctx, T, xs_at=RS0)

    p = Prog(nc)
    load_consts(p, ctx)
    load_l1f_consts(p, ctx)
    emit_ln_to_bf16(p, ctx, xin, stat0, bufs["xnb"], "a", "0")
    p.fence([], ["stat.mu", "stat.rs"] + [f"qr{i}" for i in range(8)])
    emit_l0_attn(p, ctx, d, bufs)
    p.fence([f"qr{i}" for i in range(8)] + ["kr0", "kr1", "vtok"] + [f"ct{i}" for i in range(8)],
            [f"h{k}" for k in range(32)] + ["w2s"])
    emit_l0_ffn(p, ctx, d, bufs)
    p.fence([], [f"a{c}" for c in range(8)] + ["vw", "ropec", "ropes",
                 "stat.mu", "stat.rs"] + [f"z{c}" for c in range(8)]
            + [f"h{k}" for k in range(32)])
    emit_ln_to_bf16(p, ctx, x1, stat0, None, "z", "z", fp32dst=(zbuf, T + 1, 1),
                    src_res=lambda c, s: [f"xB{c}.{s}"])
    p.fence([], MLA_RES + ["stat.mu", "stat.rs", "scrC", "P.0", "P.1", "P2b", "w2s"]
            + [f"qr{i}" for i in range(8)] + ["kr0", "kr1", "vtok"]
            + [f"ct{i}" for i in range(8)] + [f"h{k}" for k in range(32)])
    load_mla_consts(p, ctx)
    emit_mla_setup(p, ctx, d, mb)
    for h in range(16):
        c2 = h // 2
        half = 64 * (h % 2)
        zap = zbuf[half:half + 64, c2 * (T + 1) + 1: c2 * (T + 1) + 1 + T]
        emit_mla_head(p, ctx, mb, h, zap, f"z{c2}", x1, x2, first_for_chunk=True)
    p.fence([], L1F_RES + MLA_RES + ["stat.mu", "stat.rs", "ws0", "ws1", "w2s",
                                      "xs0", "xs1", "xs2", "xs3", "P.0", "P.1", "P2b"])
    emit_l1f(p, ctx, d, fb, zbuf, zdram, T)
    emit_final_combine(p, ctx, zdram, x2, xout, T)
    p.barrier_all_dma()
    p.emit()
    check_sbuf(nc, RS0)
    return nc


def kernel(**inputs):
    T = T_FULL
    x = np.asarray(inputs["L0_attn_x"])
    B = x.shape[0]
    dall = prep_l0(inputs, T)
    dall.update(prep_mla(inputs))
    dall.update(prep_l1f(inputs, T))
    nc = _build_fused(T)
    maps = [{**dall, "xin": np.ascontiguousarray(x[b].T)} for b in range(B)]
    res = run_bass_kernel_spmd(nc, maps, core_ids=list(range(B)))
    out = np.stack([np.asarray(res.results[b]["xout"]).T for b in range(B)])
    return out.astype(np.float32)


# version 07 (final)
# speedup vs baseline: 1.1752x; 1.0240x over previous
import numpy as np
import ml_dtypes
import concourse.bass as bass
import concourse.mybir as mybir
from concourse.bass_utils import run_bass_kernel_spmd

F32 = mybir.dt.float32
BF16 = mybir.dt.bfloat16
AF = mybir.ActivationFunctionType
ALU = mybir.AluOpType
AX = mybir.AxisListType
BF = ml_dtypes.bfloat16

NEG = -1.0e30


class _Op:
    __slots__ = ("eng", "fn", "reads", "writes", "dma_key", "idx", "deps",
                 "inc", "cnt", "tag")
    def __init__(self):
        self.deps = set()
        self.inc = False
        self.cnt = None
        self.dma_key = None
        self.tag = ""

class Prog:
    COMPUTE = ("tensor", "vector", "scalar", "gpsimd", "sync")

    def __init__(self, nc):
        self.nc = nc
        self.ops = []
        self.wmap = {}
        self.rmap = {}

    def op(self, eng, fn, reads=(), writes=(), dma_key=None, tag=""):
        idx = len(self.ops)
        o = _Op()
        o.eng = eng; o.fn = fn; o.reads = tuple(reads); o.writes = tuple(writes)
        o.dma_key = dma_key; o.idx = idx; o.tag = tag
        self.ops.append(o)
        dep = self._dep
        for r in o.reads:
            j = self.wmap.get(r)
            if j is not None:
                dep(o, j)
            self.rmap.setdefault(r, []).append(idx)
        for r in o.writes:
            j = self.wmap.get(r)
            if j is not None:
                dep(o, j)
            for k in self.rmap.get(r, ()):
                if k != idx:
                    dep(o, k)
            self.wmap[r] = idx
            self.rmap[r] = []
        return idx

    def _dep(self, o, j):
        if j == o.idx:
            return
        oj = self.ops[j]
        if o.eng == "tensor" and oj.eng == "tensor":
            return
        o.deps.add(j)

    def dma(self, eng, out, in_, key, reads=(), writes=(), **kw):
        return self.op(eng, lambda e: e.dma_start(out=out, in_=in_, **kw),
                       reads=reads, writes=writes, dma_key=key)

    def fence(self, reads, writes, eng="sync"):
        return self.op(eng, lambda e: e.nop(), reads=list(reads), writes=list(writes))

    def barrier_all_dma(self):
        deps = set()
        for o in self.ops:
            if o.dma_key is not None:
                deps.add(o.idx)
        idx = self.op("sync", None)
        self.ops[idx].deps = deps
        return idx

    def emit(self):
        nc = self.nc
        for o in self.ops:
            for j in o.deps:
                self.ops[j].inc = True
        eng_sem = {}
        eng_cnt = {}
        dma_sem = {}
        dma_cnt = {}
        for o in self.ops:
            if o.dma_key is not None:
                if o.dma_key not in dma_sem:
                    dma_sem[o.dma_key] = nc.alloc_semaphore(f"d_{o.dma_key}"[:60])
                    dma_cnt[o.dma_key] = 0
                dma_cnt[o.dma_key] += 16
                o.cnt = dma_cnt[o.dma_key]
                o.inc = True
            elif o.inc:
                if o.eng not in eng_sem:
                    eng_sem[o.eng] = nc.alloc_semaphore(f"e_{o.eng}")
                    eng_cnt[o.eng] = 0
                eng_cnt[o.eng] += 1
                o.cnt = eng_cnt[o.eng]
        def sem_of(o):
            return dma_sem[o.dma_key] if o.dma_key is not None else eng_sem[o.eng]
        streams = {}
        for o in self.ops:
            streams.setdefault(o.eng, []).append(o)
        ops_by_idx = self.ops
        with nc.Block() as blk:
            for eng_name, olist in streams.items():
                def section(e, olist=olist):
                    water = {}
                    for o in olist:
                        waits = {}
                        for j in o.deps:
                            oj = ops_by_idx[j]
                            s = sem_of(oj)
                            v = oj.cnt
                            if waits.get(s, 0) < v:
                                waits[s] = v
                        for s, v in waits.items():
                            if water.get(s, 0) >= v:
                                continue
                            e.wait_ge(s, v)
                            water[s] = v
                        if o.fn is None:
                            continue
                        ins = o.fn(e)
                        if o.inc:
                            ins.then_inc(sem_of(o), 16 if o.dma_key is not None else 1)
                getattr(blk, eng_name)(section)
        return dma_cnt


def ceil_div(a, b):
    return (a + b - 1) // b


def emit_ln(p, ctx, xdram, stat, out_emit, tag="", src_res=None):
    nc, T = ctx["nc"], ctx["T"]
    NS = T // 512
    scr = ctx["scrA"]
    S_ps = ctx["S_ps"]
    P2 = ctx["P2_ps"]
    ones_f = ctx["ones_f32"]
    xst = ctx["xstage"]
    for c in range(8):
        for s in range(NS):
            k = (c * NS + s) % 4
            xa = xst[:, 512 * k:512 * k + 512]
            xr = f"xs{k}"
            sl = slice(512 * s, 512 * s + 512)
            p.dma("sync", xa, xdram[c * 128:(c + 1) * 128, sl], xr, writes=[xr],
                  reads=(src_res(c, s) if src_res else []))
            p.op("scalar", lambda e, xa=xa, sl=sl: e.activation(scr[:, sl], xa, AF.Square),
                 reads=[xr], writes=[f"scrA{s}"], tag=f"ln{tag}.sq{c}.{s}")
            p.op("tensor", lambda e, xa=xa, sl=sl, c=c: e.matmul(
                S_ps[0:1, sl], ones_f[:, 0:1], xa, start=(c == 0), stop=(c == 7)),
                reads=[xr, "cones"], writes=[f"S{s}"], tag=f"ln{tag}.sum{c}.{s}")
            p.op("tensor", lambda e, sl=sl, c=c: e.matmul(
                S_ps[32:33, sl], ones_f[:, 0:1], scr[:, sl], start=(c == 0), stop=(c == 7)),
                reads=[f"scrA{s}", "cones"], writes=[f"S{s}"], tag=f"ln{tag}.sqs{c}.{s}")
    mu = stat[0:1, 0:T]
    m2 = scr[0:1, 0:T]
    rs = stat[0:1, T:2 * T]
    Sres = [f"S{s}" for s in range(NS)]
    scrres = [f"scrA{s}" for s in range(NS)]
    p.op("vector", lambda e: e.tensor_scalar_mul(mu, S_ps[0:1, 0:T], 1.0 / 1024.0),
         reads=Sres, writes=["stat.mu"], tag=f"ln{tag}.mu")
    p.op("vector", lambda e: e.tensor_mul(m2, mu, mu),
         reads=["stat.mu"] + scrres, writes=scrres, tag=f"ln{tag}.m2")
    p.op("vector", lambda e: e.scalar_tensor_tensor(
        m2, S_ps[32:33, 0:T], 1.0 / 1024.0, m2, op0=ALU.mult, op1=ALU.subtract),
        reads=Sres + scrres, writes=scrres, tag=f"ln{tag}.var")
    p.op("scalar", lambda e: e.activation(rs, m2, AF.Sqrt, bias=ctx["cv"](1e-5, 1)),
         reads=scrres + ["cv"], writes=["stat.rs"], tag=f"ln{tag}.sqrt")
    p.op("vector", lambda e: e.reciprocal(rs, rs),
         reads=["stat.rs"], writes=["stat.rs"], tag=f"ln{tag}.recip")
    for c in range(8):
        for s in range(NS):
            k = (c * NS + s) % 4
            xa = xst[:, 512 * k:512 * k + 512]
            xr = f"xs{k}"
            sl = slice(512 * s, 512 * s + 512)
            p.dma("sync", xa, xdram[c * 128:(c + 1) * 128, sl], xr, writes=[xr],
                  reads=(src_res(c, s) if src_res else []))
            mu_b = P2[:, 0:512]
            p.op("tensor", lambda e, sl=sl, mu_b=mu_b: e.matmul(
                mu_b, ones_f[0:1, 0:128], mu[:, sl], start=True, stop=True),
                reads=["stat.mu", "cones"], writes=["P0"], tag=f"ln{tag}.bmu{c}.{s}")
            rs_b = P2[:, 512:1024]
            p.op("tensor", lambda e, sl=sl, rs_b=rs_b: e.matmul(
                rs_b, ones_f[0:1, 0:128], rs[:, sl], start=True, stop=True),
                reads=["stat.rs", "cones"], writes=["P1"], tag=f"ln{tag}.brs{c}.{s}")
            out_emit(p, c, s, xa, xr, (mu_b, "P0"), (rs_b, "P1"))


def emit_ln_to_bf16(p, ctx, xdram, stat, xn, xn_res_prefix, tag="0", fp32dst=None,
                    src_res=None):
    T = ctx["T"]
    scrB = ctx["scrB"]

    def apply(p, c, s, xa, xr, mu_bi, rs_bi):
        mu_b, mu_r = mu_bi
        rs_b, rs_r = rs_bi
        sl = slice(512 * s, 512 * s + 512)
        t1 = scrB[:, sl]
        p.op("vector", lambda e, xa=xa, t1=t1, mu_b=mu_b: e.scalar_tensor_tensor(
            t1, xa, 1.0, mu_b, op0=ALU.mult, op1=ALU.subtract),
            reads=[xr, mu_r], writes=["scrB"], tag=f"lnap{tag}.sub{c}.{s}")
        if fp32dst is None:
            dst = xn[:, c * T + 512 * s: c * T + 512 * s + 512]
            rr = f"{xn_res_prefix}{c}"
        else:
            ap, stride, guard = fp32dst
            dst = ap[:, c * stride + guard + 512 * s: c * stride + guard + 512 * s + 512]
            rr = f"{xn_res_prefix}{c}"
        p.op("vector", lambda e, dst=dst, t1=t1, rs_b=rs_b: e.tensor_mul(dst, t1, rs_b),
             reads=["scrB", rs_r], writes=[rr], tag=f"lnap{tag}.mul{c}.{s}")

    emit_ln(p, ctx, xdram, stat, apply, tag=tag, src_res=src_res)



MTOP = 229344
RB_SZ = 88 * 1024
RB0 = MTOP - RB_SZ
RZ_SZ = 8 * (2048 + 1) * 4
RZ0 = RB0 - RZ_SZ

def alloc_at(nc, name, shape, dtype, off):
    return nc.alloc_sbuf_tensor_at(name, shape, dtype, offset=off).ap()

def make_ctx(nc, T, xstage_at=None):
    ctx = {"nc": nc, "T": T}
    ctx["S_ps"] = nc.alloc_psum_tensor("S_ps", [128, 2048], F32).ap()
    ctx["P2_ps"] = nc.alloc_psum_tensor("P2_ps", [128, 1024], F32).ap()
    ctx["TR_ps"] = nc.alloc_psum_tensor("TR_ps", [128, 2048], BF16).ap()
    ctx["scrA"] = nc.alloc_sbuf_tensor("scrA", [128, max(T, 2048)], F32).ap()
    if xstage_at is None:
        ctx["xstage"] = nc.alloc_sbuf_tensor("xstage", [128, 2048], F32).ap()
    else:
        ctx["xstage"] = alloc_at(nc, "xstage", [128, 2048], F32, xstage_at)
    ctx["scrB"] = nc.alloc_sbuf_tensor("scrB", [128, max(T, 2048)], F32).ap()
    ctx["stats2"] = nc.alloc_sbuf_tensor("stats2", [128, 64], F32).ap()
    ctx["Pbuf"] = nc.alloc_sbuf_tensor("Pbuf", [128, max(T, 2048)], BF16).ap()
    ctx["Pbuf2"] = nc.alloc_sbuf_tensor("Pbuf2", [128, max(T, 2048)], BF16).ap()
    ctx["ptsb"] = nc.alloc_sbuf_tensor("ptsb", [128, 2048], BF16).ap()
    CVALS = [1e-5, 1.1920929e-07, 0.0]
    ctx["cvals_list"] = CVALS
    cv = np.tile(np.asarray(CVALS, np.float32)[None, :], (128, 1))
    ctx["cv_dram"] = nc.inline_tensor(cv, "cvals").ap()
    ctx["cv_sb"] = nc.alloc_sbuf_tensor("cvals_s", [128, len(CVALS)], F32).ap()
    def cvap(val, parts=128):
        k = CVALS.index(val)
        return ctx["cv_sb"][0:parts, k:k + 1]
    ctx["cv"] = cvap
    ones = np.ones((128, 128), np.float32)
    ctx["ones_f32_dram"] = nc.inline_tensor(ones, "ones_f32").ap()
    ctx["ones_bf_dram"] = nc.inline_tensor(ones.astype(BF), "ones_bf").ap()
    ctx["ident_dram"] = nc.inline_tensor(np.eye(128, dtype=BF), "ident_bf").ap()
    ctx["ones_f32"] = nc.alloc_sbuf_tensor("ones_f32s", [128, 128], F32).ap()
    ctx["ones_bf"] = nc.alloc_sbuf_tensor("ones_bfs", [128, 128], BF16).ap()
    ctx["ident"] = nc.alloc_sbuf_tensor("idents", [128, 128], BF16).ap()
    idf = np.eye(128, dtype=np.float32)
    ctx["identf_dram"] = nc.inline_tensor(idf, "identf").ap()
    ctx["identf"] = nc.alloc_sbuf_tensor("identfs", [128, 128], F32).ap()
    return ctx

def load_consts(p, ctx):
    p.dma("sync", ctx["cv_sb"][:, :], ctx["cv_dram"][:, :], "cv",
          writes=["cv"])
    p.dma("sync", ctx["ones_f32"][:, :], ctx["ones_f32_dram"][:, :], "cones",
          writes=["cones"])
    p.dma("sync", ctx["ones_bf"][:, :], ctx["ones_bf_dram"][:, :], "cones_bf",
          writes=["cones_bf"])
    p.dma("sync", ctx["ident"][:, :], ctx["ident_dram"][:, :], "cident",
          writes=["cident"])
    p.dma("sync", ctx["identf"][:, :], ctx["identf_dram"][:, :], "identf", writes=["identf"])


def prep_l0(inp, T):
    D = 1024
    Wq = inp["L0_attn_Wq"]; Wk = inp["L0_attn_Wk"]; Wv = inp["L0_attn_Wv"]
    Wo = inp["L0_attn_Wo"]
    r = np.arange(512)
    h = 4 * ((r % 128) // 32) + r // 128
    j = r % 32
    pidx = 32 * h + j
    permE = 2 * pidx
    qperm = np.concatenate([permE, permE + 1])
    Wq_p = np.ascontiguousarray(Wq[:, qperm])
    rk = np.arange(128)
    gk = rk // 32; jk = rk % 32
    kpermE = 64 * gk + 2 * jk
    kperm = np.concatenate([kpermE, kpermE + 1])
    Wk_p = np.ascontiguousarray(Wk[:, kperm])
    def fmaj(W):
        nf = W.shape[1] // 128
        return np.ascontiguousarray(W.reshape(W.shape[0], nf, 128).transpose(1, 0, 2))
    out = {
        "l0_wq": fmaj(Wq_p), "l0_wk": fmaj(Wk_p), "l0_wv": np.ascontiguousarray(Wv),
        "l0_wo": fmaj(Wo),
        "l0_w1": fmaj(inp["L0_ffn_w1"]), "l0_w2": fmaj(inp["L0_ffn_w2"]),
        "l0_b1": np.ascontiguousarray(inp["L0_ffn_b1"].reshape(32, 128).T),
        "l0_b2": np.ascontiguousarray(inp["L0_ffn_b2"].reshape(8, 128).T),
    }
    t = np.arange(T, dtype=np.float64)
    inv = 1.0 / (10000.0 ** (np.arange(0, D, 2, dtype=np.float64) / D))
    ang_q = t[None, :] * inv[pidx][:, None]
    out["l0_ropeq"] = np.stack([np.cos(ang_q) / 32.0, np.sin(ang_q) / 32.0]).astype(np.float32)
    ang_k = t[None, :] * inv[jk][:, None]
    out["l0_ropek"] = np.stack([np.cos(ang_k), np.sin(ang_k)]).astype(np.float32)
    return out


def declare_l0(nc, T):
    d = {}
    def dram(name, shape):
        d[name] = nc.dram_tensor(name, list(shape), F32, kind="ExternalInput").ap()
    dram("l0_wq", (8, 1024, 128)); dram("l0_wk", (2, 1024, 128))
    dram("l0_wv", (1024, 256)); dram("l0_wo", (8, 1024, 128))
    dram("l0_w1", (32, 1024, 128)); dram("l0_w2", (8, 4096, 128))
    dram("l0_b1", (128, 32)); dram("l0_b2", (128, 8))
    dram("l0_ropeq", (2, 512, T)); dram("l0_ropek", (2, 128, T))
    return d


def emit_l0_attn(p, ctx, d, bufs):
    T = ctx["T"]; NS = T // 512; NT = T // 128
    S_ps = ctx["S_ps"]; P2 = ctx["P2_ps"]; TR = ctx["TR_ps"]
    scrA, scrB, scrC = ctx["scrA"], ctx["scrB"], bufs["scrC"]
    st2 = ctx["stats2"]
    xnb = bufs["xnb"]; qrope = bufs["qrope"]; krope = bufs["krope"]
    vtok = bufs["vtok"]; Pb = ctx["Pbuf"]; ptsb = ctx["ptsb"]; ctxT = bufs["ctxT"]
    ws = bufs["wslots"]
    vw = bufs["vw"]
    rc = bufs["ropec"]; rs_ = bufs["ropes"]
    ident = ctx["ident"]
    Sres = ["S0", "S1", "S2", "S3"]

    SCRA = [f"scrA{i}" for i in range(NS)]

    def wload(dram3, idx, slot, key):
        ap = dram3[idx].rearrange("(c p) f -> p c f", p=128)
        out = ws[slot][:, :].rearrange("p (c f) -> p c f", f=128)
        p.dma("gpsimd", out, ap, key, writes=[f"ws{slot}"])

    def proj_chunk(fm, slot, psum_banks=Sres, dst=None):
        for s in range(NS):
            sl = slice(512 * s, 512 * s + 512)
            for dk in range(8):
                p.op("tensor", lambda e, sl=sl, dk=dk, slot=slot: e.matmul(
                    S_ps[:, sl], ws[slot][:, 128 * dk:128 * dk + 128],
                    xnb[:, dk * T + sl.start: dk * T + sl.stop],
                    start=(dk == 0), stop=(dk == 7)),
                    reads=[f"ws{slot}", f"a{dk}"], writes=[f"S{s}"],
                    tag=f"l0.proj{fm}.{s}")

    def rope_load(tab, c, key):
        p.dma("sync", rc[:, 0:T], tab[c * 128:(c + 1) * 128, 0:T], "ropec",
              writes=["ropec"])
        p.dma("sync", rs_[:, 0:T], tab[512 + 128 * c if False else 0, 0:T], key,
              writes=["ropes"])

    def qk_phase(nf_half, wdram, tabdram, dst, dst_prefix, nrows):
        for c in range(nf_half):
            p.dma("sync", rc[:, 0:T], tabdram[0, 128 * c:128 * c + nrows, 0:T], "ropec",
                  writes=["ropec"])
            p.dma("sync", rs_[:, 0:T], tabdram[1, 128 * c:128 * c + nrows, 0:T], "ropes",
                  writes=["ropes"])
            wload(wdram, c, 0, "ws0")
            proj_chunk(c, 0)
            p.op("vector", lambda e: e.tensor_copy(scrA[0:nrows, 0:T], S_ps[0:nrows, 0:T]),
                 reads=Sres, writes=SCRA, tag="l0.evacE")
            wload(wdram, nf_half + c, 1, "ws1")
            proj_chunk(nf_half + c, 1)
            eT = slice(c * T, c * T + T)
            oT = slice((nf_half + c) * T, (nf_half + c) * T + T)
            p.op("vector", lambda e: e.tensor_mul(scrB[0:nrows, 0:T], scrA[0:nrows, 0:T], rc[0:nrows, 0:T]),
                 reads=SCRA + ["ropec"], writes=["scrB"], tag="l0.r1")
            p.op("vector", lambda e: e.tensor_mul(scrC[0:nrows, 0:T], S_ps[0:nrows, 0:T], rs_[0:nrows, 0:T]),
                 reads=Sres + ["ropes"], writes=["scrC"], tag="l0.r2")
            p.op("vector", lambda e, eT=eT: e.tensor_sub(dst[0:nrows, eT], scrB[0:nrows, 0:T], scrC[0:nrows, 0:T]),
                 reads=["scrB", "scrC"], writes=[f"{dst_prefix}{c}"], tag="l0.r3")
            p.op("vector", lambda e: e.tensor_mul(scrB[0:nrows, 0:T], scrA[0:nrows, 0:T], rs_[0:nrows, 0:T]),
                 reads=SCRA + ["ropes"], writes=["scrB"], tag="l0.r4")
            p.op("vector", lambda e: e.tensor_mul(scrC[0:nrows, 0:T], S_ps[0:nrows, 0:T], rc[0:nrows, 0:T]),
                 reads=Sres + ["ropec"], writes=["scrC"], tag="l0.r5")
            p.op("vector", lambda e, oT=oT: e.tensor_add(dst[0:nrows, oT], scrB[0:nrows, 0:T], scrC[0:nrows, 0:T]),
                 reads=["scrB", "scrC"], writes=[f"{dst_prefix}{nf_half + c}"], tag="l0.r6")

    qk_phase(4, d["l0_wq"], d["l0_ropeq"], qrope, "qr", 128)
    qk_phase(1, d["l0_wk"], d["l0_ropek"], krope, "kr", 128)

    p.fence([], ["scrC", "vtok"])
    p.dma("gpsimd", vw[:, :].rearrange("p (c f) -> p c f", f=256),
          d["l0_wv"].rearrange("(c p) f -> p c f", p=128), "vw", writes=["vw"])
    for ti in range(NT):
        for dk in range(8):
            p.op("tensor", lambda e, ti=ti, dk=dk: e.matmul(
                P2[:, 0:256], xnb[:, dk * T + 128 * ti: dk * T + 128 * ti + 128],
                vw[:, 256 * dk:256 * dk + 256], start=(dk == 0), stop=(dk == 7)),
                reads=[f"a{dk}", "vw"], writes=["P0"], tag=f"l0.v{ti}")
        p.op("vector", lambda e, ti=ti: e.tensor_copy(vtok[:, 256 * ti:256 * ti + 256], P2[:, 0:256]),
             reads=["P0"], writes=["vtok"], tag=f"l0.vev{ti}")

    def l0_s1(h, qt):
        base = 32 * (h // 4)
        ce_, co = h % 4, 4 + h % 4
        qsl = slice(128 * qt, 128 * qt + 128)
        for js in range(NS):
            jsl = slice(512 * js, 512 * js + 512)
            p.op("tensor", lambda e, qsl=qsl, jsl=jsl, base=base, ce_=ce_: e.matmul(
                S_ps[:, jsl],
                qrope[base:base + 32, ce_ * T + qsl.start: ce_ * T + qsl.stop],
                krope[base:base + 32, jsl], start=True, stop=False,
                tile_position=(base, 0)),
                reads=[f"qr{ce_}", "kr0"], writes=[f"S{js}"], tag=f"l0.se{h}.{qt}")
            p.op("tensor", lambda e, qsl=qsl, jsl=jsl, base=base, co=co: e.matmul(
                S_ps[:, jsl],
                qrope[base:base + 32, co * T + qsl.start: co * T + qsl.stop],
                krope[base:base + 32, T + jsl.start: T + jsl.stop],
                start=False, stop=True, tile_position=(base, 0)),
                reads=[f"qr{co}", "kr1"], writes=[f"S{js}"], tag=f"l0.so{h}.{qt}")

    def l0_s2a(h, qt):
        pe = qt % 2
        Pba = Pb if pe == 0 else bufs["pbalt"]
        sums = st2[:, 4 * pe + 1:4 * pe + 2]
        rsum = st2[:, 4 * pe + 2:4 * pe + 3]
        p.op("scalar", lambda e, sums=sums, Pba=Pba: e.activation(
            Pba[:, 0:T], S_ps[:, 0:T], AF.Exp, bias=0.0, accum_out=sums),
             reads=Sres, writes=[f"P.{pe}", f"sums{pe}"],
             tag=f"l0.exp{h}.{qt}")
        p.op("vector", lambda e, sums=sums, rsum=rsum: e.reciprocal(rsum, sums),
             reads=[f"sums{pe}"], writes=[f"rsum{pe}"], tag=f"l0.rec{h}.{qt}")
        Pb2 = ctx["Pbuf2"]
        p.op("vector", lambda e, Pb2=Pb2, Pba=Pba, rsum=rsum: e.tensor_scalar_mul(
            Pb2[:, 0:T], Pba[:, 0:T], rsum),
             reads=[f"P.{pe}", f"rsum{pe}"], writes=["P2b"], tag=f"l0.ps{h}.{qt}")

    def l0_s2b(h, qt):
        pe = qt % 2
        qsl = slice(128 * qt, 128 * qt + 128)
        Pb2 = ctx["Pbuf2"]
        for jt in range(NT):
            sl2 = jt % 2
            trof = 1024 * sl2
            p.op("tensor", lambda e, jt=jt, trof=trof, Pb2=Pb2: e.transpose(
                TR[:, trof:trof + 128], Pb2[:, 128 * jt:128 * jt + 128], ident),
                reads=["P2b", "cident"], writes=[f"TR{sl2}"], tag=f"l0.tr{h}.{qt}.{jt}")
            p.op("vector" if jt % 2 == 0 else "scalar", lambda e, jt=jt, trof=trof:
                 (e.tensor_copy if hasattr(e, "tensor_copy") else e.copy)(
                ptsb[:, 128 * jt:128 * jt + 128], TR[:, trof:trof + 128]),
                reads=[f"TR{sl2}"], writes=[f"pt{jt}"], tag=f"l0.trev{h}.{qt}.{jt}")
        pvb = (P2[64 * (h % 2):64 * (h % 2) + 64, 256:384] if pe == 0
               else P2[64 * (h % 2):64 * (h % 2) + 64, 768:896])
        pvres = "P0" if pe == 0 else "P1"
        for jt in range(NT):
            p.op("tensor", lambda e, jt=jt, h=h, pvb=pvb: e.matmul(
                pvb,
                vtok[:, 256 * jt + 64 * (h // 4): 256 * jt + 64 * (h // 4) + 64],
                ptsb[:, 128 * jt:128 * jt + 128],
                start=(jt == 0), stop=(jt == NT - 1), tile_position=(0, 64 * (h % 2))),
                reads=["vtok", f"pt{jt}"], writes=[pvres], tag=f"l0.pv{h}.{qt}.{jt}")
        half = 64 * (h % 2)
        p.op("vector", lambda e, h=h, qsl=qsl, half=half, pvb=pvb: e.tensor_copy(
            ctxT[half:half + 64, (h // 2) * T + qsl.start: (h // 2) * T + qsl.stop],
            pvb),
            reads=[pvres], writes=[f"ct{h // 2}"], tag=f"l0.cev{h}.{qt}")

    tiles0 = [(h, qt) for h in range(16) for qt in range(NT)]
    l0_s1(*tiles0[0])
    l0_s2a(*tiles0[0])
    for ti_ in range(1, len(tiles0)):
        l0_s1(*tiles0[ti_])
        l0_s2b(*tiles0[ti_ - 1])
        l0_s2a(*tiles0[ti_])
    l0_s2b(*tiles0[-1])

    xst = ctx["xstage"]
    xsrc = bufs["xsrc"]; xdst = bufs["xdst"]
    for e_ in range(8):
        wload(d["l0_wo"], e_, e_ % 2, f"ws{e_ % 2}")
        for s in range(NS):
            sl = slice(512 * s, 512 * s + 512)
            slot = (e_ * NS + s) % 4
            xa = xst[:, 512 * slot:512 * slot + 512]
            xr = f"xs{slot}"
            p.dma("sync", xa, xsrc[e_ * 128:(e_ + 1) * 128, sl], xr, writes=[xr])
            for ic in range(8):
                p.op("tensor", lambda e, sl=sl, ic=ic, slot2=e_ % 2: e.matmul(
                    S_ps[:, sl], ws[slot2][:, 128 * ic:128 * ic + 128],
                    ctxT[:, ic * T + sl.start: ic * T + sl.stop],
                    start=(ic == 0), stop=(ic == 7)),
                    reads=[f"ws{e_ % 2}", f"ct{ic}"], writes=[f"S{s}"],
                    tag=f"l0.wo{e_}.{s}")
            p.op("vector", lambda e, xa=xa, s=s, sl=sl: e.tensor_add(
                xa, xa, S_ps[:, sl]),
                reads=[f"S{s}", xr], writes=[xr], tag=f"l0.xadd{e_}.{s}")
            p.dma("sync", xdst[e_ * 128:(e_ + 1) * 128, sl], xa, xr, reads=[xr],
                  writes=[f"xA{e_}.{s}"])


def emit_l0_ffn(p, ctx, d, bufs):
    T = ctx["T"]; NS = T // 512
    S_ps = ctx["S_ps"]; P2 = ctx["P2_ps"]
    xnb = bufs["xnb"]; hbuf = bufs["hbuf"]
    ws = bufs["wslots"]; w2s = bufs["w2slot"]
    b1 = bufs["b1sb"]; b2 = bufs["b2sb"]
    p.dma("sync", b1[:, :], d["l0_b1"][:, :], "b1sb", writes=["b1"])
    p.dma("sync", b2[:, :], d["l0_b2"][:, :], "b2sb", writes=["b2"])
    TH = T // 1024 if T >= 1024 else 1
    HW = T // TH
    NS2 = HW // 512
    for th in range(TH):
        t0 = th * HW
        for fm in range(32):
            slot = fm % 2
            p.dma("gpsimd", ws[slot][:, :].rearrange("p (c f) -> p c f", f=128),
                  d["l0_w1"][fm].rearrange("(c p) f -> p c f", p=128),
                  f"ws{slot}", writes=[f"ws{slot}"])
            for s in range(NS2):
                sl = slice(t0 + 512 * s, t0 + 512 * s + 512)
                bank = "P0" if fm % 2 == 0 else "P1"
                pa = P2[:, 0:512] if fm % 2 == 0 else P2[:, 512:1024]
                for dk in range(8):
                    p.op("tensor", lambda e, sl=sl, dk=dk, slot=slot, pa=pa: e.matmul(
                        pa, ws[slot][:, 128 * dk:128 * dk + 128],
                        xnb[:, dk * T + sl.start: dk * T + sl.stop],
                        start=(dk == 0), stop=(dk == 7)),
                        reads=[f"ws{slot}", f"a{dk}"], writes=[bank],
                        tag=f"l0f.w1.{fm}.{s}")
                p.op("scalar", lambda e, fm=fm, s=s, pa=pa: e.activation(
                    hbuf[:, fm * HW + 512 * s: fm * HW + 512 * s + 512], pa,
                    AF.Gelu, bias=b1[:, fm:fm + 1]),
                    reads=[bank, "b1"], writes=[f"h{fm}"], tag=f"l0f.gelu.{fm}.{s}")
        for e_ in range(8):
            p.dma("gpsimd", w2s[:, :].rearrange("p (c f) -> p c f", f=128),
                  d["l0_w2"][e_].rearrange("(c p) f -> p c f", p=128),
                  "w2s", writes=["w2s"])
            for s in range(NS2):
                sl = slice(t0 + 512 * s, t0 + 512 * s + 512)
                sps = s % 4
                for k in range(32):
                    p.op("tensor", lambda e, s=s, k=k, sps=sps: e.matmul(
                        S_ps[:, 512 * sps:512 * sps + 512], w2s[:, 128 * k:128 * k + 128],
                        hbuf[:, k * HW + 512 * s: k * HW + 512 * s + 512],
                        start=(k == 0), stop=(k == 31)),
                        reads=["w2s", f"h{k}"], writes=[f"S{sps}"],
                        tag=f"l0f.w2.{e_}.{s}")
                slot = (e_ * NS2 + s) % 4
                xa = ctx["xstage"][:, 512 * slot:512 * slot + 512]
                xr = f"xs{slot}"
                p.dma("sync", xa, bufs["xsrc2"][e_ * 128:(e_ + 1) * 128, sl], xr,
                      writes=[xr], reads=[f"xA{e_}.{s}"])
                p.op("vector", lambda e, xa=xa, sps=sps, e_=e_: e.scalar_tensor_tensor(
                    xa, S_ps[:, 512 * sps:512 * sps + 512], b2[:, e_:e_ + 1],
                    xa, op0=ALU.add, op1=ALU.add),
                    reads=[f"S{sps}", "b2", xr], writes=[xr],
                    tag=f"l0f.add.{e_}.{s}")
                p.dma("sync", bufs["xdst2"][e_ * 128:(e_ + 1) * 128, sl], xa, xr,
                      reads=[xr], writes=[f"xB{e_}.{s}"])


def declare_mla(nc, T):
    d = {}
    def dram(name, shape):
        d[name] = nc.dram_tensor(name, list(shape), F32, kind="ExternalInput").ap()
    dram("mla_wdq", (512, 64)); dram("mla_wuq", (64, 512))
    dram("mla_wdkv", (1024, 64)); dram("mla_wukv", (128, 1024))
    dram("mla_wqr", (64, 64)); dram("mla_wkr", (64, 64)); dram("mla_wo", (64, 64))
    dram("mla_qw", (64, 1)); dram("mla_kw", (64, 1))
    return d


def prep_mla(inp):
    return {
        "mla_wdq": inp["L1_attn_W_DQ"], "mla_wuq": inp["L1_attn_W_UQ"],
        "mla_wdkv": inp["L1_attn_W_DKV"], "mla_wukv": inp["L1_attn_W_UKV"],
        "mla_wqr": inp["L1_attn_W_QR"], "mla_wkr": inp["L1_attn_W_KR"],
        "mla_wo": inp["L1_attn_W_O"],
        "mla_qw": np.ascontiguousarray(inp["L1_attn_q_norm_w"].reshape(64, 1)),
        "mla_kw": np.ascontiguousarray(inp["L1_attn_k_norm_w"].reshape(64, 1)),
    }


def alloc_mla_bufs(nc, ctx, T, base):
    NT = T // 128
    b = {}
    off = [base]
    def A(name, shape, dtype):
        ap = alloc_at(nc, name, shape, dtype, off[0])
        off[0] += shape[-1] * (4 if dtype == F32 else 2)
        off[0] = (off[0] + 31) & ~31
        return ap
    b["qcomb"] = A("m_qcomb", [128, T], BF16)
    b["kcomb"] = A("m_kcomb", [128, T], BF16)
    b["vf"] = A("m_vf", [128, T], BF16)
    b["vtokh"] = A("m_vtokh", [128, NT * 64], BF16)
    b["ctxh"] = A("m_ctxh", [64, T], BF16)
    b["wdq"] = A("m_wdq", [128, 256], F32)
    b["wuq"] = A("m_wuq", [64, 512], F32)
    b["wuqT"] = A("m_wuqT", [128, 256], F32)
    b["wdkv"] = A("m_wdkv", [128, 512], F32)
    b["wukv"] = A("m_wukv", [128, 1024], F32)
    b["wukvT"] = A("m_wukvT", [128, 1024], F32)
    b["mq"] = A("m_mq", [128, 64], F32)
    b["mkv"] = A("m_mkv", [128, 128], F32)
    b["wqrT"] = A("m_wqrT", [64, 64], BF16)
    b["wkrT"] = A("m_wkrT", [128, 64], F32)
    b["woT"] = A("m_woT", [64, 64], BF16)
    b["wn"] = A("m_wn", [64, 2], F32)
    b["ss"] = A("m_ss", [16, T], F32)
    b["rstd"] = A("m_rstd", [16, T], F32)
    b["pbalt"] = A("m_pbalt", [128, T], BF16)
    ctx["oh64"] = A("oh64s", [64, 256], F32)
    ctx["e16"] = A("e16s", [16, 1024], F32)
    ctx["diagm"] = A("diagms", [128, 128], F32)
    b["_end"] = off[0]
    return b


def check_sbuf(nc, limit=None):
    fill = 16481 + (212863 - nc.sbuf_bytes_remaining)
    lim = RZ0 if limit is None else limit
    assert fill <= lim, f"auto sbuf fill {fill} exceeds manual region base {lim}"


def mla_consts(nc, ctx):
    oh = np.zeros((64, 16 * 16), np.float32)
    for h in range(16):
        oh[:, 16 * h + h] = 1.0
    ctx["oh64_dram"] = nc.inline_tensor(oh, "oh64").ap()
    e16 = np.zeros((16, 16 * 64), np.float32)
    for h in range(16):
        e16[h, 64 * h:64 * h + 64] = 1.0
    ctx["e16_dram"] = nc.inline_tensor(e16, "e16").ap()
    dm = np.zeros((128, 128), np.float32)
    for i in range(128):
        dm[i, i + 1:] = NEG
    ctx["diag_dram"] = nc.inline_tensor(dm, "diagm").ap()


def load_mla_consts(p, ctx):
    p.dma("sync", ctx["oh64"][:, :], ctx["oh64_dram"][:, :], "oh64", writes=["oh64"])
    p.dma("sync", ctx["e16"][:, :], ctx["e16_dram"][:, :], "e16", writes=["e16"])
    p.dma("sync", ctx["diagm"][:, :], ctx["diag_dram"][:, :], "diagm", writes=["diagm"])



def emit_mla_setup(p, ctx, d, b):
    S_ps = ctx["S_ps"]; P2 = ctx["P2_ps"]
    idf = ctx["identf"]
    def ld(dst, src, key, rearr=None):
        p.dma("sync", dst, src, key, writes=[key])
    ld(b["wdq"][:, :].rearrange("p (c f) -> p c f", f=64),
       d["mla_wdq"].rearrange("(c p) f -> p c f", p=128), "m_wdq")
    ld(b["wuq"][:, :], d["mla_wuq"][:, :], "m_wuq")
    ld(b["wdkv"][:, :].rearrange("p (c f) -> p c f", f=64),
       d["mla_wdkv"].rearrange("(c p) f -> p c f", p=128), "m_wdkv")
    ld(b["wukv"][:, :], d["mla_wukv"][:, :], "m_wukv")
    ld(b["wn"][:, 0:1], d["mla_qw"][:, :], "m_wn")
    p.dma("sync", b["wn"][:, 1:2], d["mla_kw"][:, :], "m_wn", writes=["m_wn"])
    p.op("vector", lambda e: e.tensor_scalar_mul(b["wn"][:, 0:1], b["wn"][:, 0:1], 0.125),
         reads=["m_wn"], writes=["m_wn"], tag="mla.wq8")
    for c in range(4):
        p.op("tensor", lambda e, c=c: e.transpose(
            S_ps[:, 0:64], b["wuq"][0:64, 128 * c:128 * c + 128], idf[0:64, 0:64]),
            reads=["m_wuq", "identf"], writes=["S0"], tag=f"mla.tuq{c}")
        p.op("vector", lambda e, c=c: e.tensor_copy(b["wuqT"][:, 64 * c:64 * c + 64], S_ps[:, 0:64]),
             reads=["S0"], writes=["m_wuqT"], tag=f"mla.tuqe{c}")
    for c in range(8):
        p.op("tensor", lambda e, c=c: e.transpose(
            S_ps[:, 0:128], b["wukv"][:, 128 * c:128 * c + 128], idf),
            reads=["m_wukv", "identf"], writes=["S0"], tag=f"mla.tukv{c}")
        p.op("vector", lambda e, c=c: e.tensor_copy(b["wukvT"][:, 128 * c:128 * c + 128], S_ps[:, 0:128]),
             reads=["S0"], writes=["m_wukvT"], tag=f"mla.tukve{c}")
    for half in (0, 64):
        for c in range(4):
            p.op("tensor", lambda e, c=c, half=half: e.matmul(
                P2[half:half + 64, 0:64], b["wdq"][:, 64 * c:64 * c + 64],
                b["wuqT"][:, 64 * c:64 * c + 64], start=(c == 0), stop=(c == 3),
                tile_position=(0, half)),
                reads=["m_wdq", "m_wuqT"], writes=["P0"], tag=f"mla.mq{half}.{c}")
        p.op("vector", lambda e, half=half: e.tensor_copy(
            b["mq"][half:half + 64, 0:64], P2[half:half + 64, 0:64]),
            reads=["P0"], writes=["m_mq"], tag=f"mla.mqe{half}")
        for c in range(8):
            p.op("tensor", lambda e, c=c, half=half: e.matmul(
                P2[half:half + 64, 512:640], b["wdkv"][:, 64 * c:64 * c + 64],
                b["wukvT"][:, 128 * c:128 * c + 128], start=(c == 0), stop=(c == 7),
                tile_position=(0, half)),
                reads=["m_wdkv", "m_wukvT"], writes=["P1"], tag=f"mla.mkv{half}.{c}")
        p.op("vector", lambda e, half=half: e.tensor_copy(
            b["mkv"][half:half + 64, 0:128], P2[half:half + 64, 512:640]),
            reads=["P1"], writes=["m_mkv"], tag=f"mla.mkve{half}")
    ld(b["wuq"][0:64, 0:64], d["mla_wqr"][:, :], "m_wuq")
    p.op("tensor", lambda e: e.transpose(S_ps[0:64, 0:64], b["wuq"][0:64, 0:64], idf[0:64, 0:64]),
         reads=["m_wuq", "identf"], writes=["S0"], tag="mla.tqr")
    p.op("vector", lambda e: e.tensor_copy(b["wqrT"][:, :], S_ps[0:64, 0:64]),
         reads=["S0"], writes=["m_wqrT"], tag="mla.tqre")
    ld(b["wuq"][0:64, 64:128], d["mla_wo"][:, :], "m_wuq")
    p.op("tensor", lambda e: e.transpose(S_ps[0:64, 64:128], b["wuq"][0:64, 64:128], idf[0:64, 0:64]),
         reads=["m_wuq", "identf"], writes=["S0"], tag="mla.two")
    p.op("vector", lambda e: e.tensor_copy(b["woT"][:, :], S_ps[0:64, 64:128]),
         reads=["S0"], writes=["m_woT"], tag="mla.twoe")
    ld(b["wuq"][0:64, 128:192], d["mla_wkr"][:, :], "m_wuq")
    p.op("tensor", lambda e: e.transpose(
        S_ps[0:64, 128:192], b["wuq"][0:64, 128:192], idf[0:64, 0:64]),
        reads=["m_wuq", "identf"], writes=["S0"], tag="mla.tkr")
    p.op("vector", lambda e: e.tensor_copy(
        b["wkrT"][0:64, 0:64], S_ps[0:64, 128:192]),
        reads=["S0"], writes=["m_wkrT"], tag="mla.tkre")
    p.dma("sync", b["wkrT"][64:128, 0:64], b["wkrT"][0:64, 0:64], "m_wkrTd",
          reads=["m_wkrT"], writes=["m_wkrT"])


def emit_mla_head(p, ctx, b, h, zap, zres, xsrc, xdst, first_for_chunk):
    T = ctx["T"]; NS = T // 512; NT = T // 128
    S_ps = ctx["S_ps"]; P2 = ctx["P2_ps"]; TR = ctx["TR_ps"]
    scrA, scrB = ctx["scrA"], ctx["scrB"]
    st2 = ctx["stats2"]
    Pb = ctx["Pbuf"]; ptsb = ctx["ptsb"]
    half = 64 * (h % 2)
    c2 = h // 2
    Sres = [f"S{i}" for i in range(NS)]
    SCRA = [f"scrA{i}" for i in range(NS)]
    eps = ctx["cv"](1.1920929e-07, 16)

    def norm_side(src_dst, w_col, tagp):
        p.op("scalar", lambda e: e.activation(scrB[0:64, 0:T], scrA[0:64, 0:T], AF.Square),
             reads=SCRA, writes=["scrB"], tag=f"{tagp}.sq")
        for s in range(NS):
            sl = slice(512 * s, 512 * s + 512)
            p.op("tensor", lambda e, sl=sl: e.matmul(
                P2[0:16, 0:512], ctx["oh64"][:, 16 * h:16 * h + 16], scrB[0:64, sl],
                start=True, stop=True),
                reads=["scrB", "oh64"], writes=["P0"], tag=f"{tagp}.ss{s}")
            p.op("vector", lambda e, sl=sl: e.tensor_copy(b["ss"][:, sl], P2[0:16, 0:512]),
                 reads=["P0"], writes=["m_ss"], tag=f"{tagp}.sse{s}")
        p.op("vector", lambda e: e.tensor_scalar(
            b["rstd"][:, 0:T], b["ss"][:, 0:T], 1.0 / 64.0, eps, op0=ALU.mult, op1=ALU.add),
            reads=["m_ss", "cv"], writes=["m_rstd"], tag=f"{tagp}.ms")
        p.op("scalar", lambda e: e.activation(b["rstd"][:, 0:T], b["rstd"][:, 0:T], AF.Sqrt),
             reads=["m_rstd"], writes=["m_rstd"], tag=f"{tagp}.sqrt")
        p.op("vector", lambda e: e.reciprocal(b["rstd"][:, 0:T], b["rstd"][:, 0:T]),
             reads=["m_rstd"], writes=["m_rstd"], tag=f"{tagp}.rec")
        for s in range(NS):
            sl = slice(512 * s, 512 * s + 512)
            p.op("tensor", lambda e, sl=sl: e.matmul(
                P2[0:64, 0:512], ctx["e16"][:, 64 * h:64 * h + 64], b["rstd"][:, sl],
                start=True, stop=True),
                reads=["m_rstd", "e16"], writes=["P0"], tag=f"{tagp}.bc{s}")
            p.op("vector", lambda e, sl=sl, w_col=w_col: e.scalar_tensor_tensor(
                src_dst[0:64, sl], scrA[0:64, sl], b["wn"][:, w_col:w_col + 1],
                P2[0:64, 0:512], op0=ALU.mult, op1=ALU.mult),
                reads=[f"scrA{sl.start // 512}", "m_wn", "P0"], writes=[src_dst is b["qcomb"] and "m_qc" or "m_kc"],
                tag=f"{tagp}.ap{s}")

    for s in range(NS):
        sl = slice(512 * s, 512 * s + 512)
        p.op("tensor", lambda e, sl=sl: e.matmul(
            S_ps[0:64, sl], b["mq"][half:half + 64, 0:64], zap[:, sl],
            start=True, stop=True),
            reads=["m_mq", zres], writes=[f"S{s}"], tag=f"mla{h}.qc{s}")
    p.op("vector", lambda e: e.tensor_copy(scrA[0:64, 0:T], S_ps[0:64, 0:T]),
         reads=Sres, writes=SCRA, tag=f"mla{h}.qce")
    norm_side(b["qcomb"], 0, f"mla{h}.qn")
    for s in range(NS):
        sl = slice(512 * s, 512 * s + 512)
        p.op("tensor", lambda e, sl=sl: e.matmul(
            S_ps[64:128, sl], b["wqrT"][:, :], b["qcomb"][0:64, sl],
            start=True, stop=True, tile_position=(0, 64)),
            reads=["m_wqrT", "m_qc"], writes=[f"S{s}"], tag=f"mla{h}.qr{s}")
        p.op("vector", lambda e, sl=sl: e.tensor_copy(
            b["qcomb"][64:128, sl], S_ps[64:128, sl]),
            reads=[f"S{s}"], writes=["m_qc"], tag=f"mla{h}.qre{s}")
    for s in range(NS):
        sl = slice(512 * s, 512 * s + 512)
        p.op("tensor", lambda e, sl=sl: e.matmul(
            S_ps[:, sl], b["mkv"][half:half + 64, 0:128], zap[:, sl],
            start=True, stop=True),
            reads=["m_mkv", zres], writes=[f"S{s}"], tag=f"mla{h}.kv{s}")
        p.op("vector", lambda e, sl=sl: e.tensor_copy(scrA[0:64, sl], S_ps[0:64, sl]),
             reads=[f"S{s}"], writes=[f"scrA{s}"], tag=f"mla{h}.kce{s}")
        p.op("vector", lambda e, sl=sl: e.tensor_copy(b["vf"][64:128, sl], S_ps[64:128, sl]),
             reads=[f"S{s}"], writes=["m_vf"], tag=f"mla{h}.vce{s}")
    norm_side(b["kcomb"], 1, f"mla{h}.kn")
    for s in range(NS):
        sl = slice(512 * s, 512 * s + 512)
        p.op("tensor", lambda e, sl=sl: e.matmul(
            S_ps[64:128, sl], b["wkrT"][half:half + 64, 0:64], zap[:, sl],
            start=True, stop=True, tile_position=(half, 64)),
            reads=["m_wkrT", zres], writes=[f"S{s}"], tag=f"mla{h}.kr{s}")
        p.op("vector", lambda e, sl=sl: e.tensor_copy(
            b["kcomb"][64:128, sl], S_ps[64:128, sl]),
            reads=[f"S{s}"], writes=["m_kc"], tag=f"mla{h}.kre{s}")
    for jt in range(NT):
        sl2 = jt % 2
        trof = 1024 * sl2
        p.op("tensor", lambda e, jt=jt, trof=trof: e.transpose(
            TR[:, trof:trof + 64], b["vf"][64:128, 128 * jt:128 * jt + 128],
            ctx["ident"][64:128, 64:128]),
            reads=["m_vf", "cident"], writes=[f"TR{sl2}"], tag=f"mla{h}.vt{jt}")
        p.op("vector", lambda e, jt=jt, trof=trof: e.tensor_copy(
            b["vtokh"][:, 64 * jt:64 * jt + 64], TR[:, trof:trof + 64]),
            reads=[f"TR{sl2}"], writes=["m_vtokh"], tag=f"mla{h}.vte{jt}")
    def m_stage1(qt):
        nj = 128 * (qt + 1)
        njs = ceil_div(nj, 512)
        pe = qt % 2
        negm = st2[:, 4 * pe:4 * pe + 1]
        for js in range(njs):
            jl = min(512, nj - 512 * js)
            sl = slice(512 * js, 512 * js + jl)
            p.op("tensor", lambda e, sl=sl, qt=qt: e.matmul(
                S_ps[:, sl], b["qcomb"][:, 128 * qt:128 * qt + 128], b["kcomb"][:, sl],
                start=True, stop=True),
                reads=["m_qc", "m_kc"], writes=[f"S{js}"], tag=f"mla{h}.s{qt}.{js}")
        dt_ = slice(128 * qt, 128 * qt + 128)
        p.op("vector", lambda e, dt_=dt_: e.tensor_add(S_ps[:, dt_], S_ps[:, dt_], ctx["diagm"][:, :]),
             reads=[f"S{qt // 4}", "diagm"], writes=[f"S{qt // 4}"], tag=f"mla{h}.dm{qt}")
        p.op("vector", lambda e, nj=nj, negm=negm: e.reduce_max(negm, S_ps[:, 0:nj], axis=AX.X, negate=True),
             reads=[f"S{js2}" for js2 in range(njs)], writes=[f"negm{pe}"], tag=f"mla{h}.mx{qt}")

    def m_stage2a(qt):
        nj = 128 * (qt + 1)
        njs = ceil_div(nj, 512)
        pe = qt % 2
        Pba = Pb if pe == 0 else b["pbalt"]
        negm = st2[:, 4 * pe:4 * pe + 1]
        sums = st2[:, 4 * pe + 1:4 * pe + 2]
        rsum = st2[:, 4 * pe + 2:4 * pe + 3]
        p.op("scalar", lambda e, nj=nj, negm=negm, sums=sums, Pba=Pba: e.activation(
            Pba[:, 0:nj], S_ps[:, 0:nj], AF.Exp, bias=negm, accum_out=sums),
             reads=[f"S{js2}" for js2 in range(njs)] + [f"negm{pe}"],
             writes=[f"P.{pe}", f"sums{pe}"], tag=f"mla{h}.ex{qt}")
        p.op("vector", lambda e, sums=sums, rsum=rsum: e.reciprocal(rsum, sums),
             reads=[f"sums{pe}"], writes=[f"rsum{pe}"], tag=f"mla{h}.rc{qt}")
        Pb2 = ctx["Pbuf2"]
        p.op("vector", lambda e, nj=nj, Pb2=Pb2, Pba=Pba, rsum=rsum: e.tensor_scalar_mul(
            Pb2[:, 0:nj], Pba[:, 0:nj], rsum),
            reads=[f"P.{pe}", f"rsum{pe}"], writes=["P2b"], tag=f"mla{h}.pm{qt}")

    def m_stage2b(qt):
        pe = qt % 2
        Pb2 = ctx["Pbuf2"]
        for jt in range(qt + 1):
            sl2 = jt % 2
            trof = 1024 * sl2
            p.op("tensor", lambda e, jt=jt, trof=trof, Pb2=Pb2: e.transpose(
                TR[:, trof:trof + 128], Pb2[:, 128 * jt:128 * jt + 128], ctx["ident"]),
                reads=["P2b", "cident"], writes=[f"TR{sl2}"], tag=f"mla{h}.tr{qt}.{jt}")
            p.op("vector" if jt % 2 == 0 else "scalar", lambda e, jt=jt, trof=trof:
                 (e.tensor_copy if hasattr(e, "tensor_copy") else e.copy)(
                ptsb[:, 128 * jt:128 * jt + 128], TR[:, trof:trof + 128]),
                reads=[f"TR{sl2}"], writes=[f"pt{jt}"], tag=f"mla{h}.tre{qt}.{jt}")
        pvb = P2[0:64, 256:384] if pe == 0 else P2[0:64, 768:896]
        pvres = "P0" if pe == 0 else "P1"
        for jt in range(qt + 1):
            p.op("tensor", lambda e, jt=jt, qt=qt, pvb=pvb: e.matmul(
                pvb, b["vtokh"][:, 64 * jt:64 * jt + 64],
                ptsb[:, 128 * jt:128 * jt + 128],
                start=(jt == 0), stop=(jt == qt)),
                reads=["m_vtokh", f"pt{jt}"], writes=[pvres], tag=f"mla{h}.pv{qt}.{jt}")
        p.op("vector", lambda e, qt=qt, pvb=pvb: e.tensor_copy(
            b["ctxh"][0:64, 128 * qt:128 * qt + 128], pvb),
            reads=[pvres], writes=["m_ctxh"], tag=f"mla{h}.ce{qt}")

    m_stage1(0)
    m_stage2a(0)
    for qt in range(1, NT):
        m_stage1(qt)
        m_stage2b(qt - 1)
        m_stage2a(qt)
    m_stage2b(NT - 1)
    xst = ctx["xstage"]
    for s in range(NS):
        sl = slice(512 * s, 512 * s + 512)
        p.op("tensor", lambda e, sl=sl: e.matmul(
            P2[half:half + 64, 0:512], b["woT"][:, :], b["ctxh"][0:64, sl],
            start=True, stop=True, tile_position=(0, half)),
            reads=["m_woT", "m_ctxh"], writes=["P0"], tag=f"mla{h}.wo{s}")
        slot = s % 4
        xa = xst[:, 512 * slot:512 * slot + 512]
        xr = f"xs{slot}"
        p.dma("sync", xa[half:half + 64, :], xsrc[c2 * 128 + half: c2 * 128 + half + 64, sl],
              xr, writes=[xr], reads=[f"xB{c2}.{s}"])
        p.op("vector", lambda e, xa=xa: e.tensor_add(
            xa[half:half + 64, :], xa[half:half + 64, :], P2[half:half + 64, 0:512]),
            reads=["P0", xr], writes=[xr], tag=f"mla{h}.xa{s}")
        p.dma("sync", xdst[c2 * 128 + half: c2 * 128 + half + 64, sl], xa[half:half + 64, :],
              xr, reads=[xr], writes=[f"xC{c2}.{s}.{half}"])


L1F_DEPTH = 6
DIM_INNER = 2730
DIM_PAD = 2816
NCH_H = DIM_PAD // 128

def prep_l1f(inp, T):
    out = {}
    Wqkv = inp["L1_ffn_Wqkv"]
    r = np.arange(512)
    c = r // 128; b = (r % 128) // 32; j = r % 32
    h = 4 * c + b
    qA = 64 * h + j; qB = 64 * h + 32 + j
    perm_qk = np.concatenate([qA, qB, 1024 + qA, 1024 + qB])
    qk = Wqkv[:, :, perm_qk]
    out["l1_wqk"] = np.ascontiguousarray(
        qk.reshape(6, 8, 128, 16, 128).transpose(0, 2, 3, 1, 4).reshape(6, 128, 16384))
    out["l1_wv"] = np.ascontiguousarray(Wqkv[:, :, 2048:3072])
    out["l1_wo"] = np.ascontiguousarray(
        inp["L1_ffn_Wo"].reshape(6, 8, 128, 8, 128).transpose(0, 2, 3, 1, 4)
        .reshape(6, 128, 8192))
    W1 = inp["L1_ffn_W1"]; b1 = inp["L1_ffn_b1"]
    W1a = np.zeros((6, 1024, DIM_PAD), np.float32); W1a[:, :, :DIM_INNER] = W1[:, :, :DIM_INNER]
    W1g = np.zeros((6, 1024, DIM_PAD), np.float32); W1g[:, :, :DIM_INNER] = W1[:, :, DIM_INNER:]
    b1a = np.zeros((6, DIM_PAD), np.float32); b1a[:, :DIM_INNER] = b1[:, :DIM_INNER]
    b1g = np.zeros((6, DIM_PAD), np.float32); b1g[:, :DIM_INNER] = b1[:, DIM_INNER:]
    w1i = np.zeros((6, 2 * NCH_H, 1024, 128), np.float32)
    b1i = np.zeros((6, 128, 2 * NCH_H), np.float32)
    for jc in range(NCH_H):
        w1i[:, 2 * jc] = W1a[:, :, 128 * jc:128 * jc + 128]
        w1i[:, 2 * jc + 1] = W1g[:, :, 128 * jc:128 * jc + 128]
        b1i[:, :, 2 * jc] = b1a[:, 128 * jc:128 * jc + 128]
        b1i[:, :, 2 * jc + 1] = b1g[:, 128 * jc:128 * jc + 128]
    out["l1_w1"] = np.ascontiguousarray(w1i)
    out["l1_b1"] = np.ascontiguousarray(b1i)
    W2 = inp["L1_ffn_W2"]
    W2p = np.zeros((6, DIM_PAD, 1024), np.float32); W2p[:, :DIM_INNER] = W2
    out["l1_w2"] = np.ascontiguousarray(
        W2p.reshape(6, DIM_PAD, 8, 128).transpose(0, 2, 1, 3))
    out["l1_b2"] = np.ascontiguousarray(
        inp["L1_ffn_b2"].reshape(6, 8, 128).transpose(0, 2, 1))
    out["l1_lng"] = np.ascontiguousarray(
        inp["L1_ffn_ln_g"].reshape(6, 8, 128).transpose(0, 2, 1))
    out["l1_lnb"] = np.ascontiguousarray(
        inp["L1_ffn_ln_b"].reshape(6, 8, 128).transpose(0, 2, 1))
    qs = inp["L1_ffn_q_scale"]; ks = inp["L1_ffn_k_scale"]
    def tile2(v):
        return np.stack([np.tile(v[:, 0:32], (1, 4)), np.tile(v[:, 32:64], (1, 4))],
                        axis=2)
    out["l1_qs"] = np.ascontiguousarray(tile2(qs))
    out["l1_ks"] = np.ascontiguousarray(tile2(ks))
    return out


def l1f_consts(nc, ctx, T):
    DIM_HEAD = 64; WINDOW = 256
    half = np.arange(0, DIM_HEAD, 2, dtype=np.float64)
    inv = 1.0 / 10000.0 ** (half / DIM_HEAD)
    t = np.arange(T, dtype=np.float64)
    freqs = t[:, None] * inv[None, :]
    base_scale = (half + 0.4 * DIM_HEAD) / (1.4 * DIM_HEAD)
    power = (t - T // 2) / (WINDOW // 2)
    scale = base_scale[None, :] ** power[:, None]
    cs = np.cos(freqs); sn = np.sin(freqs)
    jj = np.arange(128) % 32
    csq = (cs * scale * 8.0).T[jj]
    ssq = (sn * scale * 8.0).T[jj]
    cik = (cs / scale).T[jj]
    sik = (sn / scale).T[jj]
    NW = T // 256
    tp_ = np.stack([a.reshape(128, NW, 256) for a in (csq, ssq, cik, sik)], axis=2)
    ctx["tabpack_dram"] = nc.inline_tensor(
        np.ascontiguousarray(tp_.reshape(128, NW * 1024)).astype(BF), "tabpack").ap()
    m = np.zeros((2, 128, 512), np.float32)
    for pp in range(2):
        qi = np.arange(128)[:, None]
        jx = np.arange(512)[None, :]
        rel = 256 + 128 * pp + qi - jx
        m[pp] = np.where((rel >= 0) & (rel <= 256), 0.0, NEG)
    ctx["lmask_dram"] = nc.inline_tensor(
        np.ascontiguousarray(m.transpose(1, 0, 2).reshape(128, 1024)).astype(BF), "lmask").ap()
    ctx["lmask"] = nc.alloc_sbuf_tensor("lmasks", [128, 1024], BF16).ap()
    I = np.zeros((128, 4, 16), np.float32)
    for cc in range(4):
        for rr_ in range(128):
            I[rr_, cc, 4 * cc + rr_ // 32] = 1.0
    ctx["i16_dram"] = nc.inline_tensor(np.ascontiguousarray(I.reshape(128, 64)), "i16").ap()
    ctx["i16"] = nc.alloc_sbuf_tensor("i16s", [128, 64], F32).ap()
    E = np.zeros((16, 4, 128), np.float32)
    for cc in range(4):
        for p_ in range(128):
            E[4 * cc + p_ // 32, cc, p_] = 1.0
    ctx["e16c_dram"] = nc.inline_tensor(np.ascontiguousarray(E.reshape(16, 512)), "e16c").ap()
    ctx["e16c"] = nc.alloc_sbuf_tensor("e16cs", [16, 512], F32).ap()


def load_l1f_consts(p, ctx):
    p.dma("sync", ctx["lmask"][:, :], ctx["lmask_dram"][:, :], "lmask", writes=["lmask"])
    p.dma("sync", ctx["i16"][:, :], ctx["i16_dram"][:, :], "i16", writes=["i16"])
    p.dma("sync", ctx["e16c"][:, :], ctx["e16c_dram"][:, :], "e16c", writes=["e16c"])


def declare_l1f(nc, T):
    d = {}
    def dram(name, shape):
        d[name] = nc.dram_tensor(name, list(shape), F32, kind="ExternalInput").ap()
    dram("l1_wqk", (6, 128, 16384)); dram("l1_wv", (6, 1024, 1024))
    dram("l1_wo", (6, 128, 8192))
    dram("l1_w1", (6, 2 * NCH_H, 1024, 128)); dram("l1_b1", (6, 128, 2 * NCH_H))
    dram("l1_w2", (6, 8, DIM_PAD, 128)); dram("l1_b2", (6, 128, 8))
    dram("l1_lng", (6, 128, 8)); dram("l1_lnb", (6, 128, 8))
    dram("l1_qs", (6, 128, 2)); dram("l1_ks", (6, 128, 2))
    return d

def alloc_l1f_bufs(nc, ctx, T, xs_at=None):
    b = {}
    b["qkw"] = alloc_at(nc, "f_qkw", [128, 16 * 1024], BF16, RB0)
    b["vw1"] = alloc_at(nc, "f_vw1", [128, 8 * 1024], BF16, RB0 + 32 * 1024)
    b["wo1"] = alloc_at(nc, "f_wo1", [128, 8 * 1024], BF16, RB0 + 48 * 1024)
    b["kring"] = alloc_at(nc, "f_kring", [128, 2 * 2048], BF16, RB0 + 64 * 1024)
    b["vring"] = alloc_at(nc, "f_vring", [128, 2 * 2048], BF16, RB0 + 72 * 1024)
    b["ctxw"] = alloc_at(nc, "f_ctxw", [128, 2048], BF16, RB0 + 80 * 1024)
    b["wtab"] = alloc_at(nc, "f_wtab", [128, 2048], BF16, RB0 + 84 * 1024)
    b["zn2"] = alloc_at(nc, "f_zn2", [128, 8 * 2048], BF16, RB0)
    b["stat"] = alloc_at(nc, "f_stat", [1, 2 * 2048], F32, RB0 + 32 * 1024)
    b["hR"] = alloc_at(nc, "f_hR", [128, 6 * 2048], BF16, RB0 + 48 * 1024)
    b["w1s"] = [alloc_at(nc, f"f_w1s{i}", [128, 1024], BF16,
                         RB0 + 72 * 1024 + i * 2048 if i < 2 else RB0 + 82 * 1024 + (i - 2) * 2048)
                for i in range(4)]
    b["w2s"] = alloc_at(nc, "f_w2s", [128, NCH_H * 128], BF16, RB0 + 76 * 1024)
    b["hZ"] = alloc_at(nc, "f_hZ", [128, 16 * 2048], BF16, RZ0)
    if xs_at is None:
        b["zsw"] = nc.alloc_sbuf_tensor("f_zsw", [128, 2048], BF16).ap()
        b["qw"] = nc.alloc_sbuf_tensor("f_qw", [128, 2048], BF16).ap()
    else:
        b["zsw"] = alloc_at(nc, "f_zsw", [128, 2048], BF16, xs_at)
        b["qw"] = alloc_at(nc, "f_qw", [128, 2048], BF16, xs_at + 4096)
    b["ssw"] = nc.alloc_sbuf_tensor("f_ssw", [16, 256], F32).ap()
    b["rstdw"] = nc.alloc_sbuf_tensor("f_rstdw", [16, 256], F32).ap()
    b["rqt"] = nc.alloc_sbuf_tensor("f_rqt", [128, 32], F32).ap()
    b["tabst"] = nc.alloc_sbuf_tensor("f_tabst", [128, 1024], BF16).ap()
    b["qs"] = nc.alloc_sbuf_tensor("f_qs", [128, 4], F32).ap()
    b["b1"] = nc.alloc_sbuf_tensor("f_b1", [128, 2 * NCH_H], F32).ap()
    b["b2g"] = nc.alloc_sbuf_tensor("f_b2g", [128, 24], F32).ap()
    return b


def emit_l1f_layer(p, ctx, d, b, l, zbuf, zdram, T):
    NS = T // 512
    NW = T // 256
    S_ps = ctx["S_ps"]; P2 = ctx["P2_ps"]; TR = ctx["TR_ps"]
    scrA, scrB = ctx["scrA"], ctx["scrB"]
    st2 = ctx["stats2"]
    Pb = ctx["Pbuf"]; Pb2 = ctx["Pbuf2"]; ptsb = ctx["ptsb"]
    SCRA = [f"scrA{i}" for i in range(NS)]
    G = T + 1

    def zap(c, t0, n, shift=0):
        o = c * G + 1 + t0 - shift
        return zbuf[:, o:o + n]

    for fc_ in range(16):
        p.dma("gpsimd", b["qkw"][:, 1024 * fc_:1024 * fc_ + 1024],
              d["l1_wqk"][l][:, 1024 * fc_:1024 * fc_ + 1024],
              f"f_qkw{fc_}", writes=[f"f_qkw.{fc_}"])
    for dk_ in range(8):
        p.dma("gpsimd", b["vw1"][:, 1024 * dk_:1024 * dk_ + 1024],
              d["l1_wv"][l][128 * dk_:128 * dk_ + 128, :],
              f"f_vw1{dk_}", writes=[f"f_vw1.{dk_}"])
    for e_ in range(8):
        p.dma("gpsimd", b["wo1"][:, 1024 * e_:1024 * e_ + 1024],
              d["l1_wo"][l][:, 1024 * e_:1024 * e_ + 1024],
              f"f_wo1{e_}", writes=[f"f_wo1.{e_}"])
    p.dma("sync", b["qs"][:, 0:2], d["l1_qs"][l], "f_qs", writes=["f_qs"])
    p.dma("sync", b["qs"][:, 2:4], d["l1_ks"][l], "f_qs", writes=["f_qs"])

    for wi in range(NW):
        t0 = 256 * wi
        ks_ = wi % 2
        kp_ = (wi + 1) % 2
        j0 = 256 if wi == 0 else 0
        for dk in range(8):
            sh = 1 if dk >= 4 else 0
            p.op("vector", lambda e, dk=dk, sh=sh, t0=t0: e.tensor_copy(
                b["zsw"][:, 256 * dk:256 * dk + 256], zap(dk, t0, 256, sh)),
                reads=[f"z{dk}"], writes=["f_zsw"], tag=f"f{l}.zs{wi}.{dk}")
        p.dma("sync", b["tabst"][:, :], ctx["tabpack_dram"][:, 1024 * wi:1024 * wi + 1024],
              "f_tabst", writes=["f_tabst"])
        for ki in range(4):
            qcol = 0 if ki < 2 else 2
            for half_ in (0, 1):
                dst = b["wtab"][:, 256 * (2 * ki + half_):256 * (2 * ki + half_) + 256]
                p.op("vector", lambda e, ki=ki, dst=dst, qcol=qcol, half_=half_:
                     e.tensor_scalar_mul(dst, b["tabst"][:, 256 * ki:256 * ki + 256],
                                         b["qs"][:, qcol + half_:qcol + half_ + 1]),
                     reads=["f_tabst", "f_qs"], writes=["f_wtab"], tag=f"f{l}.tab{wi}.{ki}.{half_}")
        def wtap(idx):
            return b["wtab"][:, 256 * idx:256 * idx + 256]
        CSQA, CSQB = wtap(0), wtap(1)
        SSQA, SSQB = wtap(2), wtap(3)
        CIKA, CIKB = wtap(4), wtap(5)
        SIKA, SIKB = wtap(6), wtap(7)
        def qk_chunks(base_fc, dst_scr, dres):
            for fc in range(8):
                bank = fc % 2
                pa = S_ps[:, 512 * bank:512 * bank + 256]
                for dk in range(8):
                    p.op("tensor", lambda e, fc=fc, dk=dk, pa=pa, base_fc=base_fc: e.matmul(
                        pa, b["qkw"][:, 1024 * (base_fc + fc) + 128 * dk:
                                     1024 * (base_fc + fc) + 128 * dk + 128],
                        b["zsw"][:, 256 * dk:256 * dk + 256],
                        start=(dk == 0), stop=(dk == 7)),
                        reads=[f"f_qkw.{base_fc + fc}", "f_zsw"], writes=[f"S{bank}"],
                        tag=f"f{l}.qk{wi}.{base_fc + fc}")
                p.op("vector", lambda e, fc=fc, pa=pa: e.tensor_copy(
                    dst_scr[:, 256 * fc:256 * fc + 256], pa),
                    reads=[f"S{bank}"], writes=[dres], tag=f"f{l}.qke{wi}.{base_fc + fc}")
        qk_chunks(0, scrA, "scrA0")
        p.op("scalar", lambda e: e.activation(scrB[:, 0:2048], scrA[:, 0:2048], AF.Square),
             reads=["scrA0"], writes=["scrB"], tag=f"f{l}.qsq{wi}")
        for fc in range(8):
            p.op("tensor", lambda e, fc=fc: e.matmul(
                P2[0:16, 0:256], ctx["i16"][:, 16 * (fc % 4):16 * (fc % 4) + 16],
                scrB[:, 256 * fc:256 * fc + 256], start=(fc == 0), stop=(fc == 7)),
                reads=["scrB", "i16"], writes=["P0"], tag=f"f{l}.qss{wi}.{fc}")
        p.op("vector", lambda e: e.tensor_copy(b["ssw"][:, 0:256], P2[0:16, 0:256]),
             reads=["P0"], writes=["f_ssw"], tag=f"f{l}.qsse{wi}")
        p.op("scalar", lambda e: e.activation(b["rstdw"][:, :], b["ssw"][:, 0:256], AF.Sqrt),
             reads=["f_ssw"], writes=["f_rstdw"], tag=f"f{l}.qsrt{wi}")
        p.op("vector", lambda e: e.tensor_scalar_max(b["rstdw"][:, :], b["rstdw"][:, :], 1e-12),
             reads=["f_rstdw"], writes=["f_rstdw"], tag=f"f{l}.qmx{wi}")
        p.op("vector", lambda e: e.reciprocal(b["rstdw"][:, :], b["rstdw"][:, :]),
             reads=["f_rstdw"], writes=["f_rstdw"], tag=f"f{l}.qrc{wi}")
        for pp in range(2):
            p.op("tensor", lambda e, pp=pp: e.transpose(
                TR[:, 1024 * pp:1024 * pp + 16].bitcast(F32) if False else
                S_ps[0:128, 768 + 16 * pp: 784 + 16 * pp],
                b["rstdw"][:, 128 * pp:128 * pp + 128], ctx["identf"][0:16, 0:16]),
                reads=["f_rstdw", "identf"], writes=["S1"], tag=f"f{l}.qrt{wi}.{pp}")
            p.op("vector", lambda e, pp=pp: e.tensor_copy(
                b["rqt"][:, 16 * pp:16 * pp + 16], S_ps[0:128, 768 + 16 * pp:784 + 16 * pp]),
                reads=["S1"], writes=["f_rqt"], tag=f"f{l}.qrte{wi}.{pp}")
        for c in range(4):
            qA = scrA[:, 256 * c:256 * c + 256]
            qB = scrA[:, 256 * (4 + c):256 * (4 + c) + 256]
            m1 = scrB[:, 0:256]; m2 = scrB[:, 256:512]
            m3 = scrB[:, 512:768]; m4 = scrB[:, 768:1024]
            p.op("vector", lambda e, qA=qA, m1=m1, CSQA=CSQA: e.tensor_mul(m1, qA, CSQA),
                 reads=["scrA0", "f_wtab"], writes=["scrB"], tag=f"f{l}.qr1{wi}.{c}")
            p.op("vector", lambda e, qB=qB, m2=m2, SSQB=SSQB: e.tensor_mul(m2, qB, SSQB),
                 reads=["scrA0", "f_wtab"], writes=["scrB"], tag=f"f{l}.qr2{wi}.{c}")
            p.op("vector", lambda e, c=c, m1=m1, m2=m2: e.tensor_sub(
                b["qw"][:, 256 * c:256 * c + 256], m1, m2),
                reads=["scrB"], writes=["f_qw"], tag=f"f{l}.qr3{wi}.{c}")
            p.op("vector", lambda e, qB=qB, m3=m3, CSQB=CSQB: e.tensor_mul(m3, qB, CSQB),
                 reads=["scrA0", "f_wtab"], writes=["scrB"], tag=f"f{l}.qr4{wi}.{c}")
            p.op("vector", lambda e, qA=qA, m4=m4, SSQA=SSQA: e.tensor_mul(m4, qA, SSQA),
                 reads=["scrA0", "f_wtab"], writes=["scrB"], tag=f"f{l}.qr5{wi}.{c}")
            p.op("vector", lambda e, c=c, m3=m3, m4=m4: e.tensor_add(
                b["qw"][:, 256 * (4 + c):256 * (4 + c) + 256], m3, m4),
                reads=["scrB"], writes=["f_qw"], tag=f"f{l}.qr6{wi}.{c}")
        qk_chunks(8, scrA, "scrA0")
        p.op("scalar", lambda e: e.activation(scrB[:, 0:2048], scrA[:, 0:2048], AF.Square),
             reads=["scrA0"], writes=["scrB"], tag=f"f{l}.ksq{wi}")
        for fc in range(8):
            p.op("tensor", lambda e, fc=fc: e.matmul(
                P2[0:16, 0:256], ctx["i16"][:, 16 * (fc % 4):16 * (fc % 4) + 16],
                scrB[:, 256 * fc:256 * fc + 256], start=(fc == 0), stop=(fc == 7)),
                reads=["scrB", "i16"], writes=["P0"], tag=f"f{l}.kss{wi}.{fc}")
        p.op("vector", lambda e: e.tensor_copy(b["ssw"][:, 0:256], P2[0:16, 0:256]),
             reads=["P0"], writes=["f_ssw"], tag=f"f{l}.ksse{wi}")
        p.op("scalar", lambda e: e.activation(b["rstdw"][:, :], b["ssw"][:, 0:256], AF.Sqrt),
             reads=["f_ssw"], writes=["f_rstdw"], tag=f"f{l}.ksrt{wi}")
        p.op("vector", lambda e: e.tensor_scalar_max(b["rstdw"][:, :], b["rstdw"][:, :], 1e-12),
             reads=["f_rstdw"], writes=["f_rstdw"], tag=f"f{l}.kmx{wi}")
        p.op("vector", lambda e: e.reciprocal(b["rstdw"][:, :], b["rstdw"][:, :]),
             reads=["f_rstdw"], writes=["f_rstdw"], tag=f"f{l}.krc{wi}")
        for c in range(4):
            p.op("tensor", lambda e, c=c: e.matmul(
                P2[:, 0:256], ctx["e16c"][:, 128 * c:128 * c + 128], b["rstdw"][:, :],
                start=True, stop=True),
                reads=["f_rstdw", "e16c"], writes=["P0"], tag=f"f{l}.kbc{wi}.{c}")
            kA = scrA[:, 256 * c:256 * c + 256]
            kB = scrA[:, 256 * (4 + c):256 * (4 + c) + 256]
            m1 = scrB[:, 0:256]; m2 = scrB[:, 256:512]
            m3 = scrB[:, 512:768]; m4 = scrB[:, 768:1024]
            d1 = scrB[:, 1024:1280]; d2 = scrB[:, 1280:1536]
            ksl = b["kring"][:, 2048 * ks_:2048 * ks_ + 2048]
            p.op("vector", lambda e, kA=kA, m1=m1, CIKA=CIKA: e.tensor_mul(m1, kA, CIKA),
                 reads=["scrA0", "f_wtab"], writes=["scrB"], tag=f"f{l}.kr1{wi}.{c}")
            p.op("vector", lambda e, kB=kB, m2=m2, SIKB=SIKB: e.tensor_mul(m2, kB, SIKB),
                 reads=["scrA0", "f_wtab"], writes=["scrB"], tag=f"f{l}.kr2{wi}.{c}")
            p.op("vector", lambda e, d1=d1, m1=m1, m2=m2: e.tensor_sub(d1, m1, m2),
                 reads=["scrB"], writes=["scrB"], tag=f"f{l}.kr3{wi}.{c}")
            p.op("vector", lambda e, c=c, d1=d1, ksl=ksl: e.tensor_mul(
                ksl[:, 256 * c:256 * c + 256], d1, P2[:, 0:256]),
                reads=["scrB", "P0"], writes=[f"f_kr{ks_}"], tag=f"f{l}.kr3b{wi}.{c}")
            p.op("vector", lambda e, kB=kB, m3=m3, CIKB=CIKB: e.tensor_mul(m3, kB, CIKB),
                 reads=["scrA0", "f_wtab"], writes=["scrB"], tag=f"f{l}.kr4{wi}.{c}")
            p.op("vector", lambda e, kA=kA, m4=m4, SIKA=SIKA: e.tensor_mul(m4, kA, SIKA),
                 reads=["scrA0", "f_wtab"], writes=["scrB"], tag=f"f{l}.kr5{wi}.{c}")
            p.op("vector", lambda e, d2=d2, m3=m3, m4=m4: e.tensor_add(d2, m3, m4),
                 reads=["scrB"], writes=["scrB"], tag=f"f{l}.kr6{wi}.{c}")
            p.op("vector", lambda e, c=c, d2=d2, ksl=ksl: e.tensor_mul(
                ksl[:, 256 * (4 + c):256 * (4 + c) + 256], d2, P2[:, 0:256]),
                reads=["scrB", "P0"], writes=[f"f_kr{ks_}"], tag=f"f{l}.kr6b{wi}.{c}")
        for ti in range(2):
            for half2 in range(2):
                pa = P2[:, 512 * half2:512 * half2 + 512]
                for dk in range(8):
                    p.op("tensor", lambda e, ti=ti, dk=dk, half2=half2, pa=pa: e.matmul(
                        pa, b["zsw"][:, 256 * dk + 128 * ti:256 * dk + 128 * ti + 128],
                        b["vw1"][:, 1024 * dk + 512 * half2:1024 * dk + 512 * half2 + 512],
                        start=(dk == 0), stop=(dk == 7)),
                        reads=["f_zsw", f"f_vw1.{dk}"], writes=[f"P{half2}"],
                        tag=f"f{l}.v{wi}.{ti}.{half2}")
                p.op("vector", lambda e, ti=ti, half2=half2, pa=pa, ks_=ks_: e.tensor_copy(
                    b["vring"][:, 2048 * ks_ + 1024 * ti + 512 * half2:
                               2048 * ks_ + 1024 * ti + 512 * half2 + 512], pa),
                    reads=[f"P{half2}"], writes=[f"f_vr{ks_}"],
                    tag=f"f{l}.ve{wi}.{ti}.{half2}")
        def f_stage1(h, pp):
            bb = 32 * (h % 4)
            cc = h // 4
            tp = (bb, 0)
            sb = (2 * h + pp) % 4
            g = (2 * h + pp) % 4
            SS = S_ps[:, 512 * sb:512 * sb + 512]
            negm = st2[:, 4 * g:4 * g + 1]
            srcs = []
            if wi > 0:
                srcs.append((b["kring"], kp_, 0))
            srcs.append((b["kring"], ks_, 256))
            for (kr, slot, joff) in srcs:
                kAp = kr[bb:bb + 32, 2048 * slot + 256 * cc: 2048 * slot + 256 * cc + 256]
                kBp = kr[bb:bb + 32, 2048 * slot + 256 * (4 + cc): 2048 * slot + 256 * (4 + cc) + 256]
                qAp = b["qw"][bb:bb + 32, 256 * cc + 128 * pp: 256 * cc + 128 * pp + 128]
                qBp = b["qw"][bb:bb + 32, 256 * (4 + cc) + 128 * pp: 256 * (4 + cc) + 128 * pp + 128]
                p.op("tensor", lambda e, SS=SS, qAp=qAp, kAp=kAp, joff=joff, tp=tp: e.matmul(
                    SS[:, joff:joff + 256], qAp, kAp, start=True, stop=False,
                    tile_position=tp),
                    reads=["f_qw", f"f_kr{ks_}" if joff else f"f_kr{kp_}"],
                    writes=[f"S{sb}"], tag=f"f{l}.sA{wi}.{h}.{pp}")
                p.op("tensor", lambda e, SS=SS, qBp=qBp, kBp=kBp, joff=joff, tp=tp: e.matmul(
                    SS[:, joff:joff + 256], qBp, kBp, start=False, stop=True,
                    tile_position=tp),
                    reads=["f_qw", f"f_kr{ks_}" if joff else f"f_kr{kp_}"],
                    writes=[f"S{sb}"], tag=f"f{l}.sB{wi}.{h}.{pp}")
            p.op("vector", lambda e, SS=SS, h=h, pp=pp, j0=j0: e.scalar_tensor_tensor(
                SS[:, j0:512], SS[:, j0:512], b["rqt"][:, 16 * pp + h:16 * pp + h + 1],
                ctx["lmask"][:, 512 * pp + j0:512 * pp + 512],
                op0=ALU.mult, op1=ALU.add),
                reads=[f"S{sb}", "f_rqt", "lmask"], writes=[f"S{sb}"],
                tag=f"f{l}.msk{wi}.{h}.{pp}")
            p.op("vector", lambda e, SS=SS, j0=j0, negm=negm: e.reduce_max(
                negm, SS[:, j0:512], axis=AX.X, negate=True),
                reads=[f"S{sb}"], writes=[f"negm{g}"], tag=f"f{l}.mx{wi}.{h}.{pp}")

        def f_stage2(h, pp):
            sb = (2 * h + pp) % 4
            g = (2 * h + pp) % 4
            g2 = (2 * h + pp) % 2
            SS = S_ps[:, 512 * sb:512 * sb + 512]
            negm = st2[:, 4 * g:4 * g + 1]
            sums = st2[:, 4 * g + 1:4 * g + 2]
            rsum = st2[:, 4 * g + 2:4 * g + 3]
            Pba = Pb[:, 1024 * g2:1024 * g2 + 512]
            Pba2 = Pb2[:, 1024 * g2:1024 * g2 + 512]
            pvb = P2[64 * (h % 2):64 * (h % 2) + 64,
                     (256 if g2 == 0 else 768):(256 if g2 == 0 else 768) + 128]
            pvres = "P0" if g2 == 0 else "P1"
            p.op("scalar", lambda e, SS=SS, j0=j0, negm=negm, sums=sums, Pba=Pba:
                 e.activation(Pba[:, j0:512], SS[:, j0:512], AF.Exp, bias=negm,
                              accum_out=sums),
                 reads=[f"S{sb}", f"negm{g}"], writes=[f"P.{g2}", f"sums{g}"],
                 tag=f"f{l}.ex{wi}.{h}.{pp}")
            p.op("vector", lambda e, sums=sums, rsum=rsum: e.reciprocal(rsum, sums),
                 reads=[f"sums{g}"], writes=[f"rsum{g}"], tag=f"f{l}.rc{wi}.{h}.{pp}")
            p.op("vector", lambda e, j0=j0, Pba=Pba, Pba2=Pba2, rsum=rsum:
                 e.tensor_scalar_mul(Pba2[:, j0:512], Pba[:, j0:512], rsum),
                 reads=[f"P.{g2}", f"rsum{g}"], writes=[f"P2b.{g2}"],
                 tag=f"f{l}.pm{wi}.{h}.{pp}")
            njt0 = j0 // 128
            for jt in range(njt0, 4):
                sl2 = jt % 2
                trof = 1024 * sl2
                p.op("tensor", lambda e, jt=jt, trof=trof, Pba2=Pba2: e.transpose(
                    TR[:, trof:trof + 128], Pba2[:, 128 * jt:128 * jt + 128],
                    ctx["ident"]),
                    reads=[f"P2b.{g2}", "cident"], writes=[f"TR{sl2}"],
                    tag=f"f{l}.tr{wi}.{h}.{pp}.{jt}")
                p.op("vector" if jt % 2 == 0 else "scalar",
                     lambda e, jt=jt, trof=trof:
                     (e.tensor_copy if hasattr(e, "tensor_copy") else e.copy)(
                    ptsb[:, 128 * jt:128 * jt + 128], TR[:, trof:trof + 128]),
                    reads=[f"TR{sl2}"], writes=[f"pt{jt}"],
                    tag=f"f{l}.tre{wi}.{h}.{pp}.{jt}")
            for jt in range(njt0, 4):
                slot = kp_ if jt < 2 else ks_
                ti = jt % 2
                p.op("tensor", lambda e, jt=jt, slot=slot, ti=ti, h=h, njt0=njt0,
                     pvb=pvb: e.matmul(
                    pvb,
                    b["vring"][:, 2048 * slot + 1024 * ti + 64 * h:
                               2048 * slot + 1024 * ti + 64 * h + 64],
                    ptsb[:, 128 * jt:128 * jt + 128],
                    start=(jt == njt0), stop=(jt == 3),
                    tile_position=(0, 64 * (h % 2))),
                    reads=[f"f_vr{slot}", f"pt{jt}"], writes=[pvres],
                    tag=f"f{l}.pv{wi}.{h}.{pp}.{jt}")
            p.op("vector", lambda e, h=h, pp=pp, pvb=pvb: e.tensor_copy(
                b["ctxw"][64 * (h % 2):64 * (h % 2) + 64,
                          256 * (h // 2) + 128 * pp:256 * (h // 2) + 128 * pp + 128],
                pvb),
                reads=[pvres], writes=[f"f_cw{h // 2}"],
                tag=f"f{l}.ce{wi}.{h}.{pp}")

        tiles = [(h, pp) for h in range(16) for pp in range(2)]
        prev = None
        for t_ in tiles:
            f_stage1(*t_)
            if prev is not None:
                f_stage2(*prev)
            prev = t_
        f_stage2(*prev)
        for e_ in range(8):
            sb = e_ % 2
            pa = S_ps[:, 512 * sb:512 * sb + 256]
            for ic in range(8):
                p.op("tensor", lambda e, e_=e_, ic=ic, pa=pa: e.matmul(
                    pa, b["wo1"][:, 1024 * e_ + 128 * ic:1024 * e_ + 128 * ic + 128],
                    b["ctxw"][:, 256 * ic:256 * ic + 256],
                    start=(ic == 0), stop=(ic == 7)),
                    reads=[f"f_wo1.{e_}", f"f_cw{ic}"], writes=[f"S{sb}"],
                    tag=f"f{l}.wo{wi}.{e_}")
            p.op("vector", lambda e, e_=e_, t0=t0, pa=pa: e.tensor_add(
                zap(e_, t0, 256), zap(e_, t0, 256), pa),
                reads=[f"S{sb}", f"z{e_}"], writes=[f"z{e_}"],
                tag=f"f{l}.za{wi}.{e_}")

    p.fence([], ['f_qkw.0', 'f_qkw.1', 'f_qkw.2', 'f_qkw.3', 'f_qkw.4', 'f_qkw.5', 'f_qkw.6', 'f_qkw.7', 'f_qkw.8', 'f_qkw.9', 'f_qkw.10', 'f_qkw.11', 'f_qkw.12', 'f_qkw.13', 'f_qkw.14', 'f_qkw.15', 'f_vw1.0', 'f_vw1.1', 'f_vw1.2', 'f_vw1.3', 'f_vw1.4', 'f_vw1.5', 'f_vw1.6', 'f_vw1.7', 'f_wo1.0', 'f_wo1.1', 'f_wo1.2', 'f_wo1.3', 'f_wo1.4', 'f_wo1.5', 'f_wo1.6', 'f_wo1.7'] + ["stat.mu", "stat.rs",
                 "f_kr0", "f_kr1", "f_vr0", "f_vr1", "f_wtab",
                 "f_w1s0", "f_w1s1", "f_w1s2", "f_w1s3", "f_w2s.0", "f_w2s.1", "f_zsw", "f_qw",
                 "xs0", "xs1", "xs2", "xs3"]
            + [f"f_cw{i}" for i in range(8)]
            + [f"f_zn{c}" for c in range(8)]
            + [f"h{k}" for k in range(NCH_H)])
    stat = b["stat"]
    mu = stat[0:1, 0:T]
    rs = stat[0:1, T:2 * T]
    ones_f = ctx["ones_f32"]
    for c in range(8):
        sh = 1 if c >= 4 else 0
        p.op("scalar", lambda e, c=c, sh=sh: e.activation(
            scrA[:, 0:T], zap(c, 0, T, sh), AF.Square),
            reads=[f"z{c}"], writes=SCRA, tag=f"f{l}.lsq{c}")
        for s in range(NS):
            sl = slice(512 * s, 512 * s + 512)
            p.op("tensor", lambda e, c=c, s=s, sh=sh, sl=sl: e.matmul(
                S_ps[0:1, sl], ones_f[:, 0:1], zap(c, 512 * s, 512, sh),
                start=(c == 0), stop=(c == 7)),
                reads=[f"z{c}", "cones"], writes=[f"S{s}"], tag=f"f{l}.lsum{c}.{s}")
            p.op("tensor", lambda e, sl=sl, c=c: e.matmul(
                S_ps[32:33, sl], ones_f[:, 0:1], scrA[:, sl], start=(c == 0), stop=(c == 7)),
                reads=[f"scrA{s}"] + (["cones"] if True else []), writes=[f"S{s}"],
                tag=f"f{l}.lsqs{c}.{s}")
    Sres = [f"S{s}" for s in range(NS)]
    p.op("vector", lambda e: e.tensor_scalar_mul(mu, S_ps[0:1, 0:T], 1.0 / 1024.0),
         reads=Sres, writes=["stat.mu"], tag=f"f{l}.lmu")
    p.op("vector", lambda e: e.tensor_mul(scrA[0:1, 0:T], mu, mu),
         reads=["stat.mu"] + SCRA, writes=SCRA, tag=f"f{l}.lm2")
    p.op("vector", lambda e: e.scalar_tensor_tensor(
        scrA[0:1, 0:T], S_ps[32:33, 0:T], 1.0 / 1024.0, scrA[0:1, 0:T],
        op0=ALU.mult, op1=ALU.subtract),
        reads=Sres + SCRA, writes=SCRA, tag=f"f{l}.lvar")
    p.op("scalar", lambda e: e.activation(rs, scrA[0:1, 0:T], AF.Sqrt, bias=ctx["cv"](1e-5, 1)),
         reads=SCRA + ["cv"], writes=["stat.rs"], tag=f"f{l}.lsqrt")
    p.op("vector", lambda e: e.reciprocal(rs, rs),
         reads=["stat.rs"], writes=["stat.rs"], tag=f"f{l}.lrecip")
    p.dma("sync", b["b2g"][:, 0:8], d["l1_b2"][l], "f_b2g", writes=["f_b2g"])
    p.dma("sync", b["b2g"][:, 8:16], d["l1_lng"][l], "f_b2g", writes=["f_b2g"])
    p.dma("sync", b["b2g"][:, 16:24], d["l1_lnb"][l], "f_b2g", writes=["f_b2g"])
    p.dma("sync", b["b1"][:, :], d["l1_b1"][l], "f_b1", writes=["f_b1"])
    for c in range(8):
        sh = 1 if c >= 4 else 0
        for s in range(NS):
            sl = slice(512 * s, 512 * s + 512)
            p.op("tensor", lambda e, sl=sl: e.matmul(
                P2[:, 0:512], ones_f[0:1, 0:128], mu[:, sl], start=True, stop=True),
                reads=["stat.mu", "cones"], writes=["P0"], tag=f"f{l}.lbmu{c}.{s}")
            p.op("tensor", lambda e, sl=sl: e.matmul(
                P2[:, 512:1024], ones_f[0:1, 0:128], rs[:, sl], start=True, stop=True),
                reads=["stat.rs", "cones"], writes=["P1"], tag=f"f{l}.lbrs{c}.{s}")
            p.op("vector", lambda e, c=c, s=s, sh=sh: e.scalar_tensor_tensor(
                scrA[:, 512 * s:512 * s + 512], zap(c, 512 * s, 512, sh), 1.0,
                P2[:, 0:512], op0=ALU.mult, op1=ALU.subtract),
                reads=[f"z{c}", "P0"], writes=[f"scrA{s}"], tag=f"f{l}.lap1{c}.{s}")
            p.op("vector", lambda e, c=c, s=s: e.tensor_mul(
                scrB[:, 512 * s:512 * s + 512], scrA[:, 512 * s:512 * s + 512],
                P2[:, 512:1024]),
                reads=[f"scrA{s}", "P1"], writes=["scrB"], tag=f"f{l}.lap2{c}.{s}")
            p.op("vector", lambda e, c=c, s=s: e.tensor_scalar(
                b["zn2"][:, T * c + 512 * s:T * c + 512 * s + 512],
                scrB[:, 512 * s:512 * s + 512],
                b["b2g"][:, 8 + c:9 + c], b["b2g"][:, 16 + c:17 + c],
                op0=ALU.mult, op1=ALU.add),
                reads=["scrB", "f_b2g"], writes=[f"f_zn{c}"], tag=f"f{l}.lap3{c}.{s}")
    for c in range(8):
        p.dma("sync", zdram[128 * c:128 * c + 128, :], zap(c, 0, T), f"zsp{c}",
              reads=[f"z{c}"], writes=[f"zd{c}"])
    p.fence([], [f"z{c}" for c in range(8)] + [f"h{k}" for k in range(NCH_H)])
    def hap(jc, sl):
        if jc < 16:
            return b["hZ"][:, T * jc + sl.start:T * jc + sl.stop]
        return b["hR"][:, T * (jc - 16) + sl.start:T * (jc - 16) + sl.stop]
    for jc in range(NCH_H):
        par = jc % 2
        for ag in range(2):
            slw = 2 * par + ag
            p.dma("gpsimd", b["w1s"][slw][:, :].rearrange("p (c f) -> p c f", f=128),
                  d["l1_w1"][l, 2 * jc + ag].rearrange("(c p) f -> p c f", p=128),
                  f"f_w1s{slw}", writes=[f"f_w1s{slw}"])
        for s in range(NS):
            sl = slice(512 * s, 512 * s + 512)
            for ag in range(2):
                slw = 2 * par + ag
                pa = P2[:, 0:512] if ag == 0 else P2[:, 512:1024]
                bank = "P0" if ag == 0 else "P1"
                for dk in range(8):
                    p.op("tensor", lambda e, dk=dk, sl=sl, pa=pa, slw=slw: e.matmul(
                        pa, b["w1s"][slw][:, 128 * dk:128 * dk + 128],
                        b["zn2"][:, T * dk + sl.start:T * dk + sl.stop],
                        start=(dk == 0), stop=(dk == 7)),
                        reads=[f"f_w1s{slw}", f"f_zn{dk}"], writes=[bank],
                        tag=f"f{l}.w1.{jc}.{ag}.{s}")
            p.op("scalar", lambda e, jc=jc, s=s: e.activation(
                scrB[:, 512 * s:512 * s + 512], P2[:, 512:1024], AF.Gelu,
                bias=b["b1"][:, 2 * jc + 1:2 * jc + 2]),
                reads=["P1", "f_b1"], writes=["scrB"], tag=f"f{l}.gg{jc}.{s}")
            p.op("vector", lambda e, jc=jc, s=s, sl=sl: e.scalar_tensor_tensor(
                hap(jc, sl), P2[:, 0:512], b["b1"][:, 2 * jc:2 * jc + 1],
                scrB[:, 512 * s:512 * s + 512], op0=ALU.add, op1=ALU.mult),
                reads=["P0", "f_b1", "scrB"], writes=[f"h{jc}"],
                tag=f"f{l}.hm{jc}.{s}")
    for e_ in range(8):
        p.dma("gpsimd", b["w2s"][:, 0:11 * 128].rearrange("p (c f) -> p c f", f=128),
              d["l1_w2"][l, e_][0:11 * 128].rearrange("(c p) f -> p c f", p=128),
              "f_w2sa", writes=["f_w2s.0"])
        p.dma("gpsimd", b["w2s"][:, 11 * 128:NCH_H * 128].rearrange("p (c f) -> p c f", f=128),
              d["l1_w2"][l, e_][11 * 128:NCH_H * 128].rearrange("(c p) f -> p c f", p=128),
              "f_w2sb", writes=["f_w2s.1"])
        for s in range(NS):
            sl = slice(512 * s, 512 * s + 512)
            sb = s % 4
            for k in range(NCH_H):
                p.op("tensor", lambda e, k=k, s=s, sl=sl, sb=sb: e.matmul(
                    S_ps[:, 512 * sb:512 * sb + 512], b["w2s"][:, 128 * k:128 * k + 128],
                    hap(k, sl), start=(k == 0), stop=(k == NCH_H - 1)),
                    reads=[f"f_w2s.{0 if k < 11 else 1}", f"h{k}"], writes=[f"S{sb}"],
                    tag=f"f{l}.w2.{e_}.{s}")
            slot = (e_ * NS + s) % 4
            xa = ctx["xstage"][:, 512 * slot:512 * slot + 512]
            xr = f"xs{slot}"
            p.dma("sync", xa, zdram[128 * e_:128 * e_ + 128, sl], xr,
                  reads=[f"zd{e_}"], writes=[xr])
            p.op("vector", lambda e, xa=xa, sb=sb, e_=e_: e.scalar_tensor_tensor(
                xa, S_ps[:, 512 * sb:512 * sb + 512], b["b2g"][:, e_:e_ + 1], xa,
                op0=ALU.add, op1=ALU.add),
                reads=[f"S{sb}", "f_b2g", xr], writes=[xr], tag=f"f{l}.w2a{e_}.{s}")
            p.dma("sync", zdram[128 * e_:128 * e_ + 128, sl], xa, xr,
                  reads=[xr], writes=[f"zd{e_}"])

def emit_l1f(p, ctx, d, b, zbuf, zdram, T):
    G = T + 1
    for c in range(8):
        p.op("vector", lambda e, c=c: e.memset(zbuf[:, c * G:c * G + 1], 0.0),
             reads=[], writes=[f"z{c}"], tag=f"f.guard{c}")
    SCRA = [f"scrA{i}" for i in range(T // 512)]
    for l in range(L1F_DEPTH):
        emit_l1f_layer(p, ctx, d, b, l, zbuf, zdram, T)
        if l < L1F_DEPTH - 1:
            p.fence([], [f"h{k}" for k in range(NCH_H)] + [f"z{c}" for c in range(8)]
                    + SCRA + ["scrB", "f_wtab"] + ['f_qkw.0', 'f_qkw.1', 'f_qkw.2', 'f_qkw.3', 'f_qkw.4', 'f_qkw.5', 'f_qkw.6', 'f_qkw.7', 'f_qkw.8', 'f_qkw.9', 'f_qkw.10', 'f_qkw.11', 'f_qkw.12', 'f_qkw.13', 'f_qkw.14', 'f_qkw.15', 'f_vw1.0', 'f_vw1.1', 'f_vw1.2', 'f_vw1.3', 'f_vw1.4', 'f_vw1.5', 'f_vw1.6', 'f_vw1.7', 'f_wo1.0', 'f_wo1.1', 'f_wo1.2', 'f_wo1.3', 'f_wo1.4', 'f_wo1.5', 'f_wo1.6', 'f_wo1.7'] + [
                              "f_kr0", "f_kr1", "f_vr0", "f_vr1",
                              "f_w1s0", "f_w1s1", "f_w1s2", "f_w1s3", "f_w2s.0", "f_w2s.1", "stat.mu", "stat.rs",
                              "f_zsw", "f_qw", "xs0", "xs1", "xs2", "xs3"]
                    + [f"f_cw{i}" for i in range(8)]
                    + [f"f_zn{c}" for c in range(8)])
            for c in range(8):
                p.op("vector", lambda e, c=c: e.memset(zbuf[:, c * G:c * G + 1], 0.0),
                     reads=[], writes=[f"z{c}"], tag=f"f.g{l}.{c}")
                p.dma("sync", zbuf[:, c * G + 1:c * G + 1 + T],
                      zdram[128 * c:128 * c + 128, :], f"zrl{c}",
                      reads=[f"zd{c}"], writes=[f"z{c}"])


def emit_final_combine(p, ctx, zdram, xsrc, xdst, T):
    NS = T // 512
    xst = ctx["xstage"]
    scrB = ctx["scrB"]
    p.fence([], ["scrB", "xs0", "xs1", "fc_z0", "fc_z1"])
    for c in range(8):
        for s in range(NS):
            sl = slice(512 * s, 512 * s + 512)
            slot = (c * NS + s) % 2
            xa = xst[:, 512 * slot:512 * slot + 512]
            xr = f"xs{slot}"
            za = scrB[:, 512 * (2 + slot):512 * (2 + slot) + 512]
            zr = f"fc_z{slot}"
            p.dma("sync", xa, xsrc[128 * c:128 * c + 128, sl], xr, writes=[xr],
                  reads=[f"xC{c}.{s}.0", f"xC{c}.{s}.64"])
            p.dma("sync", za, zdram[128 * c:128 * c + 128, sl], zr,
                  reads=[f"zd{c}"], writes=[zr])
            p.op("vector", lambda e, xa=xa, za=za: e.tensor_add(xa, xa, za),
                 reads=[xr, zr], writes=[xr], tag=f"fc.add{c}.{s}")
            p.dma("sync", xdst[128 * c:128 * c + 128, sl], xa, xr, reads=[xr])


T_FULL = 2048

MLA_RES = ["m_qc", "m_kc", "m_vf", "m_vtokh", "m_ctxh", "m_wdq", "m_wuq",
           "m_wuqT", "m_wdkv", "m_wukv", "m_wukvT", "m_mq", "m_mkv", "m_wqrT",
           "m_wkrT", "m_woT", "m_wn", "m_ss", "m_rstd", "oh64", "e16", "diagm"]
L1F_RES = ([f"f_qkw.{i}" for i in range(16)] + [f"f_vw1.{i}" for i in range(8)]
           + [f"f_wo1.{i}" for i in range(8)]
           + ["f_wtab", "f_kr0", "f_kr1", "f_vr0",
            "f_vr1", "f_w1s0", "f_w1s1", "f_w1s2", "f_w1s3", "f_w2s.0", "f_w2s.1", "f_zsw", "f_qw", "f_ssw",
            "f_rstdw", "f_rqt", "f_tabst", "f_qs", "f_b1", "f_b2g"]
           + [f"f_cw{i}" for i in range(8)] + [f"f_zn{c}" for c in range(8)])

def _build_fused(T):
    nc = bass.Bass("TRN2", target_bir_lowering=False, debug=False)
    xin = nc.dram_tensor("xin", [1024, T], F32, kind="ExternalInput").ap()
    xout = nc.dram_tensor("xout", [1024, T], F32, kind="ExternalOutput").ap()
    xmA = nc.dram_tensor("xmA", [1024, T], F32, kind="Internal").ap()
    x1 = nc.dram_tensor("x1", [1024, T], F32, kind="Internal").ap()
    x2 = nc.dram_tensor("x2", [1024, T], F32, kind="Internal").ap()
    zdram = nc.dram_tensor("zdram", [1024, T], F32, kind="Internal").ap()
    d = declare_l0(nc, T)
    d.update(declare_mla(nc, T))
    d.update(declare_l1f(nc, T))
    RS0 = RZ0 - 8192
    ctx = make_ctx(nc, T, xstage_at=RS0)
    mla_consts(nc, ctx)
    l1f_consts(nc, ctx, T)
    stat0 = alloc_at(nc, "stat0", [1, 2 * T], F32, RB0)
    bufs = {}
    bufs["xsrc"] = xin; bufs["xdst"] = xmA
    bufs["xsrc2"] = xmA; bufs["xdst2"] = x1
    bufs["xnb"] = alloc_at(nc, "xnb", [128, 8 * T], BF16, RZ0)
    bufs["vw"] = alloc_at(nc, "vwb", [128, 8 * 256], BF16, RZ0 + 16 * T)
    bufs["ropec"] = alloc_at(nc, "ropec", [128, T], F32, RZ0 + 16 * T + 4096)
    bufs["ropes"] = alloc_at(nc, "ropes", [128, T], F32, RZ0 + 16 * T + 4096 + 4 * T)
    bufs["qrope"] = alloc_at(nc, "qrope", [128, 8 * T], BF16, RB0)
    bufs["krope"] = alloc_at(nc, "krope", [128, 2 * T], BF16, RB0 + 16 * T)
    bufs["vtok"] = alloc_at(nc, "vtok", [128, 2 * T], BF16, RB0 + 20 * T)
    bufs["scrC"] = alloc_at(nc, "scrC", [128, T], F32, RB0 + 20 * T)
    bufs["ctxT"] = alloc_at(nc, "ctxT", [128, 8 * T], BF16, RB0 + 24 * T)
    bufs["hbuf"] = alloc_at(nc, "hbuf", [128, 32 * 1024], BF16, RB0)
    bufs["pbalt"] = alloc_at(nc, "l0_pbalt", [128, T], BF16, RB0 + 80 * 1024)
    bufs["wslots"] = [alloc_at(nc, f"wsl{i}", [128, 1024], BF16, RB0 + 84 * 1024 + 2048 * i)
                      for i in range(2)]
    bufs["w2slot"] = alloc_at(nc, "w2slot", [128, 32 * 128], BF16, RB0 + 64 * 1024)
    bufs["b1sb"] = nc.alloc_sbuf_tensor("b1sb", [128, 32], F32).ap()
    bufs["b2sb"] = nc.alloc_sbuf_tensor("b2sb", [128, 8], F32).ap()
    zbuf = alloc_at(nc, "zbuf", [128, 8 * (T + 1)], F32, RZ0)
    mb = alloc_mla_bufs(nc, ctx, T, RB0)
    fb = alloc_l1f_bufs(nc, ctx, T, xs_at=RS0)

    p = Prog(nc)
    load_consts(p, ctx)
    load_l1f_consts(p, ctx)
    emit_ln_to_bf16(p, ctx, xin, stat0, bufs["xnb"], "a", "0")
    p.fence([], ["stat.mu", "stat.rs"] + [f"qr{i}" for i in range(8)])
    emit_l0_attn(p, ctx, d, bufs)
    p.fence([f"qr{i}" for i in range(8)] + ["kr0", "kr1", "vtok"] + [f"ct{i}" for i in range(8)],
            [f"h{k}" for k in range(32)] + ["w2s"])
    emit_l0_ffn(p, ctx, d, bufs)
    p.fence([], [f"a{c}" for c in range(8)] + ["vw", "ropec", "ropes",
                 "stat.mu", "stat.rs"] + [f"z{c}" for c in range(8)]
            + [f"h{k}" for k in range(32)])
    emit_ln_to_bf16(p, ctx, x1, stat0, None, "z", "z", fp32dst=(zbuf, T + 1, 1),
                    src_res=lambda c, s: [f"xB{c}.{s}"])
    p.fence([], MLA_RES + ["stat.mu", "stat.rs", "scrC", "P.0", "P.1", "P2b", "w2s"]
            + [f"qr{i}" for i in range(8)] + ["kr0", "kr1", "vtok"]
            + [f"ct{i}" for i in range(8)] + [f"h{k}" for k in range(32)])
    load_mla_consts(p, ctx)
    emit_mla_setup(p, ctx, d, mb)
    for h in range(16):
        c2 = h // 2
        half = 64 * (h % 2)
        zap = zbuf[half:half + 64, c2 * (T + 1) + 1: c2 * (T + 1) + 1 + T]
        emit_mla_head(p, ctx, mb, h, zap, f"z{c2}", x1, x2, first_for_chunk=True)
    p.fence([], L1F_RES + MLA_RES + ["stat.mu", "stat.rs", "ws0", "ws1", "w2s",
                                      "xs0", "xs1", "xs2", "xs3", "P.0", "P.1", "P2b"])
    emit_l1f(p, ctx, d, fb, zbuf, zdram, T)
    emit_final_combine(p, ctx, zdram, x2, xout, T)
    p.barrier_all_dma()
    p.emit()
    check_sbuf(nc, RS0)
    return nc


def kernel(**inputs):
    T = T_FULL
    x = np.asarray(inputs["L0_attn_x"])
    B = x.shape[0]
    dall = prep_l0(inputs, T)
    dall.update(prep_mla(inputs))
    dall.update(prep_l1f(inputs, T))
    nc = _build_fused(T)
    maps = [{**dall, "xin": np.ascontiguousarray(x[b].T)} for b in range(B)]
    res = run_bass_kernel_spmd(nc, maps, core_ids=list(range(B)))
    out = np.stack([np.asarray(res.results[b]["xout"]).T for b in range(B)])
    return out.astype(np.float32)
